# version 01
# baseline (speedup 1.0000x reference)
import contextlib
import numpy as np
import ml_dtypes
import concourse.bass as bass
import concourse.mybir as mybir
from concourse.bass import IndirectOffsetOnAxis
from concourse.bass_utils import run_bass_kernel_spmd

F32 = mybir.dt.float32
BF16 = mybir.dt.bfloat16
I32 = mybir.dt.int32
AF = mybir.ActivationFunctionType
ALU = mybir.AluOpType
AX = mybir.AxisListType

NB, TC, D, NCH = 8, 512, 1024, 8
SEG = TC + 1
NTOK = NB * TC
NTILE = 32
MAGIC = 1.5 * 2.0 ** 23
RMS_EPS = 1e-8
LN_EPS = 1e-5
ENGINES = ("sync", "vector", "scalar", "tensor", "gpsimd")
NCORE = 8


class Sem:
    def __init__(self, h):
        self.h, self.n = h, 0


AUTO_SYNC = ("vector", "scalar", "gpsimd")


class Ctx:
    def __init__(self):
        self.q = {e: [] for e in ENGINES}
        self.sems = {}

    def emit(self, eng, fn):
        if eng in AUTO_SYNC:
            sem = self.sems[eng]
            prev = sem.n
            sem.n += 1
            def wrapped(e, f=fn, s=sem.h, prev=prev):
                if prev > 0:
                    e.wait_ge(s, prev)
                f(e).then_inc(s, 1)
            self.q[eng].append(wrapped)
        else:
            self.q[eng].append(fn)

    def sig(self, eng, k=1):
        if eng in AUTO_SYNC:
            return self.sems[eng].n
        sem = self.sems[eng]
        f = self.q[eng].pop()
        sem.n += k
        self.q[eng].append(lambda e, f=f, s=sem.h, k=k: f(e).then_inc(s, k))
        return sem.n

    def wait(self, eng, other, val=None):
        sem = self.sems[other]
        v = sem.n if val is None else val
        if v > 0:
            self.q[eng].append(lambda e, s=sem.h, v=v: e.wait_ge(s, v))

    def barrier(self):
        vals = {e: self.sems[e].n for e in ENGINES}
        for e1 in ENGINES:
            for e2 in ENGINES:
                if e1 != e2 and vals[e2] > 0:
                    self.q[e1].append(lambda e, s=self.sems[e2].h, v=vals[e2]: e.wait_ge(s, v))
            for sx in self.dma_sems:
                if sx.n > 0:
                    self.q[e1].append(lambda e, s=sx.h, v=sx.n: e.wait_ge(s, v))


def build_program():
    nc = bass.Bass("TRN2", target_bir_lowering=False, debug=False)
    C = Ctx()
    st = contextlib.ExitStack()
    din = lambda n, s, d=F32: nc.dram_tensor(n, s, d, kind="ExternalInput").ap()
    dou = lambda n, s, d=F32: nc.dram_tensor(n, s, d, kind="ExternalOutput").ap()
    dit = lambda n, s, d=F32: nc.dram_tensor(n, s, d, kind="Internal").ap()
    dsh = lambda n, s, d=F32: nc.dram_tensor(n, s, d, kind="Internal", addr_space="Shared").ap()
    sb = lambda n, s, d=F32: st.enter_context(nc.sbuf_tensor(n, s, d))
    psm = lambda n, s, d=F32: st.enter_context(nc.psum_tensor(n, s, d))
    E, SIG, WT, BAR = C.emit, C.sig, C.wait, C.barrier

    x_in = din("x_in", [NB, SEG, D])
    w_mats = [din(f"w_{m}", [D, D]) for m in ("key", "value", "recep", "output")]
    rms_in = [din(f"rms_{m}", [D]) for m in ("key", "value", "recep", "output")]
    tmix_in = [din(f"tmix_{m}", [D]) for m in ("k", "v", "r")]
    tdecay = din("tdecay", [D]); tfirst = din("tfirst", [D])
    rankm = din("rankm", [NCORE])
    rankw = din("rankw", [NCORE])
    wcat_in = din("wcat_in", [16, 128, 12288])
    shg_in = din("shg_in", [128, NCH, D])
    shu_in = din("shu_in", [128, NCH, D])
    shd_in = din("shd_in", [128, NCH, D])
    rw_in = din("rw_in", [16, D])
    ebias_in = din("ebias_in", [16])

    kv_spill = dit("kv_spill", [2, NCH, 128, NTOK], F32)
    cc_in = dit("cc_in", [128, 128], F32)
    cc_out = dit("cc_out", [NCORE, 128, 128], F32)
    xn_dram = dit("xn_dram", [NTILE + 1, 128, D], BF16)
    xnf_dram = dit("xnf_dram", [128, NCH, NTOK], BF16)
    wq_dram = dit("wq_dram", [4, 128, NCH * D], BF16)
    r_dram = dit("r_dram", [128, NCH, NTOK], BF16)
    s0_dram = dit("s0_dram", [NTILE, 128, D], F32)
    wcat_bf = dit("wcat_bf", [16, 128, 12288], BF16)
    NSLOT = 48 * 512
    slotbuf = dit("slotbuf", [NSLOT + 1, D], BF16)
    yobuf = dit("yobuf", [NSLOT + 1, D], BF16)
    x1_dram = dit("x1_dram", [NTILE, 128, D], F32)

    wqT_in = din("wqT_in", [128, NCH, D])
    wkT_in = din("wkT_in", [128, NCH, D])
    wvT_in = din("wvT_in", [128, NCH, D])
    pe_in = din("pe_in", [88, 128, 16])
    selc_in = din("selc_in", [8, 128, 16], BF16)
    rsel_in = din("rsel_in", [16, NCH, 128], BF16)
    xn1_dram = dit("xn1_dram", [NTILE, 128, D], BF16)
    xnf1_dram = dit("xnf1_dram", [128, NCH, NTOK], BF16)
    s1_dram = dit("s1_dram", [8, TC, D], BF16)
    rw1T_in = din("rw1T_in", [128, NCH, 4096])
    rw2T_in = din("rw2T_in", [32, 128, 8])
    rb1_in = din("rb1_in", [128, 32])
    rb2_in = din("rb2_in", [8, 1])
    trib8_in = din("trib8_in", [128, 64])
    mskc512_in = din("mskc512_in", [1, 512])
    corebase_in = din("corebase_in", [1, 1])
    ew1T_in = din("ew1T_in", [128, NCH, 4096])
    ew2T_in = din("ew2T_in", [128, 32, D])
    eb1_in = din("eb1_in", [128, 32])
    eb2_in = din("eb2_in", [128, D])
    xn1_ag = dit("xn1_ag", [NCORE, NTILE * 128, D], BF16)
    cnt2_in = dit("cnt2_in", [64, 1], F32)
    cnt2_out = dit("cnt2_out", [NCORE, 64, 1], F32)
    dispT = dit("dispT", [40960, 2], F32)
    disp_rs = dit("disp_rs", [5120, 2], F32)
    ew1_bf = dit("ew1_bf", [128, NCH, 4096], BF16)
    ew2_bf = dit("ew2_bf", [128, 32, D], BF16)
    h_dram = dit("h_dram", [8, 32, 128, TC], BF16)
    yo_dram = dit("yo_dram", [NTOK, D], BF16)
    yo_ag = dit("yo_ag", [NCORE, NTOK, D], BF16)
    out_dram = dou("out_dram", [NTILE, 128, D], F32)

    io_col = sb("io_col", [128, 128]); io_row = sb("io_row", [128, 128])
    ident = sb("ident", [128, 128], BF16)
    identf = sb("identf", [128, 128])
    onesNf = sb("onesNf", [1, 128])
    ones4 = sb("ones4", [4, 128])
    onesKb = sb("onesKb", [128, 1], BF16)
    zcol = sb("zcol", [128, 1])
    rmsv = sb("rmsv", [128, 4 * NCH])
    tmixv = sb("tmixv", [128, 3 * NCH])
    tmixm1 = sb("tmixm1", [128, 3 * NCH])
    xnb_sb = sb("xnb_sb", [128, NCH, NB], BF16)
    tdec = sb("tdec", [128, NCH]); tfir = sb("tfir", [128, NCH])
    expu = sb("expu", [128, NCH])
    wlog = sb("wlog", [128, NCH])
    Dmat = sb("Dmat", [128, NCH, TC])
    powm = sb("powm", [128, NCH, TC])
    kv_t = sb("kv_t", [128, 2, NCH, TC])
    wq_sb = sb("wq_sb", [128, 2, NCH, D], BF16)
    inv_sw = sb("inv_sw", [128, 4])
    sw_rep = sb("sw_rep", [128, 4])
    wstage = sb("wstage", [128, NCH, D], BF16)
    d_sh = wstage[:, :, 0:TC]
    m_t = wstage[:, :, TC:2*TC]
    wabs = sb("wabs", [128, 2])
    wabs_bf = sb("wabs_bf", [128, 1], BF16)
    wm_sb = sb("wm_sb", [1, 1])

    xn_st = sb("xn_st", [128, 2, D], BF16)
    xnf_blk = sb("xnf_blk", [128, 2, NCH, 128], BF16)
    xnf_seg = sb("xnf_seg", [128, NCH, TC], BF16)
    x_tile = sb("x_tile", [128, 2, D])
    sqf = sb("sqf", [128, 2, D])
    s1_all = sb("s1_all", [128, NTILE + 1]); s2_all = sb("s2_all", [128, NTILE + 1])
    rstd_all = sb("rstd_all", [128, NTILE + 1]); nmu_all = sb("nmu_all", [128, NTILE + 1])

    mix_t = sb("mix_t", [128, NCH, TC], BF16)
    camax = sb("camax", [128, TC]); csum = sb("csum", [128, TC])
    stat_tm = sb("stat_tm", [128, 8])
    rs_tm = sb("rs_tm", [128, 4]); isx_tm = sb("isx_tm", [128, 4])
    stag4 = sb("stag4", [128, 128])
    rhs512 = sb("rhs512", [128, 512])
    onesF = sb("onesF", [128, 128])
    rs_fm = sb("rs_fm", [128, TC]); isx_fm = sb("isx_fm", [128, TC])
    rnd_f = sb("rnd_f", [128, TC])
    csum2 = rnd_f
    xq_t = sb("xq_t", [128, NCH, TC], BF16)

    r_seg = sb("r_seg", [128, NCH, TC], BF16)
    e_t = sb("e_t", [128, TC]); ev_t = sb("ev_t", [128, TC])
    a_scan = sb("a_scan", [128, TC]); b_scan = sb("b_scan", [128, TC])
    scan_ar = sb("scan_ar", [128, 4, TC])
    eu_t = scan_ar[:, 0, :]; evu_t = scan_ar[:, 1, :]
    n1_t = scan_ar[:, 2, :]; dd_t = scan_ar[:, 3, :]
    gin_t = mix_t
    aend = sb("aend", [128, 2 * NCH * NB])
    carry = sb("carry", [128, 2 * NCH * NB])
    ccg = sb("ccg", [128, NCORE, 128])
    rk_sb = sb("rk_sb", [1, 2 * NCORE])
    rkrep = sb("rkrep", [128, 2 * NCORE])
    powr = sb("powr", [128, NCORE, NCH])
    s0_seg = kv_t[:, 0, :, :].rearrange("p c t -> p (c t)").rearrange("p (s d) -> p s d", s=4)

    scores_tm = scan_ar[:, 0, :].rearrange("p (i e) -> p i e", e=16)
    S_tm = scan_ar[:, 1, :].rearrange("p (i e) -> p i e", e=16)
    cw_tm = scan_ar[:, 2, :].rearrange("p (i e) -> p i e", e=16)
    rnk_sc = rs_fm[:].rearrange("p (i e) -> p i e", e=16)
    rank16 = isx_fm[:].rearrange("p (i e) -> p i e", e=16)
    xq_f32 = xq_t[:].rearrange("p c t -> p (c t)").bitcast(F32)
    ones128s = xq_f32[:, 0:1024]
    cmp_sc = xq_f32[:, 1024:1536].rearrange("p (i e) -> p i e", e=16)
    tri16b = xq_f32[:, 1536:1792].rearrange("p (i j) -> p i j", j=16)
    iota16 = xq_f32[:, 1792:1808]
    tri4b = xq_f32[:, 1808:1824].rearrange("p (i j) -> p i j", j=4)
    gsc4 = csum[:, 0:128].rearrange("p (i g) -> p i g", g=4)
    grk4 = csum[:, 128:256].rearrange("p (i g) -> p i g", g=4)
    powm_f32 = powm[:].rearrange("p c t -> p (c t)")
    powm_i32 = powm[:].rearrange("p c t -> p (c t)").bitcast(I32)
    powm_bf = powm[:].rearrange("p c t -> p (c t)").bitcast(BF16)
    tri16a = powm_f32[:, 3072:3328].rearrange("p (i j) -> p i j", j=16)
    tri4a = powm_f32[:, 3328:3344].rearrange("p (i j) -> p i j", j=4)
    ebias_rep = powm_f32[:, 3344:3360]
    rwT = powm_bf[:, 6784:6912].rearrange("p (c e) -> p c e", e=16)
    rw_sb = wstage[0:16, :, :].rearrange("p c d -> p (c d)").bitcast(F32)[:, 1024:2048]
    rw_bf = mix_t[0:16, :, :].rearrange("p c t -> p (c t)")[:, 2048:3072]
    rseg_f32 = r_seg[:].rearrange("p c t -> p (c t)").bitcast(F32)
    Me128 = rseg_f32[:, 0:1024]; pos128 = rseg_f32[:, 1024:2048]
    segsum = powm_f32[:, 3400:3401]; segoff = powm_f32[:, 3401:3402]
    strip1 = Dmat[0:1, :, :].rearrange("p c t -> p (c t)")[:, 0:656]
    sjf = powm_f32[:, 3424:3552].rearrange("p (i j) -> p i j", j=4)
    sji = powm_i32[:, 3728:3856].rearrange("p (i j) -> p i j", j=4)
    gj_tm = powm_f32[:, 3552:3680].rearrange("p (i j) -> p i j", j=4)
    texp_f = powm_f32[:, 3680:3728]
    widx = powm_i32[:, 3856:3904]
    wbuf = kv_t[:].rearrange("p a c t -> p (a c t)").bitcast(BF16)[:, 0:12288]
    xgF = powm[:].rearrange("p c t -> p (c t)").bitcast(BF16)[:, 0:4096].rearrange("p (c t) -> p c t", t=TC)
    h_rt = powm[:].rearrange("p c t -> p (c t)").bitcast(BF16)[:, 4096:6144].rearrange("p (c t) -> p c t", t=TC)
    h_sh = kv_t[:].rearrange("p a c t -> p (a c t)").bitcast(BF16)[:, 12288:16384].rearrange("p (c t) -> p c t", t=TC)
    xg_tm = wstage[:, 0:4, :]
    g0acc = Dmat[:].rearrange("p c t -> p (c t)")[:, 0:4096].rearrange("p (a d) -> p a d", d=D)
    scan_bf = scan_ar[:].rearrange("p a t -> p (a t)").bitcast(BF16)
    peT = scan_bf[:, 0:1408].rearrange("p (j h) -> p j h", h=16)
    selT = scan_bf[:, 1408:1536].rearrange("p (c h) -> p c h", h=16)
    rselT = e_t[0:16, :].bitcast(BF16)[:, 0:1024].rearrange("p (c d) -> p c d", d=128)
    psA = psm("psA", [128, 512]); psB = psm("psB", [128, 512])
    psT = psm("psT", [128, 128], BF16); psU = psm("psU", [128, 128], BF16)
    psTf = psm("psTf", [128, 128])
    psR = psm("psR", [128, 512])
    psS = psR[0:1, :]
    psW = psm("psW", [128, 1024])

    for e in ENGINES:
        C.sems[e] = Sem(st.enter_context(nc.semaphore("sem_" + e)))
    s_ld = Sem(st.enter_context(nc.semaphore("s_ld")))
    s_st = Sem(st.enter_context(nc.semaphore("s_st")))
    C.dma_sems = (s_ld, s_st)
    block = st.enter_context(nc.Block())

    def DMAL(out, in_, **kw):
        s_ld.n += 16
        C.q["sync"].append(lambda e, out=out, in_=in_, kw=kw: e.dma_start(out=out, in_=in_, **kw).then_inc(s_ld.h, 16))
        return s_ld.n

    def DMAS(out, in_, **kw):
        s_st.n += 16
        C.q["sync"].append(lambda e, out=out, in_=in_, kw=kw: e.dma_start(out=out, in_=in_, **kw).then_inc(s_st.h, 16))
        return s_st.n

    def WL(eng, val):
        C.q[eng].append(lambda e, v=val: e.wait_ge(s_ld.h, v))

    def WS(eng, val):
        C.q[eng].append(lambda e, v=val: e.wait_ge(s_st.h, v))

    twk = kv_t[:, 0, :, :]
    E("gpsimd", lambda e: e.iota(out=io_col[:], pattern=[[1, 128]], base=0, channel_multiplier=0, allow_small_or_imprecise_dtypes=True))
    E("gpsimd", lambda e: e.iota(out=io_row[:], pattern=[[0, 128]], base=0, channel_multiplier=1, allow_small_or_imprecise_dtypes=True))
    E("gpsimd", lambda e: e.iota(out=twk.rearrange("p c t -> p (c t)"), pattern=[[0, NCH], [1, TC]], base=0, channel_multiplier=0, allow_small_or_imprecise_dtypes=True))
    SIG("gpsimd")
    for vec, dst in (
        (rms_in[0], rmsv[:, 0:NCH]), (rms_in[1], rmsv[:, NCH:2*NCH]),
        (rms_in[2], rmsv[:, 2*NCH:3*NCH]), (rms_in[3], rmsv[:, 3*NCH:4*NCH]),
        (tmix_in[0], tmixv[:, 0:NCH]), (tmix_in[1], tmixv[:, NCH:2*NCH]), (tmix_in[2], tmixv[:, 2*NCH:3*NCH]),
        (tdecay, tdec[:]), (tfirst, tfir[:]),
    ):
        DMAL(dst, vec.rearrange("(c p) -> p c", p=128), allow_slow_non_contiguous=True)
    DMAL(rk_sb[0:1, 0:NCORE], rankm[None, :], allow_slow_non_contiguous=True)
    DMAL(rk_sb[0:1, NCORE:2*NCORE], rankw[None, :], allow_slow_non_contiguous=True)
    WL("vector", s_ld.n)
    WT("vector", "gpsimd")
    E("vector", lambda e: e.tensor_tensor(out=ident[:], in0=io_col[:], in1=io_row[:], op=ALU.is_equal))
    E("vector", lambda e: e.tensor_tensor(out=identf[:], in0=io_col[:], in1=io_row[:], op=ALU.is_equal))
    E("vector", lambda e: e.memset(onesNf[:], 1.0))
    E("vector", lambda e: e.memset(ones4[:], 1.0))
    E("vector", lambda e: e.memset(onesF[:], 1.0))
    E("vector", lambda e: e.memset(stag4[:], 0.0))
    E("vector", lambda e: e.memset(rhs512[:], 0.0))
    E("vector", lambda e: e.memset(onesKb[:], 1.0))
    E("vector", lambda e: e.memset(zcol[:], 0.0))
    E("vector", lambda e: e.tensor_scalar(out=tmixm1[:], in0=tmixv[:], scalar1=-1.0, scalar2=None, op0=ALU.add))
    SIG("vector")
    WL("scalar", s_ld.n)
    E("scalar", lambda e: e.activation(out=wlog[:], in_=tdec[:], func=AF.Exp))
    E("scalar", lambda e: e.activation(out=expu[:], in_=tfir[:], func=AF.Exp))
    SIG("scalar")
    WT("vector", "scalar")
    E("vector", lambda e: e.tensor_scalar(out=wlog[:], in0=wlog[:], scalar1=-1.0, scalar2=None, op0=ALU.mult))
    SIG("vector")
    for c in range(NCH):
        E("vector", lambda e, c=c: e.tensor_scalar(out=twk[:, c, :], in0=twk[:, c, :], scalar1=wlog[:, c:c+1], scalar2=None, op0=ALU.mult))
    SIG("vector")
    WT("scalar", "vector")
    E("scalar", lambda e: e.activation(out=powm[:].rearrange("p c t -> p (c t)"), in_=twk.rearrange("p c t -> p (c t)"), func=AF.Exp))
    for c in range(NCH):
        E("scalar", lambda e, c=c: e.activation(out=Dmat[:, c, :], in_=twk[:, c, 1:2].broadcast_to([128, TC]), func=AF.Exp))
    SIG("scalar")
    BAR()

    for wi in range(4):
        wsrc = w_mats[wi]
        for i in range(NCH):
            if i >= 2:
                WT("sync", "vector")
            tl = DMAL(x_tile[:, i % 2, :], wsrc[128*i:128*(i+1), :])
            WL("vector", tl)
            if i == 0:
                E("vector", lambda e: e.tensor_reduce(out=wabs[:, 0:1], in_=x_tile[:, 0, :], axis=AX.X, op=ALU.add, apply_absolute_value=True))
            else:
                E("vector", lambda e, i=i: e.tensor_reduce(out=wabs[:, 1:2], in_=x_tile[:, i % 2, :], axis=AX.X, op=ALU.add, apply_absolute_value=True))
                E("vector", lambda e: e.tensor_add(out=wabs[:, 0:1], in0=wabs[:, 0:1], in1=wabs[:, 1:2]))
            SIG("vector")
        E("vector", lambda e: e.tensor_copy(out=wabs_bf[:], in_=wabs[:, 0:1]))
        SIG("vector")
        BAR()
        E("tensor", lambda e: e.matmul(psS[0:1, 0:1], wabs_bf[:], onesKb[:], start=True, stop=True))
        SIG("tensor")
        WT("vector", "tensor")
        E("vector", lambda e: e.tensor_scalar(out=wm_sb[0:1, 0:1], in0=psS[0:1, 0:1], scalar1=1.0/(D*D), scalar2=1e-5, op0=ALU.mult, op1=ALU.max))
        SIG("vector")
        WT("tensor", "vector")
        E("tensor", lambda e: e.matmul(psR[:, 0:1], onesNf[:], wm_sb[0:1, 0:1], start=True, stop=True))
        SIG("tensor")
        WT("vector", "tensor")
        E("vector", lambda e, wi=wi: e.tensor_copy(out=inv_sw[:, wi:wi+1], in_=psR[:, 0:1]))
        E("vector", lambda e, wi=wi: e.reciprocal(out=sw_rep[:, wi:wi+1], in_=psR[:, 0:1]))
        SIG("vector")
        for i in range(NCH):
            WT("sync", "vector")
            tl = DMAL(x_tile[:, i % 2, :], wsrc[128*i:128*(i+1), :])
            WL("vector", tl)
            E("vector", lambda e, i=i, wi=wi: e.tensor_scalar(out=sqf[:, i % 2, :], in0=x_tile[:, i % 2, :], scalar1=sw_rep[:, wi:wi+1], scalar2=MAGIC, op0=ALU.mult, op1=ALU.add))
            E("vector", lambda e, i=i: e.tensor_scalar(out=sqf[:, i % 2, :], in0=sqf[:, i % 2, :], scalar1=MAGIC, scalar2=None, op0=ALU.subtract))
            E("vector", lambda e, i=i: e.tensor_scalar(out=wstage[:, i, :], in0=sqf[:, i % 2, :], scalar1=1.0, scalar2=-1.0, op0=ALU.min, op1=ALU.max))
            SIG("vector")
        BAR()
        slot = wi % 2
        for c in range(NCH):
            for dd2 in range(NCH):
                pp = psT if (c * NCH + dd2) % 2 == 0 else psU
                if c * NCH + dd2 >= 2:
                    WT("tensor", "scalar", C.sems["scalar"].n - 1)
                E("tensor", lambda e, c=c, dd2=dd2, pp=pp: e.transpose(pp[:], wstage[:, dd2, 128*c:128*(c+1)], ident[:]))
                tm = SIG("tensor")
                WT("scalar", "tensor", tm)
                E("scalar", lambda e, c=c, dd2=dd2, pp=pp, slot=slot: e.copy(out=wq_sb[:, slot, c, 128*dd2:128*(dd2+1)], in_=pp[:]))
                SIG("scalar")
        WT("sync", "scalar")
        DMAS(wq_dram[wi, :, :], wq_sb[:, slot, :, :].rearrange("p c d -> p (c d)"))
        BAR()

    def x_src_dst(i):
        if i < NTILE:
            b = i // 4; toff = (i % 4) * 128
            return x_in[b, 1+toff:1+toff+128, :], x_tile[:, i % 2, :]
        return x_in[:, 0, :], x_tile[0:NB, i % 2, :]

    for i in range(NTILE + 1):
        src, dst = x_src_dst(i)
        if i >= 2:
            WT("sync", "scalar"); WT("sync", "vector")
        tl = DMAL(dst, src)
        WL("scalar", tl)
        E("scalar", lambda e, i=i: e.activation(out=sqf[:, i % 2, :], in_=x_tile[:, i % 2, :], func=AF.Square, accum_out=s2_all[:, i:i+1]))
        SIG("scalar")
        WL("vector", tl)
        E("vector", lambda e, i=i: e.reduce_sum(out=s1_all[:, i:i+1], in_=x_tile[:, i % 2, :], axis=AX.X))
        SIG("vector")
        BAR()
    BAR()
    E("vector", lambda e: e.tensor_scalar(out=s1_all[:], in0=s1_all[:], scalar1=1.0/D, scalar2=None, op0=ALU.mult))
    E("vector", lambda e: e.tensor_scalar(out=s2_all[:], in0=s2_all[:], scalar1=1.0/D, scalar2=None, op0=ALU.mult))
    SIG("vector")
    WT("scalar", "vector")
    E("scalar", lambda e: e.activation(out=rstd_all[:], in_=s1_all[:], func=AF.Square))
    SIG("scalar")
    WT("vector", "scalar")
    E("vector", lambda e: e.tensor_sub(out=rstd_all[:], in0=s2_all[:], in1=rstd_all[:]))
    E("vector", lambda e: e.tensor_scalar(out=rstd_all[:], in0=rstd_all[:], scalar1=LN_EPS, scalar2=None, op0=ALU.add))
    SIG("vector")
    WT("scalar", "vector")
    E("scalar", lambda e: e.activation(out=rstd_all[:], in_=rstd_all[:], func=AF.Sqrt))
    SIG("scalar")
    WT("vector", "scalar")
    E("vector", lambda e: e.reciprocal(out=rstd_all[:], in_=rstd_all[:]))
    E("vector", lambda e: e.tensor_scalar(out=nmu_all[:], in0=s1_all[:], scalar1=-1.0, scalar2=None, op0=ALU.mult))
    E("vector", lambda e: e.tensor_tensor(out=nmu_all[:], in0=nmu_all[:], in1=rstd_all[:], op=ALU.mult))
    SIG("vector")
    WT("sync", "vector")
    BAR()
    xn_store_t = {}
    xnf_store_t = {}
    for i in range(NTILE + 1):
        src, dst = x_src_dst(i)
        if i >= 2:
            WT("sync", "scalar")
        tl = DMAL(dst, src)
        WL("scalar", tl)
        if i >= 2:
            WS("scalar", xn_store_t[i - 2])
        E("scalar", lambda e, i=i: e.activation(out=xn_st[:, i % 2, :], in_=x_tile[:, i % 2, :], func=AF.Identity, bias=nmu_all[:, i:i+1], scale=rstd_all[:, i:i+1]))
        tap = SIG("scalar")
        WT("sync", "scalar", tap)
        xn_store_t[i] = DMAS(xn_dram[i, :, :], xn_st[:, i % 2, :])
        WT("tensor", "scalar", tap)
        for c in range(NCH):
            pp = psT if c % 2 == 0 else psU
            if i > 0 or c >= 2:
                WT("tensor", "scalar", C.sems["scalar"].n - 1)
            E("tensor", lambda e, i=i, c=c, pp=pp: e.transpose(pp[:], xn_st[:, i % 2, 128*c:128*(c+1)], ident[:]))
            tm = SIG("tensor")
            WT("scalar", "tensor", tm)
            if i < NTILE:
                if c == 0 and i >= 2:
                    WS("scalar", xnf_store_t[i - 2])
                E("scalar", lambda e, i=i, c=c, pp=pp: e.copy(out=xnf_blk[:, i % 2, c, :], in_=pp[:]))
            else:
                E("scalar", lambda e, c=c, pp=pp: e.copy(out=xnb_sb[:, c, :], in_=pp[:, 0:NB]))
            SIG("scalar")
        WT("sync", "scalar")
        if i < NTILE:
            b = i // 4; toff = (i % 4) * 128
            xnf_store_t[i] = DMAS(xnf_dram[:, :, b*TC+toff : b*TC+toff+128], xnf_blk[:, i % 2, :, :])
    BAR()

    def quant_chain(rms_col, src_of_chunk):
        for c in range(NCH):
            E("vector", lambda e, c=c, src_of_chunk=src_of_chunk, rms_col=rms_col: e.tensor_scalar(
                out=m_t[:, c, :], in0=src_of_chunk(c), scalar1=rmsv[:, rms_col*NCH+c : rms_col*NCH+c+1], scalar2=None, op0=ALU.mult))
        SIG("vector")
        for c in range(NCH):
            E("scalar", lambda e, c=c, src_of_chunk=src_of_chunk: e.activation(out=sqf[:, c % 2, 0:TC], in_=src_of_chunk(c), func=AF.Square))
            tsq = SIG("scalar")
            WT("vector", "scalar", tsq)
            if c == 0:
                E("vector", lambda e: e.tensor_copy(out=csum[:], in_=sqf[:, 0, 0:TC]))
            else:
                E("vector", lambda e, c=c: e.tensor_add(out=csum[:], in0=csum[:], in1=sqf[:, c % 2, 0:TC]))
            tvv = SIG("vector")
            if c >= 1:
                WT("scalar", "vector", tvv - 1)
        E("vector", lambda e: e.tensor_copy(out=camax[:], in_=m_t[:, 0, :]))
        E("vector", lambda e: e.tensor_copy(out=csum2[:], in_=m_t[:, 0, :]))
        for c in range(1, NCH):
            E("vector", lambda e, c=c: e.tensor_tensor(out=camax[:], in0=camax[:], in1=m_t[:, c, :], op=ALU.max))
            E("vector", lambda e, c=c: e.tensor_tensor(out=csum2[:], in0=csum2[:], in1=m_t[:, c, :], op=ALU.min))
        E("vector", lambda e: e.tensor_scalar(out=csum2[:], in0=csum2[:], scalar1=-1.0, scalar2=None, op0=ALU.mult))
        E("vector", lambda e: e.tensor_tensor(out=camax[:], in0=camax[:], in1=csum2[:], op=ALU.max))
        SIG("vector")
        BAR()
        for blk in range(4):
            for wch, srcbuf in ((0, csum), (1, camax)):
                E("tensor", lambda e, blk=blk, srcbuf=srcbuf: e.transpose(psTf[:], srcbuf[:, 128*blk:128*(blk+1)], identf[:]))
                tm = SIG("tensor")
                WT("vector", "tensor", tm)
                if wch == 0:
                    E("vector", lambda e, blk=blk: e.reduce_sum(out=stat_tm[:, blk:blk+1], in_=psTf[:], axis=AX.X))
                else:
                    E("vector", lambda e, blk=blk: e.reduce_max(out=stat_tm[:, 4+blk:4+blk+1], in_=psTf[:], axis=AX.X))
                tv = SIG("vector")
                WT("tensor", "vector", tv)
        E("vector", lambda e: e.tensor_scalar(out=rs_tm[:], in0=stat_tm[:, 0:4], scalar1=1.0/D, scalar2=RMS_EPS, op0=ALU.mult, op1=ALU.add))
        SIG("vector")
        WT("scalar", "vector")
        E("scalar", lambda e: e.activation(out=rs_tm[:], in_=rs_tm[:], func=AF.Sqrt))
        SIG("scalar")
        WT("vector", "scalar")
        E("vector", lambda e: e.reciprocal(out=rs_tm[:], in_=rs_tm[:]))
        E("vector", lambda e: e.tensor_tensor(out=isx_tm[:], in0=stat_tm[:, 4:8], in1=rs_tm[:], op=ALU.mult))
        E("vector", lambda e: e.tensor_scalar(out=isx_tm[:], in0=isx_tm[:], scalar1=1e-5, scalar2=None, op0=ALU.max))
        E("vector", lambda e: e.reciprocal(out=stat_tm[:, 0:4], in_=isx_tm[:]))
        E("vector", lambda e: e.tensor_tensor(out=rs_tm[:], in0=rs_tm[:], in1=stat_tm[:, 0:4], op=ALU.mult))
        E("vector", lambda e: e.tensor_scalar(out=rs_tm[:], in0=rs_tm[:], scalar1=127.0, scalar2=None, op0=ALU.mult))
        E("vector", lambda e: e.tensor_scalar(out=isx_tm[:], in0=isx_tm[:], scalar1=1.0/127.0, scalar2=None, op0=ALU.mult))
        SIG("vector")
        for stm, fmbuf in ((rs_tm, rs_fm), (isx_tm, isx_fm)):
            BAR()
            for blk in range(4):
                E("vector", lambda e, stm=stm, blk=blk: e.tensor_copy(out=stag4[:, 32*blk:32*blk+1], in_=stm[:, blk:blk+1]))
            tv0 = SIG("vector")
            WT("tensor", "vector", tv0)
            E("tensor", lambda e: e.transpose(psTf[:], stag4[:], identf[:]))
            tm = SIG("tensor")
            WT("vector", "tensor", tm)
            for blk in range(4):
                E("vector", lambda e, blk=blk: e.tensor_copy(out=rhs512[32*blk:32*blk+32, 128*blk:128*(blk+1)], in_=psTf[32*blk:32*blk+32, 0:128]))
            tv = SIG("vector")
            WT("tensor", "vector", tv)
            E("tensor", lambda e: e.matmul(psR[:], onesF[:], rhs512[:], start=True, stop=True))
            tm2 = SIG("tensor")
            WT("vector", "tensor", tm2)
            E("vector", lambda e, fmbuf=fmbuf: e.tensor_copy(out=fmbuf[:], in_=psR[:]))
            SIG("vector")
        BAR()
        for c in range(NCH):
            E("vector", lambda e, c=c: e.tensor_tensor(out=rnd_f[:], in0=m_t[:, c, :], in1=rs_fm[:], op=ALU.mult))
            E("vector", lambda e, c=c: e.tensor_scalar(out=rnd_f[:], in0=rnd_f[:], scalar1=MAGIC, scalar2=MAGIC, op0=ALU.add, op1=ALU.subtract))
            E("vector", lambda e, c=c: e.tensor_copy(out=xq_t[:, c, :], in_=rnd_f[:]))
        SIG("vector")
        BAR()

    def project_fm(wslot, winv, dst_of_dc, sigmoid=False):
        for dc in range(NCH):
            pp = psA if dc % 2 == 0 else psB
            if dc >= 2:
                WT("tensor", "vector", C.sems["vector"].n)
            for c in range(NCH):
                E("tensor", lambda e, dc=dc, c=c, pp=pp, wslot=wslot: e.matmul(pp[:], wq_sb[:, wslot, c, 128*dc:128*(dc+1)], xq_t[:, c, :], start=(c == 0), stop=(c == NCH-1)))
            tm = SIG("tensor")
            WT("vector", "tensor", tm)
            if not sigmoid:
                E("vector", lambda e, dc=dc, pp=pp, winv=winv, dst_of_dc=dst_of_dc: e.scalar_tensor_tensor(out=dst_of_dc(dc), in0=pp[:], scalar=inv_sw[:, winv:winv+1], in1=isx_fm[:], op0=ALU.mult, op1=ALU.mult))
                SIG("vector")
            else:
                E("vector", lambda e, dc=dc, pp=pp, winv=winv: e.scalar_tensor_tensor(out=sqf[:, dc % 2, 0:TC], in0=pp[:], scalar=inv_sw[:, winv:winv+1], in1=isx_fm[:], op0=ALU.mult, op1=ALU.mult))
                tv = SIG("vector")
                WT("scalar", "vector", tv)
                E("scalar", lambda e, dc=dc, dst_of_dc=dst_of_dc: e.activation(out=dst_of_dc(dc), in_=sqf[:, dc % 2, 0:TC], func=AF.Sigmoid))
                SIG("scalar")
        BAR()

    def load_seg_xnf(b):
        return DMAL(xnf_seg[:], xnf_dram[:, :, b*TC:(b+1)*TC])

    def emit_mix(mi):
        for c in range(NCH):
            E("vector", lambda e, c=c, mi=mi: e.scalar_tensor_tensor(
                out=mix_t[:, c, :], in0=d_sh[:, c, :], scalar=tmixm1[:, mi*NCH+c:mi*NCH+c+1],
                in1=xnf_seg[:, c, 0:TC], op0=ALU.mult, op1=ALU.add))
        SIG("vector")

    def emit_dsh(b):
        for c in range(NCH):
            E("vector", lambda e, c=c: e.tensor_sub(out=d_sh[:, c, 1:TC], in0=xnf_seg[:, c, 1:TC], in1=xnf_seg[:, c, 0:TC-1]))
            E("vector", lambda e, c=c, b=b: e.tensor_sub(out=d_sh[:, c, 0:1], in0=xnf_seg[:, c, 0:1], in1=xnb_sb[:, c, b:b+1]))
        SIG("vector")

    DMAL(wq_sb[:, 0, :, :].rearrange("p c d -> p (c d)"), wq_dram[0, :, :])
    DMAL(wq_sb[:, 1, :, :].rearrange("p c d -> p (c d)"), wq_dram[1, :, :])
    WL("tensor", s_ld.n)
    BAR()

    kv_store_t = 0
    for b in range(NB):
        tl = load_seg_xnf(b)
        WL("vector", tl)
        if b > 0:
            WS("vector", kv_store_t)
        emit_dsh(b)
        BAR()
        for mi in range(2):
            emit_mix(mi)
            BAR()
            quant_chain(mi, lambda c: mix_t[:, c, :])
            project_fm(mi, mi, lambda dc, mi=mi: kv_t[:, mi, dc, :])
            BAR()
        WT("sync", "vector")
        for kvi in (0, 1):
            for c in range(NCH):
                kv_store_t = DMAS(kv_spill[kvi, c, :, b*TC:(b+1)*TC], kv_t[:, kvi, c, :])
        for c in range(NCH):
            E("scalar", lambda e, c=c: e.activation(out=e_t[:], in_=kv_t[:, 0, c, :], func=AF.Exp))
            tse = SIG("scalar")
            WT("vector", "scalar", tse)
            E("vector", lambda e, c=c: e.tensor_tensor(out=ev_t[:], in0=e_t[:], in1=kv_t[:, 1, c, :], op=ALU.mult))
            E("vector", lambda e, c=c: e.tensor_tensor_scan(out=a_scan[:], data0=Dmat[:, c, :], data1=ev_t[:], initial=zcol[:, 0:1], op0=ALU.mult, op1=ALU.add))
            E("vector", lambda e, c=c, b=b: e.tensor_copy(out=aend[:, b*NCH+c : b*NCH+c+1], in_=a_scan[:, TC-1:TC]))
            E("vector", lambda e, c=c: e.tensor_tensor_scan(out=b_scan[:], data0=Dmat[:, c, :], data1=e_t[:], initial=zcol[:, 0:1], op0=ALU.mult, op1=ALU.add))
            E("vector", lambda e, c=c, b=b: e.tensor_copy(out=aend[:, NCH*NB + b*NCH+c : NCH*NB + b*NCH+c+1], in_=b_scan[:, TC-1:TC]))
            tvs = SIG("vector")
            WT("scalar", "vector", tvs)
            BAR()
        BAR()

    DMAL(wq_sb[:, 0, :, :].rearrange("p c d -> p (c d)"), wq_dram[2, :, :])
    WL("tensor", s_ld.n)
    BAR()
    r_store_t = 0
    for b in range(NB):
        tl = load_seg_xnf(b)
        WL("vector", tl)
        emit_dsh(b)
        if b > 0:
            WS("scalar", r_store_t)
        emit_mix(2)
        quant_chain(2, lambda c: mix_t[:, c, :])
        project_fm(0, 2, lambda dc: r_seg[:, dc, :], sigmoid=True)
        WT("sync", "scalar")
        for c in range(NCH):
            r_store_t = DMAS(r_dram[:, c, b*TC:(b+1)*TC], r_seg[:, c, :])
    BAR()

    WT("sync", "vector")
    tst = DMAS(cc_in[:, :], aend[:])
    WS("gpsimd", tst)
    E("gpsimd", lambda e: e.collective_compute("AllGather", ALU.bypass, replica_groups=[list(range(NCORE))],
                                               ins=[cc_in[:, :]], outs=[cc_out[:, :, :]]))
    SIG("gpsimd")
    WT("sync", "gpsimd")
    DMAL(ccg[:], cc_out.rearrange("r q i -> q r i"))
    BAR()
    E("tensor", lambda e: e.matmul(psR[:, 0:2*NCORE], onesNf[:], rk_sb[0:1, :], start=True, stop=True))
    SIG("tensor")
    WT("vector", "tensor")
    E("vector", lambda e: e.tensor_copy(out=rkrep[:], in_=psR[:, 0:2*NCORE]))
    SIG("vector")
    for r in range(NCORE):
        E("vector", lambda e, r=r: e.tensor_scalar(out=powr[:, r, :], in0=wlog[:], scalar1=rkrep[:, NCORE+r:NCORE+r+1], scalar2=None, op0=ALU.mult))
    E("vector", lambda e: e.tensor_scalar(out=powr[:].rearrange("p r c -> p (r c)"), in0=powr[:].rearrange("p r c -> p (r c)"), scalar1=float(TC), scalar2=None, op0=ALU.mult))
    SIG("vector")
    WT("scalar", "vector")
    E("scalar", lambda e: e.activation(out=powr[:].rearrange("p r c -> p (r c)"), in_=powr[:].rearrange("p r c -> p (r c)"), func=AF.Exp))
    SIG("scalar")
    WT("vector", "scalar")
    for r in range(NCORE):
        E("vector", lambda e, r=r: e.tensor_scalar(out=powr[:, r, :], in0=powr[:, r, :], scalar1=rkrep[:, r:r+1], scalar2=None, op0=ALU.mult))
    E("vector", lambda e: e.memset(carry[:], 0.0))
    SIG("vector")
    for r in range(NCORE):
        for kv in range(2):
            for b in range(NB):
                idx0 = kv * NB * NCH + b * NCH
                E("vector", lambda e, r=r, idx0=idx0: e.tensor_tensor(out=sqf[:, 0, 0:NCH], in0=ccg[:, r, idx0:idx0+NCH], in1=powr[:, r, :], op=ALU.mult))
                E("vector", lambda e, idx0=idx0: e.tensor_add(out=carry[:, idx0:idx0+NCH], in0=carry[:, idx0:idx0+NCH], in1=sqf[:, 0, 0:NCH]))
    SIG("vector")
    BAR()

    DMAL(wq_sb[:, 1, :, :].rearrange("p c d -> p (c d)"), wq_dram[3, :, :])
    WL("tensor", s_ld.n)
    BAR()
    for b in range(NB):
        for kvi in range(2):
            for c in range(NCH):
                DMAL(kv_t[:, kvi, c, :], kv_spill[kvi, c, :, b*TC:(b+1)*TC])
        for c in range(NCH):
            DMAL(r_seg[:, c, :], r_dram[:, c, b*TC:(b+1)*TC])
        WL("scalar", s_ld.n); WL("vector", s_ld.n)
        for c in range(NCH):
            E("scalar", lambda e, c=c: e.activation(out=e_t[:], in_=kv_t[:, 0, c, :], func=AF.Exp))
            tse = SIG("scalar")
            WT("vector", "scalar", tse)
            E("vector", lambda e, c=c: e.tensor_tensor(out=ev_t[:], in0=e_t[:], in1=kv_t[:, 1, c, :], op=ALU.mult))
            E("vector", lambda e, c=c: e.tensor_tensor_scan(out=a_scan[:], data0=Dmat[:, c, :], data1=ev_t[:], initial=zcol[:, 0:1], op0=ALU.mult, op1=ALU.add))
            E("vector", lambda e, c=c: e.tensor_tensor_scan(out=b_scan[:], data0=Dmat[:, c, :], data1=e_t[:], initial=zcol[:, 0:1], op0=ALU.mult, op1=ALU.add))
            E("vector", lambda e, c=c: e.tensor_scalar(out=eu_t[:], in0=e_t[:], scalar1=expu[:, c:c+1], scalar2=None, op0=ALU.mult))
            E("vector", lambda e, c=c: e.tensor_scalar(out=evu_t[:], in0=ev_t[:], scalar1=expu[:, c:c+1], scalar2=None, op0=ALU.mult))
            E("vector", lambda e, c=c, b=b: e.scalar_tensor_tensor(out=n1_t[:, 1:TC], in0=powm[:, c, 1:TC], scalar=carry[:, b*NCH+c : b*NCH+c+1], in1=a_scan[:, 0:TC-1], op0=ALU.mult, op1=ALU.add))
            E("vector", lambda e, c=c, b=b: e.tensor_scalar(out=n1_t[:, 0:1], in0=powm[:, c, 0:1], scalar1=carry[:, b*NCH+c : b*NCH+c+1], scalar2=None, op0=ALU.mult))
            E("vector", lambda e, c=c, b=b: e.scalar_tensor_tensor(out=dd_t[:, 1:TC], in0=powm[:, c, 1:TC], scalar=carry[:, NCH*NB + b*NCH+c : NCH*NB + b*NCH+c+1], in1=b_scan[:, 0:TC-1], op0=ALU.mult, op1=ALU.add))
            E("vector", lambda e, c=c, b=b: e.tensor_scalar(out=dd_t[:, 0:1], in0=powm[:, c, 0:1], scalar1=carry[:, NCH*NB + b*NCH+c : NCH*NB + b*NCH+c+1], scalar2=None, op0=ALU.mult))
            E("vector", lambda e, c=c: e.tensor_add(out=n1_t[:], in0=n1_t[:], in1=evu_t[:]))
            E("vector", lambda e, c=c: e.tensor_add(out=dd_t[:], in0=dd_t[:], in1=eu_t[:]))
            E("vector", lambda e, c=c: e.reciprocal(out=dd_t[:], in_=dd_t[:]))
            E("vector", lambda e, c=c: e.tensor_tensor(out=n1_t[:], in0=n1_t[:], in1=dd_t[:], op=ALU.mult))
            E("vector", lambda e, c=c: e.tensor_tensor(out=gin_t[:, c, :], in0=n1_t[:], in1=r_seg[:, c, :], op=ALU.mult))
            tvs = SIG("vector")
            WT("scalar", "vector", tvs)
        quant_chain(3, lambda c: gin_t[:, c, :])
        for s in range(4):
            for half in range(2):
                WT("tensor", "vector", C.sems["vector"].n)
                for c in range(NCH):
                    E("tensor", lambda e, s=s, half=half, c=c: e.matmul(
                        psW[:, half*512:(half+1)*512], xq_t[:, c, 128*s:128*(s+1)],
                        wq_sb[:, 1, c, half*512:(half+1)*512], start=(c == 0), stop=(c == NCH-1)))
                tm = SIG("tensor")
                WT("vector", "tensor", tm)
                E("vector", lambda e, s=s, half=half: e.tensor_scalar(
                    out=s0_seg[:, s, half*512:(half+1)*512], in0=psW[:, half*512:(half+1)*512],
                    scalar1=isx_tm[:, s:s+1], scalar2=inv_sw[:, 3:4], op0=ALU.mult, op1=ALU.mult))
                SIG("vector")
        WT("sync", "vector")
        for s in range(4):
            DMAS(s0_dram[b*4+s, :, :], s0_seg[:, s, :])
        BAR()
    BAR()
    s_gd = Sem(st.enter_context(nc.semaphore("s_gd")))
    C.dma_sems = (s_ld, s_st, s_gd)

    def GDMA(out, in_, out_off=None, in_off=None, **kw):
        s_gd.n += 16
        C.q["gpsimd"].append(lambda e, out=out, in_=in_, oo=out_off, io=in_off, kw=kw:
            e.indirect_dma_start(out=out, out_offset=oo, in_=in_, in_offset=io, **kw).then_inc(s_gd.h, 16))
        return s_gd.n

    def WG(eng, val):
        C.q[eng].append(lambda e, v=val: e.wait_ge(s_gd.h, v))

    E("gpsimd", lambda e: e.iota(out=iota16[:], pattern=[[1, 16]], base=0, channel_multiplier=0, allow_small_or_imprecise_dtypes=True))
    E("gpsimd", lambda e: e.iota(out=tri16a[:].rearrange("p i j -> p (i j)"), pattern=[[0, 16], [1, 16]], base=0, channel_multiplier=0, allow_small_or_imprecise_dtypes=True))
    E("gpsimd", lambda e: e.iota(out=tri16b[:].rearrange("p i j -> p (i j)"), pattern=[[1, 16], [0, 16]], base=0, channel_multiplier=0, allow_small_or_imprecise_dtypes=True))
    E("gpsimd", lambda e: e.iota(out=tri4a[:].rearrange("p i j -> p (i j)"), pattern=[[0, 4], [1, 4]], base=0, channel_multiplier=0, allow_small_or_imprecise_dtypes=True))
    E("gpsimd", lambda e: e.iota(out=tri4b[:].rearrange("p i j -> p (i j)"), pattern=[[1, 4], [0, 4]], base=0, channel_multiplier=0, allow_small_or_imprecise_dtypes=True))
    BAR()
    E("vector", lambda e: e.tensor_tensor(out=tri16a[:].rearrange("p i j -> p (i j)"), in0=tri16a[:].rearrange("p i j -> p (i j)"), in1=tri16b[:].rearrange("p i j -> p (i j)"), op=ALU.is_lt))
    E("vector", lambda e: e.tensor_tensor(out=tri4a[:].rearrange("p i j -> p (i j)"), in0=tri4a[:].rearrange("p i j -> p (i j)"), in1=tri4b[:].rearrange("p i j -> p (i j)"), op=ALU.is_lt))
    E("vector", lambda e: e.memset(ones128s[:], 1.0))
    tl = DMAL(rw_sb[:], rw_in[:, :])
    WL("vector", tl)
    E("vector", lambda e: e.tensor_copy(out=rw_bf[:], in_=rw_sb[:]))
    BAR()
    for c in range(NCH):
        pp = psT if c % 2 == 0 else psU
        E("tensor", lambda e, c=c, pp=pp: e.transpose(pp[:], rw_bf[0:16, 128*c:128*(c+1)], ident[0:16, :]))
        tm = SIG("tensor")
        WT("scalar", "tensor", tm)
        E("scalar", lambda e, c=c, pp=pp: e.copy(out=rwT[:, c, :], in_=pp[:, 0:16]))
        tsc = C.sems["scalar"].n
        WT("tensor", "scalar", tsc)
    BAR()
    tl = DMAL(strip1[0:1, 0:16], ebias_in[None, :], allow_slow_non_contiguous=True)
    WL("tensor", tl)
    E("tensor", lambda e: e.matmul(psR[:, 0:16], onesNf[:], strip1[0:1, 0:16], start=True, stop=True))
    tm = SIG("tensor")
    WT("vector", "tensor", tm)
    E("vector", lambda e: e.tensor_copy(out=ebias_rep[:], in_=psR[:, 0:16]))
    BAR()
    cast_f32 = sqf[:].rearrange("p a d -> p (a d)")
    cast_bf = mix_t[:].rearrange("p c t -> p (c t)")[:, 0:2048]
    for e_i in range(16):
        for q in range(6):
            tl = DMAL(cast_f32[:, :], wcat_in[e_i, :, 2048*q:2048*(q+1)])
            WL("vector", tl)
            E("vector", lambda e: e.tensor_copy(out=cast_bf[:, :], in_=cast_f32[:, :]))
            tv = SIG("vector")
            WT("sync", "vector", tv)
            tstw = DMAS(wcat_bf[e_i, :, 2048*q:2048*(q+1)], cast_bf[:, :])
            WS("vector", tstw)
            WS("sync", tstw)
    for mat_src, dstslot in ((shg_in, 0), (shu_in, 1)):
        for q in range(4):
            tl = DMAL(cast_f32[:, :], mat_src.rearrange("p c d -> p (c d)")[:, 2048*q:2048*(q+1)])
            WL("vector", tl)
            E("vector", lambda e, dstslot=dstslot, q=q: e.tensor_copy(out=wq_sb[:, dstslot, :, :].rearrange("p c d -> p (c d)")[:, 2048*q:2048*(q+1)], in_=cast_f32[:, :]))
            tv = SIG("vector")
            WT("sync", "vector", tv)
    BAR()
    for g in range(8):
        tl = DMAL(xnf_seg[:], xnf_dram[:, :, g*TC:(g+1)*TC])
        WL("tensor", tl)
        BAR()
        for c in range(NCH):
            E("tensor", lambda e, c=c: e.matmul(psA[0:16, :], rwT[:, c, :], xnf_seg[:, c, :], start=(c == 0), stop=(c == NCH-1)))
        tm = SIG("tensor")
        WT("vector", "tensor", tm)
        E("vector", lambda e: e.tensor_copy(out=rw_sb[0:16, 0:512], in_=psA[0:16, :]))
        tv = SIG("vector")
        WT("tensor", "vector", tv)
        for qi in range(4):
            E("tensor", lambda e, qi=qi: e.transpose(psTf[:], rw_sb[0:16, 128*qi:128*(qi+1)], identf[0:16, :]))
            tm = SIG("tensor")
            WT("scalar", "tensor", tm)
            E("scalar", lambda e, g=g, qi=qi: e.activation(out=scores_tm[:, 4*g+qi, :], in_=psTf[:, 0:16], func=AF.Sigmoid))
            tsc = C.sems["scalar"].n
            WT("tensor", "scalar", tsc)
        BAR()
    NT = NTILE
    cmp1 = cmp_sc[:].rearrange("p i e -> p (i e)")
    cmp2 = wstage[:].rearrange("p c d -> p (c d)").bitcast(F32)[:, 0:512]
    cmp2b = wstage[:].rearrange("p c d -> p (c d)").bitcast(F32)[:, 512:1024]
    rank4 = rank16[:].rearrange("p i e -> p (i e)")
    E("vector", lambda e: e.tensor_tensor(out=rnk_sc[:, 0:NT, :], in0=scores_tm[:, 0:NT, :], in1=ebias_rep[:].unsqueeze(1).broadcast_to([128, NT, 16]), op=ALU.add))
    BAR()
    E("vector", lambda e: e.memset(rank4[:, :], 0.0))
    for j4 in range(4):
        xj = rnk_sc[:, 0:NT, :].rearrange("p i (g m) -> p i g m", m=4)[:, :, :, j4:j4+1].broadcast_to([128, NT, 4, 4])
        xi = rnk_sc[:, 0:NT, :].rearrange("p i (g m) -> p i g m", m=4)
        E("vector", lambda e, xj=xj, xi=xi: e.tensor_tensor(out=cmp1[:, 0:512].rearrange("p (i g m) -> p i g m", g=4, m=4), in0=xj, in1=xi, op=ALU.is_gt))
        E("vector", lambda e: e.tensor_add(out=rank4[:, :], in0=rank4[:, :], in1=cmp1[:, 0:512]))
        E("vector", lambda e, xj=xj, xi=xi: e.tensor_tensor(out=cmp1[:, 0:512].rearrange("p (i g m) -> p i g m", g=4, m=4), in0=xj, in1=xi, op=ALU.is_equal))
        ltj = tri4a[:, :, j4]
        E("vector", lambda e, ltj=ltj: e.tensor_tensor(out=cmp2[:, 0:512].rearrange("p (i g m) -> p i g m", g=4, m=4), in0=cmp1[:, 0:512].rearrange("p (i g m) -> p i g m", g=4, m=4), in1=ltj.unsqueeze(1).unsqueeze(2).broadcast_to([128, NT, 4, 4]), op=ALU.mult))
        E("vector", lambda e: e.tensor_add(out=rank4[:, :], in0=rank4[:, :], in1=cmp2[:, 0:512]))
    BAR()
    E("vector", lambda e: e.tensor_scalar(out=cmp1[:, 0:512], in0=rank4[:, :], scalar1=2.0, scalar2=1.0, op0=ALU.is_lt, op1=ALU.mult))
    E("vector", lambda e: e.tensor_tensor(out=cmp1[:, 0:512], in0=cmp1[:, 0:512], in1=rnk_sc[:, 0:NT, :].rearrange("p i e -> p (i e)"), op=ALU.mult))
    E("vector", lambda e: e.tensor_reduce(out=gsc4[:, 0:NT, :], in_=cmp1[:, 0:512].rearrange("p (i g m) -> p i g m", g=4, m=4), axis=AX.X, op=ALU.add))
    BAR()
    E("vector", lambda e: e.memset(grk4[:].rearrange("p i g -> p (i g)"), 0.0))
    for j4 in range(4):
        xj = gsc4[:, 0:NT, j4:j4+1].broadcast_to([128, NT, 4])
        E("vector", lambda e, xj=xj: e.tensor_tensor(out=cmp1[:, 0:128].rearrange("p (i g) -> p i g", g=4), in0=xj, in1=gsc4[:, 0:NT, :], op=ALU.is_gt))
        E("vector", lambda e: e.tensor_add(out=grk4[:].rearrange("p i g -> p (i g)"), in0=grk4[:].rearrange("p i g -> p (i g)"), in1=cmp1[:, 0:128]))
        E("vector", lambda e, xj=xj: e.tensor_tensor(out=cmp1[:, 0:128].rearrange("p (i g) -> p i g", g=4), in0=xj, in1=gsc4[:, 0:NT, :], op=ALU.is_equal))
        ltj = tri4a[:, :, j4]
        E("vector", lambda e, ltj=ltj: e.tensor_tensor(out=cmp2[:, 0:128].rearrange("p (i g) -> p i g", g=4), in0=cmp1[:, 0:128].rearrange("p (i g) -> p i g", g=4), in1=ltj.unsqueeze(1).broadcast_to([128, NT, 4]), op=ALU.mult))
        E("vector", lambda e: e.tensor_add(out=grk4[:].rearrange("p i g -> p (i g)"), in0=grk4[:].rearrange("p i g -> p (i g)"), in1=cmp2[:, 0:128]))
    E("vector", lambda e: e.tensor_scalar(out=grk4[:].rearrange("p i g -> p (i g)"), in0=grk4[:].rearrange("p i g -> p (i g)"), scalar1=2.0, scalar2=1.0, op0=ALU.is_lt, op1=ALU.mult))
    BAR()
    E("vector", lambda e: e.tensor_tensor(out=rnk_sc[:, 0:NT, :].rearrange("p i (g m) -> p i g m", m=4), in0=rnk_sc[:, 0:NT, :].rearrange("p i (g m) -> p i g m", m=4), in1=grk4[:, 0:NT, :].unsqueeze(3).broadcast_to([128, NT, 4, 4]), op=ALU.mult))
    BAR()
    E("vector", lambda e: e.memset(rank16[:].rearrange("p i e -> p (i e)"), 0.0))
    for j16 in range(16):
        xj = rnk_sc[:, 0:NT, j16:j16+1].broadcast_to([128, NT, 16])
        E("vector", lambda e, xj=xj: e.tensor_tensor(out=cmp1[:, 0:512].rearrange("p (i e) -> p i e", e=16), in0=xj, in1=rnk_sc[:, 0:NT, :], op=ALU.is_gt))
        E("vector", lambda e: e.tensor_add(out=rank16[:].rearrange("p i e -> p (i e)"), in0=rank16[:].rearrange("p i e -> p (i e)"), in1=cmp1[:, 0:512]))
        E("vector", lambda e, xj=xj: e.tensor_tensor(out=cmp1[:, 0:512].rearrange("p (i e) -> p i e", e=16), in0=xj, in1=rnk_sc[:, 0:NT, :], op=ALU.is_equal))
        ltj = tri16a[:, :, j16]
        E("vector", lambda e, ltj=ltj: e.tensor_tensor(out=cmp2[:, 0:512].rearrange("p (i e) -> p i e", e=16), in0=cmp1[:, 0:512].rearrange("p (i e) -> p i e", e=16), in1=ltj.unsqueeze(1).broadcast_to([128, NT, 16]), op=ALU.mult))
        E("vector", lambda e: e.tensor_add(out=rank16[:].rearrange("p i e -> p (i e)"), in0=rank16[:].rearrange("p i e -> p (i e)"), in1=cmp2[:, 0:512]))
    BAR()
    E("vector", lambda e: e.tensor_scalar(out=cmp1[:, 0:512], in0=rank16[:].rearrange("p i e -> p (i e)"), scalar1=4.0, scalar2=1.0, op0=ALU.is_lt, op1=ALU.mult))
    E("vector", lambda e: e.tensor_tensor(out=cmp2[:, 0:512], in0=cmp1[:, 0:512], in1=scores_tm[:, 0:NT, :].rearrange("p i e -> p (i e)"), op=ALU.mult))
    E("vector", lambda e: e.tensor_reduce(out=cmp2b[:, 0:32].rearrange("p (i q) -> p i q", q=1), in_=cmp2[:, 0:512].rearrange("p (i e) -> p i e", e=16), axis=AX.X, op=ALU.add))
    E("vector", lambda e: e.tensor_scalar(out=cmp2b[:, 0:32], in0=cmp2b[:, 0:32], scalar1=1e-20, scalar2=0.4, op0=ALU.add, op1=ALU.mult))
    E("vector", lambda e: e.reciprocal(out=cmp2b[:, 32:64], in_=cmp2b[:, 0:32]))
    E("vector", lambda e: e.tensor_tensor(out=cw_tm[:, 0:NT, :], in0=cmp2[:, 0:512].rearrange("p (i e) -> p i e", e=16), in1=cmp2b[:, 32:64].unsqueeze(2).broadcast_to([128, NT, 16]), op=ALU.mult))
    BAR()
    WT("sync", "vector")
    selm = cmp1[:, 0:512].rearrange("p (i e) -> p i e", e=16)
    for i in range(NT):
        si = i // 8
        E("tensor", lambda e, i=i: e.transpose(psTf[0:16, :], selm[:, i, :], identf[:]))
        tm = SIG("tensor")
        WT("vector", "tensor", tm)
        E("vector", lambda e, i=i, si=si: e.tensor_copy(out=Me128[32*si:32*si+16, 128*(i % 8):128*(i % 8)+128], in_=psTf[0:16, 0:128]))
        tv = SIG("vector")
        WT("tensor", "vector", tv)
    BAR()
    E("vector", lambda e: e.tensor_tensor_scan(out=pos128[:], data0=ones128s[:], data1=Me128[:], initial=zcol[:, 0:1], op0=ALU.mult, op1=ALU.add))
    E("vector", lambda e: e.tensor_copy(out=segsum[:], in_=pos128[:, 1023:1024]))
    E("vector", lambda e: e.tensor_sub(out=pos128[:], in0=pos128[:], in1=Me128[:]))
    BAR()
    E("tensor", lambda e: e.matmul(psR[0:1, 0:128], segsum[:], identf[:], start=True, stop=True))
    tm = SIG("tensor")
    WT("vector", "tensor", tm)
    E("vector", lambda e: e.tensor_copy(out=strip1[0:1, 0:128], in_=psR[0:1, 0:128]))
    E("vector", lambda e: e.tensor_tensor(out=strip1[0:1, 128:160], in0=strip1[0:1, 0:32], in1=strip1[0:1, 32:64], op=ALU.add))
    E("vector", lambda e: e.tensor_tensor(out=strip1[0:1, 160:192], in0=strip1[0:1, 64:96], in1=strip1[0:1, 96:128], op=ALU.add))
    E("vector", lambda e: e.tensor_tensor(out=strip1[0:1, 128:160], in0=strip1[0:1, 128:160], in1=strip1[0:1, 160:192], op=ALU.add))
    E("vector", lambda e: e.memset(strip1[0:1, 192:224], 0.0))
    E("vector", lambda e: e.tensor_copy(out=strip1[0:1, 224:256], in_=strip1[0:1, 0:32]))
    E("vector", lambda e: e.tensor_tensor(out=strip1[0:1, 256:288], in0=strip1[0:1, 224:256], in1=strip1[0:1, 32:64], op=ALU.add))
    E("vector", lambda e: e.tensor_tensor(out=strip1[0:1, 288:320], in0=strip1[0:1, 256:288], in1=strip1[0:1, 64:96], op=ALU.add))
    E("vector", lambda e: e.tensor_scalar(out=strip1[0:1, 320:336], in0=strip1[0:1, 128:144], scalar1=1.0/512.0, scalar2=(511.0/512.0 - 0.5 + 2.0**-12), op0=ALU.mult, op1=ALU.add))
    E("vector", lambda e: e.tensor_scalar(out=strip1[0:1, 320:336], in0=strip1[0:1, 320:336], scalar1=MAGIC, scalar2=MAGIC, op0=ALU.add, op1=ALU.subtract))
    E("vector", lambda e: e.tensor_copy(out=strip1[0:1, 336:352], in_=strip1[0:1, 320:336]))
    for sh in (1, 2, 4, 8):
        E("vector", lambda e, sh=sh: e.tensor_tensor(out=strip1[0:1, 352+sh:368], in0=strip1[0:1, 336+sh:352], in1=strip1[0:1, 336:352-sh], op=ALU.add))
        E("vector", lambda e, sh=sh: e.tensor_copy(out=strip1[0:1, 352:352+sh], in_=strip1[0:1, 336:336+sh]))
        E("vector", lambda e: e.tensor_copy(out=strip1[0:1, 336:352], in_=strip1[0:1, 352:368]))
    E("vector", lambda e: e.tensor_sub(out=strip1[0:1, 368:384], in0=strip1[0:1, 336:352], in1=strip1[0:1, 320:336]))
    E("vector", lambda e: e.tensor_scalar(out=strip1[0:1, 368:384], in0=strip1[0:1, 368:384], scalar1=512.0, scalar2=None, op0=ALU.mult))
    BAR()
    E("gpsimd", lambda e: e.iota(out=strip1[0:1, 400:448], pattern=[[512, 48]], base=0, channel_multiplier=0, allow_small_or_imprecise_dtypes=True))
    BAR()
    E("vector", lambda e: e.tensor_tensor(out=rw_sb[0:1, 0:768].rearrange("p (j e) -> p j e", e=16), in0=strip1[0:1, 368:384].unsqueeze(1).broadcast_to([1, 48, 16]), in1=strip1[0:1, 400:448].unsqueeze(2).broadcast_to([1, 48, 16]), op=ALU.is_le))
    E("vector", lambda e: e.tensor_reduce(out=strip1[0:1, 448:496].rearrange("p (j q) -> p j q", q=1), in_=rw_sb[0:1, 0:768].rearrange("p (j e) -> p j e", e=16), axis=AX.X, op=ALU.add))
    E("vector", lambda e: e.tensor_scalar(out=strip1[0:1, 448:496], in0=strip1[0:1, 448:496], scalar1=-1.0, scalar2=None, op0=ALU.add))
    E("vector", lambda e: e.memset(strip1[0:1, 624:656], 0.0))
    E("vector", lambda e: e.tensor_copy(out=strip1[0:1, 624:640], in_=strip1[0:1, 368:384]))
    E("vector", lambda e: e.tensor_tensor(out=strip1[0:1, 496:624].rearrange("p (s q) -> p s q", q=32), in0=strip1[0:1, 192:320].rearrange("p (s q) -> p s q", q=32), in1=strip1[0:1, 624:656].unsqueeze(1).broadcast_to([1, 4, 32]), op=ALU.add))
    BAR()
    E("tensor", lambda e: e.matmul(psR[:, 0:1], strip1[0:1, 496:624], onesNf[0:1, 0:1], start=True, stop=True))
    tm = SIG("tensor")
    WT("vector", "tensor", tm)
    E("vector", lambda e: e.tensor_copy(out=segoff[:], in_=psR[:, 0:1]))
    E("vector", lambda e: e.tensor_scalar(out=pos128[:], in0=pos128[:], scalar1=segoff[:, 0:1], scalar2=0.0, op0=ALU.add, op1=ALU.add))
    BAR()
    for i in range(NT):
        si = i // 8
        E("tensor", lambda e, i=i: e.transpose(psTf[:], pos128[:, 128*(i % 8):128*(i % 8)+128].rearrange("p (q t) -> p q t", q=1)[:, 0, :] if False else pos128[:, 128*(i % 8):128*(i % 8)+128], identf[:]))
        tm = SIG("tensor")
        WT("vector", "tensor", tm)
        E("vector", lambda e, i=i, si=si: e.tensor_copy(out=S_tm[:, i, :], in_=psTf[:, 32*si:32*si+16]))
        tv = SIG("vector")
        WT("tensor", "vector", tv)
    BAR()
    for j in range(4):
        E("vector", lambda e, j=j: e.tensor_scalar(out=cmp1[:, 0:512], in0=rank16[:].rearrange("p i e -> p (i e)"), scalar1=float(j), scalar2=1.0, op0=ALU.is_equal, op1=ALU.mult))
        E("vector", lambda e, j=j: e.tensor_tensor(out=cmp2[:, 0:512], in0=cmp1[:, 0:512], in1=S_tm[:, 0:NT, :].rearrange("p i e -> p (i e)"), op=ALU.mult))
        E("vector", lambda e, j=j: e.tensor_reduce(out=sjf[:, :, j:j+1], in_=cmp2[:, 0:512].rearrange("p (i e) -> p i e", e=16), axis=AX.X, op=ALU.add))
        E("vector", lambda e, j=j: e.tensor_tensor(out=cmp2[:, 0:512], in0=cmp1[:, 0:512], in1=cw_tm[:, 0:NT, :].rearrange("p i e -> p (i e)"), op=ALU.mult))
        E("vector", lambda e, j=j: e.tensor_reduce(out=gj_tm[:, :, j:j+1], in_=cmp2[:, 0:512].rearrange("p (i e) -> p i e", e=16), axis=AX.X, op=ALU.add))
    E("vector", lambda e: e.tensor_copy(out=sji[:].rearrange("p i j -> p (i j)"), in_=sjf[:].rearrange("p i j -> p (i j)")))
    BAR()
    WT("sync", "vector")
    E("tensor", lambda e: e.matmul(psR[:, 0:48], onesNf[:], strip1[0:1, 448:496], start=True, stop=True))
    tm = SIG("tensor")
    WT("vector", "tensor", tm)
    E("vector", lambda e: e.tensor_copy(out=texp_f[:], in_=psR[:, 0:48]))
    E("vector", lambda e: e.tensor_scalar(out=texp_f[:], in0=texp_f[:], scalar1=128.0, scalar2=0.0, op0=ALU.mult, op1=ALU.add))
    E("vector", lambda e: e.tensor_tensor(out=texp_f[:], in0=texp_f[:], in1=io_row[:, 0:48], op=ALU.add))
    E("vector", lambda e: e.tensor_copy(out=widx[:], in_=texp_f[:]))
    BAR()
    xn_sc = camax[:].bitcast(BF16)[:, 0:1024]
    for i in range(NT):
        tl = DMAL(xn_sc, xn_dram[i, :, :])
        WL("gpsimd", tl)
        tg = 0
        for j in range(4):
            tg = GDMA(slotbuf[:, :], xn_sc, out_off=IndirectOffsetOnAxis(sji[:, i, j:j+1], 0))
        WG("sync", tg)
    BAR()
    wcat_rows = wcat_bf.rearrange("e p q -> (e p) q")
    hs_bf = rnd_f[:].bitcast(BF16)[:, 0:512]
    yo_bf = camax[:].bitcast(BF16)[:, 0:1024]
    for tj in range(48):
        GDMA(wbuf, wcat_rows, in_off=IndirectOffsetOnAxis(widx[:, tj:tj+1], 0))
        tl = DMAL(xg_tm[:], slotbuf[512*tj:512*(tj+1), :].rearrange("(a p) d -> p a d", p=128))
        BAR()
        for a4 in range(4):
            for c in range(NCH):
                pp = psT if (a4 * NCH + c) % 2 == 0 else psU
                E("tensor", lambda e, a4=a4, c=c, pp=pp: e.transpose(pp[:], xg_tm[:, a4, 128*c:128*(c+1)], ident[:]))
                tm = SIG("tensor")
                WT("scalar", "tensor", tm)
                E("scalar", lambda e, a4=a4, c=c, pp=pp: e.copy(out=xgF[:, c, 128*a4:128*(a4+1)], in_=pp[:]))
                tsc = C.sems["scalar"].n
                WT("tensor", "scalar", tsc)
        BAR()
        for mi in range(4):
            for c in range(NCH):
                E("tensor", lambda e, mi=mi, c=c: e.matmul(psA[:], wbuf[:, c*512+128*mi : c*512+128*mi+128], xgF[:, c, :], start=(c == 0), stop=(c == NCH-1)))
            for c in range(NCH):
                E("tensor", lambda e, mi=mi, c=c: e.matmul(psB[:], wbuf[:, 4096+c*512+128*mi : 4096+c*512+128*mi+128], xgF[:, c, :], start=(c == 0), stop=(c == NCH-1)))
            tm = SIG("tensor")
            WT("scalar", "tensor", tm)
            E("scalar", lambda e, mi=mi: e.activation(out=hs_bf[:, :], in_=psA[:], func=AF.Silu))
            tsc = C.sems["scalar"].n
            WT("vector", "scalar", tsc)
            E("vector", lambda e, mi=mi: e.tensor_tensor(out=h_rt[:, mi, :], in0=hs_bf[:, :], in1=psB[:], op=ALU.mult))
            tv = SIG("vector")
            WT("tensor", "vector", tv)
        BAR()
        for s4 in range(4):
            for half in range(2):
                for ci in range(4):
                    E("tensor", lambda e, s4=s4, half=half, ci=ci: e.matmul(psW[:, 512*half:512*(half+1)], h_rt[:, ci, 128*s4:128*(s4+1)], wbuf[:, 8192+ci*1024+512*half : 8192+ci*1024+512*half+512], start=(ci == 0), stop=(ci == 3)))
            tm = SIG("tensor")
            WT("vector", "tensor", tm)
            E("vector", lambda e, s4=s4: e.tensor_copy(out=yo_bf[:, :], in_=psW[:]))
            tv = SIG("vector")
            WT("sync", "vector", tv)
            tst = DMAS(yobuf[512*tj+128*s4 : 512*tj+128*s4+128, :], yo_bf[:, :])
            WS("vector", tst)
            WT("tensor", "vector", tv)
        BAR()
    shd_res = kv_t[:].rearrange("p a c t -> p (a c t)").bitcast(BF16)[:, 0:8192].rearrange("p (c d) -> p c d", d=D)
    for q in range(4):
        tl = DMAL(cast_f32[:, :], shd_in.rearrange("p c d -> p (c d)")[:, 2048*q:2048*(q+1)])
        WL("vector", tl)
        E("vector", lambda e, q=q: e.tensor_copy(out=shd_res.rearrange("p c d -> p (c d)")[:, 2048*q:2048*(q+1)], in_=cast_f32[:, :]))
        tv = SIG("vector")
        WT("sync", "vector", tv)
    BAR()
    ybg = camax[:].bitcast(BF16)[:, 0:1024]
    gsil = sqf[:, 0, :]
    s0buf = sqf[:, 1, :]
    for grp in range(8):
        for it in range(4):
            i = grp * 4 + it
            for j in range(4):
                tg = GDMA(ybg, yobuf[:, :], in_off=IndirectOffsetOnAxis(sji[:, i, j:j+1], 0))
                WG("vector", tg)
                if j == 0:
                    E("vector", lambda e, it=it, i=i: e.tensor_scalar(out=g0acc[:, it, :], in0=ybg, scalar1=gj_tm[:, i, 0:1], scalar2=0.0, op0=ALU.mult, op1=ALU.add))
                else:
                    E("vector", lambda e, it=it, i=i, j=j: e.scalar_tensor_tensor(out=g0acc[:, it, :], in0=ybg, scalar=gj_tm[:, i, j:j+1], in1=g0acc[:, it, :], op0=ALU.mult, op1=ALU.add))
                tv = SIG("vector")
                WT("gpsimd", "vector", tv)
        BAR()
        tl = DMAL(xnf_seg[:], xnf_dram[:, :, grp*TC:(grp+1)*TC])
        WL("tensor", tl)
        hsh_bf = rnd_f[:].bitcast(BF16)[:, 0:512]
        for mi in range(8):
            for c in range(NCH):
                E("tensor", lambda e, mi=mi, c=c: e.matmul(psA[:], wq_sb[:, 0, c, 128*mi:128*(mi+1)], xnf_seg[:, c, :], start=(c == 0), stop=(c == NCH-1)))
            for c in range(NCH):
                E("tensor", lambda e, mi=mi, c=c: e.matmul(psB[:], wq_sb[:, 1, c, 128*mi:128*(mi+1)], xnf_seg[:, c, :], start=(c == 0), stop=(c == NCH-1)))
            tm = SIG("tensor")
            WT("scalar", "tensor", tm)
            E("scalar", lambda e, mi=mi: e.activation(out=hsh_bf[:, :], in_=psA[:], func=AF.Silu))
            tsc = C.sems["scalar"].n
            WT("vector", "scalar", tsc)
            E("vector", lambda e, mi=mi: e.tensor_tensor(out=h_sh[:, mi, :], in0=hsh_bf[:, :], in1=psB[:], op=ALU.mult))
            tv = SIG("vector")
            WT("tensor", "vector", tv)
        BAR()
        for s4 in range(4):
            i = grp * 4 + s4
            for half in range(2):
                for ci in range(NCH):
                    E("tensor", lambda e, s4=s4, half=half, ci=ci: e.matmul(psW[:, 512*half:512*(half+1)], h_sh[:, ci, 128*s4:128*(s4+1)], shd_res[:, ci, 512*half:512*(half+1)], start=(ci == 0), stop=(ci == NCH-1)))
            tm = SIG("tensor")
            WT("vector", "tensor", tm)
            E("vector", lambda e, s4=s4: e.tensor_tensor(out=g0acc[:, s4, :], in0=g0acc[:, s4, :], in1=psW[:], op=ALU.add))
            tv = SIG("vector")
            WT("tensor", "vector", tv)
            WT("sync", "vector", tv)
            tl1 = DMAL(x_tile[:, 0, :], x_in[i // 4, 1+(i % 4)*128 : 1+(i % 4)*128+128, :])
            tl2 = DMAL(s0buf, s0_dram[i, :, :])
            WT("scalar", "vector", tv)
            E("scalar", lambda e, s4=s4: e.activation(out=gsil, in_=g0acc[:, s4, :], func=AF.Silu))
            tsc = C.sems["scalar"].n
            WL("vector", tl2)
            WL("vector", tl1)
            WT("vector", "scalar", tsc)
            E("vector", lambda e: e.tensor_tensor(out=gsil, in0=s0buf, in1=gsil, op=ALU.mult))
            E("vector", lambda e: e.tensor_tensor(out=s0buf, in0=x_tile[:, 0, :], in1=gsil, op=ALU.add))
            tv = SIG("vector")
            WT("sync", "vector", tv)
            t1 = DMAS(x1_dram[i, :, :], s0buf)
            WS("vector", t1)
            WS("scalar", t1)
        BAR()
    BAR()
    WS("sync", s_st.n)
    BAR()


    for mat_src, dsel in ((wqT_in, 0), (wkT_in, 1), (wvT_in, 2)):
        for qq in range(4):
            tl = DMAL(cast_f32[:, :], mat_src.rearrange("p c d -> p (c d)")[:, 2048*qq:2048*(qq+1)])
            WL("vector", tl)
            if dsel < 2:
                dstap = wq_sb[:, dsel, :, :].rearrange("p c d -> p (c d)")[:, 2048*qq:2048*(qq+1)]
            else:
                dstap = wstage[:].rearrange("p c d -> p (c d)")[:, 2048*qq:2048*(qq+1)]
            E("vector", lambda e, dstap=dstap: e.tensor_copy(out=dstap, in_=cast_f32[:, :]))
            tv = SIG("vector")
            WT("sync", "vector", tv)
    tl = DMAL(cast_f32[:, 0:1408].rearrange("p (j h) -> p j h", h=16), pe_in.rearrange("j p h -> p j h"))
    WL("vector", tl)
    E("vector", lambda e: e.tensor_copy(out=peT[:].rearrange("p j h -> p (j h)"), in_=cast_f32[:, 0:1408]))
    SIG("vector")
    tl = DMAL(selT[:], selc_in.rearrange("c p h -> p c h"))
    tl = DMAL(rselT[:].rearrange("p c d -> p (c d)"), rsel_in.rearrange("p c d -> p (c d)"))
    BAR()
    for i in range(NTILE):
        if i >= 2:
            WT("sync", "scalar"); WT("sync", "vector")
        tl = DMAL(x_tile[:, i % 2, :], x1_dram[i, :, :])
        WL("scalar", tl)
        E("scalar", lambda e, i=i: e.activation(out=sqf[:, i % 2, :], in_=x_tile[:, i % 2, :], func=AF.Square, accum_out=s2_all[:, i:i+1]))
        SIG("scalar")
        WL("vector", tl)
        E("vector", lambda e, i=i: e.reduce_sum(out=s1_all[:, i:i+1], in_=x_tile[:, i % 2, :], axis=AX.X))
        SIG("vector")
        BAR()
    BAR()
    E("vector", lambda e: e.tensor_scalar(out=s1_all[:], in0=s1_all[:], scalar1=1.0/D, scalar2=None, op0=ALU.mult))
    E("vector", lambda e: e.tensor_scalar(out=s2_all[:], in0=s2_all[:], scalar1=1.0/D, scalar2=None, op0=ALU.mult))
    SIG("vector")
    WT("scalar", "vector")
    E("scalar", lambda e: e.activation(out=rstd_all[:], in_=s1_all[:], func=AF.Square))
    SIG("scalar")
    WT("vector", "scalar")
    E("vector", lambda e: e.tensor_sub(out=rstd_all[:], in0=s2_all[:], in1=rstd_all[:]))
    E("vector", lambda e: e.tensor_scalar(out=rstd_all[:], in0=rstd_all[:], scalar1=LN_EPS, scalar2=None, op0=ALU.add))
    SIG("vector")
    WT("scalar", "vector")
    E("scalar", lambda e: e.activation(out=rstd_all[:], in_=rstd_all[:], func=AF.Sqrt))
    SIG("scalar")
    WT("vector", "scalar")
    E("vector", lambda e: e.reciprocal(out=rstd_all[:], in_=rstd_all[:]))
    E("vector", lambda e: e.tensor_scalar(out=nmu_all[:], in0=s1_all[:], scalar1=-1.0, scalar2=None, op0=ALU.mult))
    E("vector", lambda e: e.tensor_tensor(out=nmu_all[:], in0=nmu_all[:], in1=rstd_all[:], op=ALU.mult))
    SIG("vector")
    BAR()
    xn1_store_t = {}
    xnf1_store_t = {}
    for i in range(NTILE):
        if i >= 2:
            WT("sync", "scalar")
        tl = DMAL(x_tile[:, i % 2, :], x1_dram[i, :, :])
        WL("scalar", tl)
        if i >= 2:
            WS("scalar", xn1_store_t[i - 2])
        E("scalar", lambda e, i=i: e.activation(out=xn_st[:, i % 2, :], in_=x_tile[:, i % 2, :], func=AF.Identity, bias=nmu_all[:, i:i+1], scale=rstd_all[:, i:i+1]))
        tap = SIG("scalar")
        WT("sync", "scalar", tap)
        xn1_store_t[i] = DMAS(xn1_dram[i, :, :], xn_st[:, i % 2, :])
        WT("tensor", "scalar", tap)
        for c in range(NCH):
            pp = psT if c % 2 == 0 else psU
            if i > 0 or c >= 2:
                WT("tensor", "scalar", C.sems["scalar"].n - 1)
            E("tensor", lambda e, i=i, c=c, pp=pp: e.transpose(pp[:], xn_st[:, i % 2, 128*c:128*(c+1)], ident[:]))
            tm = SIG("tensor")
            WT("scalar", "tensor", tm)
            if c == 0 and i >= 2:
                WS("scalar", xnf1_store_t[i - 2])
            E("scalar", lambda e, i=i, c=c, pp=pp: e.copy(out=xnf_blk[:, i % 2, c, :], in_=pp[:]))
            SIG("scalar")
        WT("sync", "scalar")
        b = i // 4; toff = (i % 4) * 128
        xnf1_store_t[i] = DMAS(xnf1_dram[:, :, b*TC+toff : b*TC+toff+128], xnf_blk[:, i % 2, :, :])
    BAR()
    WS("sync", s_st.n)
    BAR()
    BAR()
    E("gpsimd", lambda e: e.collective_compute("AllGather", ALU.bypass, replica_groups=[list(range(NCORE))],
                                               ins=[xn1_dram.rearrange("i p d -> (i p) d")], outs=[xn1_ag[:, :, :]]))
    SIG("gpsimd")
    BAR()
    qC = r_seg
    kC = mix_t
    vC = xq_t
    e_full = kv_t[0:16, :, :, :].rearrange("p a c t -> p (a c t)")
    e_buf = e_full[:, 0:5632].rearrange("p (j t) -> p j t", t=TC)
    mrow = e_full[:, 5632:6144]
    m32 = e_full[:, 6144:6656]
    denb = e_full[:, 6656:7168]
    rden = e_full[:, 7168:7680]
    abf_full = kv_t[0:16, :, :, :].rearrange("p a c t -> p (a c t)").bitcast(BF16)
    abf = [abf_full[:, 15360:15872], abf_full[:, 15872:16384]]
    dmat_bf = Dmat[:].rearrange("p c t -> p (c t)").bitcast(BF16)
    pC = [dmat_bf[:, 0:4096].rearrange("p (c t) -> p c t", t=TC),
          dmat_bf[:, 4096:8192].rearrange("p (c t) -> p c t", t=TC)]
    s1bf = dmat_bf[:, 4096:8192].rearrange("p (c t) -> p c t", t=TC)
    s1acc = powm
    tmpv = camax
    s1tm_st = xn_st
    xnf1_4d = xnf1_dram.rearrange("p c (l t) -> p c l t", t=TC)
    for ct in range(8):
        tl = DMAL(xnf_seg[:].rearrange("p c (l t) -> p c l t", t=64), xnf1_4d[:, :, :, 64*ct:64*ct+64])
        WL("tensor", tl)
        tsc_pp = {0: 0, 1: 0}
        for si, dsts in enumerate((qC, kC, vC)):
            for mi in range(NCH):
                gidx = si * NCH + mi
                pp = psA if gidx % 2 == 0 else psB
                WT("tensor", "scalar", tsc_pp[gidx % 2])
                for c in range(NCH):
                    if si < 2:
                        lh = wq_sb[:, si, c, 128*mi:128*(mi+1)]
                    else:
                        lh = wstage[:, c, 128*mi:128*(mi+1)]
                    E("tensor", lambda e, lh=lh, c=c, pp=pp: e.matmul(pp[:], lh, xnf_seg[:, c, :], start=(c == 0), stop=(c == NCH-1)))
                tm = SIG("tensor")
                WT("scalar", "tensor", tm)
                E("scalar", lambda e, dsts=dsts, mi=mi, pp=pp: e.copy(out=dsts[:, mi, :], in_=pp[:]))
                tsc_pp[gidx % 2] = SIG("scalar")
        BAR()
        tm_sel = {}
        tv_evac = {}
        for j in range(11):
            dlt = j - 5
            lo = max(0, -64*dlt); hi = 512 - max(0, 64*dlt)
            if j >= 2:
                WT("vector", "tensor", tm_sel[j - 2])
            for c in range(NCH):
                if lo > 0:
                    E("vector", lambda e, c=c, lo=lo, j=j: e.memset(pC[j % 2][:, c, 0:lo], 0.0))
                if hi < TC:
                    E("vector", lambda e, c=c, hi=hi, j=j: e.memset(pC[j % 2][:, c, hi:TC], 0.0))
                E("vector", lambda e, c=c, lo=lo, hi=hi, dlt=dlt, j=j: e.tensor_tensor(
                    out=pC[j % 2][:, c, lo:hi], in0=qC[:, c, lo:hi], in1=kC[:, c, lo+64*dlt:hi+64*dlt], op=ALU.mult))
            tv_prod = SIG("vector")
            if j >= 1:
                WT("tensor", "vector", tv_evac[j - 1])
            for c in range(NCH):
                E("tensor", lambda e, j=j, c=c: e.matmul(psR[0:16, :], peT[:, j*8+c, :], qC[:, c, :], start=(c == 0), stop=False))
            WT("tensor", "vector", tv_prod)
            for c in range(NCH):
                E("tensor", lambda e, c=c, j=j: e.matmul(psR[0:16, :], selT[:, c, :], pC[j % 2][:, c, :], start=False, stop=(c == NCH-1)))
            tm_sel[j] = SIG("tensor")
            WT("vector", "tensor", tm_sel[j])
            E("vector", lambda e, j=j: e.tensor_copy(out=e_buf[:, j, :], in_=psR[0:16, :]))
            tv_evac[j] = SIG("vector")
        BAR()
        E("vector", lambda e: e.tensor_copy(out=mrow[:], in_=e_buf[:, 0, :]))
        for j in range(1, 11):
            E("vector", lambda e, j=j: e.tensor_tensor(out=mrow[:], in0=mrow[:], in1=e_buf[:, j, :], op=ALU.max))
        E("vector", lambda e: e.tensor_scalar(out=m32[:], in0=mrow[:], scalar1=0.03125, scalar2=None, op0=ALU.mult))
        SIG("vector")
        for j in range(11):
            E("vector", lambda e, j=j: e.tensor_scalar(out=e_buf[:, j, :], in0=e_buf[:, j, :], scalar1=0.03125, scalar2=None, op0=ALU.mult))
            E("vector", lambda e, j=j: e.tensor_sub(out=e_buf[:, j, :], in0=e_buf[:, j, :], in1=m32[:]))
            tv = SIG("vector")
            WT("scalar", "vector", tv)
            E("scalar", lambda e, j=j: e.activation(out=e_buf[:, j, :], in_=e_buf[:, j, :], func=AF.Exp))
            tsc = SIG("scalar")
            WT("vector", "scalar", tsc)
            if j == 0:
                E("vector", lambda e: e.tensor_copy(out=denb[:], in_=e_buf[:, 0, :]))
            else:
                E("vector", lambda e, j=j: e.tensor_add(out=denb[:], in0=denb[:], in1=e_buf[:, j, :]))
        E("vector", lambda e: e.reciprocal(out=rden[:], in_=denb[:]))
        E("vector", lambda e: e.memset(s1acc[:].rearrange("p c t -> p (c t)"), 0.0))
        SIG("vector")
        BAR()
        tv_pp = {0: 0, 1: 0}
        for j in range(11):
            dlt = j - 5
            lo = max(0, -64*dlt); hi = 512 - max(0, 64*dlt)
            E("vector", lambda e, j=j: e.tensor_tensor(out=abf[j % 2][:, :], in0=e_buf[:, j, :], in1=rden[:], op=ALU.mult))
            tv_abf = SIG("vector")
            WT("tensor", "vector", tv_abf)
            for c in range(NCH):
                pp = psA if c % 2 == 0 else psB
                WT("tensor", "vector", tv_pp[c % 2])
                E("tensor", lambda e, c=c, pp=pp, lo=lo, hi=hi, j=j: e.matmul(pp[:, lo:hi], rselT[:, c, :], abf[j % 2][:, lo:hi], start=True, stop=True))
                tm = SIG("tensor")
                WT("vector", "tensor", tm)
                E("vector", lambda e, c=c, pp=pp, lo=lo, hi=hi, dlt=dlt: e.tensor_tensor(out=tmpv[:, lo:hi], in0=pp[:, lo:hi], in1=vC[:, c, lo+64*dlt:hi+64*dlt], op=ALU.mult))
                E("vector", lambda e, c=c, lo=lo, hi=hi: e.tensor_add(out=s1acc[:, c, lo:hi], in0=s1acc[:, c, lo:hi], in1=tmpv[:, lo:hi]))
                tv_pp[c % 2] = SIG("vector")
        BAR()
        for c in range(NCH):
            E("vector", lambda e, c=c: e.tensor_copy(out=s1bf[:, c, :], in_=s1acc[:, c, :]))
        SIG("vector")
        BAR()
        s1st_t = {}
        for q4 in range(4):
            for c in range(NCH):
                pp = psT if c % 2 == 0 else psU
                E("tensor", lambda e, q4=q4, c=c, pp=pp: e.transpose(pp[:], s1bf[:, c, 128*q4:128*(q4+1)], ident[:]))
                tm = SIG("tensor")
                WT("scalar", "tensor", tm)
                if c == 0 and q4 >= 2:
                    WS("scalar", s1st_t[q4 - 2])
                E("scalar", lambda e, q4=q4, c=c, pp=pp: e.copy(out=s1tm_st[:, q4 % 2, 128*c:128*(c+1)], in_=pp[:]))
                tsc = SIG("scalar")
                WT("tensor", "scalar", tsc)
            WT("sync", "scalar")
            s1st_t[q4] = DMAS(s1_dram[ct, 128*q4:128*(q4+1), :], s1tm_st[:, q4 % 2, :])
        BAR()
    BAR()
    WS("sync", s_st.n)
    BAR()


    kv_bf = kv_t[:].rearrange("p a c t -> p (a c t)").bitcast(BF16)
    wq_bf = wq_sb[:].rearrange("p s c d -> p (s c d)")
    rw1a = kv_bf[:, 0:16384].rearrange("p (c h) -> p c h", h=4096)
    rw1b = wq_bf[:, 0:16384].rearrange("p (c h) -> p c h", h=4096)
    hbuf0 = dmat_bf[:, 0:8192].rearrange("p (c t) -> p c t", t=TC)
    hbuf1 = powm_bf[:, 0:8192].rearrange("p (c t) -> p c t", t=TC)
    sd = scan_ar[:].rearrange("p a t -> p (a t)")
    sdi = sd.bitcast(I32)
    lg_tm = sd[:, 0:256].rearrange("p (i e) -> p i e", e=8)
    sel0 = sd[:, 256:512].rearrange("p (i e) -> p i e", e=8)
    sel1 = sd[:, 512:768].rearrange("p (i e) -> p i e", e=8)
    cmpD = sd[:, 768:1024].rearrange("p (i e) -> p i e", e=8)
    rnk8 = sd[:, 1024:1280].rearrange("p (i e) -> p i e", e=8)
    e0f = sd[:, 1280:1312]; e1f = sd[:, 1312:1344]
    p0c = sd[:, 1344:1376]; p1c = sd[:, 1376:1408]
    pos0 = sd[:, 1408:1440]; pos1 = sd[:, 1440:1472]
    keep0 = sd[:, 1472:1504]; keep1 = sd[:, 1504:1536]
    sid0 = sd[:, 1536:1568]; sid1 = sd[:, 1568:1600]
    tmp32 = sd[:, 1600:1632]; tmp32b = sd[:, 1632:1664]
    sid0_i = sdi[:, 1664:1696]; sid1_i = sdi[:, 1696:1728]
    isc0_i = sdi[:, 1728:1760]; isc1_i = sdi[:, 1760:1792]
    tmp8a = sd[:, 1792:1800]; tmp8b = sd[:, 1800:1808]
    vaC = sd[:, 1808:1840]; vbC = sd[:, 1840:1872]; dvab = sd[:, 1872:1904]
    rw2_sb = a_scan[:].bitcast(BF16)[:, 0:256].rearrange("p (c e) -> p c e", e=8)
    base_rep = b_scan[:, 0:64]
    eb1_sb = b_scan[:, 64:96]
    rb1_sb = b_scan[:, 96:128]
    rb2_sb = b_scan[:, 128:129]
    corebase_sb = b_scan[:, 129:130]
    crep = b_scan[:, 130:131]
    payst = b_scan[:, 131:133]
    trib8_sb = b_scan[:, 192:256]
    ilst = b_scan[0:8, 256:512].rearrange("p (t j) -> p t j", j=2)
    tpst = b_scan[0:16, 256:384]
    lgf = ev_t[0:16, :]
    DmatF = Dmat[:].rearrange("p c t -> p (c t)")
    A_all = DmatF[:, 0:1024]
    onesD = DmatF[:, 1024:2048]
    Acum = DmatF[:, 2048:3072]
    Acum3 = DmatF[:, 2048:3072].rearrange("p (t j) -> p t j", j=2)
    cnt_s = DmatF[0:1, 3072:3584]
    cntl_s = DmatF[0:1, 3584:3648]
    base1_s = DmatF[0:1, 3648:3712]
    base2_s = DmatF[0:1, 3712:3776]
    baseT_s = DmatF[0:1, 3776:3840]
    mask_sb = csum[0:1, 0:512]
    for k in range(16):
        tl = DMAL(cast_f32[:, :], rw1T_in.rearrange("p c h -> p (c h)")[:, 2048*k:2048*(k+1)])
        WL("vector", tl)
        dstap = (kv_bf if k < 8 else wq_bf)[:, 2048*(k % 8):2048*(k % 8)+2048]
        E("vector", lambda e, dstap=dstap: e.tensor_copy(out=dstap, in_=cast_f32[:, :]))
        tv = SIG("vector")
        WT("sync", "vector", tv)
    tl = DMAL(cast_f32[:, 0:256].rearrange("p (c e) -> p c e", e=8), rw2T_in.rearrange("c p e -> p c e"))
    WL("vector", tl)
    E("vector", lambda e: e.tensor_copy(out=rw2_sb[:].rearrange("p c e -> p (c e)"), in_=cast_f32[:, 0:256]))
    SIG("vector")
    DMAL(rb1_sb, rb1_in[:, :])
    DMAL(rb2_sb[0:8, :], rb2_in[:, :])
    DMAL(trib8_sb, trib8_in[:, :])
    DMAL(mskc512_in_sb := mask_sb, mskc512_in[:, :])
    DMAL(corebase_sb[0:1, :], corebase_in[:, :])
    DMAL(eb1_sb, eb1_in[:, :])
    E("vector", lambda e: e.memset(lgf[:, :], 0.0))
    SIG("vector")
    BAR()
    for grp in range(8):
        tl = DMAL(xnf_seg[:], xnf1_dram[:, :, grp*TC:(grp+1)*TC])
        WL("tensor", tl)
        tsc_pp = {0: 0, 1: 0}
        for hc in range(32):
            pp = psA if hc % 2 == 0 else psB
            WT("tensor", "scalar", tsc_pp[hc % 2])
            for c in range(NCH):
                lh = (rw1a if c < 4 else rw1b)[:, c % 4, 128*hc:128*hc+128]
                E("tensor", lambda e, lh=lh, c=c, pp=pp: e.matmul(pp[:], lh, xnf_seg[:, c, :], start=(c == 0), stop=(c == NCH-1)))
            tm = SIG("tensor")
            WT("scalar", "tensor", tm)
            hb = (hbuf0 if hc < 16 else hbuf1)
            E("scalar", lambda e, hb=hb, hc=hc, pp=pp: e.activation(out=hb[:, hc % 16, :], in_=pp[:], func=AF.Relu, bias=rb1_sb[:, hc:hc+1]))
            tsc_pp[hc % 2] = SIG("scalar")
        BAR()
        for hc in range(32):
            hb = (hbuf0 if hc < 16 else hbuf1)
            E("tensor", lambda e, hb=hb, hc=hc: e.matmul(psR[0:8, :], rw2_sb[:, hc, :], hb[:, hc % 16, :], start=(hc == 0), stop=(hc == 31)))
        tm = SIG("tensor")
        WT("scalar", "tensor", tm)
        E("scalar", lambda e: e.activation(out=lgf[0:8, :], in_=psR[0:8, :], func=AF.Identity, bias=rb2_sb[0:8, 0:1]))
        tsc = SIG("scalar")
        WT("tensor", "scalar", tsc)
        for q4 in range(4):
            E("tensor", lambda e, q4=q4: e.transpose(psTf[:], lgf[:, 128*q4:128*(q4+1)], identf[0:16, :]))
            tm = SIG("tensor")
            WT("vector", "tensor", tm)
            E("vector", lambda e, grp=grp, q4=q4: e.tensor_copy(out=lg_tm[:, grp*4+q4, :], in_=psTf[:, 0:8]))
            tv = SIG("vector")
            WT("tensor", "vector", tv)
        BAR()
    WT("sync", "vector")
    E("vector", lambda e: e.memset(rnk8[:].rearrange("p i e -> p (i e)"), 0.0))
    for e2 in range(8):
        E("vector", lambda e, e2=e2: e.tensor_tensor(out=cmpD[:], in0=lg_tm[:, :, e2:e2+1].broadcast_to([128, NTILE, 8]), in1=lg_tm[:], op=ALU.is_gt))
        E("vector", lambda e: e.tensor_add(out=rnk8[:], in0=rnk8[:], in1=cmpD[:]))
        E("vector", lambda e, e2=e2: e.tensor_tensor(out=cmpD[:], in0=lg_tm[:, :, e2:e2+1].broadcast_to([128, NTILE, 8]), in1=lg_tm[:], op=ALU.is_equal))
        E("vector", lambda e, e2=e2: e.tensor_tensor(out=cmpD[:], in0=cmpD[:], in1=trib8_sb[:, e2*8:(e2+1)*8].unsqueeze(1).broadcast_to([128, NTILE, 8]), op=ALU.mult))
        E("vector", lambda e: e.tensor_add(out=rnk8[:], in0=rnk8[:], in1=cmpD[:]))
    E("vector", lambda e: e.tensor_scalar(out=sel0[:].rearrange("p i e -> p (i e)"), in0=rnk8[:].rearrange("p i e -> p (i e)"), scalar1=0.0, scalar2=None, op0=ALU.is_equal))
    E("vector", lambda e: e.tensor_scalar(out=sel1[:].rearrange("p i e -> p (i e)"), in0=rnk8[:].rearrange("p i e -> p (i e)"), scalar1=1.0, scalar2=None, op0=ALU.is_equal))
    for sel, ef, vf in ((sel0, e0f, vaC), (sel1, e1f, vbC)):
        E("vector", lambda e, sel=sel: e.tensor_tensor(out=cmpD[:], in0=sel[:], in1=io_col[:, 0:8].unsqueeze(1).broadcast_to([128, NTILE, 8]), op=ALU.mult))
        E("vector", lambda e, ef=ef: e.tensor_reduce(out=ef.rearrange("p (i q) -> p i q", q=1), in_=cmpD[:], axis=AX.X, op=ALU.add))
        E("vector", lambda e, sel=sel: e.tensor_tensor(out=cmpD[:], in0=sel[:], in1=lg_tm[:], op=ALU.mult))
        E("vector", lambda e, vf=vf: e.tensor_reduce(out=vf.rearrange("p (i q) -> p i q", q=1), in_=cmpD[:], axis=AX.X, op=ALU.add))
    E("vector", lambda e: e.tensor_sub(out=dvab, in0=vaC, in1=vbC))
    tv = SIG("vector")
    WT("scalar", "vector", tv)
    E("scalar", lambda e: e.activation(out=p0c, in_=dvab, func=AF.Sigmoid))
    tsc = SIG("scalar")
    WT("vector", "scalar", tsc)
    E("vector", lambda e: e.tensor_scalar(out=p1c, in0=p0c, scalar1=-1.0, scalar2=1.0, op0=ALU.mult, op1=ALU.add))
    E("vector", lambda e: e.memset(A_all, 0.0))
    E("vector", lambda e: e.memset(onesD, 1.0))
    SIG("vector")
    BAR()
    ilst_t = 0
    tsc_a = 0
    for i in range(NTILE):
        l = i // 4
        for j, sel in ((0, sel0), (1, sel1)):
            WT("tensor", "scalar", tsc_a)
            E("tensor", lambda e, sel=sel, i=i: e.transpose(psTf[0:8, :], sel[:, i, :], identf[:]))
            tm = SIG("tensor")
            WT("scalar", "tensor", tm)
            if j == 0:
                WS("scalar", ilst_t)
            E("scalar", lambda e, j=j: e.copy(out=ilst[:, :, j], in_=psTf[0:8, :]))
            tsc_a = SIG("scalar")
        WT("sync", "scalar")
        ilst_t = DMAS(DmatF[l*8:l*8+8, 256*(i % 4):256*(i % 4)+256], ilst[:].rearrange("p t j -> p (t j)"))
    BAR()
    WS("vector", s_st.n)
    E("vector", lambda e: e.tensor_tensor_scan(out=Acum, data0=onesD, data1=A_all, initial=zcol[:, 0:1], op0=ALU.mult, op1=ALU.add))
    tv = SIG("vector")
    WT("sync", "vector", tv)
    tst = DMAS(cnt2_in[:, :], DmatF[0:64, 3071:3072])
    WS("gpsimd", tst)
    BAR()
    E("gpsimd", lambda e: e.collective_compute("AllGather", ALU.bypass, replica_groups=[list(range(NCORE))],
                                               ins=[cnt2_in[:, :]], outs=[cnt2_out[:, :, :]]))
    SIG("gpsimd")
    BAR()
    WT("sync", "gpsimd")
    tl = DMAL(cnt_s, cnt2_out.rearrange("c q o -> o (c q)"))
    WL("vector", tl)
    E("vector", lambda e: e.tensor_copy(out=cntl_s, in_=cnt_s[0:1, 0:64]))
    for cc in range(1, 8):
        E("vector", lambda e, cc=cc: e.tensor_add(out=cntl_s, in0=cntl_s, in1=cnt_s[0:1, 64*cc:64*cc+64]))
    E("vector", lambda e: e.memset(base1_s[0:1, 0:8], 0.0))
    for l in range(1, 8):
        E("vector", lambda e, l=l: e.tensor_add(out=base1_s[0:1, 8*l:8*l+8], in0=base1_s[0:1, 8*(l-1):8*l], in1=cntl_s[0:1, 8*(l-1):8*l]))
    E("vector", lambda e: e.tensor_tensor(out=cnt_s, in0=cnt_s, in1=mask_sb, op=ALU.mult))
    E("vector", lambda e: e.tensor_copy(out=base2_s, in_=cnt_s[0:1, 0:64]))
    for cc in range(1, 8):
        E("vector", lambda e, cc=cc: e.tensor_add(out=base2_s, in0=base2_s, in1=cnt_s[0:1, 64*cc:64*cc+64]))
    E("vector", lambda e: e.tensor_add(out=baseT_s, in0=base1_s, in1=base2_s))
    SIG("vector")
    BAR()
    E("tensor", lambda e: e.matmul(psR[:, 0:64], onesNf[:], baseT_s, start=True, stop=True))
    tm = SIG("tensor")
    WT("vector", "tensor", tm)
    E("vector", lambda e: e.tensor_copy(out=base_rep, in_=psR[:, 0:64]))
    E("vector", lambda e: e.memset(tpst[:, :], 0.0))
    SIG("vector")
    BAR()
    tm_tp = 0
    for i in range(NTILE):
        l = i // 4; toff = (i % 4) * 128
        for j, sel, posX in ((0, sel0, pos0), (1, sel1, pos1)):
            WT("sync", "tensor", tm_tp)
            tst = DMAL(tpst[0:8, :], Acum3[l*8:l*8+8, toff:toff+128, j], allow_slow_non_contiguous=True)
            WL("tensor", tst)
            E("tensor", lambda e: e.transpose(psTf[:], tpst[:, :], identf[0:16, :]))
            tm_tp = SIG("tensor")
            WT("vector", "tensor", tm_tp)
            E("vector", lambda e: e.tensor_copy(out=tmp8a, in_=psTf[:, 0:8]))
            E("vector", lambda e, l=l: e.tensor_tensor(out=tmp8b, in0=tmp8a, in1=base_rep[:, l*8:l*8+8], op=ALU.add))
            E("vector", lambda e, sel=sel, i=i: e.tensor_tensor(out=tmp8b, in0=tmp8b, in1=sel[:, i, :], op=ALU.mult))
            E("vector", lambda e, posX=posX, i=i: e.tensor_reduce(out=posX[:, i:i+1].rearrange("p (i q) -> p i q", q=1), in_=tmp8b.rearrange("p (i e) -> p i e", e=8), axis=AX.X, op=ALU.add))
            E("vector", lambda e, posX=posX, i=i: e.tensor_scalar(out=posX[:, i:i+1], in0=posX[:, i:i+1], scalar1=-1.0, scalar2=None, op0=ALU.add))
            tv = SIG("vector")
    BAR()
    for posX, keepX, eXf, sidX, sidX_i, iscX_i in ((pos0, keep0, e0f, sid0, sid0_i, isc0_i), (pos1, keep1, e1f, sid1, sid1_i, isc1_i)):
        E("vector", lambda e, posX=posX, keepX=keepX: e.tensor_scalar(out=keepX, in0=posX, scalar1=4096.0, scalar2=None, op0=ALU.is_lt))
        E("vector", lambda e, posX=posX: e.tensor_scalar(out=tmp32, in0=posX, scalar1=4095.0, scalar2=None, op0=ALU.min))
        E("vector", lambda e, eXf=eXf: e.tensor_scalar(out=tmp32b, in0=eXf, scalar1=4096.0, scalar2=None, op0=ALU.mult))
        E("vector", lambda e, sidX=sidX: e.tensor_add(out=sidX, in0=tmp32b, in1=tmp32))
        E("vector", lambda e, sidX=sidX, sidX_i=sidX_i: e.tensor_copy(out=sidX_i, in_=sidX))
        E("vector", lambda e, posX=posX: e.tensor_scalar(out=tmp32, in0=posX, scalar1=5119.0, scalar2=None, op0=ALU.min))
        E("vector", lambda e, eXf=eXf: e.tensor_scalar(out=tmp32b, in0=eXf, scalar1=5120.0, scalar2=None, op0=ALU.mult))
        E("vector", lambda e: e.tensor_add(out=tmp32, in0=tmp32, in1=tmp32b))
        E("vector", lambda e, iscX_i=iscX_i: e.tensor_copy(out=iscX_i, in_=tmp32))
    SIG("vector")
    WT("sync", "vector")
    BAR()
    WS("sync", s_st.n)
    BAR()


    xqf = xq_t[:].rearrange("p c t -> p (c t)").bitcast(F32)
    xqi = xq_t[:].rearrange("p c t -> p (c t)").bitcast(I32)
    eb2r = xqf[:, 0:1024]
    disp_sb = xqf[:, 1024:1088].rearrange("p (a w) -> p a w", w=2)
    idc_i = xqi[:, 1088:1120]
    E("vector", lambda e: e.memset(cast_f32[:, 0:640], 0.0))
    tv = SIG("vector")
    WT("sync", "vector", tv)
    tz = DMAS(dispT.rearrange("(p n) w -> p (n w)", p=128), cast_f32[:, 0:640])
    E("tensor", lambda e: e.matmul(psR[:, 0:1], onesNf[:], corebase_sb[0:1, :], start=True, stop=True))
    tm = SIG("tensor")
    WT("vector", "tensor", tm)
    E("vector", lambda e: e.tensor_copy(out=crep, in_=psR[:, 0:1]))
    E("vector", lambda e: e.tensor_add(out=crep, in0=crep, in1=io_row[:, 0:1]))
    SIG("vector")
    WS("gpsimd", tz)
    tgd = 0
    for i in range(NTILE):
        l = i // 4; toff = (i % 4) * 128
        for j, pc, iscX in ((0, p0c, isc0_i), (1, p1c, isc1_i)):
            WG("vector", tgd)
            E("vector", lambda e, l=l, toff=toff: e.tensor_scalar(out=payst[:, 0:1], in0=crep, scalar1=float(l*512 + toff), scalar2=None, op0=ALU.add))
            E("vector", lambda e, pc=pc, i=i: e.tensor_copy(out=payst[:, 1:2], in_=pc[:, i:i+1]))
            tv = SIG("vector")
            WT("gpsimd", "vector", tv)
            tgd = GDMA(dispT[:, :], payst, out_off=IndirectOffsetOnAxis(iscX[:, i:i+1], 0))
    BAR()
    E("gpsimd", lambda e: e.collective_compute("ReduceScatter", ALU.add, replica_groups=[list(range(NCORE))],
                                               ins=[dispT[:, :]], outs=[disp_rs[:, :]]))
    SIG("gpsimd")
    BAR()
    tl = DMAL(disp_sb[:], disp_rs[0:4096, :].rearrange("(a p) w -> p a w", p=128))
    tl2 = DMAL(eb2r, eb2_in[:, :])
    WL("vector", tl)
    E("vector", lambda e: e.tensor_copy(out=idc_i, in_=disp_sb[:, :, 0]))
    SIG("vector")
    BAR()
    for mat_src, mat_dst in ((ew1T_in, ew1_bf), (ew2T_in, ew2_bf)):
        msrc = mat_src.rearrange("p c h -> p (c h)")
        mdst = mat_dst.rearrange("p c h -> p (c h)")
        for k in range(16):
            tl = DMAL(cast_f32[:, :], msrc[:, 2048*k:2048*(k+1)])
            WL("vector", tl)
            E("vector", lambda e: e.tensor_copy(out=cast_bf[:, :], in_=cast_f32[:, :]))
            tv = SIG("vector")
            WT("sync", "vector", tv)
            tstw = DMAS(mdst[:, 2048*k:2048*(k+1)], cast_bf[:, :])
            WS("vector", tstw)
            WS("sync", tstw)
    BAR()
    xeF = [(kv_bf if t < 4 else wq_bf)[:, 4096*(t % 4):4096*(t % 4)+4096].rearrange("p (c s) -> p c s", s=TC) for t in range(8)]
    xg_st = [camax[:].bitcast(BF16)[:, 0:1024], csum[:].bitcast(BF16)[:, 0:1024]]
    xn1_flat = xn1_ag.rearrange("c n d -> (c n) d")
    tm_xg = {0: 0, 1: 0}
    for t in range(8):
        for q in range(4):
            a = t * 4 + q
            pb = a % 2
            WT("gpsimd", "tensor", tm_xg[pb])
            tg = GDMA(xg_st[pb], xn1_flat, in_off=IndirectOffsetOnAxis(idc_i[:, a:a+1], 0))
            WG("tensor", tg)
            for c in range(NCH):
                pp = psT if c % 2 == 0 else psU
                E("tensor", lambda e, pb=pb, c=c, pp=pp: e.transpose(pp[:], xg_st[pb][:, 128*c:128*(c+1)], ident[:]))
                tmx = SIG("tensor")
                WT("scalar", "tensor", tmx)
                E("scalar", lambda e, t=t, c=c, q=q, pp=pp: e.copy(out=xeF[t][:, c, 128*q:128*(q+1)], in_=pp[:]))
                tsc = SIG("scalar")
                WT("tensor", "scalar", tsc)
            tm_xg[pb] = C.sems["tensor"].n
    BAR()
    mixf = mix_t[:].rearrange("p c t -> p (c t)")
    ew1st = [mixf[:, 0:1024].rearrange("p (c d) -> p c d", d=128), mixf[:, 1024:2048].rearrange("p (c d) -> p c d", d=128)]
    hst = [mixf[:, 2048:2560], mixf[:, 2560:3072]]
    tls = {0: DMAL(ew1st[0][:, :, :], ew1_bf[:, :, 0:128])}
    tm_done = {}
    hst_st = {0: 0, 1: 0}
    tsc_pp = {0: 0, 1: 0}
    for hc in range(32):
        if hc + 1 < 32:
            if hc - 1 >= 0:
                WT("sync", "tensor", tm_done[hc - 1])
            tls[hc + 1] = DMAL(ew1st[(hc + 1) % 2][:, :, :], ew1_bf[:, :, 128*(hc+1):128*(hc+2)])
        WL("tensor", tls[hc])
        for t in range(8):
            pp = psA if t % 2 == 0 else psB
            WT("tensor", "scalar", tsc_pp[t % 2])
            for c in range(NCH):
                E("tensor", lambda e, hc=hc, c=c, t=t, pp=pp: e.matmul(pp[:], ew1st[hc % 2][:, c, :], xeF[t][:, c, :], start=(c == 0), stop=(c == NCH-1)))
            tmx = SIG("tensor")
            WT("scalar", "tensor", tmx)
            WS("scalar", hst_st[t % 2])
            E("scalar", lambda e, t=t, hc=hc: e.activation(out=hst[t % 2], in_=(psA if t % 2 == 0 else psB)[:], func=AF.Relu, bias=eb1_sb[:, hc:hc+1]))
            tsc_pp[t % 2] = SIG("scalar")
            WT("sync", "scalar", tsc_pp[t % 2])
            hst_st[t % 2] = DMAS(h_dram[t, hc, :, :], hst[t % 2])
        tm_done[hc] = C.sems["tensor"].n
    BAR()
    WS("sync", s_st.n)
    for hc in range(32):
        dst = (kv_bf if hc < 16 else wq_bf)[:, 1024*(hc % 16):1024*(hc % 16)+1024]
        DMAL(dst, ew2_bf[:, hc, :])
    BAR()
    yo_bf2 = camax[:].bitcast(BF16)[:, 0:1024]
    tmpW = cast_f32[:, 0:1024]
    for t in range(8):
        tlh1 = DMAL(hbuf0[:, :, :], h_dram[t, 0:16].rearrange("c p t -> p c t"))
        tlh2 = DMAL(hbuf1[:, :, :], h_dram[t, 16:32].rearrange("c p t -> p c t"))
        WL("tensor", tlh2)
        tv_yo = 0
        for s4 in range(4):
            a = t * 4 + s4
            WT("tensor", "vector", tv_yo)
            for half in range(2):
                for hc in range(32):
                    hb = hbuf0 if hc < 16 else hbuf1
                    ew2r = (kv_bf if hc < 16 else wq_bf)[:, 1024*(hc % 16):1024*(hc % 16)+1024]
                    E("tensor", lambda e, hb=hb, hc=hc, s4=s4, half=half, ew2r=ew2r: e.matmul(
                        psW[:, 512*half:512*(half+1)], hb[:, hc % 16, 128*s4:128*(s4+1)], ew2r[:, 512*half:512*(half+1)], start=(hc == 0), stop=(hc == 31)))
            tmx = SIG("tensor")
            WT("vector", "tensor", tmx)
            E("vector", lambda e: e.tensor_add(out=tmpW, in0=psW[:], in1=eb2r))
            E("vector", lambda e, a=a: e.tensor_scalar(out=yo_bf2, in0=tmpW, scalar1=disp_sb[:, a, 1:2], scalar2=None, op0=ALU.mult))
            tv_yo = SIG("vector")
            WT("sync", "vector", tv_yo)
            tst = DMAS(yo_dram[512*t+128*s4 : 512*t+128*s4+128, :], yo_bf2)
            WS("vector", tst)
        BAR()
    BAR()
    WS("sync", s_st.n)
    BAR()
    E("gpsimd", lambda e: e.collective_compute("AllGather", ALU.bypass, replica_groups=[list(range(NCORE))],
                                               ins=[yo_dram[:, :]], outs=[yo_ag[:, :, :]]))
    SIG("gpsimd")
    E("gpsimd", lambda e: e.collective_compute("AllGather", ALU.bypass, replica_groups=[list(range(NCORE))],
                                               ins=[cnt2_in[:, :]], outs=[cnt2_out[:, :, :]]))
    SIG("gpsimd")
    BAR()
    yoflat = yo_ag.rearrange("c n d -> (c n) d")
    g1acc = cast_f32[:, 0:1024]
    gsil2 = cast_f32[:, 1024:2048]
    ybg1 = camax[:].bitcast(BF16)[:, 0:1024]
    ybg2 = csum[:].bitcast(BF16)[:, 0:1024]
    s1sb = camax[:].bitcast(BF16)[:, 0:1024]
    tv_fin = 0
    to_st = 0
    for i in list(range(1, NTILE)) + [0]:
        l = i // 4
        WT("gpsimd", "vector", tv_fin)
        tg1 = GDMA(ybg1, yoflat, in_off=IndirectOffsetOnAxis(sid0_i[:, i:i+1], 0))
        tg2 = GDMA(ybg2, yoflat, in_off=IndirectOffsetOnAxis(sid1_i[:, i:i+1], 0))
        WG("vector", tg1)
        E("vector", lambda e, i=i: e.tensor_scalar(out=g1acc, in0=ybg1, scalar1=keep0[:, i:i+1], scalar2=0.0, op0=ALU.mult, op1=ALU.add))
        WG("vector", tg2)
        E("vector", lambda e, i=i: e.scalar_tensor_tensor(out=g1acc, in0=ybg2, scalar=keep1[:, i:i+1], in1=g1acc, op0=ALU.mult, op1=ALU.add))
        tv = SIG("vector")
        WT("sync", "vector", tv)
        tl1 = DMAL(x_tile[:, 0, :], x1_dram[i, :, :])
        tl2 = DMAL(s1sb.rearrange("p (q d) -> p q d", d=D)[0:64, 0, :], s1_dram[2*(i % 4), l*64:l*64+64, :])
        tl3 = DMAL(s1sb.rearrange("p (q d) -> p q d", d=D)[64:128, 0, :], s1_dram[2*(i % 4)+1, l*64:l*64+64, :])
        WT("scalar", "vector", tv)
        E("scalar", lambda e: e.activation(out=gsil2, in_=g1acc, func=AF.Silu))
        tsc = SIG("scalar")
        WL("vector", tl2)
        WL("vector", tl3)
        WL("vector", tl1)
        WT("vector", "scalar", tsc)
        WS("vector", to_st)
        E("vector", lambda e: e.tensor_tensor(out=gsil2, in0=s1sb, in1=gsil2, op=ALU.mult))
        E("vector", lambda e: e.tensor_tensor(out=x_tile[:, 1, :], in0=x_tile[:, 0, :], in1=gsil2, op=ALU.add))
        tv_fin = SIG("vector")
        WT("sync", "vector", tv_fin)
        to_st = DMAS(out_dram[i, :, :], x_tile[:, 1, :])
    BAR()
    WS("sync", s_st.n)
    BAR()

    @block.sync
    def _(eng):
        for f in C.q["sync"]:
            f(eng)

    @block.vector
    def _(eng):
        for f in C.q["vector"]:
            f(eng)

    @block.scalar
    def _(eng):
        for f in C.q["scalar"]:
            f(eng)

    @block.tensor
    def _(eng):
        for f in C.q["tensor"]:
            f(eng)

    @block.gpsimd
    def _(eng):
        for f in C.q["gpsimd"]:
            f(eng)

    st.close()
    return nc




_CACHE = {}


def _host_prep(ins):
    x = np.ascontiguousarray(ins["L0_attn_hidden"], np.float32)
    gw = ins["L0_ffn_gate_w"]; uw = ins["L0_ffn_up_w"]; dw = ins["L0_ffn_down_w"]
    gpart = np.transpose(np.ascontiguousarray(gw, np.float32).reshape(16, 512, 8, 128), (0, 3, 2, 1)).reshape(16, 128, 4096)
    upart = np.transpose(np.ascontiguousarray(uw, np.float32).reshape(16, 512, 8, 128), (0, 3, 2, 1)).reshape(16, 128, 4096)
    dpart = np.transpose(np.ascontiguousarray(dw, np.float32).reshape(16, 1024, 4, 128), (0, 3, 2, 1)).reshape(16, 128, 4096)
    wcat = np.ascontiguousarray(np.concatenate([gpart, upart, dpart], axis=2), np.float32)
    shg = np.ascontiguousarray(np.transpose(np.asarray(ins["L0_ffn_sh_gate_w"], np.float32).reshape(1024, 8, 128), (2, 1, 0)))
    shu = np.ascontiguousarray(np.transpose(np.asarray(ins["L0_ffn_sh_up_w"], np.float32).reshape(1024, 8, 128), (2, 1, 0)))
    shd = np.ascontiguousarray(np.transpose(np.asarray(ins["L0_ffn_sh_down_w"], np.float32).reshape(1024, 8, 128), (2, 1, 0)))
    wqT = np.ascontiguousarray(np.transpose(np.asarray(ins["L1_attn_Wq"], np.float32).reshape(1024, 8, 128), (2, 1, 0)))
    wkT = np.ascontiguousarray(np.transpose(np.asarray(ins["L1_attn_Wk"], np.float32).reshape(1024, 8, 128), (2, 1, 0)))
    wvT = np.ascontiguousarray(np.transpose(np.asarray(ins["L1_attn_Wv"], np.float32).reshape(1024, 8, 128), (2, 1, 0)))
    pe = np.asarray(ins["L1_attn_pe"], np.float32)
    pe_arr = np.zeros((88, 128, 16), np.float32)
    selc = np.zeros((8, 128, 16), np.float32)
    rsel = np.zeros((16, 8, 128), np.float32)
    for c8 in range(8):
        for hh in range(2):
            h = 2*c8 + hh
            for j in range(11):
                pe_arr[j*8+c8, 64*hh:64*hh+64, h] = pe[h, :, j]
            selc[c8, 64*hh:64*hh+64, h] = 1.0
            rsel[h, c8, 64*hh:64*hh+64] = 1.0
    selc_bf = selc.astype(ml_dtypes.bfloat16)
    rsel_bf = rsel.astype(ml_dtypes.bfloat16)
    rw1T = np.ascontiguousarray(np.transpose(np.asarray(ins["L1_ffn_rw1"], np.float32).reshape(8, 128, 4096), (1, 0, 2)))
    rw2T = np.ascontiguousarray(np.asarray(ins["L1_ffn_rw2"], np.float32).reshape(32, 128, 8))
    rb1h = np.ascontiguousarray(np.asarray(ins["L1_ffn_rb1"], np.float32).reshape(32, 128).T)
    rb2h = np.asarray(ins["L1_ffn_rb2"], np.float32).reshape(8, 1)
    tri8 = (np.arange(8)[:, None] < np.arange(8)[None, :]).astype(np.float32).reshape(64)
    trib8h = np.tile(tri8, (128, 1)).astype(np.float32)
    ew1h = np.ascontiguousarray(np.transpose(np.asarray(ins["L1_ffn_ew1"], np.float32).reshape(8, 8, 128, 4096), (0, 2, 1, 3)))
    ew2h = np.ascontiguousarray(np.transpose(np.asarray(ins["L1_ffn_ew2"], np.float32).reshape(8, 32, 128, 1024), (0, 2, 1, 3)))
    eb1h = np.ascontiguousarray(np.transpose(np.asarray(ins["L1_ffn_eb1"], np.float32).reshape(8, 32, 128), (0, 2, 1)))
    eb2h = np.asarray(ins["L1_ffn_eb2"], np.float32)
    per_core = []
    for r in range(NCORE):
        xs = np.zeros((NB, SEG, D), np.float32)
        t0 = r * TC
        xs[:, 1:, :] = x[:, t0:t0+TC, :]
        if r > 0:
            xs[:, 0, :] = x[:, t0-1, :]
        m = {
            "x_in": xs,
            "w_key": np.asarray(ins["L0_attn_w_key"], np.float32),
            "w_value": np.asarray(ins["L0_attn_w_value"], np.float32),
            "w_recep": np.asarray(ins["L0_attn_w_receptance"], np.float32),
            "w_output": np.asarray(ins["L0_attn_w_output"], np.float32),
            "rms_key": np.asarray(ins["L0_attn_rms_key"], np.float32),
            "rms_value": np.asarray(ins["L0_attn_rms_value"], np.float32),
            "rms_recep": np.asarray(ins["L0_attn_rms_receptance"], np.float32),
            "rms_output": np.asarray(ins["L0_attn_rms_output"], np.float32),
            "tmix_k": np.asarray(ins["L0_attn_time_mix_key"], np.float32).ravel(),
            "tmix_v": np.asarray(ins["L0_attn_time_mix_value"], np.float32).ravel(),
            "tmix_r": np.asarray(ins["L0_attn_time_mix_receptance"], np.float32).ravel(),
            "tdecay": np.asarray(ins["L0_attn_time_decay"], np.float32),
            "tfirst": np.asarray(ins["L0_attn_time_first"], np.float32),
            "rankm": (np.arange(NCORE) < r).astype(np.float32),
            "rankw": np.maximum(r - 1 - np.arange(NCORE), 0).astype(np.float32),
            "wcat_in": wcat,
            "shg_in": shg, "shu_in": shu, "shd_in": shd,
            "rw_in": np.asarray(ins["L0_ffn_router_weight"], np.float32),
            "ebias_in": np.asarray(ins["L0_ffn_e_bias"], np.float32),
            "wqT_in": wqT, "wkT_in": wkT, "wvT_in": wvT,
            "pe_in": pe_arr, "selc_in": selc_bf, "rsel_in": rsel_bf,
            "rw1T_in": rw1T, "rw2T_in": rw2T, "rb1_in": rb1h, "rb2_in": rb2h,
            "trib8_in": trib8h,
            "mskc512_in": np.tile((np.arange(8) < r).astype(np.float32)[:, None], (1, 64)).reshape(1, 512),
            "corebase_in": np.array([[r * 4096.0]], np.float32),
            "ew1T_in": ew1h[r], "ew2T_in": ew2h[r], "eb1_in": eb1h[r],
            "eb2_in": np.tile(eb2h[r], (128, 1)),
        }
        per_core.append(m)
    return per_core


def kernel(**inputs):
    if "nc" not in _CACHE:
        _CACHE["nc"] = build_program()
    nc = _CACHE["nc"]
    per_core = _host_prep(inputs)
    res = run_bass_kernel_spmd(nc, per_core, core_ids=list(range(NCORE)))
    full = np.empty((NB, NTOK, D), np.float32)
    for r in range(NCORE):
        o = np.asarray(res.results[r]["out_dram"], np.float32).reshape(NB, TC, D)
        full[:, r*TC:(r+1)*TC, :] = o
    return full


# version 02
# speedup vs baseline: 1.0202x; 1.0202x over previous
import contextlib
import numpy as np
import ml_dtypes
import concourse.bass as bass
import concourse.mybir as mybir
from concourse.bass import IndirectOffsetOnAxis
from concourse.bass_utils import run_bass_kernel_spmd

F32 = mybir.dt.float32
BF16 = mybir.dt.bfloat16
I32 = mybir.dt.int32
AF = mybir.ActivationFunctionType
ALU = mybir.AluOpType
AX = mybir.AxisListType

NB, TC, D, NCH = 8, 512, 1024, 8
SEG = TC + 1
NTOK = NB * TC
NTILE = 32
MAGIC = 1.5 * 2.0 ** 23
RMS_EPS = 1e-8
LN_EPS = 1e-5
ENGINES = ("sync", "vector", "scalar", "tensor", "gpsimd")
NCORE = 8


class Sem:
    def __init__(self, h):
        self.h, self.n = h, 0


AUTO_SYNC = ("vector", "scalar", "gpsimd")


class Ctx:
    def __init__(self):
        self.q = {e: [] for e in ENGINES}
        self.sems = {}

    def emit(self, eng, fn):
        if eng in AUTO_SYNC:
            sem = self.sems[eng]
            prev = sem.n
            sem.n += 1
            def wrapped(e, f=fn, s=sem.h, prev=prev):
                if prev > 0:
                    e.wait_ge(s, prev)
                f(e).then_inc(s, 1)
            self.q[eng].append(wrapped)
        else:
            self.q[eng].append(fn)

    def sig(self, eng, k=1):
        if eng in AUTO_SYNC:
            return self.sems[eng].n
        sem = self.sems[eng]
        f = self.q[eng].pop()
        sem.n += k
        self.q[eng].append(lambda e, f=f, s=sem.h, k=k: f(e).then_inc(s, k))
        return sem.n

    def wait(self, eng, other, val=None):
        sem = self.sems[other]
        v = sem.n if val is None else val
        if v > 0:
            self.q[eng].append(lambda e, s=sem.h, v=v: e.wait_ge(s, v))

    def barrier(self):
        vals = {e: self.sems[e].n for e in ENGINES}
        for e1 in ENGINES:
            for e2 in ENGINES:
                if e1 != e2 and vals[e2] > 0:
                    self.q[e1].append(lambda e, s=self.sems[e2].h, v=vals[e2]: e.wait_ge(s, v))
            for sx in self.dma_sems:
                if sx.n > 0:
                    self.q[e1].append(lambda e, s=sx.h, v=sx.n: e.wait_ge(s, v))


def build_program():
    nc = bass.Bass("TRN2", target_bir_lowering=False, debug=False)
    C = Ctx()
    st = contextlib.ExitStack()
    din = lambda n, s, d=F32: nc.dram_tensor(n, s, d, kind="ExternalInput").ap()
    dou = lambda n, s, d=F32: nc.dram_tensor(n, s, d, kind="ExternalOutput").ap()
    dit = lambda n, s, d=F32: nc.dram_tensor(n, s, d, kind="Internal").ap()
    dsh = lambda n, s, d=F32: nc.dram_tensor(n, s, d, kind="Internal", addr_space="Shared").ap()
    sb = lambda n, s, d=F32: st.enter_context(nc.sbuf_tensor(n, s, d))
    psm = lambda n, s, d=F32: st.enter_context(nc.psum_tensor(n, s, d))
    E, SIG, WT, BAR = C.emit, C.sig, C.wait, C.barrier

    x_in = din("x_in", [NB, SEG, D])
    w_mats = [din(f"w_{m}", [D, D]) for m in ("key", "value", "recep", "output")]
    rms_in = [din(f"rms_{m}", [D]) for m in ("key", "value", "recep", "output")]
    tmix_in = [din(f"tmix_{m}", [D]) for m in ("k", "v", "r")]
    tdecay = din("tdecay", [D]); tfirst = din("tfirst", [D])
    rankm = din("rankm", [NCORE])
    rankw = din("rankw", [NCORE])
    wcat_in = din("wcat_in", [16, 128, 12288])
    shg_in = din("shg_in", [128, NCH, D])
    shu_in = din("shu_in", [128, NCH, D])
    shd_in = din("shd_in", [128, NCH, D])
    rw_in = din("rw_in", [16, D])
    ebias_in = din("ebias_in", [16])

    kv_spill = dit("kv_spill", [2, NCH, 128, NTOK], F32)
    cc_in = dit("cc_in", [128, 128], F32)
    cc_out = dit("cc_out", [NCORE, 128, 128], F32)
    xn_dram = dit("xn_dram", [NTILE + 1, 128, D], BF16)
    xnf_dram = dit("xnf_dram", [128, NCH, NTOK], BF16)
    wq_dram = dit("wq_dram", [4, 128, NCH * D], BF16)
    r_dram = dit("r_dram", [128, NCH, NTOK], BF16)
    s0_dram = dit("s0_dram", [NTILE, 128, D], F32)
    wcat_bf = dit("wcat_bf", [16, 128, 12288], BF16)
    NSLOT = 48 * 512
    slotbuf = dit("slotbuf", [NSLOT + 1, D], BF16)
    yobuf = dit("yobuf", [NSLOT + 1, D], BF16)
    x1_dram = dit("x1_dram", [NTILE, 128, D], F32)

    wqT_in = din("wqT_in", [128, NCH, D])
    wkT_in = din("wkT_in", [128, NCH, D])
    wvT_in = din("wvT_in", [128, NCH, D])
    pe_in = din("pe_in", [88, 128, 16])
    selc_in = din("selc_in", [8, 128, 16], BF16)
    rsel_in = din("rsel_in", [16, NCH, 128], BF16)
    xn1_dram = dit("xn1_dram", [NTILE, 128, D], BF16)
    xnf1_dram = dit("xnf1_dram", [128, NCH, NTOK], BF16)
    s1_dram = dit("s1_dram", [8, TC, D], BF16)
    rw1T_in = din("rw1T_in", [128, NCH, 4096])
    rw2T_in = din("rw2T_in", [32, 128, 8])
    rb1_in = din("rb1_in", [128, 32])
    rb2_in = din("rb2_in", [8, 1])
    trib8_in = din("trib8_in", [128, 64])
    mskc512_in = din("mskc512_in", [1, 512])
    corebase_in = din("corebase_in", [1, 1])
    ew1T_in = din("ew1T_in", [128, NCH, 4096])
    ew2T_in = din("ew2T_in", [128, 32, D])
    eb1_in = din("eb1_in", [128, 32])
    eb2_in = din("eb2_in", [128, D])
    xn1_ag = dit("xn1_ag", [NCORE, NTILE * 128, D], BF16)
    cnt2_in = dit("cnt2_in", [64, 1], F32)
    cnt2_out = dit("cnt2_out", [NCORE, 64, 1], F32)
    dispT = dit("dispT", [40960, 2], F32)
    disp_rs = dit("disp_rs", [5120, 2], F32)
    ew1_bf = dit("ew1_bf", [128, NCH, 4096], BF16)
    ew2_bf = dit("ew2_bf", [128, 32, D], BF16)
    h_dram = dit("h_dram", [8, 32, 128, TC], BF16)
    yo_dram = dit("yo_dram", [NTOK, D], BF16)
    yo_ag = dit("yo_ag", [NCORE, NTOK, D], BF16)
    out_dram = dou("out_dram", [NTILE, 128, D], F32)

    io_col = sb("io_col", [128, 128]); io_row = sb("io_row", [128, 128])
    ident = sb("ident", [128, 128], BF16)
    identf = sb("identf", [128, 128])
    onesNf = sb("onesNf", [1, 128])
    ones4 = sb("ones4", [4, 128])
    onesKb = sb("onesKb", [128, 1], BF16)
    zcol = sb("zcol", [128, 1])
    rmsv = sb("rmsv", [128, 4 * NCH])
    tmixv = sb("tmixv", [128, 3 * NCH])
    tmixm1 = sb("tmixm1", [128, 3 * NCH])
    xnb_sb = sb("xnb_sb", [128, NCH, NB], BF16)
    tdec = sb("tdec", [128, NCH]); tfir = sb("tfir", [128, NCH])
    expu = sb("expu", [128, NCH])
    wlog = sb("wlog", [128, NCH])
    Dmat = sb("Dmat", [128, NCH, TC])
    powm = sb("powm", [128, NCH, TC])
    kv_t = sb("kv_t", [128, 2, NCH, TC])
    wq_sb = sb("wq_sb", [128, 2, NCH, D], BF16)
    inv_sw = sb("inv_sw", [128, 4])
    sw_rep = sb("sw_rep", [128, 4])
    wstage = sb("wstage", [128, NCH, D], BF16)
    d_sh = wstage[:, :, 0:TC]
    m_t = wstage[:, :, TC:2*TC]
    wabs = sb("wabs", [128, 2])
    wabs_bf = sb("wabs_bf", [128, 1], BF16)
    wm_sb = sb("wm_sb", [1, 1])

    xn_st = sb("xn_st", [128, 2, D], BF16)
    xnf_blk = sb("xnf_blk", [128, 2, NCH, 128], BF16)
    xnf_seg = sb("xnf_seg", [128, NCH, TC], BF16)
    x_tile = sb("x_tile", [128, 2, D])
    sqf = sb("sqf", [128, 2, D])
    s1_all = sb("s1_all", [128, NTILE + 1]); s2_all = sb("s2_all", [128, NTILE + 1])
    rstd_all = sb("rstd_all", [128, NTILE + 1]); nmu_all = sb("nmu_all", [128, NTILE + 1])

    mix_t = sb("mix_t", [128, NCH, TC], BF16)
    camax = sb("camax", [128, TC]); csum = sb("csum", [128, TC])
    stat_tm = sb("stat_tm", [128, 8])
    rs_tm = sb("rs_tm", [128, 4]); isx_tm = sb("isx_tm", [128, 4])
    stag4 = sb("stag4", [128, 128])
    rhs512 = sb("rhs512", [128, 512])
    onesF = sb("onesF", [128, 128])
    rs_fm = sb("rs_fm", [128, TC]); isx_fm = sb("isx_fm", [128, TC])
    rnd_f = sb("rnd_f", [128, TC])
    csum2 = rnd_f
    xq_t = sb("xq_t", [128, NCH, TC], BF16)

    r_seg = sb("r_seg", [128, NCH, TC], BF16)
    e_t = sb("e_t", [128, TC]); ev_t = sb("ev_t", [128, TC])
    a_scan = sb("a_scan", [128, TC]); b_scan = sb("b_scan", [128, TC])
    scan_ar = sb("scan_ar", [128, 4, TC])
    eu_t = scan_ar[:, 0, :]; evu_t = scan_ar[:, 1, :]
    n1_t = scan_ar[:, 2, :]; dd_t = scan_ar[:, 3, :]
    gin_t = mix_t
    aend = sb("aend", [128, 2 * NCH * NB])
    carry = sb("carry", [128, 2 * NCH * NB])
    ccg = sb("ccg", [128, NCORE, 128])
    rk_sb = sb("rk_sb", [1, 2 * NCORE])
    rkrep = sb("rkrep", [128, 2 * NCORE])
    powr = sb("powr", [128, NCORE, NCH])
    s0_seg = kv_t[:, 0, :, :].rearrange("p c t -> p (c t)").rearrange("p (s d) -> p s d", s=4)

    scores_tm = scan_ar[:, 0, :].rearrange("p (i e) -> p i e", e=16)
    S_tm = scan_ar[:, 1, :].rearrange("p (i e) -> p i e", e=16)
    cw_tm = scan_ar[:, 2, :].rearrange("p (i e) -> p i e", e=16)
    rnk_sc = rs_fm[:].rearrange("p (i e) -> p i e", e=16)
    rank16 = isx_fm[:].rearrange("p (i e) -> p i e", e=16)
    xq_f32 = xq_t[:].rearrange("p c t -> p (c t)").bitcast(F32)
    ones128s = xq_f32[:, 0:1024]
    cmp_sc = xq_f32[:, 1024:1536].rearrange("p (i e) -> p i e", e=16)
    tri16b = xq_f32[:, 1536:1792].rearrange("p (i j) -> p i j", j=16)
    iota16 = xq_f32[:, 1792:1808]
    tri4b = xq_f32[:, 1808:1824].rearrange("p (i j) -> p i j", j=4)
    gsc4 = csum[:, 0:128].rearrange("p (i g) -> p i g", g=4)
    grk4 = csum[:, 128:256].rearrange("p (i g) -> p i g", g=4)
    powm_f32 = powm[:].rearrange("p c t -> p (c t)")
    powm_i32 = powm[:].rearrange("p c t -> p (c t)").bitcast(I32)
    powm_bf = powm[:].rearrange("p c t -> p (c t)").bitcast(BF16)
    tri16a = powm_f32[:, 3072:3328].rearrange("p (i j) -> p i j", j=16)
    tri4a = powm_f32[:, 3328:3344].rearrange("p (i j) -> p i j", j=4)
    ebias_rep = powm_f32[:, 3344:3360]
    rwT = powm_bf[:, 6784:6912].rearrange("p (c e) -> p c e", e=16)
    rw_sb = wstage[0:16, :, :].rearrange("p c d -> p (c d)").bitcast(F32)[:, 1024:2048]
    rw_bf = mix_t[0:16, :, :].rearrange("p c t -> p (c t)")[:, 2048:3072]
    rseg_f32 = r_seg[:].rearrange("p c t -> p (c t)").bitcast(F32)
    Me128 = rseg_f32[:, 0:1024]; pos128 = rseg_f32[:, 1024:2048]
    segsum = powm_f32[:, 3400:3401]; segoff = powm_f32[:, 3401:3402]
    strip1 = Dmat[0:1, :, :].rearrange("p c t -> p (c t)")[:, 0:656]
    sjf = powm_f32[:, 3424:3552].rearrange("p (i j) -> p i j", j=4)
    sji = powm_i32[:, 3728:3856].rearrange("p (i j) -> p i j", j=4)
    gj_tm = powm_f32[:, 3552:3680].rearrange("p (i j) -> p i j", j=4)
    texp_f = powm_f32[:, 3680:3728]
    widx = powm_i32[:, 3856:3904]
    wbuf = kv_t[:].rearrange("p a c t -> p (a c t)").bitcast(BF16)[:, 0:12288]
    xgF = powm[:].rearrange("p c t -> p (c t)").bitcast(BF16)[:, 0:4096].rearrange("p (c t) -> p c t", t=TC)
    h_rt = powm[:].rearrange("p c t -> p (c t)").bitcast(BF16)[:, 4096:6144].rearrange("p (c t) -> p c t", t=TC)
    h_sh = kv_t[:].rearrange("p a c t -> p (a c t)").bitcast(BF16)[:, 12288:16384].rearrange("p (c t) -> p c t", t=TC)
    xg_tm = wstage[:, 0:4, :]
    g0acc = Dmat[:].rearrange("p c t -> p (c t)")[:, 0:4096].rearrange("p (a d) -> p a d", d=D)
    scan_bf = scan_ar[:].rearrange("p a t -> p (a t)").bitcast(BF16)
    peT = scan_bf[:, 0:1408].rearrange("p (j h) -> p j h", h=16)
    selT = scan_bf[:, 1408:1536].rearrange("p (c h) -> p c h", h=16)
    rselT = e_t[0:16, :].bitcast(BF16)[:, 0:1024].rearrange("p (c d) -> p c d", d=128)
    psA = psm("psA", [128, 512]); psB = psm("psB", [128, 512])
    psT = psm("psT", [128, 128], BF16); psU = psm("psU", [128, 128], BF16)
    psTf = psm("psTf", [128, 128])
    psR = psm("psR", [128, 512])
    psS = psR[0:1, :]
    psW = psm("psW", [128, 1024])

    for e in ENGINES:
        C.sems[e] = Sem(st.enter_context(nc.semaphore("sem_" + e)))
    s_ld = Sem(st.enter_context(nc.semaphore("s_ld")))
    s_st = Sem(st.enter_context(nc.semaphore("s_st")))
    C.dma_sems = (s_ld, s_st)
    block = st.enter_context(nc.Block())

    def DMAL(out, in_, **kw):
        s_ld.n += 16
        C.q["sync"].append(lambda e, out=out, in_=in_, kw=kw: e.dma_start(out=out, in_=in_, **kw).then_inc(s_ld.h, 16))
        return s_ld.n

    def DMAS(out, in_, **kw):
        s_st.n += 16
        C.q["sync"].append(lambda e, out=out, in_=in_, kw=kw: e.dma_start(out=out, in_=in_, **kw).then_inc(s_st.h, 16))
        return s_st.n

    def WL(eng, val):
        C.q[eng].append(lambda e, v=val: e.wait_ge(s_ld.h, v))

    def WS(eng, val):
        C.q[eng].append(lambda e, v=val: e.wait_ge(s_st.h, v))

    twk = kv_t[:, 0, :, :]
    E("gpsimd", lambda e: e.iota(out=io_col[:], pattern=[[1, 128]], base=0, channel_multiplier=0, allow_small_or_imprecise_dtypes=True))
    E("gpsimd", lambda e: e.iota(out=io_row[:], pattern=[[0, 128]], base=0, channel_multiplier=1, allow_small_or_imprecise_dtypes=True))
    E("gpsimd", lambda e: e.iota(out=twk.rearrange("p c t -> p (c t)"), pattern=[[0, NCH], [1, TC]], base=0, channel_multiplier=0, allow_small_or_imprecise_dtypes=True))
    SIG("gpsimd")
    for vec, dst in (
        (rms_in[0], rmsv[:, 0:NCH]), (rms_in[1], rmsv[:, NCH:2*NCH]),
        (rms_in[2], rmsv[:, 2*NCH:3*NCH]), (rms_in[3], rmsv[:, 3*NCH:4*NCH]),
        (tmix_in[0], tmixv[:, 0:NCH]), (tmix_in[1], tmixv[:, NCH:2*NCH]), (tmix_in[2], tmixv[:, 2*NCH:3*NCH]),
        (tdecay, tdec[:]), (tfirst, tfir[:]),
    ):
        DMAL(dst, vec.rearrange("(c p) -> p c", p=128), allow_slow_non_contiguous=True)
    DMAL(rk_sb[0:1, 0:NCORE], rankm[None, :], allow_slow_non_contiguous=True)
    DMAL(rk_sb[0:1, NCORE:2*NCORE], rankw[None, :], allow_slow_non_contiguous=True)
    WL("vector", s_ld.n)
    WT("vector", "gpsimd")
    E("vector", lambda e: e.tensor_tensor(out=ident[:], in0=io_col[:], in1=io_row[:], op=ALU.is_equal))
    E("vector", lambda e: e.tensor_tensor(out=identf[:], in0=io_col[:], in1=io_row[:], op=ALU.is_equal))
    E("vector", lambda e: e.memset(onesNf[:], 1.0))
    E("vector", lambda e: e.memset(ones4[:], 1.0))
    E("vector", lambda e: e.memset(onesF[:], 1.0))
    E("vector", lambda e: e.memset(stag4[:], 0.0))
    E("vector", lambda e: e.memset(rhs512[:], 0.0))
    E("vector", lambda e: e.memset(onesKb[:], 1.0))
    E("vector", lambda e: e.memset(zcol[:], 0.0))
    E("vector", lambda e: e.tensor_scalar(out=tmixm1[:], in0=tmixv[:], scalar1=-1.0, scalar2=None, op0=ALU.add))
    SIG("vector")
    WL("scalar", s_ld.n)
    E("scalar", lambda e: e.activation(out=wlog[:], in_=tdec[:], func=AF.Exp))
    E("scalar", lambda e: e.activation(out=expu[:], in_=tfir[:], func=AF.Exp))
    SIG("scalar")
    WT("vector", "scalar")
    E("vector", lambda e: e.tensor_scalar(out=wlog[:], in0=wlog[:], scalar1=-1.0, scalar2=None, op0=ALU.mult))
    SIG("vector")
    for c in range(NCH):
        E("vector", lambda e, c=c: e.tensor_scalar(out=twk[:, c, :], in0=twk[:, c, :], scalar1=wlog[:, c:c+1], scalar2=None, op0=ALU.mult))
    SIG("vector")
    WT("scalar", "vector")
    E("scalar", lambda e: e.activation(out=powm[:].rearrange("p c t -> p (c t)"), in_=twk.rearrange("p c t -> p (c t)"), func=AF.Exp))
    for c in range(NCH):
        E("scalar", lambda e, c=c: e.activation(out=Dmat[:, c, :], in_=twk[:, c, 1:2].broadcast_to([128, TC]), func=AF.Exp))
    SIG("scalar")
    BAR()

    for wi in range(4):
        wsrc = w_mats[wi]
        for i in range(NCH):
            if i >= 2:
                WT("sync", "vector")
            tl = DMAL(x_tile[:, i % 2, :], wsrc[128*i:128*(i+1), :])
            WL("vector", tl)
            if i == 0:
                E("vector", lambda e: e.tensor_reduce(out=wabs[:, 0:1], in_=x_tile[:, 0, :], axis=AX.X, op=ALU.add, apply_absolute_value=True))
            else:
                E("vector", lambda e, i=i: e.tensor_reduce(out=wabs[:, 1:2], in_=x_tile[:, i % 2, :], axis=AX.X, op=ALU.add, apply_absolute_value=True))
                E("vector", lambda e: e.tensor_add(out=wabs[:, 0:1], in0=wabs[:, 0:1], in1=wabs[:, 1:2]))
            SIG("vector")
        E("vector", lambda e: e.tensor_copy(out=wabs_bf[:], in_=wabs[:, 0:1]))
        SIG("vector")
        BAR()
        E("tensor", lambda e: e.matmul(psS[0:1, 0:1], wabs_bf[:], onesKb[:], start=True, stop=True))
        SIG("tensor")
        WT("vector", "tensor")
        E("vector", lambda e: e.tensor_scalar(out=wm_sb[0:1, 0:1], in0=psS[0:1, 0:1], scalar1=1.0/(D*D), scalar2=1e-5, op0=ALU.mult, op1=ALU.max))
        SIG("vector")
        WT("tensor", "vector")
        E("tensor", lambda e: e.matmul(psR[:, 0:1], onesNf[:], wm_sb[0:1, 0:1], start=True, stop=True))
        SIG("tensor")
        WT("vector", "tensor")
        E("vector", lambda e, wi=wi: e.tensor_copy(out=inv_sw[:, wi:wi+1], in_=psR[:, 0:1]))
        E("vector", lambda e, wi=wi: e.reciprocal(out=sw_rep[:, wi:wi+1], in_=psR[:, 0:1]))
        SIG("vector")
        for i in range(NCH):
            WT("sync", "vector")
            tl = DMAL(x_tile[:, i % 2, :], wsrc[128*i:128*(i+1), :])
            WL("vector", tl)
            E("vector", lambda e, i=i, wi=wi: e.tensor_scalar(out=sqf[:, i % 2, :], in0=x_tile[:, i % 2, :], scalar1=sw_rep[:, wi:wi+1], scalar2=MAGIC, op0=ALU.mult, op1=ALU.add))
            E("vector", lambda e, i=i: e.tensor_scalar(out=sqf[:, i % 2, :], in0=sqf[:, i % 2, :], scalar1=MAGIC, scalar2=None, op0=ALU.subtract))
            E("vector", lambda e, i=i: e.tensor_scalar(out=wstage[:, i, :], in0=sqf[:, i % 2, :], scalar1=1.0, scalar2=-1.0, op0=ALU.min, op1=ALU.max))
            SIG("vector")
        BAR()
        slot = wi % 2
        for c in range(NCH):
            for dd2 in range(NCH):
                pp = psT if (c * NCH + dd2) % 2 == 0 else psU
                if c * NCH + dd2 >= 2:
                    WT("tensor", "scalar", C.sems["scalar"].n - 1)
                E("tensor", lambda e, c=c, dd2=dd2, pp=pp: e.transpose(pp[:], wstage[:, dd2, 128*c:128*(c+1)], ident[:]))
                tm = SIG("tensor")
                WT("scalar", "tensor", tm)
                E("scalar", lambda e, c=c, dd2=dd2, pp=pp, slot=slot: e.copy(out=wq_sb[:, slot, c, 128*dd2:128*(dd2+1)], in_=pp[:]))
                SIG("scalar")
        WT("sync", "scalar")
        DMAS(wq_dram[wi, :, :], wq_sb[:, slot, :, :].rearrange("p c d -> p (c d)"))
        BAR()

    def x_src_dst(i):
        if i < NTILE:
            b = i // 4; toff = (i % 4) * 128
            return x_in[b, 1+toff:1+toff+128, :], x_tile[:, i % 2, :]
        return x_in[:, 0, :], x_tile[0:NB, i % 2, :]

    for i in range(NTILE + 1):
        src, dst = x_src_dst(i)
        if i >= 2:
            WT("sync", "scalar"); WT("sync", "vector")
        tl = DMAL(dst, src)
        WL("scalar", tl)
        E("scalar", lambda e, i=i: e.activation(out=sqf[:, i % 2, :], in_=x_tile[:, i % 2, :], func=AF.Square, accum_out=s2_all[:, i:i+1]))
        SIG("scalar")
        WL("vector", tl)
        E("vector", lambda e, i=i: e.reduce_sum(out=s1_all[:, i:i+1], in_=x_tile[:, i % 2, :], axis=AX.X))
        SIG("vector")
    BAR()
    E("vector", lambda e: e.tensor_scalar(out=s1_all[:], in0=s1_all[:], scalar1=1.0/D, scalar2=None, op0=ALU.mult))
    E("vector", lambda e: e.tensor_scalar(out=s2_all[:], in0=s2_all[:], scalar1=1.0/D, scalar2=None, op0=ALU.mult))
    SIG("vector")
    WT("scalar", "vector")
    E("scalar", lambda e: e.activation(out=rstd_all[:], in_=s1_all[:], func=AF.Square))
    SIG("scalar")
    WT("vector", "scalar")
    E("vector", lambda e: e.tensor_sub(out=rstd_all[:], in0=s2_all[:], in1=rstd_all[:]))
    E("vector", lambda e: e.tensor_scalar(out=rstd_all[:], in0=rstd_all[:], scalar1=LN_EPS, scalar2=None, op0=ALU.add))
    SIG("vector")
    WT("scalar", "vector")
    E("scalar", lambda e: e.activation(out=rstd_all[:], in_=rstd_all[:], func=AF.Sqrt))
    SIG("scalar")
    WT("vector", "scalar")
    E("vector", lambda e: e.reciprocal(out=rstd_all[:], in_=rstd_all[:]))
    E("vector", lambda e: e.tensor_scalar(out=nmu_all[:], in0=s1_all[:], scalar1=-1.0, scalar2=None, op0=ALU.mult))
    E("vector", lambda e: e.tensor_tensor(out=nmu_all[:], in0=nmu_all[:], in1=rstd_all[:], op=ALU.mult))
    SIG("vector")
    WT("sync", "vector")
    BAR()
    xn_store_t = {}
    xnf_store_t = {}
    for i in range(NTILE + 1):
        src, dst = x_src_dst(i)
        if i >= 2:
            WT("sync", "scalar")
        tl = DMAL(dst, src)
        WL("scalar", tl)
        if i >= 2:
            WS("scalar", xn_store_t[i - 2])
        E("scalar", lambda e, i=i: e.activation(out=xn_st[:, i % 2, :], in_=x_tile[:, i % 2, :], func=AF.Identity, bias=nmu_all[:, i:i+1], scale=rstd_all[:, i:i+1]))
        tap = SIG("scalar")
        WT("sync", "scalar", tap)
        xn_store_t[i] = DMAS(xn_dram[i, :, :], xn_st[:, i % 2, :])
        WT("tensor", "scalar", tap)
        for c in range(NCH):
            pp = psT if c % 2 == 0 else psU
            if i > 0 or c >= 2:
                WT("tensor", "scalar", C.sems["scalar"].n - 1)
            E("tensor", lambda e, i=i, c=c, pp=pp: e.transpose(pp[:], xn_st[:, i % 2, 128*c:128*(c+1)], ident[:]))
            tm = SIG("tensor")
            WT("scalar", "tensor", tm)
            if i < NTILE:
                if c == 0 and i >= 2:
                    WS("scalar", xnf_store_t[i - 2])
                E("scalar", lambda e, i=i, c=c, pp=pp: e.copy(out=xnf_blk[:, i % 2, c, :], in_=pp[:]))
            else:
                E("scalar", lambda e, c=c, pp=pp: e.copy(out=xnb_sb[:, c, :], in_=pp[:, 0:NB]))
            SIG("scalar")
        WT("sync", "scalar")
        if i < NTILE:
            b = i // 4; toff = (i % 4) * 128
            xnf_store_t[i] = DMAS(xnf_dram[:, :, b*TC+toff : b*TC+toff+128], xnf_blk[:, i % 2, :, :])
    BAR()

    def quant_chain(rms_col, src_of_chunk):
        for c in range(NCH):
            E("vector", lambda e, c=c, src_of_chunk=src_of_chunk, rms_col=rms_col: e.tensor_scalar(
                out=m_t[:, c, :], in0=src_of_chunk(c), scalar1=rmsv[:, rms_col*NCH+c : rms_col*NCH+c+1], scalar2=None, op0=ALU.mult))
        SIG("vector")
        for c in range(NCH):
            E("scalar", lambda e, c=c, src_of_chunk=src_of_chunk: e.activation(out=sqf[:, c % 2, 0:TC], in_=src_of_chunk(c), func=AF.Square))
            tsq = SIG("scalar")
            WT("vector", "scalar", tsq)
            if c == 0:
                E("vector", lambda e: e.tensor_copy(out=csum[:], in_=sqf[:, 0, 0:TC]))
            else:
                E("vector", lambda e, c=c: e.tensor_add(out=csum[:], in0=csum[:], in1=sqf[:, c % 2, 0:TC]))
            tvv = SIG("vector")
            if c >= 1:
                WT("scalar", "vector", tvv - 1)
        E("vector", lambda e: e.tensor_copy(out=camax[:], in_=m_t[:, 0, :]))
        E("vector", lambda e: e.tensor_copy(out=csum2[:], in_=m_t[:, 0, :]))
        for c in range(1, NCH):
            E("vector", lambda e, c=c: e.tensor_tensor(out=camax[:], in0=camax[:], in1=m_t[:, c, :], op=ALU.max))
            E("vector", lambda e, c=c: e.tensor_tensor(out=csum2[:], in0=csum2[:], in1=m_t[:, c, :], op=ALU.min))
        E("vector", lambda e: e.tensor_scalar(out=csum2[:], in0=csum2[:], scalar1=-1.0, scalar2=None, op0=ALU.mult))
        E("vector", lambda e: e.tensor_tensor(out=camax[:], in0=camax[:], in1=csum2[:], op=ALU.max))
        SIG("vector")
        BAR()
        for blk in range(4):
            for wch, srcbuf in ((0, csum), (1, camax)):
                E("tensor", lambda e, blk=blk, srcbuf=srcbuf: e.transpose(psTf[:], srcbuf[:, 128*blk:128*(blk+1)], identf[:]))
                tm = SIG("tensor")
                WT("vector", "tensor", tm)
                if wch == 0:
                    E("vector", lambda e, blk=blk: e.reduce_sum(out=stat_tm[:, blk:blk+1], in_=psTf[:], axis=AX.X))
                else:
                    E("vector", lambda e, blk=blk: e.reduce_max(out=stat_tm[:, 4+blk:4+blk+1], in_=psTf[:], axis=AX.X))
                tv = SIG("vector")
                WT("tensor", "vector", tv)
        E("vector", lambda e: e.tensor_scalar(out=rs_tm[:], in0=stat_tm[:, 0:4], scalar1=1.0/D, scalar2=RMS_EPS, op0=ALU.mult, op1=ALU.add))
        SIG("vector")
        WT("scalar", "vector")
        E("scalar", lambda e: e.activation(out=rs_tm[:], in_=rs_tm[:], func=AF.Sqrt))
        SIG("scalar")
        WT("vector", "scalar")
        E("vector", lambda e: e.reciprocal(out=rs_tm[:], in_=rs_tm[:]))
        E("vector", lambda e: e.tensor_tensor(out=isx_tm[:], in0=stat_tm[:, 4:8], in1=rs_tm[:], op=ALU.mult))
        E("vector", lambda e: e.tensor_scalar(out=isx_tm[:], in0=isx_tm[:], scalar1=1e-5, scalar2=None, op0=ALU.max))
        E("vector", lambda e: e.reciprocal(out=stat_tm[:, 0:4], in_=isx_tm[:]))
        E("vector", lambda e: e.tensor_tensor(out=rs_tm[:], in0=rs_tm[:], in1=stat_tm[:, 0:4], op=ALU.mult))
        E("vector", lambda e: e.tensor_scalar(out=rs_tm[:], in0=rs_tm[:], scalar1=127.0, scalar2=None, op0=ALU.mult))
        E("vector", lambda e: e.tensor_scalar(out=isx_tm[:], in0=isx_tm[:], scalar1=1.0/127.0, scalar2=None, op0=ALU.mult))
        SIG("vector")
        for stm, fmbuf in ((rs_tm, rs_fm), (isx_tm, isx_fm)):
            BAR()
            for blk in range(4):
                E("vector", lambda e, stm=stm, blk=blk: e.tensor_copy(out=stag4[:, 32*blk:32*blk+1], in_=stm[:, blk:blk+1]))
            tv0 = SIG("vector")
            WT("tensor", "vector", tv0)
            E("tensor", lambda e: e.transpose(psTf[:], stag4[:], identf[:]))
            tm = SIG("tensor")
            WT("vector", "tensor", tm)
            for blk in range(4):
                E("vector", lambda e, blk=blk: e.tensor_copy(out=rhs512[32*blk:32*blk+32, 128*blk:128*(blk+1)], in_=psTf[32*blk:32*blk+32, 0:128]))
            tv = SIG("vector")
            WT("tensor", "vector", tv)
            E("tensor", lambda e: e.matmul(psR[:], onesF[:], rhs512[:], start=True, stop=True))
            tm2 = SIG("tensor")
            WT("vector", "tensor", tm2)
            E("vector", lambda e, fmbuf=fmbuf: e.tensor_copy(out=fmbuf[:], in_=psR[:]))
            SIG("vector")
        BAR()
        for c in range(NCH):
            E("vector", lambda e, c=c: e.tensor_tensor(out=rnd_f[:], in0=m_t[:, c, :], in1=rs_fm[:], op=ALU.mult))
            E("vector", lambda e, c=c: e.tensor_scalar(out=rnd_f[:], in0=rnd_f[:], scalar1=MAGIC, scalar2=MAGIC, op0=ALU.add, op1=ALU.subtract))
            E("vector", lambda e, c=c: e.tensor_copy(out=xq_t[:, c, :], in_=rnd_f[:]))
        SIG("vector")
        BAR()

    def project_fm(wslot, winv, dst_of_dc, sigmoid=False):
        for dc in range(NCH):
            pp = psA if dc % 2 == 0 else psB
            if dc >= 2:
                WT("tensor", "vector", C.sems["vector"].n)
            for c in range(NCH):
                E("tensor", lambda e, dc=dc, c=c, pp=pp, wslot=wslot: e.matmul(pp[:], wq_sb[:, wslot, c, 128*dc:128*(dc+1)], xq_t[:, c, :], start=(c == 0), stop=(c == NCH-1)))
            tm = SIG("tensor")
            WT("vector", "tensor", tm)
            if not sigmoid:
                E("vector", lambda e, dc=dc, pp=pp, winv=winv, dst_of_dc=dst_of_dc: e.scalar_tensor_tensor(out=dst_of_dc(dc), in0=pp[:], scalar=inv_sw[:, winv:winv+1], in1=isx_fm[:], op0=ALU.mult, op1=ALU.mult))
                SIG("vector")
            else:
                E("vector", lambda e, dc=dc, pp=pp, winv=winv: e.scalar_tensor_tensor(out=sqf[:, dc % 2, 0:TC], in0=pp[:], scalar=inv_sw[:, winv:winv+1], in1=isx_fm[:], op0=ALU.mult, op1=ALU.mult))
                tv = SIG("vector")
                WT("scalar", "vector", tv)
                E("scalar", lambda e, dc=dc, dst_of_dc=dst_of_dc: e.activation(out=dst_of_dc(dc), in_=sqf[:, dc % 2, 0:TC], func=AF.Sigmoid))
                SIG("scalar")
        BAR()

    def load_seg_xnf(b):
        return DMAL(xnf_seg[:], xnf_dram[:, :, b*TC:(b+1)*TC])

    def emit_mix(mi):
        for c in range(NCH):
            E("vector", lambda e, c=c, mi=mi: e.scalar_tensor_tensor(
                out=mix_t[:, c, :], in0=d_sh[:, c, :], scalar=tmixm1[:, mi*NCH+c:mi*NCH+c+1],
                in1=xnf_seg[:, c, 0:TC], op0=ALU.mult, op1=ALU.add))
        SIG("vector")

    def emit_dsh(b):
        for c in range(NCH):
            E("vector", lambda e, c=c: e.tensor_sub(out=d_sh[:, c, 1:TC], in0=xnf_seg[:, c, 1:TC], in1=xnf_seg[:, c, 0:TC-1]))
            E("vector", lambda e, c=c, b=b: e.tensor_sub(out=d_sh[:, c, 0:1], in0=xnf_seg[:, c, 0:1], in1=xnb_sb[:, c, b:b+1]))
        SIG("vector")

    DMAL(wq_sb[:, 0, :, :].rearrange("p c d -> p (c d)"), wq_dram[0, :, :])
    DMAL(wq_sb[:, 1, :, :].rearrange("p c d -> p (c d)"), wq_dram[1, :, :])
    WL("tensor", s_ld.n)
    BAR()

    kv_store_t = 0
    for b in range(NB):
        tl = load_seg_xnf(b)
        WL("vector", tl)
        if b > 0:
            WS("vector", kv_store_t)
        emit_dsh(b)
        BAR()
        for mi in range(2):
            emit_mix(mi)
            BAR()
            quant_chain(mi, lambda c: mix_t[:, c, :])
            project_fm(mi, mi, lambda dc, mi=mi: kv_t[:, mi, dc, :])
            BAR()
        WT("sync", "vector")
        for kvi in (0, 1):
            for c in range(NCH):
                kv_store_t = DMAS(kv_spill[kvi, c, :, b*TC:(b+1)*TC], kv_t[:, kvi, c, :])
        for c in range(NCH):
            E("scalar", lambda e, c=c: e.activation(out=e_t[:], in_=kv_t[:, 0, c, :], func=AF.Exp))
            tse = SIG("scalar")
            WT("vector", "scalar", tse)
            E("vector", lambda e, c=c: e.tensor_tensor(out=ev_t[:], in0=e_t[:], in1=kv_t[:, 1, c, :], op=ALU.mult))
            E("vector", lambda e, c=c: e.tensor_tensor_scan(out=a_scan[:], data0=Dmat[:, c, :], data1=ev_t[:], initial=zcol[:, 0:1], op0=ALU.mult, op1=ALU.add))
            E("vector", lambda e, c=c, b=b: e.tensor_copy(out=aend[:, b*NCH+c : b*NCH+c+1], in_=a_scan[:, TC-1:TC]))
            E("vector", lambda e, c=c: e.tensor_tensor_scan(out=b_scan[:], data0=Dmat[:, c, :], data1=e_t[:], initial=zcol[:, 0:1], op0=ALU.mult, op1=ALU.add))
            E("vector", lambda e, c=c, b=b: e.tensor_copy(out=aend[:, NCH*NB + b*NCH+c : NCH*NB + b*NCH+c+1], in_=b_scan[:, TC-1:TC]))
            tvs = SIG("vector")
            WT("scalar", "vector", tvs)
        BAR()

    DMAL(wq_sb[:, 0, :, :].rearrange("p c d -> p (c d)"), wq_dram[2, :, :])
    WL("tensor", s_ld.n)
    BAR()
    r_store_t = 0
    for b in range(NB):
        tl = load_seg_xnf(b)
        WL("vector", tl)
        emit_dsh(b)
        if b > 0:
            WS("scalar", r_store_t)
        emit_mix(2)
        quant_chain(2, lambda c: mix_t[:, c, :])
        project_fm(0, 2, lambda dc: r_seg[:, dc, :], sigmoid=True)
        WT("sync", "scalar")
        for c in range(NCH):
            r_store_t = DMAS(r_dram[:, c, b*TC:(b+1)*TC], r_seg[:, c, :])
    BAR()

    WT("sync", "vector")
    tst = DMAS(cc_in[:, :], aend[:])
    WS("gpsimd", tst)
    E("gpsimd", lambda e: e.collective_compute("AllGather", ALU.bypass, replica_groups=[list(range(NCORE))],
                                               ins=[cc_in[:, :]], outs=[cc_out[:, :, :]]))
    SIG("gpsimd")
    WT("sync", "gpsimd")
    DMAL(ccg[:], cc_out.rearrange("r q i -> q r i"))
    BAR()
    E("tensor", lambda e: e.matmul(psR[:, 0:2*NCORE], onesNf[:], rk_sb[0:1, :], start=True, stop=True))
    SIG("tensor")
    WT("vector", "tensor")
    E("vector", lambda e: e.tensor_copy(out=rkrep[:], in_=psR[:, 0:2*NCORE]))
    SIG("vector")
    for r in range(NCORE):
        E("vector", lambda e, r=r: e.tensor_scalar(out=powr[:, r, :], in0=wlog[:], scalar1=rkrep[:, NCORE+r:NCORE+r+1], scalar2=None, op0=ALU.mult))
    E("vector", lambda e: e.tensor_scalar(out=powr[:].rearrange("p r c -> p (r c)"), in0=powr[:].rearrange("p r c -> p (r c)"), scalar1=float(TC), scalar2=None, op0=ALU.mult))
    SIG("vector")
    WT("scalar", "vector")
    E("scalar", lambda e: e.activation(out=powr[:].rearrange("p r c -> p (r c)"), in_=powr[:].rearrange("p r c -> p (r c)"), func=AF.Exp))
    SIG("scalar")
    WT("vector", "scalar")
    for r in range(NCORE):
        E("vector", lambda e, r=r: e.tensor_scalar(out=powr[:, r, :], in0=powr[:, r, :], scalar1=rkrep[:, r:r+1], scalar2=None, op0=ALU.mult))
    E("vector", lambda e: e.memset(carry[:], 0.0))
    SIG("vector")
    for r in range(NCORE):
        for kv in range(2):
            for b in range(NB):
                idx0 = kv * NB * NCH + b * NCH
                E("vector", lambda e, r=r, idx0=idx0: e.tensor_tensor(out=sqf[:, 0, 0:NCH], in0=ccg[:, r, idx0:idx0+NCH], in1=powr[:, r, :], op=ALU.mult))
                E("vector", lambda e, idx0=idx0: e.tensor_add(out=carry[:, idx0:idx0+NCH], in0=carry[:, idx0:idx0+NCH], in1=sqf[:, 0, 0:NCH]))
    SIG("vector")
    BAR()

    DMAL(wq_sb[:, 1, :, :].rearrange("p c d -> p (c d)"), wq_dram[3, :, :])
    WL("tensor", s_ld.n)
    BAR()
    for b in range(NB):
        for kvi in range(2):
            for c in range(NCH):
                DMAL(kv_t[:, kvi, c, :], kv_spill[kvi, c, :, b*TC:(b+1)*TC])
        for c in range(NCH):
            DMAL(r_seg[:, c, :], r_dram[:, c, b*TC:(b+1)*TC])
        WL("scalar", s_ld.n); WL("vector", s_ld.n)
        for c in range(NCH):
            E("scalar", lambda e, c=c: e.activation(out=e_t[:], in_=kv_t[:, 0, c, :], func=AF.Exp))
            tse = SIG("scalar")
            WT("vector", "scalar", tse)
            E("vector", lambda e, c=c: e.tensor_tensor(out=ev_t[:], in0=e_t[:], in1=kv_t[:, 1, c, :], op=ALU.mult))
            E("vector", lambda e, c=c: e.tensor_tensor_scan(out=a_scan[:], data0=Dmat[:, c, :], data1=ev_t[:], initial=zcol[:, 0:1], op0=ALU.mult, op1=ALU.add))
            E("vector", lambda e, c=c: e.tensor_tensor_scan(out=b_scan[:], data0=Dmat[:, c, :], data1=e_t[:], initial=zcol[:, 0:1], op0=ALU.mult, op1=ALU.add))
            E("vector", lambda e, c=c: e.tensor_scalar(out=eu_t[:], in0=e_t[:], scalar1=expu[:, c:c+1], scalar2=None, op0=ALU.mult))
            E("vector", lambda e, c=c: e.tensor_scalar(out=evu_t[:], in0=ev_t[:], scalar1=expu[:, c:c+1], scalar2=None, op0=ALU.mult))
            E("vector", lambda e, c=c, b=b: e.scalar_tensor_tensor(out=n1_t[:, 1:TC], in0=powm[:, c, 1:TC], scalar=carry[:, b*NCH+c : b*NCH+c+1], in1=a_scan[:, 0:TC-1], op0=ALU.mult, op1=ALU.add))
            E("vector", lambda e, c=c, b=b: e.tensor_scalar(out=n1_t[:, 0:1], in0=powm[:, c, 0:1], scalar1=carry[:, b*NCH+c : b*NCH+c+1], scalar2=None, op0=ALU.mult))
            E("vector", lambda e, c=c, b=b: e.scalar_tensor_tensor(out=dd_t[:, 1:TC], in0=powm[:, c, 1:TC], scalar=carry[:, NCH*NB + b*NCH+c : NCH*NB + b*NCH+c+1], in1=b_scan[:, 0:TC-1], op0=ALU.mult, op1=ALU.add))
            E("vector", lambda e, c=c, b=b: e.tensor_scalar(out=dd_t[:, 0:1], in0=powm[:, c, 0:1], scalar1=carry[:, NCH*NB + b*NCH+c : NCH*NB + b*NCH+c+1], scalar2=None, op0=ALU.mult))
            E("vector", lambda e, c=c: e.tensor_add(out=n1_t[:], in0=n1_t[:], in1=evu_t[:]))
            E("vector", lambda e, c=c: e.tensor_add(out=dd_t[:], in0=dd_t[:], in1=eu_t[:]))
            E("vector", lambda e, c=c: e.reciprocal(out=dd_t[:], in_=dd_t[:]))
            E("vector", lambda e, c=c: e.tensor_tensor(out=n1_t[:], in0=n1_t[:], in1=dd_t[:], op=ALU.mult))
            E("vector", lambda e, c=c: e.tensor_tensor(out=gin_t[:, c, :], in0=n1_t[:], in1=r_seg[:, c, :], op=ALU.mult))
            tvs = SIG("vector")
            WT("scalar", "vector", tvs)
        quant_chain(3, lambda c: gin_t[:, c, :])
        for s in range(4):
            for half in range(2):
                WT("tensor", "vector", C.sems["vector"].n)
                for c in range(NCH):
                    E("tensor", lambda e, s=s, half=half, c=c: e.matmul(
                        psW[:, half*512:(half+1)*512], xq_t[:, c, 128*s:128*(s+1)],
                        wq_sb[:, 1, c, half*512:(half+1)*512], start=(c == 0), stop=(c == NCH-1)))
                tm = SIG("tensor")
                WT("vector", "tensor", tm)
                E("vector", lambda e, s=s, half=half: e.tensor_scalar(
                    out=s0_seg[:, s, half*512:(half+1)*512], in0=psW[:, half*512:(half+1)*512],
                    scalar1=isx_tm[:, s:s+1], scalar2=inv_sw[:, 3:4], op0=ALU.mult, op1=ALU.mult))
                SIG("vector")
        WT("sync", "vector")
        for s in range(4):
            DMAS(s0_dram[b*4+s, :, :], s0_seg[:, s, :])
        BAR()
    BAR()
    s_gd = Sem(st.enter_context(nc.semaphore("s_gd")))
    C.dma_sems = (s_ld, s_st, s_gd)

    def GDMA(out, in_, out_off=None, in_off=None, **kw):
        s_gd.n += 16
        C.q["gpsimd"].append(lambda e, out=out, in_=in_, oo=out_off, io=in_off, kw=kw:
            e.indirect_dma_start(out=out, out_offset=oo, in_=in_, in_offset=io, **kw).then_inc(s_gd.h, 16))
        return s_gd.n

    def WG(eng, val):
        C.q[eng].append(lambda e, v=val: e.wait_ge(s_gd.h, v))

    E("gpsimd", lambda e: e.iota(out=iota16[:], pattern=[[1, 16]], base=0, channel_multiplier=0, allow_small_or_imprecise_dtypes=True))
    E("gpsimd", lambda e: e.iota(out=tri16a[:].rearrange("p i j -> p (i j)"), pattern=[[0, 16], [1, 16]], base=0, channel_multiplier=0, allow_small_or_imprecise_dtypes=True))
    E("gpsimd", lambda e: e.iota(out=tri16b[:].rearrange("p i j -> p (i j)"), pattern=[[1, 16], [0, 16]], base=0, channel_multiplier=0, allow_small_or_imprecise_dtypes=True))
    E("gpsimd", lambda e: e.iota(out=tri4a[:].rearrange("p i j -> p (i j)"), pattern=[[0, 4], [1, 4]], base=0, channel_multiplier=0, allow_small_or_imprecise_dtypes=True))
    E("gpsimd", lambda e: e.iota(out=tri4b[:].rearrange("p i j -> p (i j)"), pattern=[[1, 4], [0, 4]], base=0, channel_multiplier=0, allow_small_or_imprecise_dtypes=True))
    BAR()
    E("vector", lambda e: e.tensor_tensor(out=tri16a[:].rearrange("p i j -> p (i j)"), in0=tri16a[:].rearrange("p i j -> p (i j)"), in1=tri16b[:].rearrange("p i j -> p (i j)"), op=ALU.is_lt))
    E("vector", lambda e: e.tensor_tensor(out=tri4a[:].rearrange("p i j -> p (i j)"), in0=tri4a[:].rearrange("p i j -> p (i j)"), in1=tri4b[:].rearrange("p i j -> p (i j)"), op=ALU.is_lt))
    E("vector", lambda e: e.memset(ones128s[:], 1.0))
    tl = DMAL(rw_sb[:], rw_in[:, :])
    WL("vector", tl)
    E("vector", lambda e: e.tensor_copy(out=rw_bf[:], in_=rw_sb[:]))
    BAR()
    for c in range(NCH):
        pp = psT if c % 2 == 0 else psU
        E("tensor", lambda e, c=c, pp=pp: e.transpose(pp[:], rw_bf[0:16, 128*c:128*(c+1)], ident[0:16, :]))
        tm = SIG("tensor")
        WT("scalar", "tensor", tm)
        E("scalar", lambda e, c=c, pp=pp: e.copy(out=rwT[:, c, :], in_=pp[:, 0:16]))
        tsc = C.sems["scalar"].n
        WT("tensor", "scalar", tsc)
    BAR()
    tl = DMAL(strip1[0:1, 0:16], ebias_in[None, :], allow_slow_non_contiguous=True)
    WL("tensor", tl)
    E("tensor", lambda e: e.matmul(psR[:, 0:16], onesNf[:], strip1[0:1, 0:16], start=True, stop=True))
    tm = SIG("tensor")
    WT("vector", "tensor", tm)
    E("vector", lambda e: e.tensor_copy(out=ebias_rep[:], in_=psR[:, 0:16]))
    BAR()
    cast_f32 = sqf[:].rearrange("p a d -> p (a d)")
    cast_bf = mix_t[:].rearrange("p c t -> p (c t)")[:, 0:2048]
    for e_i in range(16):
        for q in range(6):
            tl = DMAL(cast_f32[:, :], wcat_in[e_i, :, 2048*q:2048*(q+1)])
            WL("vector", tl)
            E("vector", lambda e: e.tensor_copy(out=cast_bf[:, :], in_=cast_f32[:, :]))
            tv = SIG("vector")
            WT("sync", "vector", tv)
            tstw = DMAS(wcat_bf[e_i, :, 2048*q:2048*(q+1)], cast_bf[:, :])
            WS("vector", tstw)
            WS("sync", tstw)
    for mat_src, dstslot in ((shg_in, 0), (shu_in, 1)):
        for q in range(4):
            tl = DMAL(cast_f32[:, :], mat_src.rearrange("p c d -> p (c d)")[:, 2048*q:2048*(q+1)])
            WL("vector", tl)
            E("vector", lambda e, dstslot=dstslot, q=q: e.tensor_copy(out=wq_sb[:, dstslot, :, :].rearrange("p c d -> p (c d)")[:, 2048*q:2048*(q+1)], in_=cast_f32[:, :]))
            tv = SIG("vector")
            WT("sync", "vector", tv)
    BAR()
    for g in range(8):
        tl = DMAL(xnf_seg[:], xnf_dram[:, :, g*TC:(g+1)*TC])
        WL("tensor", tl)
        BAR()
        for c in range(NCH):
            E("tensor", lambda e, c=c: e.matmul(psA[0:16, :], rwT[:, c, :], xnf_seg[:, c, :], start=(c == 0), stop=(c == NCH-1)))
        tm = SIG("tensor")
        WT("vector", "tensor", tm)
        E("vector", lambda e: e.tensor_copy(out=rw_sb[0:16, 0:512], in_=psA[0:16, :]))
        tv = SIG("vector")
        WT("tensor", "vector", tv)
        for qi in range(4):
            E("tensor", lambda e, qi=qi: e.transpose(psTf[:], rw_sb[0:16, 128*qi:128*(qi+1)], identf[0:16, :]))
            tm = SIG("tensor")
            WT("scalar", "tensor", tm)
            E("scalar", lambda e, g=g, qi=qi: e.activation(out=scores_tm[:, 4*g+qi, :], in_=psTf[:, 0:16], func=AF.Sigmoid))
            tsc = C.sems["scalar"].n
            WT("tensor", "scalar", tsc)
        BAR()
    NT = NTILE
    cmp1 = cmp_sc[:].rearrange("p i e -> p (i e)")
    cmp2 = wstage[:].rearrange("p c d -> p (c d)").bitcast(F32)[:, 0:512]
    cmp2b = wstage[:].rearrange("p c d -> p (c d)").bitcast(F32)[:, 512:1024]
    rank4 = rank16[:].rearrange("p i e -> p (i e)")
    E("vector", lambda e: e.tensor_tensor(out=rnk_sc[:, 0:NT, :], in0=scores_tm[:, 0:NT, :], in1=ebias_rep[:].unsqueeze(1).broadcast_to([128, NT, 16]), op=ALU.add))
    BAR()
    E("vector", lambda e: e.memset(rank4[:, :], 0.0))
    for j4 in range(4):
        xj = rnk_sc[:, 0:NT, :].rearrange("p i (g m) -> p i g m", m=4)[:, :, :, j4:j4+1].broadcast_to([128, NT, 4, 4])
        xi = rnk_sc[:, 0:NT, :].rearrange("p i (g m) -> p i g m", m=4)
        E("vector", lambda e, xj=xj, xi=xi: e.tensor_tensor(out=cmp1[:, 0:512].rearrange("p (i g m) -> p i g m", g=4, m=4), in0=xj, in1=xi, op=ALU.is_gt))
        E("vector", lambda e: e.tensor_add(out=rank4[:, :], in0=rank4[:, :], in1=cmp1[:, 0:512]))
        E("vector", lambda e, xj=xj, xi=xi: e.tensor_tensor(out=cmp1[:, 0:512].rearrange("p (i g m) -> p i g m", g=4, m=4), in0=xj, in1=xi, op=ALU.is_equal))
        ltj = tri4a[:, :, j4]
        E("vector", lambda e, ltj=ltj: e.tensor_tensor(out=cmp2[:, 0:512].rearrange("p (i g m) -> p i g m", g=4, m=4), in0=cmp1[:, 0:512].rearrange("p (i g m) -> p i g m", g=4, m=4), in1=ltj.unsqueeze(1).unsqueeze(2).broadcast_to([128, NT, 4, 4]), op=ALU.mult))
        E("vector", lambda e: e.tensor_add(out=rank4[:, :], in0=rank4[:, :], in1=cmp2[:, 0:512]))
    BAR()
    E("vector", lambda e: e.tensor_scalar(out=cmp1[:, 0:512], in0=rank4[:, :], scalar1=2.0, scalar2=1.0, op0=ALU.is_lt, op1=ALU.mult))
    E("vector", lambda e: e.tensor_tensor(out=cmp1[:, 0:512], in0=cmp1[:, 0:512], in1=rnk_sc[:, 0:NT, :].rearrange("p i e -> p (i e)"), op=ALU.mult))
    E("vector", lambda e: e.tensor_reduce(out=gsc4[:, 0:NT, :], in_=cmp1[:, 0:512].rearrange("p (i g m) -> p i g m", g=4, m=4), axis=AX.X, op=ALU.add))
    BAR()
    E("vector", lambda e: e.memset(grk4[:].rearrange("p i g -> p (i g)"), 0.0))
    for j4 in range(4):
        xj = gsc4[:, 0:NT, j4:j4+1].broadcast_to([128, NT, 4])
        E("vector", lambda e, xj=xj: e.tensor_tensor(out=cmp1[:, 0:128].rearrange("p (i g) -> p i g", g=4), in0=xj, in1=gsc4[:, 0:NT, :], op=ALU.is_gt))
        E("vector", lambda e: e.tensor_add(out=grk4[:].rearrange("p i g -> p (i g)"), in0=grk4[:].rearrange("p i g -> p (i g)"), in1=cmp1[:, 0:128]))
        E("vector", lambda e, xj=xj: e.tensor_tensor(out=cmp1[:, 0:128].rearrange("p (i g) -> p i g", g=4), in0=xj, in1=gsc4[:, 0:NT, :], op=ALU.is_equal))
        ltj = tri4a[:, :, j4]
        E("vector", lambda e, ltj=ltj: e.tensor_tensor(out=cmp2[:, 0:128].rearrange("p (i g) -> p i g", g=4), in0=cmp1[:, 0:128].rearrange("p (i g) -> p i g", g=4), in1=ltj.unsqueeze(1).broadcast_to([128, NT, 4]), op=ALU.mult))
        E("vector", lambda e: e.tensor_add(out=grk4[:].rearrange("p i g -> p (i g)"), in0=grk4[:].rearrange("p i g -> p (i g)"), in1=cmp2[:, 0:128]))
    E("vector", lambda e: e.tensor_scalar(out=grk4[:].rearrange("p i g -> p (i g)"), in0=grk4[:].rearrange("p i g -> p (i g)"), scalar1=2.0, scalar2=1.0, op0=ALU.is_lt, op1=ALU.mult))
    BAR()
    E("vector", lambda e: e.tensor_tensor(out=rnk_sc[:, 0:NT, :].rearrange("p i (g m) -> p i g m", m=4), in0=rnk_sc[:, 0:NT, :].rearrange("p i (g m) -> p i g m", m=4), in1=grk4[:, 0:NT, :].unsqueeze(3).broadcast_to([128, NT, 4, 4]), op=ALU.mult))
    BAR()
    E("vector", lambda e: e.memset(rank16[:].rearrange("p i e -> p (i e)"), 0.0))
    for j16 in range(16):
        xj = rnk_sc[:, 0:NT, j16:j16+1].broadcast_to([128, NT, 16])
        E("vector", lambda e, xj=xj: e.tensor_tensor(out=cmp1[:, 0:512].rearrange("p (i e) -> p i e", e=16), in0=xj, in1=rnk_sc[:, 0:NT, :], op=ALU.is_gt))
        E("vector", lambda e: e.tensor_add(out=rank16[:].rearrange("p i e -> p (i e)"), in0=rank16[:].rearrange("p i e -> p (i e)"), in1=cmp1[:, 0:512]))
        E("vector", lambda e, xj=xj: e.tensor_tensor(out=cmp1[:, 0:512].rearrange("p (i e) -> p i e", e=16), in0=xj, in1=rnk_sc[:, 0:NT, :], op=ALU.is_equal))
        ltj = tri16a[:, :, j16]
        E("vector", lambda e, ltj=ltj: e.tensor_tensor(out=cmp2[:, 0:512].rearrange("p (i e) -> p i e", e=16), in0=cmp1[:, 0:512].rearrange("p (i e) -> p i e", e=16), in1=ltj.unsqueeze(1).broadcast_to([128, NT, 16]), op=ALU.mult))
        E("vector", lambda e: e.tensor_add(out=rank16[:].rearrange("p i e -> p (i e)"), in0=rank16[:].rearrange("p i e -> p (i e)"), in1=cmp2[:, 0:512]))
    BAR()
    E("vector", lambda e: e.tensor_scalar(out=cmp1[:, 0:512], in0=rank16[:].rearrange("p i e -> p (i e)"), scalar1=4.0, scalar2=1.0, op0=ALU.is_lt, op1=ALU.mult))
    E("vector", lambda e: e.tensor_tensor(out=cmp2[:, 0:512], in0=cmp1[:, 0:512], in1=scores_tm[:, 0:NT, :].rearrange("p i e -> p (i e)"), op=ALU.mult))
    E("vector", lambda e: e.tensor_reduce(out=cmp2b[:, 0:32].rearrange("p (i q) -> p i q", q=1), in_=cmp2[:, 0:512].rearrange("p (i e) -> p i e", e=16), axis=AX.X, op=ALU.add))
    E("vector", lambda e: e.tensor_scalar(out=cmp2b[:, 0:32], in0=cmp2b[:, 0:32], scalar1=1e-20, scalar2=0.4, op0=ALU.add, op1=ALU.mult))
    E("vector", lambda e: e.reciprocal(out=cmp2b[:, 32:64], in_=cmp2b[:, 0:32]))
    E("vector", lambda e: e.tensor_tensor(out=cw_tm[:, 0:NT, :], in0=cmp2[:, 0:512].rearrange("p (i e) -> p i e", e=16), in1=cmp2b[:, 32:64].unsqueeze(2).broadcast_to([128, NT, 16]), op=ALU.mult))
    BAR()
    WT("sync", "vector")
    selm = cmp1[:, 0:512].rearrange("p (i e) -> p i e", e=16)
    for i in range(NT):
        si = i // 8
        E("tensor", lambda e, i=i: e.transpose(psTf[0:16, :], selm[:, i, :], identf[:]))
        tm = SIG("tensor")
        WT("vector", "tensor", tm)
        E("vector", lambda e, i=i, si=si: e.tensor_copy(out=Me128[32*si:32*si+16, 128*(i % 8):128*(i % 8)+128], in_=psTf[0:16, 0:128]))
        tv = SIG("vector")
        WT("tensor", "vector", tv)
    BAR()
    E("vector", lambda e: e.tensor_tensor_scan(out=pos128[:], data0=ones128s[:], data1=Me128[:], initial=zcol[:, 0:1], op0=ALU.mult, op1=ALU.add))
    E("vector", lambda e: e.tensor_copy(out=segsum[:], in_=pos128[:, 1023:1024]))
    E("vector", lambda e: e.tensor_sub(out=pos128[:], in0=pos128[:], in1=Me128[:]))
    BAR()
    E("tensor", lambda e: e.matmul(psR[0:1, 0:128], segsum[:], identf[:], start=True, stop=True))
    tm = SIG("tensor")
    WT("vector", "tensor", tm)
    E("vector", lambda e: e.tensor_copy(out=strip1[0:1, 0:128], in_=psR[0:1, 0:128]))
    E("vector", lambda e: e.tensor_tensor(out=strip1[0:1, 128:160], in0=strip1[0:1, 0:32], in1=strip1[0:1, 32:64], op=ALU.add))
    E("vector", lambda e: e.tensor_tensor(out=strip1[0:1, 160:192], in0=strip1[0:1, 64:96], in1=strip1[0:1, 96:128], op=ALU.add))
    E("vector", lambda e: e.tensor_tensor(out=strip1[0:1, 128:160], in0=strip1[0:1, 128:160], in1=strip1[0:1, 160:192], op=ALU.add))
    E("vector", lambda e: e.memset(strip1[0:1, 192:224], 0.0))
    E("vector", lambda e: e.tensor_copy(out=strip1[0:1, 224:256], in_=strip1[0:1, 0:32]))
    E("vector", lambda e: e.tensor_tensor(out=strip1[0:1, 256:288], in0=strip1[0:1, 224:256], in1=strip1[0:1, 32:64], op=ALU.add))
    E("vector", lambda e: e.tensor_tensor(out=strip1[0:1, 288:320], in0=strip1[0:1, 256:288], in1=strip1[0:1, 64:96], op=ALU.add))
    E("vector", lambda e: e.tensor_scalar(out=strip1[0:1, 320:336], in0=strip1[0:1, 128:144], scalar1=1.0/512.0, scalar2=(511.0/512.0 - 0.5 + 2.0**-12), op0=ALU.mult, op1=ALU.add))
    E("vector", lambda e: e.tensor_scalar(out=strip1[0:1, 320:336], in0=strip1[0:1, 320:336], scalar1=MAGIC, scalar2=MAGIC, op0=ALU.add, op1=ALU.subtract))
    E("vector", lambda e: e.tensor_copy(out=strip1[0:1, 336:352], in_=strip1[0:1, 320:336]))
    for sh in (1, 2, 4, 8):
        E("vector", lambda e, sh=sh: e.tensor_tensor(out=strip1[0:1, 352+sh:368], in0=strip1[0:1, 336+sh:352], in1=strip1[0:1, 336:352-sh], op=ALU.add))
        E("vector", lambda e, sh=sh: e.tensor_copy(out=strip1[0:1, 352:352+sh], in_=strip1[0:1, 336:336+sh]))
        E("vector", lambda e: e.tensor_copy(out=strip1[0:1, 336:352], in_=strip1[0:1, 352:368]))
    E("vector", lambda e: e.tensor_sub(out=strip1[0:1, 368:384], in0=strip1[0:1, 336:352], in1=strip1[0:1, 320:336]))
    E("vector", lambda e: e.tensor_scalar(out=strip1[0:1, 368:384], in0=strip1[0:1, 368:384], scalar1=512.0, scalar2=None, op0=ALU.mult))
    BAR()
    E("gpsimd", lambda e: e.iota(out=strip1[0:1, 400:448], pattern=[[512, 48]], base=0, channel_multiplier=0, allow_small_or_imprecise_dtypes=True))
    BAR()
    E("vector", lambda e: e.tensor_tensor(out=rw_sb[0:1, 0:768].rearrange("p (j e) -> p j e", e=16), in0=strip1[0:1, 368:384].unsqueeze(1).broadcast_to([1, 48, 16]), in1=strip1[0:1, 400:448].unsqueeze(2).broadcast_to([1, 48, 16]), op=ALU.is_le))
    E("vector", lambda e: e.tensor_reduce(out=strip1[0:1, 448:496].rearrange("p (j q) -> p j q", q=1), in_=rw_sb[0:1, 0:768].rearrange("p (j e) -> p j e", e=16), axis=AX.X, op=ALU.add))
    E("vector", lambda e: e.tensor_scalar(out=strip1[0:1, 448:496], in0=strip1[0:1, 448:496], scalar1=-1.0, scalar2=None, op0=ALU.add))
    E("vector", lambda e: e.memset(strip1[0:1, 624:656], 0.0))
    E("vector", lambda e: e.tensor_copy(out=strip1[0:1, 624:640], in_=strip1[0:1, 368:384]))
    E("vector", lambda e: e.tensor_tensor(out=strip1[0:1, 496:624].rearrange("p (s q) -> p s q", q=32), in0=strip1[0:1, 192:320].rearrange("p (s q) -> p s q", q=32), in1=strip1[0:1, 624:656].unsqueeze(1).broadcast_to([1, 4, 32]), op=ALU.add))
    BAR()
    E("tensor", lambda e: e.matmul(psR[:, 0:1], strip1[0:1, 496:624], onesNf[0:1, 0:1], start=True, stop=True))
    tm = SIG("tensor")
    WT("vector", "tensor", tm)
    E("vector", lambda e: e.tensor_copy(out=segoff[:], in_=psR[:, 0:1]))
    E("vector", lambda e: e.tensor_scalar(out=pos128[:], in0=pos128[:], scalar1=segoff[:, 0:1], scalar2=0.0, op0=ALU.add, op1=ALU.add))
    BAR()
    for i in range(NT):
        si = i // 8
        E("tensor", lambda e, i=i: e.transpose(psTf[:], pos128[:, 128*(i % 8):128*(i % 8)+128].rearrange("p (q t) -> p q t", q=1)[:, 0, :] if False else pos128[:, 128*(i % 8):128*(i % 8)+128], identf[:]))
        tm = SIG("tensor")
        WT("vector", "tensor", tm)
        E("vector", lambda e, i=i, si=si: e.tensor_copy(out=S_tm[:, i, :], in_=psTf[:, 32*si:32*si+16]))
        tv = SIG("vector")
        WT("tensor", "vector", tv)
    BAR()
    for j in range(4):
        E("vector", lambda e, j=j: e.tensor_scalar(out=cmp1[:, 0:512], in0=rank16[:].rearrange("p i e -> p (i e)"), scalar1=float(j), scalar2=1.0, op0=ALU.is_equal, op1=ALU.mult))
        E("vector", lambda e, j=j: e.tensor_tensor(out=cmp2[:, 0:512], in0=cmp1[:, 0:512], in1=S_tm[:, 0:NT, :].rearrange("p i e -> p (i e)"), op=ALU.mult))
        E("vector", lambda e, j=j: e.tensor_reduce(out=sjf[:, :, j:j+1], in_=cmp2[:, 0:512].rearrange("p (i e) -> p i e", e=16), axis=AX.X, op=ALU.add))
        E("vector", lambda e, j=j: e.tensor_tensor(out=cmp2[:, 0:512], in0=cmp1[:, 0:512], in1=cw_tm[:, 0:NT, :].rearrange("p i e -> p (i e)"), op=ALU.mult))
        E("vector", lambda e, j=j: e.tensor_reduce(out=gj_tm[:, :, j:j+1], in_=cmp2[:, 0:512].rearrange("p (i e) -> p i e", e=16), axis=AX.X, op=ALU.add))
    E("vector", lambda e: e.tensor_copy(out=sji[:].rearrange("p i j -> p (i j)"), in_=sjf[:].rearrange("p i j -> p (i j)")))
    BAR()
    WT("sync", "vector")
    E("tensor", lambda e: e.matmul(psR[:, 0:48], onesNf[:], strip1[0:1, 448:496], start=True, stop=True))
    tm = SIG("tensor")
    WT("vector", "tensor", tm)
    E("vector", lambda e: e.tensor_copy(out=texp_f[:], in_=psR[:, 0:48]))
    E("vector", lambda e: e.tensor_scalar(out=texp_f[:], in0=texp_f[:], scalar1=128.0, scalar2=0.0, op0=ALU.mult, op1=ALU.add))
    E("vector", lambda e: e.tensor_tensor(out=texp_f[:], in0=texp_f[:], in1=io_row[:, 0:48], op=ALU.add))
    E("vector", lambda e: e.tensor_copy(out=widx[:], in_=texp_f[:]))
    BAR()
    xn_sc = camax[:].bitcast(BF16)[:, 0:1024]
    for i in range(NT):
        tl = DMAL(xn_sc, xn_dram[i, :, :])
        WL("gpsimd", tl)
        tg = 0
        for j in range(4):
            tg = GDMA(slotbuf[:, :], xn_sc, out_off=IndirectOffsetOnAxis(sji[:, i, j:j+1], 0))
        WG("sync", tg)
    BAR()
    wcat_rows = wcat_bf.rearrange("e p q -> (e p) q")
    hs_bf = rnd_f[:].bitcast(BF16)[:, 0:512]
    yo_bf = camax[:].bitcast(BF16)[:, 0:1024]
    for tj in range(48):
        WT("gpsimd", "tensor")
        tgw = GDMA(wbuf, wcat_rows, in_off=IndirectOffsetOnAxis(widx[:, tj:tj+1], 0))
        WT("sync", "tensor")
        tl = DMAL(xg_tm[:], slotbuf[512*tj:512*(tj+1), :].rearrange("(a p) d -> p a d", p=128))
        WL("tensor", tl)
        for a4 in range(4):
            for c in range(NCH):
                pp = psT if (a4 * NCH + c) % 2 == 0 else psU
                E("tensor", lambda e, a4=a4, c=c, pp=pp: e.transpose(pp[:], xg_tm[:, a4, 128*c:128*(c+1)], ident[:]))
                tm = SIG("tensor")
                WT("scalar", "tensor", tm)
                E("scalar", lambda e, a4=a4, c=c, pp=pp: e.copy(out=xgF[:, c, 128*a4:128*(a4+1)], in_=pp[:]))
                tsc = C.sems["scalar"].n
                WT("tensor", "scalar", tsc)
        WG("tensor", tgw)
        for mi in range(4):
            for c in range(NCH):
                E("tensor", lambda e, mi=mi, c=c: e.matmul(psA[:], wbuf[:, c*512+128*mi : c*512+128*mi+128], xgF[:, c, :], start=(c == 0), stop=(c == NCH-1)))
            for c in range(NCH):
                E("tensor", lambda e, mi=mi, c=c: e.matmul(psB[:], wbuf[:, 4096+c*512+128*mi : 4096+c*512+128*mi+128], xgF[:, c, :], start=(c == 0), stop=(c == NCH-1)))
            tm = SIG("tensor")
            WT("scalar", "tensor", tm)
            E("scalar", lambda e, mi=mi: e.activation(out=hs_bf[:, :], in_=psA[:], func=AF.Silu))
            tsc = C.sems["scalar"].n
            WT("vector", "scalar", tsc)
            E("vector", lambda e, mi=mi: e.tensor_tensor(out=h_rt[:, mi, :], in0=hs_bf[:, :], in1=psB[:], op=ALU.mult))
            tv = SIG("vector")
            WT("tensor", "vector", tv)
        for s4 in range(4):
            for half in range(2):
                for ci in range(4):
                    E("tensor", lambda e, s4=s4, half=half, ci=ci: e.matmul(psW[:, 512*half:512*(half+1)], h_rt[:, ci, 128*s4:128*(s4+1)], wbuf[:, 8192+ci*1024+512*half : 8192+ci*1024+512*half+512], start=(ci == 0), stop=(ci == 3)))
            tm = SIG("tensor")
            WT("vector", "tensor", tm)
            E("vector", lambda e, s4=s4: e.tensor_copy(out=yo_bf[:, :], in_=psW[:]))
            tv = SIG("vector")
            WT("sync", "vector", tv)
            tst = DMAS(yobuf[512*tj+128*s4 : 512*tj+128*s4+128, :], yo_bf[:, :])
            WS("vector", tst)
            WT("tensor", "vector", tv)
    BAR()
    shd_res = kv_t[:].rearrange("p a c t -> p (a c t)").bitcast(BF16)[:, 0:8192].rearrange("p (c d) -> p c d", d=D)
    for q in range(4):
        tl = DMAL(cast_f32[:, :], shd_in.rearrange("p c d -> p (c d)")[:, 2048*q:2048*(q+1)])
        WL("vector", tl)
        E("vector", lambda e, q=q: e.tensor_copy(out=shd_res.rearrange("p c d -> p (c d)")[:, 2048*q:2048*(q+1)], in_=cast_f32[:, :]))
        tv = SIG("vector")
        WT("sync", "vector", tv)
    BAR()
    ybg = camax[:].bitcast(BF16)[:, 0:1024]
    gsil = sqf[:, 0, :]
    s0buf = sqf[:, 1, :]
    for grp in range(8):
        for it in range(4):
            i = grp * 4 + it
            for j in range(4):
                tg = GDMA(ybg, yobuf[:, :], in_off=IndirectOffsetOnAxis(sji[:, i, j:j+1], 0))
                WG("vector", tg)
                if j == 0:
                    E("vector", lambda e, it=it, i=i: e.tensor_scalar(out=g0acc[:, it, :], in0=ybg, scalar1=gj_tm[:, i, 0:1], scalar2=0.0, op0=ALU.mult, op1=ALU.add))
                else:
                    E("vector", lambda e, it=it, i=i, j=j: e.scalar_tensor_tensor(out=g0acc[:, it, :], in0=ybg, scalar=gj_tm[:, i, j:j+1], in1=g0acc[:, it, :], op0=ALU.mult, op1=ALU.add))
                tv = SIG("vector")
                WT("gpsimd", "vector", tv)
        BAR()
        tl = DMAL(xnf_seg[:], xnf_dram[:, :, grp*TC:(grp+1)*TC])
        WL("tensor", tl)
        hsh_bf = rnd_f[:].bitcast(BF16)[:, 0:512]
        for mi in range(8):
            for c in range(NCH):
                E("tensor", lambda e, mi=mi, c=c: e.matmul(psA[:], wq_sb[:, 0, c, 128*mi:128*(mi+1)], xnf_seg[:, c, :], start=(c == 0), stop=(c == NCH-1)))
            for c in range(NCH):
                E("tensor", lambda e, mi=mi, c=c: e.matmul(psB[:], wq_sb[:, 1, c, 128*mi:128*(mi+1)], xnf_seg[:, c, :], start=(c == 0), stop=(c == NCH-1)))
            tm = SIG("tensor")
            WT("scalar", "tensor", tm)
            E("scalar", lambda e, mi=mi: e.activation(out=hsh_bf[:, :], in_=psA[:], func=AF.Silu))
            tsc = C.sems["scalar"].n
            WT("vector", "scalar", tsc)
            E("vector", lambda e, mi=mi: e.tensor_tensor(out=h_sh[:, mi, :], in0=hsh_bf[:, :], in1=psB[:], op=ALU.mult))
            tv = SIG("vector")
            WT("tensor", "vector", tv)
        BAR()
        for s4 in range(4):
            i = grp * 4 + s4
            for half in range(2):
                for ci in range(NCH):
                    E("tensor", lambda e, s4=s4, half=half, ci=ci: e.matmul(psW[:, 512*half:512*(half+1)], h_sh[:, ci, 128*s4:128*(s4+1)], shd_res[:, ci, 512*half:512*(half+1)], start=(ci == 0), stop=(ci == NCH-1)))
            tm = SIG("tensor")
            WT("vector", "tensor", tm)
            E("vector", lambda e, s4=s4: e.tensor_tensor(out=g0acc[:, s4, :], in0=g0acc[:, s4, :], in1=psW[:], op=ALU.add))
            tv = SIG("vector")
            WT("tensor", "vector", tv)
            WT("sync", "vector", tv)
            tl1 = DMAL(x_tile[:, 0, :], x_in[i // 4, 1+(i % 4)*128 : 1+(i % 4)*128+128, :])
            tl2 = DMAL(s0buf, s0_dram[i, :, :])
            WT("scalar", "vector", tv)
            E("scalar", lambda e, s4=s4: e.activation(out=gsil, in_=g0acc[:, s4, :], func=AF.Silu))
            tsc = C.sems["scalar"].n
            WL("vector", tl2)
            WL("vector", tl1)
            WT("vector", "scalar", tsc)
            E("vector", lambda e: e.tensor_tensor(out=gsil, in0=s0buf, in1=gsil, op=ALU.mult))
            E("vector", lambda e: e.tensor_tensor(out=s0buf, in0=x_tile[:, 0, :], in1=gsil, op=ALU.add))
            tv = SIG("vector")
            WT("sync", "vector", tv)
            t1 = DMAS(x1_dram[i, :, :], s0buf)
            WS("vector", t1)
            WS("scalar", t1)
        BAR()
    BAR()
    WS("sync", s_st.n)
    BAR()


    for mat_src, dsel in ((wqT_in, 0), (wkT_in, 1), (wvT_in, 2)):
        for qq in range(4):
            tl = DMAL(cast_f32[:, :], mat_src.rearrange("p c d -> p (c d)")[:, 2048*qq:2048*(qq+1)])
            WL("vector", tl)
            if dsel < 2:
                dstap = wq_sb[:, dsel, :, :].rearrange("p c d -> p (c d)")[:, 2048*qq:2048*(qq+1)]
            else:
                dstap = wstage[:].rearrange("p c d -> p (c d)")[:, 2048*qq:2048*(qq+1)]
            E("vector", lambda e, dstap=dstap: e.tensor_copy(out=dstap, in_=cast_f32[:, :]))
            tv = SIG("vector")
            WT("sync", "vector", tv)
    tl = DMAL(cast_f32[:, 0:1408].rearrange("p (j h) -> p j h", h=16), pe_in.rearrange("j p h -> p j h"))
    WL("vector", tl)
    E("vector", lambda e: e.tensor_copy(out=peT[:].rearrange("p j h -> p (j h)"), in_=cast_f32[:, 0:1408]))
    SIG("vector")
    tl = DMAL(selT[:], selc_in.rearrange("c p h -> p c h"))
    tl = DMAL(rselT[:].rearrange("p c d -> p (c d)"), rsel_in.rearrange("p c d -> p (c d)"))
    BAR()
    for i in range(NTILE):
        if i >= 2:
            WT("sync", "scalar"); WT("sync", "vector")
        tl = DMAL(x_tile[:, i % 2, :], x1_dram[i, :, :])
        WL("scalar", tl)
        E("scalar", lambda e, i=i: e.activation(out=sqf[:, i % 2, :], in_=x_tile[:, i % 2, :], func=AF.Square, accum_out=s2_all[:, i:i+1]))
        SIG("scalar")
        WL("vector", tl)
        E("vector", lambda e, i=i: e.reduce_sum(out=s1_all[:, i:i+1], in_=x_tile[:, i % 2, :], axis=AX.X))
        SIG("vector")
    BAR()
    E("vector", lambda e: e.tensor_scalar(out=s1_all[:], in0=s1_all[:], scalar1=1.0/D, scalar2=None, op0=ALU.mult))
    E("vector", lambda e: e.tensor_scalar(out=s2_all[:], in0=s2_all[:], scalar1=1.0/D, scalar2=None, op0=ALU.mult))
    SIG("vector")
    WT("scalar", "vector")
    E("scalar", lambda e: e.activation(out=rstd_all[:], in_=s1_all[:], func=AF.Square))
    SIG("scalar")
    WT("vector", "scalar")
    E("vector", lambda e: e.tensor_sub(out=rstd_all[:], in0=s2_all[:], in1=rstd_all[:]))
    E("vector", lambda e: e.tensor_scalar(out=rstd_all[:], in0=rstd_all[:], scalar1=LN_EPS, scalar2=None, op0=ALU.add))
    SIG("vector")
    WT("scalar", "vector")
    E("scalar", lambda e: e.activation(out=rstd_all[:], in_=rstd_all[:], func=AF.Sqrt))
    SIG("scalar")
    WT("vector", "scalar")
    E("vector", lambda e: e.reciprocal(out=rstd_all[:], in_=rstd_all[:]))
    E("vector", lambda e: e.tensor_scalar(out=nmu_all[:], in0=s1_all[:], scalar1=-1.0, scalar2=None, op0=ALU.mult))
    E("vector", lambda e: e.tensor_tensor(out=nmu_all[:], in0=nmu_all[:], in1=rstd_all[:], op=ALU.mult))
    SIG("vector")
    BAR()
    xn1_store_t = {}
    xnf1_store_t = {}
    for i in range(NTILE):
        if i >= 2:
            WT("sync", "scalar")
        tl = DMAL(x_tile[:, i % 2, :], x1_dram[i, :, :])
        WL("scalar", tl)
        if i >= 2:
            WS("scalar", xn1_store_t[i - 2])
        E("scalar", lambda e, i=i: e.activation(out=xn_st[:, i % 2, :], in_=x_tile[:, i % 2, :], func=AF.Identity, bias=nmu_all[:, i:i+1], scale=rstd_all[:, i:i+1]))
        tap = SIG("scalar")
        WT("sync", "scalar", tap)
        xn1_store_t[i] = DMAS(xn1_dram[i, :, :], xn_st[:, i % 2, :])
        WT("tensor", "scalar", tap)
        for c in range(NCH):
            pp = psT if c % 2 == 0 else psU
            if i > 0 or c >= 2:
                WT("tensor", "scalar", C.sems["scalar"].n - 1)
            E("tensor", lambda e, i=i, c=c, pp=pp: e.transpose(pp[:], xn_st[:, i % 2, 128*c:128*(c+1)], ident[:]))
            tm = SIG("tensor")
            WT("scalar", "tensor", tm)
            if c == 0 and i >= 2:
                WS("scalar", xnf1_store_t[i - 2])
            E("scalar", lambda e, i=i, c=c, pp=pp: e.copy(out=xnf_blk[:, i % 2, c, :], in_=pp[:]))
            SIG("scalar")
        WT("sync", "scalar")
        b = i // 4; toff = (i % 4) * 128
        xnf1_store_t[i] = DMAS(xnf1_dram[:, :, b*TC+toff : b*TC+toff+128], xnf_blk[:, i % 2, :, :])
    BAR()
    WS("sync", s_st.n)
    BAR()
    BAR()
    E("gpsimd", lambda e: e.collective_compute("AllGather", ALU.bypass, replica_groups=[list(range(NCORE))],
                                               ins=[xn1_dram.rearrange("i p d -> (i p) d")], outs=[xn1_ag[:, :, :]]))
    SIG("gpsimd")
    BAR()
    qC = r_seg
    kC = mix_t
    vC = xq_t
    e_full = kv_t[0:16, :, :, :].rearrange("p a c t -> p (a c t)")
    e_buf = e_full[:, 0:5632].rearrange("p (j t) -> p j t", t=TC)
    mrow = e_full[:, 5632:6144]
    m32 = e_full[:, 6144:6656]
    denb = e_full[:, 6656:7168]
    rden = e_full[:, 7168:7680]
    abf_full = kv_t[0:16, :, :, :].rearrange("p a c t -> p (a c t)").bitcast(BF16)
    abf = [abf_full[:, 15360:15872], abf_full[:, 15872:16384]]
    dmat_bf = Dmat[:].rearrange("p c t -> p (c t)").bitcast(BF16)
    pC = [dmat_bf[:, 0:4096].rearrange("p (c t) -> p c t", t=TC),
          dmat_bf[:, 4096:8192].rearrange("p (c t) -> p c t", t=TC)]
    s1bf = dmat_bf[:, 4096:8192].rearrange("p (c t) -> p c t", t=TC)
    s1acc = powm
    tmpv = camax
    s1tm_st = xn_st
    xnf1_4d = xnf1_dram.rearrange("p c (l t) -> p c l t", t=TC)
    for ct in range(8):
        tl = DMAL(xnf_seg[:].rearrange("p c (l t) -> p c l t", t=64), xnf1_4d[:, :, :, 64*ct:64*ct+64])
        WL("tensor", tl)
        tsc_pp = {0: 0, 1: 0}
        for si, dsts in enumerate((qC, kC, vC)):
            for mi in range(NCH):
                gidx = si * NCH + mi
                pp = psA if gidx % 2 == 0 else psB
                WT("tensor", "scalar", tsc_pp[gidx % 2])
                for c in range(NCH):
                    if si < 2:
                        lh = wq_sb[:, si, c, 128*mi:128*(mi+1)]
                    else:
                        lh = wstage[:, c, 128*mi:128*(mi+1)]
                    E("tensor", lambda e, lh=lh, c=c, pp=pp: e.matmul(pp[:], lh, xnf_seg[:, c, :], start=(c == 0), stop=(c == NCH-1)))
                tm = SIG("tensor")
                WT("scalar", "tensor", tm)
                E("scalar", lambda e, dsts=dsts, mi=mi, pp=pp: e.copy(out=dsts[:, mi, :], in_=pp[:]))
                tsc_pp[gidx % 2] = SIG("scalar")
        BAR()
        tm_sel = {}
        tv_evac = {}
        for j in range(11):
            dlt = j - 5
            lo = max(0, -64*dlt); hi = 512 - max(0, 64*dlt)
            if j >= 2:
                WT("vector", "tensor", tm_sel[j - 2])
            for c in range(NCH):
                if lo > 0:
                    E("vector", lambda e, c=c, lo=lo, j=j: e.memset(pC[j % 2][:, c, 0:lo], 0.0))
                if hi < TC:
                    E("vector", lambda e, c=c, hi=hi, j=j: e.memset(pC[j % 2][:, c, hi:TC], 0.0))
                E("vector", lambda e, c=c, lo=lo, hi=hi, dlt=dlt, j=j: e.tensor_tensor(
                    out=pC[j % 2][:, c, lo:hi], in0=qC[:, c, lo:hi], in1=kC[:, c, lo+64*dlt:hi+64*dlt], op=ALU.mult))
            tv_prod = SIG("vector")
            if j >= 1:
                WT("tensor", "vector", tv_evac[j - 1])
            for c in range(NCH):
                E("tensor", lambda e, j=j, c=c: e.matmul(psR[0:16, :], peT[:, j*8+c, :], qC[:, c, :], start=(c == 0), stop=False))
            WT("tensor", "vector", tv_prod)
            for c in range(NCH):
                E("tensor", lambda e, c=c, j=j: e.matmul(psR[0:16, :], selT[:, c, :], pC[j % 2][:, c, :], start=False, stop=(c == NCH-1)))
            tm_sel[j] = SIG("tensor")
            WT("vector", "tensor", tm_sel[j])
            E("vector", lambda e, j=j: e.tensor_copy(out=e_buf[:, j, :], in_=psR[0:16, :]))
            tv_evac[j] = SIG("vector")
        BAR()
        E("vector", lambda e: e.tensor_copy(out=mrow[:], in_=e_buf[:, 0, :]))
        for j in range(1, 11):
            E("vector", lambda e, j=j: e.tensor_tensor(out=mrow[:], in0=mrow[:], in1=e_buf[:, j, :], op=ALU.max))
        E("vector", lambda e: e.tensor_scalar(out=m32[:], in0=mrow[:], scalar1=0.03125, scalar2=None, op0=ALU.mult))
        SIG("vector")
        for j in range(11):
            E("vector", lambda e, j=j: e.tensor_scalar(out=e_buf[:, j, :], in0=e_buf[:, j, :], scalar1=0.03125, scalar2=None, op0=ALU.mult))
            E("vector", lambda e, j=j: e.tensor_sub(out=e_buf[:, j, :], in0=e_buf[:, j, :], in1=m32[:]))
            tv = SIG("vector")
            WT("scalar", "vector", tv)
            E("scalar", lambda e, j=j: e.activation(out=e_buf[:, j, :], in_=e_buf[:, j, :], func=AF.Exp))
            tsc = SIG("scalar")
            WT("vector", "scalar", tsc)
            if j == 0:
                E("vector", lambda e: e.tensor_copy(out=denb[:], in_=e_buf[:, 0, :]))
            else:
                E("vector", lambda e, j=j: e.tensor_add(out=denb[:], in0=denb[:], in1=e_buf[:, j, :]))
        E("vector", lambda e: e.reciprocal(out=rden[:], in_=denb[:]))
        E("vector", lambda e: e.memset(s1acc[:].rearrange("p c t -> p (c t)"), 0.0))
        SIG("vector")
        BAR()
        tv_pp = {0: 0, 1: 0}
        for j in range(11):
            dlt = j - 5
            lo = max(0, -64*dlt); hi = 512 - max(0, 64*dlt)
            E("vector", lambda e, j=j: e.tensor_tensor(out=abf[j % 2][:, :], in0=e_buf[:, j, :], in1=rden[:], op=ALU.mult))
            tv_abf = SIG("vector")
            WT("tensor", "vector", tv_abf)
            for c in range(NCH):
                pp = psA if c % 2 == 0 else psB
                WT("tensor", "vector", tv_pp[c % 2])
                E("tensor", lambda e, c=c, pp=pp, lo=lo, hi=hi, j=j: e.matmul(pp[:, lo:hi], rselT[:, c, :], abf[j % 2][:, lo:hi], start=True, stop=True))
                tm = SIG("tensor")
                WT("vector", "tensor", tm)
                E("vector", lambda e, c=c, pp=pp, lo=lo, hi=hi, dlt=dlt: e.tensor_tensor(out=tmpv[:, lo:hi], in0=pp[:, lo:hi], in1=vC[:, c, lo+64*dlt:hi+64*dlt], op=ALU.mult))
                E("vector", lambda e, c=c, lo=lo, hi=hi: e.tensor_add(out=s1acc[:, c, lo:hi], in0=s1acc[:, c, lo:hi], in1=tmpv[:, lo:hi]))
                tv_pp[c % 2] = SIG("vector")
        BAR()
        for c in range(NCH):
            E("vector", lambda e, c=c: e.tensor_copy(out=s1bf[:, c, :], in_=s1acc[:, c, :]))
        SIG("vector")
        BAR()
        s1st_t = {}
        for q4 in range(4):
            for c in range(NCH):
                pp = psT if c % 2 == 0 else psU
                E("tensor", lambda e, q4=q4, c=c, pp=pp: e.transpose(pp[:], s1bf[:, c, 128*q4:128*(q4+1)], ident[:]))
                tm = SIG("tensor")
                WT("scalar", "tensor", tm)
                if c == 0 and q4 >= 2:
                    WS("scalar", s1st_t[q4 - 2])
                E("scalar", lambda e, q4=q4, c=c, pp=pp: e.copy(out=s1tm_st[:, q4 % 2, 128*c:128*(c+1)], in_=pp[:]))
                tsc = SIG("scalar")
                WT("tensor", "scalar", tsc)
            WT("sync", "scalar")
            s1st_t[q4] = DMAS(s1_dram[ct, 128*q4:128*(q4+1), :], s1tm_st[:, q4 % 2, :])
        BAR()
    BAR()
    WS("sync", s_st.n)
    BAR()


    kv_bf = kv_t[:].rearrange("p a c t -> p (a c t)").bitcast(BF16)
    wq_bf = wq_sb[:].rearrange("p s c d -> p (s c d)")
    rw1a = kv_bf[:, 0:16384].rearrange("p (c h) -> p c h", h=4096)
    rw1b = wq_bf[:, 0:16384].rearrange("p (c h) -> p c h", h=4096)
    hbuf0 = dmat_bf[:, 0:8192].rearrange("p (c t) -> p c t", t=TC)
    hbuf1 = powm_bf[:, 0:8192].rearrange("p (c t) -> p c t", t=TC)
    sd = scan_ar[:].rearrange("p a t -> p (a t)")
    sdi = sd.bitcast(I32)
    lg_tm = sd[:, 0:256].rearrange("p (i e) -> p i e", e=8)
    sel0 = sd[:, 256:512].rearrange("p (i e) -> p i e", e=8)
    sel1 = sd[:, 512:768].rearrange("p (i e) -> p i e", e=8)
    cmpD = sd[:, 768:1024].rearrange("p (i e) -> p i e", e=8)
    rnk8 = sd[:, 1024:1280].rearrange("p (i e) -> p i e", e=8)
    e0f = sd[:, 1280:1312]; e1f = sd[:, 1312:1344]
    p0c = sd[:, 1344:1376]; p1c = sd[:, 1376:1408]
    pos0 = sd[:, 1408:1440]; pos1 = sd[:, 1440:1472]
    keep0 = sd[:, 1472:1504]; keep1 = sd[:, 1504:1536]
    sid0 = sd[:, 1536:1568]; sid1 = sd[:, 1568:1600]
    tmp32 = sd[:, 1600:1632]; tmp32b = sd[:, 1632:1664]
    sid0_i = sdi[:, 1664:1696]; sid1_i = sdi[:, 1696:1728]
    isc0_i = sdi[:, 1728:1760]; isc1_i = sdi[:, 1760:1792]
    tmp8a = sd[:, 1792:1800]; tmp8b = sd[:, 1800:1808]
    vaC = sd[:, 1808:1840]; vbC = sd[:, 1840:1872]; dvab = sd[:, 1872:1904]
    rw2_sb = a_scan[:].bitcast(BF16)[:, 0:256].rearrange("p (c e) -> p c e", e=8)
    base_rep = b_scan[:, 0:64]
    eb1_sb = b_scan[:, 64:96]
    rb1_sb = b_scan[:, 96:128]
    rb2_sb = b_scan[:, 128:129]
    corebase_sb = b_scan[:, 129:130]
    crep = b_scan[:, 130:131]
    payst = b_scan[:, 131:133]
    trib8_sb = b_scan[:, 192:256]
    ilst = b_scan[0:8, 256:512].rearrange("p (t j) -> p t j", j=2)
    tpst = b_scan[0:16, 256:384]
    lgf = ev_t[0:16, :]
    DmatF = Dmat[:].rearrange("p c t -> p (c t)")
    A_all = DmatF[:, 0:1024]
    onesD = DmatF[:, 1024:2048]
    Acum = DmatF[:, 2048:3072]
    Acum3 = DmatF[:, 2048:3072].rearrange("p (t j) -> p t j", j=2)
    cnt_s = DmatF[0:1, 3072:3584]
    cntl_s = DmatF[0:1, 3584:3648]
    base1_s = DmatF[0:1, 3648:3712]
    base2_s = DmatF[0:1, 3712:3776]
    baseT_s = DmatF[0:1, 3776:3840]
    mask_sb = csum[0:1, 0:512]
    for k in range(16):
        tl = DMAL(cast_f32[:, :], rw1T_in.rearrange("p c h -> p (c h)")[:, 2048*k:2048*(k+1)])
        WL("vector", tl)
        dstap = (kv_bf if k < 8 else wq_bf)[:, 2048*(k % 8):2048*(k % 8)+2048]
        E("vector", lambda e, dstap=dstap: e.tensor_copy(out=dstap, in_=cast_f32[:, :]))
        tv = SIG("vector")
        WT("sync", "vector", tv)
    tl = DMAL(cast_f32[:, 0:256].rearrange("p (c e) -> p c e", e=8), rw2T_in.rearrange("c p e -> p c e"))
    WL("vector", tl)
    E("vector", lambda e: e.tensor_copy(out=rw2_sb[:].rearrange("p c e -> p (c e)"), in_=cast_f32[:, 0:256]))
    SIG("vector")
    DMAL(rb1_sb, rb1_in[:, :])
    DMAL(rb2_sb[0:8, :], rb2_in[:, :])
    DMAL(trib8_sb, trib8_in[:, :])
    DMAL(mskc512_in_sb := mask_sb, mskc512_in[:, :])
    DMAL(corebase_sb[0:1, :], corebase_in[:, :])
    DMAL(eb1_sb, eb1_in[:, :])
    E("vector", lambda e: e.memset(lgf[:, :], 0.0))
    SIG("vector")
    BAR()
    for grp in range(8):
        tl = DMAL(xnf_seg[:], xnf1_dram[:, :, grp*TC:(grp+1)*TC])
        WL("tensor", tl)
        tsc_pp = {0: 0, 1: 0}
        for hc in range(32):
            pp = psA if hc % 2 == 0 else psB
            WT("tensor", "scalar", tsc_pp[hc % 2])
            for c in range(NCH):
                lh = (rw1a if c < 4 else rw1b)[:, c % 4, 128*hc:128*hc+128]
                E("tensor", lambda e, lh=lh, c=c, pp=pp: e.matmul(pp[:], lh, xnf_seg[:, c, :], start=(c == 0), stop=(c == NCH-1)))
            tm = SIG("tensor")
            WT("scalar", "tensor", tm)
            hb = (hbuf0 if hc < 16 else hbuf1)
            E("scalar", lambda e, hb=hb, hc=hc, pp=pp: e.activation(out=hb[:, hc % 16, :], in_=pp[:], func=AF.Relu, bias=rb1_sb[:, hc:hc+1]))
            tsc_pp[hc % 2] = SIG("scalar")
        BAR()
        for hc in range(32):
            hb = (hbuf0 if hc < 16 else hbuf1)
            E("tensor", lambda e, hb=hb, hc=hc: e.matmul(psR[0:8, :], rw2_sb[:, hc, :], hb[:, hc % 16, :], start=(hc == 0), stop=(hc == 31)))
        tm = SIG("tensor")
        WT("scalar", "tensor", tm)
        E("scalar", lambda e: e.activation(out=lgf[0:8, :], in_=psR[0:8, :], func=AF.Identity, bias=rb2_sb[0:8, 0:1]))
        tsc = SIG("scalar")
        WT("tensor", "scalar", tsc)
        for q4 in range(4):
            E("tensor", lambda e, q4=q4: e.transpose(psTf[:], lgf[:, 128*q4:128*(q4+1)], identf[0:16, :]))
            tm = SIG("tensor")
            WT("vector", "tensor", tm)
            E("vector", lambda e, grp=grp, q4=q4: e.tensor_copy(out=lg_tm[:, grp*4+q4, :], in_=psTf[:, 0:8]))
            tv = SIG("vector")
            WT("tensor", "vector", tv)
        BAR()
    WT("sync", "vector")
    E("vector", lambda e: e.memset(rnk8[:].rearrange("p i e -> p (i e)"), 0.0))
    for e2 in range(8):
        E("vector", lambda e, e2=e2: e.tensor_tensor(out=cmpD[:], in0=lg_tm[:, :, e2:e2+1].broadcast_to([128, NTILE, 8]), in1=lg_tm[:], op=ALU.is_gt))
        E("vector", lambda e: e.tensor_add(out=rnk8[:], in0=rnk8[:], in1=cmpD[:]))
        E("vector", lambda e, e2=e2: e.tensor_tensor(out=cmpD[:], in0=lg_tm[:, :, e2:e2+1].broadcast_to([128, NTILE, 8]), in1=lg_tm[:], op=ALU.is_equal))
        E("vector", lambda e, e2=e2: e.tensor_tensor(out=cmpD[:], in0=cmpD[:], in1=trib8_sb[:, e2*8:(e2+1)*8].unsqueeze(1).broadcast_to([128, NTILE, 8]), op=ALU.mult))
        E("vector", lambda e: e.tensor_add(out=rnk8[:], in0=rnk8[:], in1=cmpD[:]))
    E("vector", lambda e: e.tensor_scalar(out=sel0[:].rearrange("p i e -> p (i e)"), in0=rnk8[:].rearrange("p i e -> p (i e)"), scalar1=0.0, scalar2=None, op0=ALU.is_equal))
    E("vector", lambda e: e.tensor_scalar(out=sel1[:].rearrange("p i e -> p (i e)"), in0=rnk8[:].rearrange("p i e -> p (i e)"), scalar1=1.0, scalar2=None, op0=ALU.is_equal))
    for sel, ef, vf in ((sel0, e0f, vaC), (sel1, e1f, vbC)):
        E("vector", lambda e, sel=sel: e.tensor_tensor(out=cmpD[:], in0=sel[:], in1=io_col[:, 0:8].unsqueeze(1).broadcast_to([128, NTILE, 8]), op=ALU.mult))
        E("vector", lambda e, ef=ef: e.tensor_reduce(out=ef.rearrange("p (i q) -> p i q", q=1), in_=cmpD[:], axis=AX.X, op=ALU.add))
        E("vector", lambda e, sel=sel: e.tensor_tensor(out=cmpD[:], in0=sel[:], in1=lg_tm[:], op=ALU.mult))
        E("vector", lambda e, vf=vf: e.tensor_reduce(out=vf.rearrange("p (i q) -> p i q", q=1), in_=cmpD[:], axis=AX.X, op=ALU.add))
    E("vector", lambda e: e.tensor_sub(out=dvab, in0=vaC, in1=vbC))
    tv = SIG("vector")
    WT("scalar", "vector", tv)
    E("scalar", lambda e: e.activation(out=p0c, in_=dvab, func=AF.Sigmoid))
    tsc = SIG("scalar")
    WT("vector", "scalar", tsc)
    E("vector", lambda e: e.tensor_scalar(out=p1c, in0=p0c, scalar1=-1.0, scalar2=1.0, op0=ALU.mult, op1=ALU.add))
    E("vector", lambda e: e.memset(A_all, 0.0))
    E("vector", lambda e: e.memset(onesD, 1.0))
    SIG("vector")
    BAR()
    ilst_t = 0
    tsc_a = 0
    for i in range(NTILE):
        l = i // 4
        for j, sel in ((0, sel0), (1, sel1)):
            WT("tensor", "scalar", tsc_a)
            E("tensor", lambda e, sel=sel, i=i: e.transpose(psTf[0:8, :], sel[:, i, :], identf[:]))
            tm = SIG("tensor")
            WT("scalar", "tensor", tm)
            if j == 0:
                WS("scalar", ilst_t)
            E("scalar", lambda e, j=j: e.copy(out=ilst[:, :, j], in_=psTf[0:8, :]))
            tsc_a = SIG("scalar")
        WT("sync", "scalar")
        ilst_t = DMAS(DmatF[l*8:l*8+8, 256*(i % 4):256*(i % 4)+256], ilst[:].rearrange("p t j -> p (t j)"))
    BAR()
    WS("vector", s_st.n)
    E("vector", lambda e: e.tensor_tensor_scan(out=Acum, data0=onesD, data1=A_all, initial=zcol[:, 0:1], op0=ALU.mult, op1=ALU.add))
    tv = SIG("vector")
    WT("sync", "vector", tv)
    tst = DMAS(cnt2_in[:, :], DmatF[0:64, 3071:3072])
    WS("gpsimd", tst)
    BAR()
    E("gpsimd", lambda e: e.collective_compute("AllGather", ALU.bypass, replica_groups=[list(range(NCORE))],
                                               ins=[cnt2_in[:, :]], outs=[cnt2_out[:, :, :]]))
    SIG("gpsimd")
    BAR()
    WT("sync", "gpsimd")
    tl = DMAL(cnt_s, cnt2_out.rearrange("c q o -> o (c q)"))
    WL("vector", tl)
    E("vector", lambda e: e.tensor_copy(out=cntl_s, in_=cnt_s[0:1, 0:64]))
    for cc in range(1, 8):
        E("vector", lambda e, cc=cc: e.tensor_add(out=cntl_s, in0=cntl_s, in1=cnt_s[0:1, 64*cc:64*cc+64]))
    E("vector", lambda e: e.memset(base1_s[0:1, 0:8], 0.0))
    for l in range(1, 8):
        E("vector", lambda e, l=l: e.tensor_add(out=base1_s[0:1, 8*l:8*l+8], in0=base1_s[0:1, 8*(l-1):8*l], in1=cntl_s[0:1, 8*(l-1):8*l]))
    E("vector", lambda e: e.tensor_tensor(out=cnt_s, in0=cnt_s, in1=mask_sb, op=ALU.mult))
    E("vector", lambda e: e.tensor_copy(out=base2_s, in_=cnt_s[0:1, 0:64]))
    for cc in range(1, 8):
        E("vector", lambda e, cc=cc: e.tensor_add(out=base2_s, in0=base2_s, in1=cnt_s[0:1, 64*cc:64*cc+64]))
    E("vector", lambda e: e.tensor_add(out=baseT_s, in0=base1_s, in1=base2_s))
    SIG("vector")
    BAR()
    E("tensor", lambda e: e.matmul(psR[:, 0:64], onesNf[:], baseT_s, start=True, stop=True))
    tm = SIG("tensor")
    WT("vector", "tensor", tm)
    E("vector", lambda e: e.tensor_copy(out=base_rep, in_=psR[:, 0:64]))
    E("vector", lambda e: e.memset(tpst[:, :], 0.0))
    SIG("vector")
    BAR()
    tm_tp = 0
    for i in range(NTILE):
        l = i // 4; toff = (i % 4) * 128
        for j, sel, posX in ((0, sel0, pos0), (1, sel1, pos1)):
            WT("sync", "tensor", tm_tp)
            tst = DMAL(tpst[0:8, :], Acum3[l*8:l*8+8, toff:toff+128, j], allow_slow_non_contiguous=True)
            WL("tensor", tst)
            E("tensor", lambda e: e.transpose(psTf[:], tpst[:, :], identf[0:16, :]))
            tm_tp = SIG("tensor")
            WT("vector", "tensor", tm_tp)
            E("vector", lambda e: e.tensor_copy(out=tmp8a, in_=psTf[:, 0:8]))
            E("vector", lambda e, l=l: e.tensor_tensor(out=tmp8b, in0=tmp8a, in1=base_rep[:, l*8:l*8+8], op=ALU.add))
            E("vector", lambda e, sel=sel, i=i: e.tensor_tensor(out=tmp8b, in0=tmp8b, in1=sel[:, i, :], op=ALU.mult))
            E("vector", lambda e, posX=posX, i=i: e.tensor_reduce(out=posX[:, i:i+1].rearrange("p (i q) -> p i q", q=1), in_=tmp8b.rearrange("p (i e) -> p i e", e=8), axis=AX.X, op=ALU.add))
            E("vector", lambda e, posX=posX, i=i: e.tensor_scalar(out=posX[:, i:i+1], in0=posX[:, i:i+1], scalar1=-1.0, scalar2=None, op0=ALU.add))
            tv = SIG("vector")
    BAR()
    for posX, keepX, eXf, sidX, sidX_i, iscX_i in ((pos0, keep0, e0f, sid0, sid0_i, isc0_i), (pos1, keep1, e1f, sid1, sid1_i, isc1_i)):
        E("vector", lambda e, posX=posX, keepX=keepX: e.tensor_scalar(out=keepX, in0=posX, scalar1=4096.0, scalar2=None, op0=ALU.is_lt))
        E("vector", lambda e, posX=posX: e.tensor_scalar(out=tmp32, in0=posX, scalar1=4095.0, scalar2=None, op0=ALU.min))
        E("vector", lambda e, eXf=eXf: e.tensor_scalar(out=tmp32b, in0=eXf, scalar1=4096.0, scalar2=None, op0=ALU.mult))
        E("vector", lambda e, sidX=sidX: e.tensor_add(out=sidX, in0=tmp32b, in1=tmp32))
        E("vector", lambda e, sidX=sidX, sidX_i=sidX_i: e.tensor_copy(out=sidX_i, in_=sidX))
        E("vector", lambda e, posX=posX: e.tensor_scalar(out=tmp32, in0=posX, scalar1=5119.0, scalar2=None, op0=ALU.min))
        E("vector", lambda e, eXf=eXf: e.tensor_scalar(out=tmp32b, in0=eXf, scalar1=5120.0, scalar2=None, op0=ALU.mult))
        E("vector", lambda e: e.tensor_add(out=tmp32, in0=tmp32, in1=tmp32b))
        E("vector", lambda e, iscX_i=iscX_i: e.tensor_copy(out=iscX_i, in_=tmp32))
    SIG("vector")
    WT("sync", "vector")
    BAR()
    WS("sync", s_st.n)
    BAR()


    xqf = xq_t[:].rearrange("p c t -> p (c t)").bitcast(F32)
    xqi = xq_t[:].rearrange("p c t -> p (c t)").bitcast(I32)
    eb2r = xqf[:, 0:1024]
    disp_sb = xqf[:, 1024:1088].rearrange("p (a w) -> p a w", w=2)
    idc_i = xqi[:, 1088:1120]
    E("vector", lambda e: e.memset(cast_f32[:, 0:640], 0.0))
    tv = SIG("vector")
    WT("sync", "vector", tv)
    tz = DMAS(dispT.rearrange("(p n) w -> p (n w)", p=128), cast_f32[:, 0:640])
    E("tensor", lambda e: e.matmul(psR[:, 0:1], onesNf[:], corebase_sb[0:1, :], start=True, stop=True))
    tm = SIG("tensor")
    WT("vector", "tensor", tm)
    E("vector", lambda e: e.tensor_copy(out=crep, in_=psR[:, 0:1]))
    E("vector", lambda e: e.tensor_add(out=crep, in0=crep, in1=io_row[:, 0:1]))
    SIG("vector")
    WS("gpsimd", tz)
    tgd = 0
    for i in range(NTILE):
        l = i // 4; toff = (i % 4) * 128
        for j, pc, iscX in ((0, p0c, isc0_i), (1, p1c, isc1_i)):
            WG("vector", tgd)
            E("vector", lambda e, l=l, toff=toff: e.tensor_scalar(out=payst[:, 0:1], in0=crep, scalar1=float(l*512 + toff), scalar2=None, op0=ALU.add))
            E("vector", lambda e, pc=pc, i=i: e.tensor_copy(out=payst[:, 1:2], in_=pc[:, i:i+1]))
            tv = SIG("vector")
            WT("gpsimd", "vector", tv)
            tgd = GDMA(dispT[:, :], payst, out_off=IndirectOffsetOnAxis(iscX[:, i:i+1], 0))
    BAR()
    E("gpsimd", lambda e: e.collective_compute("ReduceScatter", ALU.add, replica_groups=[list(range(NCORE))],
                                               ins=[dispT[:, :]], outs=[disp_rs[:, :]]))
    SIG("gpsimd")
    BAR()
    tl = DMAL(disp_sb[:], disp_rs[0:4096, :].rearrange("(a p) w -> p a w", p=128))
    tl2 = DMAL(eb2r, eb2_in[:, :])
    WL("vector", tl)
    E("vector", lambda e: e.tensor_copy(out=idc_i, in_=disp_sb[:, :, 0]))
    SIG("vector")
    BAR()
    for mat_src, mat_dst in ((ew1T_in, ew1_bf), (ew2T_in, ew2_bf)):
        msrc = mat_src.rearrange("p c h -> p (c h)")
        mdst = mat_dst.rearrange("p c h -> p (c h)")
        for k in range(16):
            tl = DMAL(cast_f32[:, :], msrc[:, 2048*k:2048*(k+1)])
            WL("vector", tl)
            E("vector", lambda e: e.tensor_copy(out=cast_bf[:, :], in_=cast_f32[:, :]))
            tv = SIG("vector")
            WT("sync", "vector", tv)
            tstw = DMAS(mdst[:, 2048*k:2048*(k+1)], cast_bf[:, :])
            WS("vector", tstw)
            WS("sync", tstw)
    BAR()
    xeF = [(kv_bf if t < 4 else wq_bf)[:, 4096*(t % 4):4096*(t % 4)+4096].rearrange("p (c s) -> p c s", s=TC) for t in range(8)]
    xg_st = [camax[:].bitcast(BF16)[:, 0:1024], csum[:].bitcast(BF16)[:, 0:1024]]
    xn1_flat = xn1_ag.rearrange("c n d -> (c n) d")
    tm_xg = {0: 0, 1: 0}
    for t in range(8):
        for q in range(4):
            a = t * 4 + q
            pb = a % 2
            WT("gpsimd", "tensor", tm_xg[pb])
            tg = GDMA(xg_st[pb], xn1_flat, in_off=IndirectOffsetOnAxis(idc_i[:, a:a+1], 0))
            WG("tensor", tg)
            for c in range(NCH):
                pp = psT if c % 2 == 0 else psU
                E("tensor", lambda e, pb=pb, c=c, pp=pp: e.transpose(pp[:], xg_st[pb][:, 128*c:128*(c+1)], ident[:]))
                tmx = SIG("tensor")
                WT("scalar", "tensor", tmx)
                E("scalar", lambda e, t=t, c=c, q=q, pp=pp: e.copy(out=xeF[t][:, c, 128*q:128*(q+1)], in_=pp[:]))
                tsc = SIG("scalar")
                WT("tensor", "scalar", tsc)
            tm_xg[pb] = C.sems["tensor"].n
    BAR()
    mixf = mix_t[:].rearrange("p c t -> p (c t)")
    ew1st = [mixf[:, 0:1024].rearrange("p (c d) -> p c d", d=128), mixf[:, 1024:2048].rearrange("p (c d) -> p c d", d=128)]
    hst = [mixf[:, 2048:2560], mixf[:, 2560:3072]]
    tls = {0: DMAL(ew1st[0][:, :, :], ew1_bf[:, :, 0:128])}
    tm_done = {}
    hst_st = {0: 0, 1: 0}
    tsc_pp = {0: 0, 1: 0}
    for hc in range(32):
        if hc + 1 < 32:
            if hc - 1 >= 0:
                WT("sync", "tensor", tm_done[hc - 1])
            tls[hc + 1] = DMAL(ew1st[(hc + 1) % 2][:, :, :], ew1_bf[:, :, 128*(hc+1):128*(hc+2)])
        WL("tensor", tls[hc])
        for t in range(8):
            pp = psA if t % 2 == 0 else psB
            WT("tensor", "scalar", tsc_pp[t % 2])
            for c in range(NCH):
                E("tensor", lambda e, hc=hc, c=c, t=t, pp=pp: e.matmul(pp[:], ew1st[hc % 2][:, c, :], xeF[t][:, c, :], start=(c == 0), stop=(c == NCH-1)))
            tmx = SIG("tensor")
            WT("scalar", "tensor", tmx)
            WS("scalar", hst_st[t % 2])
            E("scalar", lambda e, t=t, hc=hc: e.activation(out=hst[t % 2], in_=(psA if t % 2 == 0 else psB)[:], func=AF.Relu, bias=eb1_sb[:, hc:hc+1]))
            tsc_pp[t % 2] = SIG("scalar")
            WT("sync", "scalar", tsc_pp[t % 2])
            hst_st[t % 2] = DMAS(h_dram[t, hc, :, :], hst[t % 2])
        tm_done[hc] = C.sems["tensor"].n
    BAR()
    WS("sync", s_st.n)
    for hc in range(32):
        dst = (kv_bf if hc < 16 else wq_bf)[:, 1024*(hc % 16):1024*(hc % 16)+1024]
        DMAL(dst, ew2_bf[:, hc, :])
    BAR()
    yo_bf2 = camax[:].bitcast(BF16)[:, 0:1024]
    tmpW = cast_f32[:, 0:1024]
    for t in range(8):
        tlh1 = DMAL(hbuf0[:, :, :], h_dram[t, 0:16].rearrange("c p t -> p c t"))
        tlh2 = DMAL(hbuf1[:, :, :], h_dram[t, 16:32].rearrange("c p t -> p c t"))
        WL("tensor", tlh2)
        tv_yo = 0
        for s4 in range(4):
            a = t * 4 + s4
            WT("tensor", "vector", tv_yo)
            for half in range(2):
                for hc in range(32):
                    hb = hbuf0 if hc < 16 else hbuf1
                    ew2r = (kv_bf if hc < 16 else wq_bf)[:, 1024*(hc % 16):1024*(hc % 16)+1024]
                    E("tensor", lambda e, hb=hb, hc=hc, s4=s4, half=half, ew2r=ew2r: e.matmul(
                        psW[:, 512*half:512*(half+1)], hb[:, hc % 16, 128*s4:128*(s4+1)], ew2r[:, 512*half:512*(half+1)], start=(hc == 0), stop=(hc == 31)))
            tmx = SIG("tensor")
            WT("vector", "tensor", tmx)
            E("vector", lambda e: e.tensor_add(out=tmpW, in0=psW[:], in1=eb2r))
            E("vector", lambda e, a=a: e.tensor_scalar(out=yo_bf2, in0=tmpW, scalar1=disp_sb[:, a, 1:2], scalar2=None, op0=ALU.mult))
            tv_yo = SIG("vector")
            WT("sync", "vector", tv_yo)
            tst = DMAS(yo_dram[512*t+128*s4 : 512*t+128*s4+128, :], yo_bf2)
            WS("vector", tst)
        BAR()
    BAR()
    WS("sync", s_st.n)
    BAR()
    E("gpsimd", lambda e: e.collective_compute("AllGather", ALU.bypass, replica_groups=[list(range(NCORE))],
                                               ins=[yo_dram[:, :]], outs=[yo_ag[:, :, :]]))
    SIG("gpsimd")
    E("gpsimd", lambda e: e.collective_compute("AllGather", ALU.bypass, replica_groups=[list(range(NCORE))],
                                               ins=[cnt2_in[:, :]], outs=[cnt2_out[:, :, :]]))
    SIG("gpsimd")
    BAR()
    yoflat = yo_ag.rearrange("c n d -> (c n) d")
    g1acc = cast_f32[:, 0:1024]
    gsil2 = cast_f32[:, 1024:2048]
    ybg1 = camax[:].bitcast(BF16)[:, 0:1024]
    ybg2 = csum[:].bitcast(BF16)[:, 0:1024]
    s1sb = camax[:].bitcast(BF16)[:, 0:1024]
    tv_fin = 0
    to_st = 0
    for i in list(range(1, NTILE)) + [0]:
        l = i // 4
        WT("gpsimd", "vector", tv_fin)
        tg1 = GDMA(ybg1, yoflat, in_off=IndirectOffsetOnAxis(sid0_i[:, i:i+1], 0))
        tg2 = GDMA(ybg2, yoflat, in_off=IndirectOffsetOnAxis(sid1_i[:, i:i+1], 0))
        WG("vector", tg1)
        E("vector", lambda e, i=i: e.tensor_scalar(out=g1acc, in0=ybg1, scalar1=keep0[:, i:i+1], scalar2=0.0, op0=ALU.mult, op1=ALU.add))
        WG("vector", tg2)
        E("vector", lambda e, i=i: e.scalar_tensor_tensor(out=g1acc, in0=ybg2, scalar=keep1[:, i:i+1], in1=g1acc, op0=ALU.mult, op1=ALU.add))
        tv = SIG("vector")
        WT("sync", "vector", tv)
        tl1 = DMAL(x_tile[:, 0, :], x1_dram[i, :, :])
        tl2 = DMAL(s1sb.rearrange("p (q d) -> p q d", d=D)[0:64, 0, :], s1_dram[2*(i % 4), l*64:l*64+64, :])
        tl3 = DMAL(s1sb.rearrange("p (q d) -> p q d", d=D)[64:128, 0, :], s1_dram[2*(i % 4)+1, l*64:l*64+64, :])
        WT("scalar", "vector", tv)
        E("scalar", lambda e: e.activation(out=gsil2, in_=g1acc, func=AF.Silu))
        tsc = SIG("scalar")
        WL("vector", tl2)
        WL("vector", tl3)
        WL("vector", tl1)
        WT("vector", "scalar", tsc)
        WS("vector", to_st)
        E("vector", lambda e: e.tensor_tensor(out=gsil2, in0=s1sb, in1=gsil2, op=ALU.mult))
        E("vector", lambda e: e.tensor_tensor(out=x_tile[:, 1, :], in0=x_tile[:, 0, :], in1=gsil2, op=ALU.add))
        tv_fin = SIG("vector")
        WT("sync", "vector", tv_fin)
        to_st = DMAS(out_dram[i, :, :], x_tile[:, 1, :])
    BAR()
    WS("sync", s_st.n)
    BAR()

    @block.sync
    def _(eng):
        for f in C.q["sync"]:
            f(eng)

    @block.vector
    def _(eng):
        for f in C.q["vector"]:
            f(eng)

    @block.scalar
    def _(eng):
        for f in C.q["scalar"]:
            f(eng)

    @block.tensor
    def _(eng):
        for f in C.q["tensor"]:
            f(eng)

    @block.gpsimd
    def _(eng):
        for f in C.q["gpsimd"]:
            f(eng)

    st.close()
    return nc




_CACHE = {}


def _host_prep(ins):
    x = np.ascontiguousarray(ins["L0_attn_hidden"], np.float32)
    gw = ins["L0_ffn_gate_w"]; uw = ins["L0_ffn_up_w"]; dw = ins["L0_ffn_down_w"]
    gpart = np.transpose(np.ascontiguousarray(gw, np.float32).reshape(16, 512, 8, 128), (0, 3, 2, 1)).reshape(16, 128, 4096)
    upart = np.transpose(np.ascontiguousarray(uw, np.float32).reshape(16, 512, 8, 128), (0, 3, 2, 1)).reshape(16, 128, 4096)
    dpart = np.transpose(np.ascontiguousarray(dw, np.float32).reshape(16, 1024, 4, 128), (0, 3, 2, 1)).reshape(16, 128, 4096)
    wcat = np.ascontiguousarray(np.concatenate([gpart, upart, dpart], axis=2), np.float32)
    shg = np.ascontiguousarray(np.transpose(np.asarray(ins["L0_ffn_sh_gate_w"], np.float32).reshape(1024, 8, 128), (2, 1, 0)))
    shu = np.ascontiguousarray(np.transpose(np.asarray(ins["L0_ffn_sh_up_w"], np.float32).reshape(1024, 8, 128), (2, 1, 0)))
    shd = np.ascontiguousarray(np.transpose(np.asarray(ins["L0_ffn_sh_down_w"], np.float32).reshape(1024, 8, 128), (2, 1, 0)))
    wqT = np.ascontiguousarray(np.transpose(np.asarray(ins["L1_attn_Wq"], np.float32).reshape(1024, 8, 128), (2, 1, 0)))
    wkT = np.ascontiguousarray(np.transpose(np.asarray(ins["L1_attn_Wk"], np.float32).reshape(1024, 8, 128), (2, 1, 0)))
    wvT = np.ascontiguousarray(np.transpose(np.asarray(ins["L1_attn_Wv"], np.float32).reshape(1024, 8, 128), (2, 1, 0)))
    pe = np.asarray(ins["L1_attn_pe"], np.float32)
    pe_arr = np.zeros((88, 128, 16), np.float32)
    selc = np.zeros((8, 128, 16), np.float32)
    rsel = np.zeros((16, 8, 128), np.float32)
    for c8 in range(8):
        for hh in range(2):
            h = 2*c8 + hh
            for j in range(11):
                pe_arr[j*8+c8, 64*hh:64*hh+64, h] = pe[h, :, j]
            selc[c8, 64*hh:64*hh+64, h] = 1.0
            rsel[h, c8, 64*hh:64*hh+64] = 1.0
    selc_bf = selc.astype(ml_dtypes.bfloat16)
    rsel_bf = rsel.astype(ml_dtypes.bfloat16)
    rw1T = np.ascontiguousarray(np.transpose(np.asarray(ins["L1_ffn_rw1"], np.float32).reshape(8, 128, 4096), (1, 0, 2)))
    rw2T = np.ascontiguousarray(np.asarray(ins["L1_ffn_rw2"], np.float32).reshape(32, 128, 8))
    rb1h = np.ascontiguousarray(np.asarray(ins["L1_ffn_rb1"], np.float32).reshape(32, 128).T)
    rb2h = np.asarray(ins["L1_ffn_rb2"], np.float32).reshape(8, 1)
    tri8 = (np.arange(8)[:, None] < np.arange(8)[None, :]).astype(np.float32).reshape(64)
    trib8h = np.tile(tri8, (128, 1)).astype(np.float32)
    ew1h = np.ascontiguousarray(np.transpose(np.asarray(ins["L1_ffn_ew1"], np.float32).reshape(8, 8, 128, 4096), (0, 2, 1, 3)))
    ew2h = np.ascontiguousarray(np.transpose(np.asarray(ins["L1_ffn_ew2"], np.float32).reshape(8, 32, 128, 1024), (0, 2, 1, 3)))
    eb1h = np.ascontiguousarray(np.transpose(np.asarray(ins["L1_ffn_eb1"], np.float32).reshape(8, 32, 128), (0, 2, 1)))
    eb2h = np.asarray(ins["L1_ffn_eb2"], np.float32)
    per_core = []
    for r in range(NCORE):
        xs = np.zeros((NB, SEG, D), np.float32)
        t0 = r * TC
        xs[:, 1:, :] = x[:, t0:t0+TC, :]
        if r > 0:
            xs[:, 0, :] = x[:, t0-1, :]
        m = {
            "x_in": xs,
            "w_key": np.asarray(ins["L0_attn_w_key"], np.float32),
            "w_value": np.asarray(ins["L0_attn_w_value"], np.float32),
            "w_recep": np.asarray(ins["L0_attn_w_receptance"], np.float32),
            "w_output": np.asarray(ins["L0_attn_w_output"], np.float32),
            "rms_key": np.asarray(ins["L0_attn_rms_key"], np.float32),
            "rms_value": np.asarray(ins["L0_attn_rms_value"], np.float32),
            "rms_recep": np.asarray(ins["L0_attn_rms_receptance"], np.float32),
            "rms_output": np.asarray(ins["L0_attn_rms_output"], np.float32),
            "tmix_k": np.asarray(ins["L0_attn_time_mix_key"], np.float32).ravel(),
            "tmix_v": np.asarray(ins["L0_attn_time_mix_value"], np.float32).ravel(),
            "tmix_r": np.asarray(ins["L0_attn_time_mix_receptance"], np.float32).ravel(),
            "tdecay": np.asarray(ins["L0_attn_time_decay"], np.float32),
            "tfirst": np.asarray(ins["L0_attn_time_first"], np.float32),
            "rankm": (np.arange(NCORE) < r).astype(np.float32),
            "rankw": np.maximum(r - 1 - np.arange(NCORE), 0).astype(np.float32),
            "wcat_in": wcat,
            "shg_in": shg, "shu_in": shu, "shd_in": shd,
            "rw_in": np.asarray(ins["L0_ffn_router_weight"], np.float32),
            "ebias_in": np.asarray(ins["L0_ffn_e_bias"], np.float32),
            "wqT_in": wqT, "wkT_in": wkT, "wvT_in": wvT,
            "pe_in": pe_arr, "selc_in": selc_bf, "rsel_in": rsel_bf,
            "rw1T_in": rw1T, "rw2T_in": rw2T, "rb1_in": rb1h, "rb2_in": rb2h,
            "trib8_in": trib8h,
            "mskc512_in": np.tile((np.arange(8) < r).astype(np.float32)[:, None], (1, 64)).reshape(1, 512),
            "corebase_in": np.array([[r * 4096.0]], np.float32),
            "ew1T_in": ew1h[r], "ew2T_in": ew2h[r], "eb1_in": eb1h[r],
            "eb2_in": np.tile(eb2h[r], (128, 1)),
        }
        per_core.append(m)
    return per_core


def kernel(**inputs):
    if "nc" not in _CACHE:
        _CACHE["nc"] = build_program()
    nc = _CACHE["nc"]
    per_core = _host_prep(inputs)
    res = run_bass_kernel_spmd(nc, per_core, core_ids=list(range(NCORE)))
    full = np.empty((NB, NTOK, D), np.float32)
    for r in range(NCORE):
        o = np.asarray(res.results[r]["out_dram"], np.float32).reshape(NB, TC, D)
        full[:, r*TC:(r+1)*TC, :] = o
    return full


# version 03
# speedup vs baseline: 1.0551x; 1.0342x over previous
import contextlib
import numpy as np
import ml_dtypes
import concourse.bass as bass
import concourse.mybir as mybir
from concourse.bass import IndirectOffsetOnAxis
from concourse.bass_utils import run_bass_kernel_spmd

F32 = mybir.dt.float32
BF16 = mybir.dt.bfloat16
I32 = mybir.dt.int32
AF = mybir.ActivationFunctionType
ALU = mybir.AluOpType
AX = mybir.AxisListType

NB, TC, D, NCH = 8, 512, 1024, 8
SEG = TC + 1
NTOK = NB * TC
NTILE = 32
MAGIC = 1.5 * 2.0 ** 23
RMS_EPS = 1e-8
LN_EPS = 1e-5
ENGINES = ("sync", "vector", "scalar", "tensor", "gpsimd")
NCORE = 8


class Sem:
    def __init__(self, h):
        self.h, self.n = h, 0


AUTO_SYNC = ("vector", "scalar", "gpsimd")


class Ctx:
    def __init__(self):
        self.q = {e: [] for e in ENGINES}
        self.sems = {}

    def emit(self, eng, fn):
        if eng in AUTO_SYNC:
            sem = self.sems[eng]
            prev = sem.n
            sem.n += 1
            def wrapped(e, f=fn, s=sem.h, prev=prev):
                if prev > 0:
                    e.wait_ge(s, prev)
                f(e).then_inc(s, 1)
            self.q[eng].append(wrapped)
        else:
            self.q[eng].append(fn)

    def sig(self, eng, k=1):
        if eng in AUTO_SYNC:
            return self.sems[eng].n
        sem = self.sems[eng]
        f = self.q[eng].pop()
        sem.n += k
        self.q[eng].append(lambda e, f=f, s=sem.h, k=k: f(e).then_inc(s, k))
        return sem.n

    def wait(self, eng, other, val=None):
        sem = self.sems[other]
        v = sem.n if val is None else val
        if v > 0:
            self.q[eng].append(lambda e, s=sem.h, v=v: e.wait_ge(s, v))

    def barrier(self):
        vals = {e: self.sems[e].n for e in ENGINES}
        for e1 in ENGINES:
            for e2 in ENGINES:
                if e1 != e2 and vals[e2] > 0:
                    self.q[e1].append(lambda e, s=self.sems[e2].h, v=vals[e2]: e.wait_ge(s, v))
            for sx in self.dma_sems:
                if sx.n > 0:
                    self.q[e1].append(lambda e, s=sx.h, v=sx.n: e.wait_ge(s, v))


def build_program():
    nc = bass.Bass("TRN2", target_bir_lowering=False, debug=False)
    C = Ctx()
    st = contextlib.ExitStack()
    din = lambda n, s, d=F32: nc.dram_tensor(n, s, d, kind="ExternalInput").ap()
    dou = lambda n, s, d=F32: nc.dram_tensor(n, s, d, kind="ExternalOutput").ap()
    dit = lambda n, s, d=F32: nc.dram_tensor(n, s, d, kind="Internal").ap()
    dsh = lambda n, s, d=F32: nc.dram_tensor(n, s, d, kind="Internal", addr_space="Shared").ap()
    sb = lambda n, s, d=F32: st.enter_context(nc.sbuf_tensor(n, s, d))
    psm = lambda n, s, d=F32: st.enter_context(nc.psum_tensor(n, s, d))
    E, SIG, WT, BAR = C.emit, C.sig, C.wait, C.barrier

    x_in = din("x_in", [NB, SEG, D])
    w_mats = [din(f"w_{m}", [D, D]) for m in ("key", "value", "recep", "output")]
    rms_in = [din(f"rms_{m}", [D]) for m in ("key", "value", "recep", "output")]
    tmix_in = [din(f"tmix_{m}", [D]) for m in ("k", "v", "r")]
    tdecay = din("tdecay", [D]); tfirst = din("tfirst", [D])
    rankm = din("rankm", [NCORE])
    rankw = din("rankw", [NCORE])
    wcat_in = din("wcat_in", [16, 128, 12288])
    shg_in = din("shg_in", [128, NCH, D])
    shu_in = din("shu_in", [128, NCH, D])
    shd_in = din("shd_in", [128, NCH, D])
    rw_in = din("rw_in", [16, D])
    ebias_in = din("ebias_in", [16])

    kv_spill = dit("kv_spill", [2, NCH, 128, NTOK], F32)
    cc_in = dit("cc_in", [128, 128], F32)
    cc_out = dit("cc_out", [NCORE, 128, 128], F32)
    xn_dram = dit("xn_dram", [NTILE + 1, 128, D], BF16)
    xnf_dram = dit("xnf_dram", [128, NCH, NTOK], BF16)
    wq_dram = dit("wq_dram", [4, 128, NCH * D], BF16)
    r_dram = dit("r_dram", [128, NCH, NTOK], BF16)
    s0_dram = dit("s0_dram", [NTILE, 128, D], F32)
    wcat_bf = dit("wcat_bf", [16, 128, 12288], BF16)
    NSLOT = 48 * 512
    slotbuf = dit("slotbuf", [NSLOT + 1, D], BF16)
    yobuf = dit("yobuf", [NSLOT + 1, D], BF16)
    x1_dram = dit("x1_dram", [NTILE, 128, D], F32)

    wqT_in = din("wqT_in", [128, NCH, D])
    wkT_in = din("wkT_in", [128, NCH, D])
    wvT_in = din("wvT_in", [128, NCH, D])
    pe_in = din("pe_in", [88, 128, 16])
    selc_in = din("selc_in", [8, 128, 16], BF16)
    rsel_in = din("rsel_in", [16, NCH, 128], BF16)
    xn1_dram = dit("xn1_dram", [NTILE, 128, D], BF16)
    xnf1_dram = dit("xnf1_dram", [128, NCH, NTOK], BF16)
    s1_dram = dit("s1_dram", [8, TC, D], BF16)
    rw1T_in = din("rw1T_in", [128, NCH, 4096])
    rw2T_in = din("rw2T_in", [32, 128, 8])
    rb1_in = din("rb1_in", [128, 32])
    rb2_in = din("rb2_in", [8, 1])
    trib8_in = din("trib8_in", [128, 64])
    mskc512_in = din("mskc512_in", [1, 512])
    corebase_in = din("corebase_in", [1, 1])
    ew1T_in = din("ew1T_in", [128, NCH, 4096])
    ew2T_in = din("ew2T_in", [128, 32, D])
    eb1_in = din("eb1_in", [128, 32])
    eb2_in = din("eb2_in", [128, D])
    xn1_ag = dit("xn1_ag", [NCORE, NTILE * 128, D], BF16)
    cnt2_in = dit("cnt2_in", [64, 1], F32)
    cnt2_out = dit("cnt2_out", [NCORE, 64, 1], F32)
    dispT = dit("dispT", [40960, 2], F32)
    disp_rs = dit("disp_rs", [5120, 2], F32)
    ew1_bf = dit("ew1_bf", [128, NCH, 4096], BF16)
    ew2_bf = dit("ew2_bf", [128, 32, D], BF16)
    h_dram = dit("h_dram", [8, 32, 128, TC], BF16)
    yo_dram = dit("yo_dram", [NTOK, D], BF16)
    yo_ag = dit("yo_ag", [NCORE, NTOK, D], BF16)
    out_dram = dou("out_dram", [NTILE, 128, D], F32)

    io_col = sb("io_col", [128, 128]); io_row = sb("io_row", [128, 128])
    ident = sb("ident", [128, 128], BF16)
    identf = sb("identf", [128, 128])
    onesNf = sb("onesNf", [1, 128])
    ones4 = sb("ones4", [4, 128])
    onesKb = sb("onesKb", [128, 1], BF16)
    zcol = sb("zcol", [128, 1])
    rmsv = sb("rmsv", [128, 4 * NCH])
    tmixv = sb("tmixv", [128, 3 * NCH])
    tmixm1 = sb("tmixm1", [128, 3 * NCH])
    xnb_sb = sb("xnb_sb", [128, NCH, NB], BF16)
    tdec = sb("tdec", [128, NCH]); tfir = sb("tfir", [128, NCH])
    expu = sb("expu", [128, NCH])
    wlog = sb("wlog", [128, NCH])
    Dmat = sb("Dmat", [128, NCH, TC])
    powm = sb("powm", [128, NCH, TC])
    kv_t = sb("kv_t", [128, 2, NCH, TC])
    wq_sb = sb("wq_sb", [128, 2, NCH, D], BF16)
    inv_sw = sb("inv_sw", [128, 4])
    sw_rep = sb("sw_rep", [128, 4])
    wstage = sb("wstage", [128, NCH, D], BF16)
    d_sh = wstage[:, :, 0:TC]
    m_t = wstage[:, :, TC:2*TC]
    wabs = sb("wabs", [128, 2])
    wabs_bf = sb("wabs_bf", [128, 1], BF16)
    wm_sb = sb("wm_sb", [1, 1])

    xn_st = sb("xn_st", [128, 2, D], BF16)
    xnf_blk = sb("xnf_blk", [128, 2, NCH, 128], BF16)
    xnf_seg = sb("xnf_seg", [128, NCH, TC], BF16)
    x_tile = sb("x_tile", [128, 2, D])
    sqf = sb("sqf", [128, 2, D])
    s1_all = sb("s1_all", [128, NTILE + 1]); s2_all = sb("s2_all", [128, NTILE + 1])
    rstd_all = sb("rstd_all", [128, NTILE + 1]); nmu_all = sb("nmu_all", [128, NTILE + 1])

    mix_t = sb("mix_t", [128, NCH, TC], BF16)
    camax = sb("camax", [128, TC]); csum = sb("csum", [128, TC])
    stat_tm = sb("stat_tm", [128, 8])
    rs_tm = sb("rs_tm", [128, 4]); isx_tm = sb("isx_tm", [128, 4])
    stag4 = sb("stag4", [128, 128])
    rhs512 = sb("rhs512", [128, 512])
    onesF = sb("onesF", [128, 128])
    rs_fm = sb("rs_fm", [128, TC]); isx_fm = sb("isx_fm", [128, TC])
    rnd_f = sb("rnd_f", [128, TC])
    csum2 = rnd_f
    xq_t = sb("xq_t", [128, NCH, TC], BF16)

    r_seg = sb("r_seg", [128, NCH, TC], BF16)
    e_t = sb("e_t", [128, TC]); ev_t = sb("ev_t", [128, TC])
    a_scan = sb("a_scan", [128, TC]); b_scan = sb("b_scan", [128, TC])
    scan_ar = sb("scan_ar", [128, 4, TC])
    eu_t = scan_ar[:, 0, :]; evu_t = scan_ar[:, 1, :]
    n1_t = scan_ar[:, 2, :]; dd_t = scan_ar[:, 3, :]
    gin_t = mix_t
    aend = sb("aend", [128, 2 * NCH * NB])
    carry = sb("carry", [128, 2 * NCH * NB])
    ccg = sb("ccg", [128, NCORE, 128])
    rk_sb = sb("rk_sb", [1, 2 * NCORE])
    rkrep = sb("rkrep", [128, 2 * NCORE])
    powr = sb("powr", [128, NCORE, NCH])
    s0_seg = kv_t[:, 0, :, :].rearrange("p c t -> p (c t)").rearrange("p (s d) -> p s d", s=4)

    scores_tm = scan_ar[:, 0, :].rearrange("p (i e) -> p i e", e=16)
    S_tm = scan_ar[:, 1, :].rearrange("p (i e) -> p i e", e=16)
    cw_tm = scan_ar[:, 2, :].rearrange("p (i e) -> p i e", e=16)
    rnk_sc = rs_fm[:].rearrange("p (i e) -> p i e", e=16)
    rank16 = isx_fm[:].rearrange("p (i e) -> p i e", e=16)
    xq_f32 = xq_t[:].rearrange("p c t -> p (c t)").bitcast(F32)
    ones128s = xq_f32[:, 0:1024]
    cmp_sc = xq_f32[:, 1024:1536].rearrange("p (i e) -> p i e", e=16)
    tri16b = xq_f32[:, 1536:1792].rearrange("p (i j) -> p i j", j=16)
    iota16 = xq_f32[:, 1792:1808]
    tri4b = xq_f32[:, 1808:1824].rearrange("p (i j) -> p i j", j=4)
    gsc4 = csum[:, 0:128].rearrange("p (i g) -> p i g", g=4)
    grk4 = csum[:, 128:256].rearrange("p (i g) -> p i g", g=4)
    powm_f32 = powm[:].rearrange("p c t -> p (c t)")
    powm_i32 = powm[:].rearrange("p c t -> p (c t)").bitcast(I32)
    powm_bf = powm[:].rearrange("p c t -> p (c t)").bitcast(BF16)
    tri16a = powm_f32[:, 3072:3328].rearrange("p (i j) -> p i j", j=16)
    tri4a = powm_f32[:, 3328:3344].rearrange("p (i j) -> p i j", j=4)
    ebias_rep = powm_f32[:, 3344:3360]
    rwT = powm_bf[:, 6784:6912].rearrange("p (c e) -> p c e", e=16)
    rw_sb = wstage[0:16, :, :].rearrange("p c d -> p (c d)").bitcast(F32)[:, 1024:2048]
    rw_bf = mix_t[0:16, :, :].rearrange("p c t -> p (c t)")[:, 2048:3072]
    rseg_f32 = r_seg[:].rearrange("p c t -> p (c t)").bitcast(F32)
    Me128 = rseg_f32[:, 0:1024]; pos128 = rseg_f32[:, 1024:2048]
    segsum = powm_f32[:, 3400:3401]; segoff = powm_f32[:, 3401:3402]
    strip1 = Dmat[0:1, :, :].rearrange("p c t -> p (c t)")[:, 0:656]
    sjf = powm_f32[:, 3424:3552].rearrange("p (i j) -> p i j", j=4)
    sji = powm_i32[:, 3728:3856].rearrange("p (i j) -> p i j", j=4)
    gj_tm = powm_f32[:, 3552:3680].rearrange("p (i j) -> p i j", j=4)
    texp_f = powm_f32[:, 3680:3728]
    widx = powm_i32[:, 3856:3904]
    wbuf = kv_t[:].rearrange("p a c t -> p (a c t)").bitcast(BF16)[:, 0:12288]
    xgF = powm[:].rearrange("p c t -> p (c t)").bitcast(BF16)[:, 0:4096].rearrange("p (c t) -> p c t", t=TC)
    h_rt = powm[:].rearrange("p c t -> p (c t)").bitcast(BF16)[:, 4096:6144].rearrange("p (c t) -> p c t", t=TC)
    h_sh = kv_t[:].rearrange("p a c t -> p (a c t)").bitcast(BF16)[:, 12288:16384].rearrange("p (c t) -> p c t", t=TC)
    xg_tm = wstage[:, 0:4, :]
    g0acc = Dmat[:].rearrange("p c t -> p (c t)")[:, 0:4096].rearrange("p (a d) -> p a d", d=D)
    scan_bf = scan_ar[:].rearrange("p a t -> p (a t)").bitcast(BF16)
    peT = scan_bf[:, 0:1408].rearrange("p (j h) -> p j h", h=16)
    selT = scan_bf[:, 1408:1536].rearrange("p (c h) -> p c h", h=16)
    rselT = e_t[0:16, :].bitcast(BF16)[:, 0:1024].rearrange("p (c d) -> p c d", d=128)
    psA = psm("psA", [128, 512]); psB = psm("psB", [128, 512])
    psT = psm("psT", [128, 128], BF16); psU = psm("psU", [128, 128], BF16)
    psTf = psm("psTf", [128, 128])
    psR = psm("psR", [128, 512])
    psS = psR[0:1, :]
    psW = psm("psW", [128, 1024])

    for e in ENGINES:
        C.sems[e] = Sem(st.enter_context(nc.semaphore("sem_" + e)))
    s_ld = Sem(st.enter_context(nc.semaphore("s_ld")))
    s_st = Sem(st.enter_context(nc.semaphore("s_st")))
    C.dma_sems = (s_ld, s_st)
    block = st.enter_context(nc.Block())

    def DMAL(out, in_, **kw):
        s_ld.n += 16
        C.q["sync"].append(lambda e, out=out, in_=in_, kw=kw: e.dma_start(out=out, in_=in_, **kw).then_inc(s_ld.h, 16))
        return s_ld.n

    def DMAS(out, in_, **kw):
        s_st.n += 16
        C.q["sync"].append(lambda e, out=out, in_=in_, kw=kw: e.dma_start(out=out, in_=in_, **kw).then_inc(s_st.h, 16))
        return s_st.n

    def WL(eng, val):
        C.q[eng].append(lambda e, v=val: e.wait_ge(s_ld.h, v))

    def WS(eng, val):
        C.q[eng].append(lambda e, v=val: e.wait_ge(s_st.h, v))

    twk = kv_t[:, 0, :, :]
    E("gpsimd", lambda e: e.iota(out=io_col[:], pattern=[[1, 128]], base=0, channel_multiplier=0, allow_small_or_imprecise_dtypes=True))
    E("gpsimd", lambda e: e.iota(out=io_row[:], pattern=[[0, 128]], base=0, channel_multiplier=1, allow_small_or_imprecise_dtypes=True))
    E("gpsimd", lambda e: e.iota(out=twk.rearrange("p c t -> p (c t)"), pattern=[[0, NCH], [1, TC]], base=0, channel_multiplier=0, allow_small_or_imprecise_dtypes=True))
    SIG("gpsimd")
    for vec, dst in (
        (rms_in[0], rmsv[:, 0:NCH]), (rms_in[1], rmsv[:, NCH:2*NCH]),
        (rms_in[2], rmsv[:, 2*NCH:3*NCH]), (rms_in[3], rmsv[:, 3*NCH:4*NCH]),
        (tmix_in[0], tmixv[:, 0:NCH]), (tmix_in[1], tmixv[:, NCH:2*NCH]), (tmix_in[2], tmixv[:, 2*NCH:3*NCH]),
        (tdecay, tdec[:]), (tfirst, tfir[:]),
    ):
        DMAL(dst, vec.rearrange("(c p) -> p c", p=128), allow_slow_non_contiguous=True)
    DMAL(rk_sb[0:1, 0:NCORE], rankm[None, :], allow_slow_non_contiguous=True)
    DMAL(rk_sb[0:1, NCORE:2*NCORE], rankw[None, :], allow_slow_non_contiguous=True)
    WL("vector", s_ld.n)
    WT("vector", "gpsimd")
    E("vector", lambda e: e.tensor_tensor(out=ident[:], in0=io_col[:], in1=io_row[:], op=ALU.is_equal))
    E("vector", lambda e: e.tensor_tensor(out=identf[:], in0=io_col[:], in1=io_row[:], op=ALU.is_equal))
    E("vector", lambda e: e.memset(onesNf[:], 1.0))
    E("vector", lambda e: e.memset(ones4[:], 1.0))
    E("vector", lambda e: e.memset(onesF[:], 1.0))
    E("vector", lambda e: e.memset(stag4[:], 0.0))
    E("vector", lambda e: e.memset(rhs512[:], 0.0))
    E("vector", lambda e: e.memset(onesKb[:], 1.0))
    E("vector", lambda e: e.memset(zcol[:], 0.0))
    E("vector", lambda e: e.tensor_scalar(out=tmixm1[:], in0=tmixv[:], scalar1=-1.0, scalar2=None, op0=ALU.add))
    SIG("vector")
    WL("scalar", s_ld.n)
    E("scalar", lambda e: e.activation(out=wlog[:], in_=tdec[:], func=AF.Exp))
    E("scalar", lambda e: e.activation(out=expu[:], in_=tfir[:], func=AF.Exp))
    SIG("scalar")
    WT("vector", "scalar")
    E("vector", lambda e: e.tensor_scalar(out=wlog[:], in0=wlog[:], scalar1=-1.0, scalar2=None, op0=ALU.mult))
    SIG("vector")
    for c in range(NCH):
        E("vector", lambda e, c=c: e.tensor_scalar(out=twk[:, c, :], in0=twk[:, c, :], scalar1=wlog[:, c:c+1], scalar2=None, op0=ALU.mult))
    SIG("vector")
    WT("scalar", "vector")
    E("scalar", lambda e: e.activation(out=powm[:].rearrange("p c t -> p (c t)"), in_=twk.rearrange("p c t -> p (c t)"), func=AF.Exp))
    for c in range(NCH):
        E("scalar", lambda e, c=c: e.activation(out=Dmat[:, c, :], in_=twk[:, c, 1:2].broadcast_to([128, TC]), func=AF.Exp))
    SIG("scalar")
    BAR()

    for wi in range(4):
        wsrc = w_mats[wi]
        for i in range(NCH):
            if i >= 2:
                WT("sync", "vector")
            tl = DMAL(x_tile[:, i % 2, :], wsrc[128*i:128*(i+1), :])
            WL("vector", tl)
            if i == 0:
                E("vector", lambda e: e.tensor_reduce(out=wabs[:, 0:1], in_=x_tile[:, 0, :], axis=AX.X, op=ALU.add, apply_absolute_value=True))
            else:
                E("vector", lambda e, i=i: e.tensor_reduce(out=wabs[:, 1:2], in_=x_tile[:, i % 2, :], axis=AX.X, op=ALU.add, apply_absolute_value=True))
                E("vector", lambda e: e.tensor_add(out=wabs[:, 0:1], in0=wabs[:, 0:1], in1=wabs[:, 1:2]))
            SIG("vector")
        E("vector", lambda e: e.tensor_copy(out=wabs_bf[:], in_=wabs[:, 0:1]))
        SIG("vector")
        BAR()
        E("tensor", lambda e: e.matmul(psS[0:1, 0:1], wabs_bf[:], onesKb[:], start=True, stop=True))
        SIG("tensor")
        WT("vector", "tensor")
        E("vector", lambda e: e.tensor_scalar(out=wm_sb[0:1, 0:1], in0=psS[0:1, 0:1], scalar1=1.0/(D*D), scalar2=1e-5, op0=ALU.mult, op1=ALU.max))
        SIG("vector")
        WT("tensor", "vector")
        E("tensor", lambda e: e.matmul(psR[:, 0:1], onesNf[:], wm_sb[0:1, 0:1], start=True, stop=True))
        SIG("tensor")
        WT("vector", "tensor")
        E("vector", lambda e, wi=wi: e.tensor_copy(out=inv_sw[:, wi:wi+1], in_=psR[:, 0:1]))
        E("vector", lambda e, wi=wi: e.reciprocal(out=sw_rep[:, wi:wi+1], in_=psR[:, 0:1]))
        SIG("vector")
        for i in range(NCH):
            WT("sync", "vector")
            tl = DMAL(x_tile[:, i % 2, :], wsrc[128*i:128*(i+1), :])
            WL("vector", tl)
            E("vector", lambda e, i=i, wi=wi: e.tensor_scalar(out=sqf[:, i % 2, :], in0=x_tile[:, i % 2, :], scalar1=sw_rep[:, wi:wi+1], scalar2=MAGIC, op0=ALU.mult, op1=ALU.add))
            E("vector", lambda e, i=i: e.tensor_scalar(out=sqf[:, i % 2, :], in0=sqf[:, i % 2, :], scalar1=MAGIC, scalar2=None, op0=ALU.subtract))
            E("vector", lambda e, i=i: e.tensor_scalar(out=wstage[:, i, :], in0=sqf[:, i % 2, :], scalar1=1.0, scalar2=-1.0, op0=ALU.min, op1=ALU.max))
            SIG("vector")
        BAR()
        slot = wi % 2
        for c in range(NCH):
            for dd2 in range(NCH):
                pp = psT if (c * NCH + dd2) % 2 == 0 else psU
                if c * NCH + dd2 >= 2:
                    WT("tensor", "scalar", C.sems["scalar"].n - 1)
                E("tensor", lambda e, c=c, dd2=dd2, pp=pp: e.transpose(pp[:], wstage[:, dd2, 128*c:128*(c+1)], ident[:]))
                tm = SIG("tensor")
                WT("scalar", "tensor", tm)
                E("scalar", lambda e, c=c, dd2=dd2, pp=pp, slot=slot: e.copy(out=wq_sb[:, slot, c, 128*dd2:128*(dd2+1)], in_=pp[:]))
                SIG("scalar")
        WT("sync", "scalar")
        DMAS(wq_dram[wi, :, :], wq_sb[:, slot, :, :].rearrange("p c d -> p (c d)"))
        BAR()

    def x_src_dst(i):
        if i < NTILE:
            b = i // 4; toff = (i % 4) * 128
            return x_in[b, 1+toff:1+toff+128, :], x_tile[:, i % 2, :]
        return x_in[:, 0, :], x_tile[0:NB, i % 2, :]

    for i in range(NTILE + 1):
        src, dst = x_src_dst(i)
        if i >= 2:
            WT("sync", "scalar"); WT("sync", "vector")
        tl = DMAL(dst, src)
        WL("scalar", tl)
        E("scalar", lambda e, i=i: e.activation(out=sqf[:, i % 2, :], in_=x_tile[:, i % 2, :], func=AF.Square, accum_out=s2_all[:, i:i+1]))
        SIG("scalar")
        WL("vector", tl)
        E("vector", lambda e, i=i: e.reduce_sum(out=s1_all[:, i:i+1], in_=x_tile[:, i % 2, :], axis=AX.X))
        SIG("vector")
    BAR()
    E("vector", lambda e: e.tensor_scalar(out=s1_all[:], in0=s1_all[:], scalar1=1.0/D, scalar2=None, op0=ALU.mult))
    E("vector", lambda e: e.tensor_scalar(out=s2_all[:], in0=s2_all[:], scalar1=1.0/D, scalar2=None, op0=ALU.mult))
    SIG("vector")
    WT("scalar", "vector")
    E("scalar", lambda e: e.activation(out=rstd_all[:], in_=s1_all[:], func=AF.Square))
    SIG("scalar")
    WT("vector", "scalar")
    E("vector", lambda e: e.tensor_sub(out=rstd_all[:], in0=s2_all[:], in1=rstd_all[:]))
    E("vector", lambda e: e.tensor_scalar(out=rstd_all[:], in0=rstd_all[:], scalar1=LN_EPS, scalar2=None, op0=ALU.add))
    SIG("vector")
    WT("scalar", "vector")
    E("scalar", lambda e: e.activation(out=rstd_all[:], in_=rstd_all[:], func=AF.Sqrt))
    SIG("scalar")
    WT("vector", "scalar")
    E("vector", lambda e: e.reciprocal(out=rstd_all[:], in_=rstd_all[:]))
    E("vector", lambda e: e.tensor_scalar(out=nmu_all[:], in0=s1_all[:], scalar1=-1.0, scalar2=None, op0=ALU.mult))
    E("vector", lambda e: e.tensor_tensor(out=nmu_all[:], in0=nmu_all[:], in1=rstd_all[:], op=ALU.mult))
    SIG("vector")
    WT("sync", "vector")
    BAR()
    xn_store_t = {}
    xnf_store_t = {}
    for i in range(NTILE + 1):
        src, dst = x_src_dst(i)
        if i >= 2:
            WT("sync", "scalar")
        tl = DMAL(dst, src)
        WL("scalar", tl)
        if i >= 2:
            WS("scalar", xn_store_t[i - 2])
        E("scalar", lambda e, i=i: e.activation(out=xn_st[:, i % 2, :], in_=x_tile[:, i % 2, :], func=AF.Identity, bias=nmu_all[:, i:i+1], scale=rstd_all[:, i:i+1]))
        tap = SIG("scalar")
        WT("sync", "scalar", tap)
        xn_store_t[i] = DMAS(xn_dram[i, :, :], xn_st[:, i % 2, :])
        WT("tensor", "scalar", tap)
        for c in range(NCH):
            pp = psT if c % 2 == 0 else psU
            if i > 0 or c >= 2:
                WT("tensor", "scalar", C.sems["scalar"].n - 1)
            E("tensor", lambda e, i=i, c=c, pp=pp: e.transpose(pp[:], xn_st[:, i % 2, 128*c:128*(c+1)], ident[:]))
            tm = SIG("tensor")
            WT("scalar", "tensor", tm)
            if i < NTILE:
                if c == 0 and i >= 2:
                    WS("scalar", xnf_store_t[i - 2])
                E("scalar", lambda e, i=i, c=c, pp=pp: e.copy(out=xnf_blk[:, i % 2, c, :], in_=pp[:]))
            else:
                E("scalar", lambda e, c=c, pp=pp: e.copy(out=xnb_sb[:, c, :], in_=pp[:, 0:NB]))
            SIG("scalar")
        WT("sync", "scalar")
        if i < NTILE:
            b = i // 4; toff = (i % 4) * 128
            xnf_store_t[i] = DMAS(xnf_dram[:, :, b*TC+toff : b*TC+toff+128], xnf_blk[:, i % 2, :, :])
    BAR()

    def quant_chain(rms_col, src_of_chunk):
        for c in range(NCH):
            E("vector", lambda e, c=c, src_of_chunk=src_of_chunk, rms_col=rms_col: e.tensor_scalar(
                out=m_t[:, c, :], in0=src_of_chunk(c), scalar1=rmsv[:, rms_col*NCH+c : rms_col*NCH+c+1], scalar2=None, op0=ALU.mult))
        SIG("vector")
        for c in range(NCH):
            E("scalar", lambda e, c=c, src_of_chunk=src_of_chunk: e.activation(out=sqf[:, c % 2, 0:TC], in_=src_of_chunk(c), func=AF.Square))
            tsq = SIG("scalar")
            WT("vector", "scalar", tsq)
            if c == 0:
                E("vector", lambda e: e.tensor_copy(out=csum[:], in_=sqf[:, 0, 0:TC]))
            else:
                E("vector", lambda e, c=c: e.tensor_add(out=csum[:], in0=csum[:], in1=sqf[:, c % 2, 0:TC]))
            tvv = SIG("vector")
            if c >= 1:
                WT("scalar", "vector", tvv - 1)
        E("vector", lambda e: e.tensor_copy(out=camax[:], in_=m_t[:, 0, :]))
        E("vector", lambda e: e.tensor_copy(out=csum2[:], in_=m_t[:, 0, :]))
        for c in range(1, NCH):
            E("vector", lambda e, c=c: e.tensor_tensor(out=camax[:], in0=camax[:], in1=m_t[:, c, :], op=ALU.max))
            E("vector", lambda e, c=c: e.tensor_tensor(out=csum2[:], in0=csum2[:], in1=m_t[:, c, :], op=ALU.min))
        E("vector", lambda e: e.tensor_scalar(out=csum2[:], in0=csum2[:], scalar1=-1.0, scalar2=None, op0=ALU.mult))
        E("vector", lambda e: e.tensor_tensor(out=camax[:], in0=camax[:], in1=csum2[:], op=ALU.max))
        SIG("vector")
        BAR()
        for blk in range(4):
            for wch, srcbuf in ((0, csum), (1, camax)):
                E("tensor", lambda e, blk=blk, srcbuf=srcbuf: e.transpose(psTf[:], srcbuf[:, 128*blk:128*(blk+1)], identf[:]))
                tm = SIG("tensor")
                WT("vector", "tensor", tm)
                if wch == 0:
                    E("vector", lambda e, blk=blk: e.reduce_sum(out=stat_tm[:, blk:blk+1], in_=psTf[:], axis=AX.X))
                else:
                    E("vector", lambda e, blk=blk: e.reduce_max(out=stat_tm[:, 4+blk:4+blk+1], in_=psTf[:], axis=AX.X))
                tv = SIG("vector")
                WT("tensor", "vector", tv)
        E("vector", lambda e: e.tensor_scalar(out=rs_tm[:], in0=stat_tm[:, 0:4], scalar1=1.0/D, scalar2=RMS_EPS, op0=ALU.mult, op1=ALU.add))
        SIG("vector")
        WT("scalar", "vector")
        E("scalar", lambda e: e.activation(out=rs_tm[:], in_=rs_tm[:], func=AF.Sqrt))
        SIG("scalar")
        WT("vector", "scalar")
        E("vector", lambda e: e.reciprocal(out=rs_tm[:], in_=rs_tm[:]))
        E("vector", lambda e: e.tensor_tensor(out=isx_tm[:], in0=stat_tm[:, 4:8], in1=rs_tm[:], op=ALU.mult))
        E("vector", lambda e: e.tensor_scalar(out=isx_tm[:], in0=isx_tm[:], scalar1=1e-5, scalar2=None, op0=ALU.max))
        E("vector", lambda e: e.reciprocal(out=stat_tm[:, 0:4], in_=isx_tm[:]))
        E("vector", lambda e: e.tensor_tensor(out=rs_tm[:], in0=rs_tm[:], in1=stat_tm[:, 0:4], op=ALU.mult))
        E("vector", lambda e: e.tensor_scalar(out=rs_tm[:], in0=rs_tm[:], scalar1=127.0, scalar2=None, op0=ALU.mult))
        E("vector", lambda e: e.tensor_scalar(out=isx_tm[:], in0=isx_tm[:], scalar1=1.0/127.0, scalar2=None, op0=ALU.mult))
        SIG("vector")
        for stm, fmbuf in ((rs_tm, rs_fm), (isx_tm, isx_fm)):
            BAR()
            for blk in range(4):
                E("vector", lambda e, stm=stm, blk=blk: e.tensor_copy(out=stag4[:, 32*blk:32*blk+1], in_=stm[:, blk:blk+1]))
            tv0 = SIG("vector")
            WT("tensor", "vector", tv0)
            E("tensor", lambda e: e.transpose(psTf[:], stag4[:], identf[:]))
            tm = SIG("tensor")
            WT("vector", "tensor", tm)
            for blk in range(4):
                E("vector", lambda e, blk=blk: e.tensor_copy(out=rhs512[32*blk:32*blk+32, 128*blk:128*(blk+1)], in_=psTf[32*blk:32*blk+32, 0:128]))
            tv = SIG("vector")
            WT("tensor", "vector", tv)
            E("tensor", lambda e: e.matmul(psR[:], onesF[:], rhs512[:], start=True, stop=True))
            tm2 = SIG("tensor")
            WT("vector", "tensor", tm2)
            E("vector", lambda e, fmbuf=fmbuf: e.tensor_copy(out=fmbuf[:], in_=psR[:]))
            SIG("vector")
        BAR()
        for c in range(NCH):
            E("vector", lambda e, c=c: e.tensor_tensor(out=rnd_f[:], in0=m_t[:, c, :], in1=rs_fm[:], op=ALU.mult))
            E("vector", lambda e, c=c: e.tensor_scalar(out=rnd_f[:], in0=rnd_f[:], scalar1=MAGIC, scalar2=MAGIC, op0=ALU.add, op1=ALU.subtract))
            E("vector", lambda e, c=c: e.tensor_copy(out=xq_t[:, c, :], in_=rnd_f[:]))
        SIG("vector")
        BAR()

    def project_fm(wslot, winv, dst_of_dc, sigmoid=False):
        for dc in range(NCH):
            pp = psA if dc % 2 == 0 else psB
            if dc >= 2:
                WT("tensor", "vector", C.sems["vector"].n)
            for c in range(NCH):
                E("tensor", lambda e, dc=dc, c=c, pp=pp, wslot=wslot: e.matmul(pp[:], wq_sb[:, wslot, c, 128*dc:128*(dc+1)], xq_t[:, c, :], start=(c == 0), stop=(c == NCH-1)))
            tm = SIG("tensor")
            WT("vector", "tensor", tm)
            if not sigmoid:
                E("vector", lambda e, dc=dc, pp=pp, winv=winv, dst_of_dc=dst_of_dc: e.scalar_tensor_tensor(out=dst_of_dc(dc), in0=pp[:], scalar=inv_sw[:, winv:winv+1], in1=isx_fm[:], op0=ALU.mult, op1=ALU.mult))
                SIG("vector")
            else:
                E("vector", lambda e, dc=dc, pp=pp, winv=winv: e.scalar_tensor_tensor(out=sqf[:, dc % 2, 0:TC], in0=pp[:], scalar=inv_sw[:, winv:winv+1], in1=isx_fm[:], op0=ALU.mult, op1=ALU.mult))
                tv = SIG("vector")
                WT("scalar", "vector", tv)
                E("scalar", lambda e, dc=dc, dst_of_dc=dst_of_dc: e.activation(out=dst_of_dc(dc), in_=sqf[:, dc % 2, 0:TC], func=AF.Sigmoid))
                SIG("scalar")
        BAR()

    def load_seg_xnf(b):
        return DMAL(xnf_seg[:], xnf_dram[:, :, b*TC:(b+1)*TC])

    def emit_mix(mi):
        for c in range(NCH):
            E("vector", lambda e, c=c, mi=mi: e.scalar_tensor_tensor(
                out=mix_t[:, c, :], in0=d_sh[:, c, :], scalar=tmixm1[:, mi*NCH+c:mi*NCH+c+1],
                in1=xnf_seg[:, c, 0:TC], op0=ALU.mult, op1=ALU.add))
        SIG("vector")

    def emit_dsh(b):
        for c in range(NCH):
            E("vector", lambda e, c=c: e.tensor_sub(out=d_sh[:, c, 1:TC], in0=xnf_seg[:, c, 1:TC], in1=xnf_seg[:, c, 0:TC-1]))
            E("vector", lambda e, c=c, b=b: e.tensor_sub(out=d_sh[:, c, 0:1], in0=xnf_seg[:, c, 0:1], in1=xnb_sb[:, c, b:b+1]))
        SIG("vector")

    DMAL(wq_sb[:, 0, :, :].rearrange("p c d -> p (c d)"), wq_dram[0, :, :])
    DMAL(wq_sb[:, 1, :, :].rearrange("p c d -> p (c d)"), wq_dram[1, :, :])
    WL("tensor", s_ld.n)
    BAR()

    kv_store_t = 0
    for b in range(NB):
        tl = load_seg_xnf(b)
        WL("vector", tl)
        if b > 0:
            WS("vector", kv_store_t)
        emit_dsh(b)
        BAR()
        for mi in range(2):
            emit_mix(mi)
            BAR()
            quant_chain(mi, lambda c: mix_t[:, c, :])
            project_fm(mi, mi, lambda dc, mi=mi: kv_t[:, mi, dc, :])
            BAR()
        WT("sync", "vector")
        for kvi in (0, 1):
            for c in range(NCH):
                kv_store_t = DMAS(kv_spill[kvi, c, :, b*TC:(b+1)*TC], kv_t[:, kvi, c, :])
        for c in range(NCH):
            E("scalar", lambda e, c=c: e.activation(out=e_t[:], in_=kv_t[:, 0, c, :], func=AF.Exp))
            tse = SIG("scalar")
            WT("vector", "scalar", tse)
            E("vector", lambda e, c=c: e.tensor_tensor(out=ev_t[:], in0=e_t[:], in1=kv_t[:, 1, c, :], op=ALU.mult))
            E("vector", lambda e, c=c: e.tensor_tensor_scan(out=a_scan[:], data0=Dmat[:, c, :], data1=ev_t[:], initial=zcol[:, 0:1], op0=ALU.mult, op1=ALU.add))
            E("vector", lambda e, c=c, b=b: e.tensor_copy(out=aend[:, b*NCH+c : b*NCH+c+1], in_=a_scan[:, TC-1:TC]))
            E("vector", lambda e, c=c: e.tensor_tensor_scan(out=b_scan[:], data0=Dmat[:, c, :], data1=e_t[:], initial=zcol[:, 0:1], op0=ALU.mult, op1=ALU.add))
            E("vector", lambda e, c=c, b=b: e.tensor_copy(out=aend[:, NCH*NB + b*NCH+c : NCH*NB + b*NCH+c+1], in_=b_scan[:, TC-1:TC]))
            tvs = SIG("vector")
            WT("scalar", "vector", tvs)
        BAR()

    DMAL(wq_sb[:, 0, :, :].rearrange("p c d -> p (c d)"), wq_dram[2, :, :])
    WL("tensor", s_ld.n)
    BAR()
    r_store_t = 0
    for b in range(NB):
        tl = load_seg_xnf(b)
        WL("vector", tl)
        emit_dsh(b)
        if b > 0:
            WS("scalar", r_store_t)
        emit_mix(2)
        quant_chain(2, lambda c: mix_t[:, c, :])
        project_fm(0, 2, lambda dc: r_seg[:, dc, :], sigmoid=True)
        WT("sync", "scalar")
        for c in range(NCH):
            r_store_t = DMAS(r_dram[:, c, b*TC:(b+1)*TC], r_seg[:, c, :])
    BAR()

    WT("sync", "vector")
    tst = DMAS(cc_in[:, :], aend[:])
    WS("gpsimd", tst)
    E("gpsimd", lambda e: e.collective_compute("AllGather", ALU.bypass, replica_groups=[list(range(NCORE))],
                                               ins=[cc_in[:, :]], outs=[cc_out[:, :, :]]))
    SIG("gpsimd")
    WT("sync", "gpsimd")
    DMAL(ccg[:], cc_out.rearrange("r q i -> q r i"))
    BAR()
    E("tensor", lambda e: e.matmul(psR[:, 0:2*NCORE], onesNf[:], rk_sb[0:1, :], start=True, stop=True))
    SIG("tensor")
    WT("vector", "tensor")
    E("vector", lambda e: e.tensor_copy(out=rkrep[:], in_=psR[:, 0:2*NCORE]))
    SIG("vector")
    for r in range(NCORE):
        E("vector", lambda e, r=r: e.tensor_scalar(out=powr[:, r, :], in0=wlog[:], scalar1=rkrep[:, NCORE+r:NCORE+r+1], scalar2=None, op0=ALU.mult))
    E("vector", lambda e: e.tensor_scalar(out=powr[:].rearrange("p r c -> p (r c)"), in0=powr[:].rearrange("p r c -> p (r c)"), scalar1=float(TC), scalar2=None, op0=ALU.mult))
    SIG("vector")
    WT("scalar", "vector")
    E("scalar", lambda e: e.activation(out=powr[:].rearrange("p r c -> p (r c)"), in_=powr[:].rearrange("p r c -> p (r c)"), func=AF.Exp))
    SIG("scalar")
    WT("vector", "scalar")
    for r in range(NCORE):
        E("vector", lambda e, r=r: e.tensor_scalar(out=powr[:, r, :], in0=powr[:, r, :], scalar1=rkrep[:, r:r+1], scalar2=None, op0=ALU.mult))
    E("vector", lambda e: e.memset(carry[:], 0.0))
    SIG("vector")
    for r in range(NCORE):
        for kv in range(2):
            for b in range(NB):
                idx0 = kv * NB * NCH + b * NCH
                E("vector", lambda e, r=r, idx0=idx0: e.tensor_tensor(out=sqf[:, 0, 0:NCH], in0=ccg[:, r, idx0:idx0+NCH], in1=powr[:, r, :], op=ALU.mult))
                E("vector", lambda e, idx0=idx0: e.tensor_add(out=carry[:, idx0:idx0+NCH], in0=carry[:, idx0:idx0+NCH], in1=sqf[:, 0, 0:NCH]))
    SIG("vector")
    BAR()

    DMAL(wq_sb[:, 1, :, :].rearrange("p c d -> p (c d)"), wq_dram[3, :, :])
    WL("tensor", s_ld.n)
    BAR()
    for b in range(NB):
        for kvi in range(2):
            for c in range(NCH):
                DMAL(kv_t[:, kvi, c, :], kv_spill[kvi, c, :, b*TC:(b+1)*TC])
        for c in range(NCH):
            DMAL(r_seg[:, c, :], r_dram[:, c, b*TC:(b+1)*TC])
        WL("scalar", s_ld.n); WL("vector", s_ld.n)
        for c in range(NCH):
            E("scalar", lambda e, c=c: e.activation(out=e_t[:], in_=kv_t[:, 0, c, :], func=AF.Exp))
            tse = SIG("scalar")
            WT("vector", "scalar", tse)
            E("vector", lambda e, c=c: e.tensor_tensor(out=ev_t[:], in0=e_t[:], in1=kv_t[:, 1, c, :], op=ALU.mult))
            E("vector", lambda e, c=c: e.tensor_tensor_scan(out=a_scan[:], data0=Dmat[:, c, :], data1=ev_t[:], initial=zcol[:, 0:1], op0=ALU.mult, op1=ALU.add))
            E("vector", lambda e, c=c: e.tensor_tensor_scan(out=b_scan[:], data0=Dmat[:, c, :], data1=e_t[:], initial=zcol[:, 0:1], op0=ALU.mult, op1=ALU.add))
            E("vector", lambda e, c=c: e.tensor_scalar(out=eu_t[:], in0=e_t[:], scalar1=expu[:, c:c+1], scalar2=None, op0=ALU.mult))
            E("vector", lambda e, c=c: e.tensor_scalar(out=evu_t[:], in0=ev_t[:], scalar1=expu[:, c:c+1], scalar2=None, op0=ALU.mult))
            E("vector", lambda e, c=c, b=b: e.scalar_tensor_tensor(out=n1_t[:, 1:TC], in0=powm[:, c, 1:TC], scalar=carry[:, b*NCH+c : b*NCH+c+1], in1=a_scan[:, 0:TC-1], op0=ALU.mult, op1=ALU.add))
            E("vector", lambda e, c=c, b=b: e.tensor_scalar(out=n1_t[:, 0:1], in0=powm[:, c, 0:1], scalar1=carry[:, b*NCH+c : b*NCH+c+1], scalar2=None, op0=ALU.mult))
            E("vector", lambda e, c=c, b=b: e.scalar_tensor_tensor(out=dd_t[:, 1:TC], in0=powm[:, c, 1:TC], scalar=carry[:, NCH*NB + b*NCH+c : NCH*NB + b*NCH+c+1], in1=b_scan[:, 0:TC-1], op0=ALU.mult, op1=ALU.add))
            E("vector", lambda e, c=c, b=b: e.tensor_scalar(out=dd_t[:, 0:1], in0=powm[:, c, 0:1], scalar1=carry[:, NCH*NB + b*NCH+c : NCH*NB + b*NCH+c+1], scalar2=None, op0=ALU.mult))
            E("vector", lambda e, c=c: e.tensor_add(out=n1_t[:], in0=n1_t[:], in1=evu_t[:]))
            E("vector", lambda e, c=c: e.tensor_add(out=dd_t[:], in0=dd_t[:], in1=eu_t[:]))
            E("vector", lambda e, c=c: e.reciprocal(out=dd_t[:], in_=dd_t[:]))
            E("vector", lambda e, c=c: e.tensor_tensor(out=n1_t[:], in0=n1_t[:], in1=dd_t[:], op=ALU.mult))
            E("vector", lambda e, c=c: e.tensor_tensor(out=gin_t[:, c, :], in0=n1_t[:], in1=r_seg[:, c, :], op=ALU.mult))
            tvs = SIG("vector")
            WT("scalar", "vector", tvs)
        quant_chain(3, lambda c: gin_t[:, c, :])
        for s in range(4):
            for half in range(2):
                WT("tensor", "vector", C.sems["vector"].n)
                for c in range(NCH):
                    E("tensor", lambda e, s=s, half=half, c=c: e.matmul(
                        psW[:, half*512:(half+1)*512], xq_t[:, c, 128*s:128*(s+1)],
                        wq_sb[:, 1, c, half*512:(half+1)*512], start=(c == 0), stop=(c == NCH-1)))
                tm = SIG("tensor")
                WT("vector", "tensor", tm)
                E("vector", lambda e, s=s, half=half: e.tensor_scalar(
                    out=s0_seg[:, s, half*512:(half+1)*512], in0=psW[:, half*512:(half+1)*512],
                    scalar1=isx_tm[:, s:s+1], scalar2=inv_sw[:, 3:4], op0=ALU.mult, op1=ALU.mult))
                SIG("vector")
        WT("sync", "vector")
        for s in range(4):
            DMAS(s0_dram[b*4+s, :, :], s0_seg[:, s, :])
        BAR()
    BAR()
    s_gd = Sem(st.enter_context(nc.semaphore("s_gd")))
    C.dma_sems = (s_ld, s_st, s_gd)

    def GDMA(out, in_, out_off=None, in_off=None, **kw):
        s_gd.n += 16
        C.q["gpsimd"].append(lambda e, out=out, in_=in_, oo=out_off, io=in_off, kw=kw:
            e.indirect_dma_start(out=out, out_offset=oo, in_=in_, in_offset=io, **kw).then_inc(s_gd.h, 16))
        return s_gd.n

    def WG(eng, val):
        C.q[eng].append(lambda e, v=val: e.wait_ge(s_gd.h, v))

    E("gpsimd", lambda e: e.iota(out=iota16[:], pattern=[[1, 16]], base=0, channel_multiplier=0, allow_small_or_imprecise_dtypes=True))
    E("gpsimd", lambda e: e.iota(out=tri16a[:].rearrange("p i j -> p (i j)"), pattern=[[0, 16], [1, 16]], base=0, channel_multiplier=0, allow_small_or_imprecise_dtypes=True))
    E("gpsimd", lambda e: e.iota(out=tri16b[:].rearrange("p i j -> p (i j)"), pattern=[[1, 16], [0, 16]], base=0, channel_multiplier=0, allow_small_or_imprecise_dtypes=True))
    E("gpsimd", lambda e: e.iota(out=tri4a[:].rearrange("p i j -> p (i j)"), pattern=[[0, 4], [1, 4]], base=0, channel_multiplier=0, allow_small_or_imprecise_dtypes=True))
    E("gpsimd", lambda e: e.iota(out=tri4b[:].rearrange("p i j -> p (i j)"), pattern=[[1, 4], [0, 4]], base=0, channel_multiplier=0, allow_small_or_imprecise_dtypes=True))
    BAR()
    E("vector", lambda e: e.tensor_tensor(out=tri16a[:].rearrange("p i j -> p (i j)"), in0=tri16a[:].rearrange("p i j -> p (i j)"), in1=tri16b[:].rearrange("p i j -> p (i j)"), op=ALU.is_lt))
    E("vector", lambda e: e.tensor_tensor(out=tri4a[:].rearrange("p i j -> p (i j)"), in0=tri4a[:].rearrange("p i j -> p (i j)"), in1=tri4b[:].rearrange("p i j -> p (i j)"), op=ALU.is_lt))
    E("vector", lambda e: e.memset(ones128s[:], 1.0))
    tl = DMAL(rw_sb[:], rw_in[:, :])
    WL("vector", tl)
    E("vector", lambda e: e.tensor_copy(out=rw_bf[:], in_=rw_sb[:]))
    BAR()
    for c in range(NCH):
        pp = psT if c % 2 == 0 else psU
        E("tensor", lambda e, c=c, pp=pp: e.transpose(pp[:], rw_bf[0:16, 128*c:128*(c+1)], ident[0:16, :]))
        tm = SIG("tensor")
        WT("scalar", "tensor", tm)
        E("scalar", lambda e, c=c, pp=pp: e.copy(out=rwT[:, c, :], in_=pp[:, 0:16]))
        tsc = C.sems["scalar"].n
        WT("tensor", "scalar", tsc)
    BAR()
    tl = DMAL(strip1[0:1, 0:16], ebias_in[None, :], allow_slow_non_contiguous=True)
    WL("tensor", tl)
    E("tensor", lambda e: e.matmul(psR[:, 0:16], onesNf[:], strip1[0:1, 0:16], start=True, stop=True))
    tm = SIG("tensor")
    WT("vector", "tensor", tm)
    E("vector", lambda e: e.tensor_copy(out=ebias_rep[:], in_=psR[:, 0:16]))
    BAR()
    cast_f32 = sqf[:].rearrange("p a d -> p (a d)")
    cast_bf = mix_t[:].rearrange("p c t -> p (c t)")[:, 0:2048]
    for e_i in range(16):
        for q in range(6):
            tl = DMAL(cast_f32[:, :], wcat_in[e_i, :, 2048*q:2048*(q+1)])
            WL("vector", tl)
            E("vector", lambda e: e.tensor_copy(out=cast_bf[:, :], in_=cast_f32[:, :]))
            tv = SIG("vector")
            WT("sync", "vector", tv)
            tstw = DMAS(wcat_bf[e_i, :, 2048*q:2048*(q+1)], cast_bf[:, :])
            WS("vector", tstw)
            WS("sync", tstw)
    for mat_src, dstslot in ((shg_in, 0), (shu_in, 1)):
        for q in range(4):
            tl = DMAL(cast_f32[:, :], mat_src.rearrange("p c d -> p (c d)")[:, 2048*q:2048*(q+1)])
            WL("vector", tl)
            E("vector", lambda e, dstslot=dstslot, q=q: e.tensor_copy(out=wq_sb[:, dstslot, :, :].rearrange("p c d -> p (c d)")[:, 2048*q:2048*(q+1)], in_=cast_f32[:, :]))
            tv = SIG("vector")
            WT("sync", "vector", tv)
    BAR()
    for g in range(8):
        tl = DMAL(xnf_seg[:], xnf_dram[:, :, g*TC:(g+1)*TC])
        WL("tensor", tl)
        BAR()
        for c in range(NCH):
            E("tensor", lambda e, c=c: e.matmul(psA[0:16, :], rwT[:, c, :], xnf_seg[:, c, :], start=(c == 0), stop=(c == NCH-1)))
        tm = SIG("tensor")
        WT("vector", "tensor", tm)
        E("vector", lambda e: e.tensor_copy(out=rw_sb[0:16, 0:512], in_=psA[0:16, :]))
        tv = SIG("vector")
        WT("tensor", "vector", tv)
        for qi in range(4):
            E("tensor", lambda e, qi=qi: e.transpose(psTf[:], rw_sb[0:16, 128*qi:128*(qi+1)], identf[0:16, :]))
            tm = SIG("tensor")
            WT("scalar", "tensor", tm)
            E("scalar", lambda e, g=g, qi=qi: e.activation(out=scores_tm[:, 4*g+qi, :], in_=psTf[:, 0:16], func=AF.Sigmoid))
            tsc = C.sems["scalar"].n
            WT("tensor", "scalar", tsc)
        BAR()
    NT = NTILE
    cmp1 = cmp_sc[:].rearrange("p i e -> p (i e)")
    cmp2 = wstage[:].rearrange("p c d -> p (c d)").bitcast(F32)[:, 0:512]
    cmp2b = wstage[:].rearrange("p c d -> p (c d)").bitcast(F32)[:, 512:1024]
    rank4 = rank16[:].rearrange("p i e -> p (i e)")
    E("vector", lambda e: e.tensor_tensor(out=rnk_sc[:, 0:NT, :], in0=scores_tm[:, 0:NT, :], in1=ebias_rep[:].unsqueeze(1).broadcast_to([128, NT, 16]), op=ALU.add))
    BAR()
    E("vector", lambda e: e.memset(rank4[:, :], 0.0))
    for j4 in range(4):
        xj = rnk_sc[:, 0:NT, :].rearrange("p i (g m) -> p i g m", m=4)[:, :, :, j4:j4+1].broadcast_to([128, NT, 4, 4])
        xi = rnk_sc[:, 0:NT, :].rearrange("p i (g m) -> p i g m", m=4)
        E("vector", lambda e, xj=xj, xi=xi: e.tensor_tensor(out=cmp1[:, 0:512].rearrange("p (i g m) -> p i g m", g=4, m=4), in0=xj, in1=xi, op=ALU.is_gt))
        E("vector", lambda e: e.tensor_add(out=rank4[:, :], in0=rank4[:, :], in1=cmp1[:, 0:512]))
        E("vector", lambda e, xj=xj, xi=xi: e.tensor_tensor(out=cmp1[:, 0:512].rearrange("p (i g m) -> p i g m", g=4, m=4), in0=xj, in1=xi, op=ALU.is_equal))
        ltj = tri4a[:, :, j4]
        E("vector", lambda e, ltj=ltj: e.tensor_tensor(out=cmp2[:, 0:512].rearrange("p (i g m) -> p i g m", g=4, m=4), in0=cmp1[:, 0:512].rearrange("p (i g m) -> p i g m", g=4, m=4), in1=ltj.unsqueeze(1).unsqueeze(2).broadcast_to([128, NT, 4, 4]), op=ALU.mult))
        E("vector", lambda e: e.tensor_add(out=rank4[:, :], in0=rank4[:, :], in1=cmp2[:, 0:512]))
    BAR()
    E("vector", lambda e: e.tensor_scalar(out=cmp1[:, 0:512], in0=rank4[:, :], scalar1=2.0, scalar2=1.0, op0=ALU.is_lt, op1=ALU.mult))
    E("vector", lambda e: e.tensor_tensor(out=cmp1[:, 0:512], in0=cmp1[:, 0:512], in1=rnk_sc[:, 0:NT, :].rearrange("p i e -> p (i e)"), op=ALU.mult))
    E("vector", lambda e: e.tensor_reduce(out=gsc4[:, 0:NT, :], in_=cmp1[:, 0:512].rearrange("p (i g m) -> p i g m", g=4, m=4), axis=AX.X, op=ALU.add))
    BAR()
    E("vector", lambda e: e.memset(grk4[:].rearrange("p i g -> p (i g)"), 0.0))
    for j4 in range(4):
        xj = gsc4[:, 0:NT, j4:j4+1].broadcast_to([128, NT, 4])
        E("vector", lambda e, xj=xj: e.tensor_tensor(out=cmp1[:, 0:128].rearrange("p (i g) -> p i g", g=4), in0=xj, in1=gsc4[:, 0:NT, :], op=ALU.is_gt))
        E("vector", lambda e: e.tensor_add(out=grk4[:].rearrange("p i g -> p (i g)"), in0=grk4[:].rearrange("p i g -> p (i g)"), in1=cmp1[:, 0:128]))
        E("vector", lambda e, xj=xj: e.tensor_tensor(out=cmp1[:, 0:128].rearrange("p (i g) -> p i g", g=4), in0=xj, in1=gsc4[:, 0:NT, :], op=ALU.is_equal))
        ltj = tri4a[:, :, j4]
        E("vector", lambda e, ltj=ltj: e.tensor_tensor(out=cmp2[:, 0:128].rearrange("p (i g) -> p i g", g=4), in0=cmp1[:, 0:128].rearrange("p (i g) -> p i g", g=4), in1=ltj.unsqueeze(1).broadcast_to([128, NT, 4]), op=ALU.mult))
        E("vector", lambda e: e.tensor_add(out=grk4[:].rearrange("p i g -> p (i g)"), in0=grk4[:].rearrange("p i g -> p (i g)"), in1=cmp2[:, 0:128]))
    E("vector", lambda e: e.tensor_scalar(out=grk4[:].rearrange("p i g -> p (i g)"), in0=grk4[:].rearrange("p i g -> p (i g)"), scalar1=2.0, scalar2=1.0, op0=ALU.is_lt, op1=ALU.mult))
    BAR()
    E("vector", lambda e: e.tensor_tensor(out=rnk_sc[:, 0:NT, :].rearrange("p i (g m) -> p i g m", m=4), in0=rnk_sc[:, 0:NT, :].rearrange("p i (g m) -> p i g m", m=4), in1=grk4[:, 0:NT, :].unsqueeze(3).broadcast_to([128, NT, 4, 4]), op=ALU.mult))
    BAR()
    E("vector", lambda e: e.memset(rank16[:].rearrange("p i e -> p (i e)"), 0.0))
    for j16 in range(16):
        xj = rnk_sc[:, 0:NT, j16:j16+1].broadcast_to([128, NT, 16])
        E("vector", lambda e, xj=xj: e.tensor_tensor(out=cmp1[:, 0:512].rearrange("p (i e) -> p i e", e=16), in0=xj, in1=rnk_sc[:, 0:NT, :], op=ALU.is_gt))
        E("vector", lambda e: e.tensor_add(out=rank16[:].rearrange("p i e -> p (i e)"), in0=rank16[:].rearrange("p i e -> p (i e)"), in1=cmp1[:, 0:512]))
        E("vector", lambda e, xj=xj: e.tensor_tensor(out=cmp1[:, 0:512].rearrange("p (i e) -> p i e", e=16), in0=xj, in1=rnk_sc[:, 0:NT, :], op=ALU.is_equal))
        ltj = tri16a[:, :, j16]
        E("vector", lambda e, ltj=ltj: e.tensor_tensor(out=cmp2[:, 0:512].rearrange("p (i e) -> p i e", e=16), in0=cmp1[:, 0:512].rearrange("p (i e) -> p i e", e=16), in1=ltj.unsqueeze(1).broadcast_to([128, NT, 16]), op=ALU.mult))
        E("vector", lambda e: e.tensor_add(out=rank16[:].rearrange("p i e -> p (i e)"), in0=rank16[:].rearrange("p i e -> p (i e)"), in1=cmp2[:, 0:512]))
    BAR()
    E("vector", lambda e: e.tensor_scalar(out=cmp1[:, 0:512], in0=rank16[:].rearrange("p i e -> p (i e)"), scalar1=4.0, scalar2=1.0, op0=ALU.is_lt, op1=ALU.mult))
    E("vector", lambda e: e.tensor_tensor(out=cmp2[:, 0:512], in0=cmp1[:, 0:512], in1=scores_tm[:, 0:NT, :].rearrange("p i e -> p (i e)"), op=ALU.mult))
    E("vector", lambda e: e.tensor_reduce(out=cmp2b[:, 0:32].rearrange("p (i q) -> p i q", q=1), in_=cmp2[:, 0:512].rearrange("p (i e) -> p i e", e=16), axis=AX.X, op=ALU.add))
    E("vector", lambda e: e.tensor_scalar(out=cmp2b[:, 0:32], in0=cmp2b[:, 0:32], scalar1=1e-20, scalar2=0.4, op0=ALU.add, op1=ALU.mult))
    E("vector", lambda e: e.reciprocal(out=cmp2b[:, 32:64], in_=cmp2b[:, 0:32]))
    E("vector", lambda e: e.tensor_tensor(out=cw_tm[:, 0:NT, :], in0=cmp2[:, 0:512].rearrange("p (i e) -> p i e", e=16), in1=cmp2b[:, 32:64].unsqueeze(2).broadcast_to([128, NT, 16]), op=ALU.mult))
    BAR()
    WT("sync", "vector")
    selm = cmp1[:, 0:512].rearrange("p (i e) -> p i e", e=16)
    for i in range(NT):
        si = i // 8
        E("tensor", lambda e, i=i: e.transpose(psTf[0:16, :], selm[:, i, :], identf[:]))
        tm = SIG("tensor")
        WT("vector", "tensor", tm)
        E("vector", lambda e, i=i, si=si: e.tensor_copy(out=Me128[32*si:32*si+16, 128*(i % 8):128*(i % 8)+128], in_=psTf[0:16, 0:128]))
        tv = SIG("vector")
        WT("tensor", "vector", tv)
    BAR()
    E("vector", lambda e: e.tensor_tensor_scan(out=pos128[:], data0=ones128s[:], data1=Me128[:], initial=zcol[:, 0:1], op0=ALU.mult, op1=ALU.add))
    E("vector", lambda e: e.tensor_copy(out=segsum[:], in_=pos128[:, 1023:1024]))
    E("vector", lambda e: e.tensor_sub(out=pos128[:], in0=pos128[:], in1=Me128[:]))
    BAR()
    E("tensor", lambda e: e.matmul(psR[0:1, 0:128], segsum[:], identf[:], start=True, stop=True))
    tm = SIG("tensor")
    WT("vector", "tensor", tm)
    E("vector", lambda e: e.tensor_copy(out=strip1[0:1, 0:128], in_=psR[0:1, 0:128]))
    E("vector", lambda e: e.tensor_tensor(out=strip1[0:1, 128:160], in0=strip1[0:1, 0:32], in1=strip1[0:1, 32:64], op=ALU.add))
    E("vector", lambda e: e.tensor_tensor(out=strip1[0:1, 160:192], in0=strip1[0:1, 64:96], in1=strip1[0:1, 96:128], op=ALU.add))
    E("vector", lambda e: e.tensor_tensor(out=strip1[0:1, 128:160], in0=strip1[0:1, 128:160], in1=strip1[0:1, 160:192], op=ALU.add))
    E("vector", lambda e: e.memset(strip1[0:1, 192:224], 0.0))
    E("vector", lambda e: e.tensor_copy(out=strip1[0:1, 224:256], in_=strip1[0:1, 0:32]))
    E("vector", lambda e: e.tensor_tensor(out=strip1[0:1, 256:288], in0=strip1[0:1, 224:256], in1=strip1[0:1, 32:64], op=ALU.add))
    E("vector", lambda e: e.tensor_tensor(out=strip1[0:1, 288:320], in0=strip1[0:1, 256:288], in1=strip1[0:1, 64:96], op=ALU.add))
    E("vector", lambda e: e.tensor_scalar(out=strip1[0:1, 320:336], in0=strip1[0:1, 128:144], scalar1=1.0/512.0, scalar2=(511.0/512.0 - 0.5 + 2.0**-12), op0=ALU.mult, op1=ALU.add))
    E("vector", lambda e: e.tensor_scalar(out=strip1[0:1, 320:336], in0=strip1[0:1, 320:336], scalar1=MAGIC, scalar2=MAGIC, op0=ALU.add, op1=ALU.subtract))
    E("vector", lambda e: e.tensor_copy(out=strip1[0:1, 336:352], in_=strip1[0:1, 320:336]))
    for sh in (1, 2, 4, 8):
        E("vector", lambda e, sh=sh: e.tensor_tensor(out=strip1[0:1, 352+sh:368], in0=strip1[0:1, 336+sh:352], in1=strip1[0:1, 336:352-sh], op=ALU.add))
        E("vector", lambda e, sh=sh: e.tensor_copy(out=strip1[0:1, 352:352+sh], in_=strip1[0:1, 336:336+sh]))
        E("vector", lambda e: e.tensor_copy(out=strip1[0:1, 336:352], in_=strip1[0:1, 352:368]))
    E("vector", lambda e: e.tensor_sub(out=strip1[0:1, 368:384], in0=strip1[0:1, 336:352], in1=strip1[0:1, 320:336]))
    E("vector", lambda e: e.tensor_scalar(out=strip1[0:1, 368:384], in0=strip1[0:1, 368:384], scalar1=512.0, scalar2=None, op0=ALU.mult))
    BAR()
    E("gpsimd", lambda e: e.iota(out=strip1[0:1, 400:448], pattern=[[512, 48]], base=0, channel_multiplier=0, allow_small_or_imprecise_dtypes=True))
    BAR()
    E("vector", lambda e: e.tensor_tensor(out=rw_sb[0:1, 0:768].rearrange("p (j e) -> p j e", e=16), in0=strip1[0:1, 368:384].unsqueeze(1).broadcast_to([1, 48, 16]), in1=strip1[0:1, 400:448].unsqueeze(2).broadcast_to([1, 48, 16]), op=ALU.is_le))
    E("vector", lambda e: e.tensor_reduce(out=strip1[0:1, 448:496].rearrange("p (j q) -> p j q", q=1), in_=rw_sb[0:1, 0:768].rearrange("p (j e) -> p j e", e=16), axis=AX.X, op=ALU.add))
    E("vector", lambda e: e.tensor_scalar(out=strip1[0:1, 448:496], in0=strip1[0:1, 448:496], scalar1=-1.0, scalar2=None, op0=ALU.add))
    E("vector", lambda e: e.memset(strip1[0:1, 624:656], 0.0))
    E("vector", lambda e: e.tensor_copy(out=strip1[0:1, 624:640], in_=strip1[0:1, 368:384]))
    E("vector", lambda e: e.tensor_tensor(out=strip1[0:1, 496:624].rearrange("p (s q) -> p s q", q=32), in0=strip1[0:1, 192:320].rearrange("p (s q) -> p s q", q=32), in1=strip1[0:1, 624:656].unsqueeze(1).broadcast_to([1, 4, 32]), op=ALU.add))
    BAR()
    E("tensor", lambda e: e.matmul(psR[:, 0:1], strip1[0:1, 496:624], onesNf[0:1, 0:1], start=True, stop=True))
    tm = SIG("tensor")
    WT("vector", "tensor", tm)
    E("vector", lambda e: e.tensor_copy(out=segoff[:], in_=psR[:, 0:1]))
    E("vector", lambda e: e.tensor_scalar(out=pos128[:], in0=pos128[:], scalar1=segoff[:, 0:1], scalar2=0.0, op0=ALU.add, op1=ALU.add))
    BAR()
    for i in range(NT):
        si = i // 8
        E("tensor", lambda e, i=i: e.transpose(psTf[:], pos128[:, 128*(i % 8):128*(i % 8)+128].rearrange("p (q t) -> p q t", q=1)[:, 0, :] if False else pos128[:, 128*(i % 8):128*(i % 8)+128], identf[:]))
        tm = SIG("tensor")
        WT("vector", "tensor", tm)
        E("vector", lambda e, i=i, si=si: e.tensor_copy(out=S_tm[:, i, :], in_=psTf[:, 32*si:32*si+16]))
        tv = SIG("vector")
        WT("tensor", "vector", tv)
    BAR()
    for j in range(4):
        E("vector", lambda e, j=j: e.tensor_scalar(out=cmp1[:, 0:512], in0=rank16[:].rearrange("p i e -> p (i e)"), scalar1=float(j), scalar2=1.0, op0=ALU.is_equal, op1=ALU.mult))
        E("vector", lambda e, j=j: e.tensor_tensor(out=cmp2[:, 0:512], in0=cmp1[:, 0:512], in1=S_tm[:, 0:NT, :].rearrange("p i e -> p (i e)"), op=ALU.mult))
        E("vector", lambda e, j=j: e.tensor_reduce(out=sjf[:, :, j:j+1], in_=cmp2[:, 0:512].rearrange("p (i e) -> p i e", e=16), axis=AX.X, op=ALU.add))
        E("vector", lambda e, j=j: e.tensor_tensor(out=cmp2[:, 0:512], in0=cmp1[:, 0:512], in1=cw_tm[:, 0:NT, :].rearrange("p i e -> p (i e)"), op=ALU.mult))
        E("vector", lambda e, j=j: e.tensor_reduce(out=gj_tm[:, :, j:j+1], in_=cmp2[:, 0:512].rearrange("p (i e) -> p i e", e=16), axis=AX.X, op=ALU.add))
    E("vector", lambda e: e.tensor_copy(out=sji[:].rearrange("p i j -> p (i j)"), in_=sjf[:].rearrange("p i j -> p (i j)")))
    BAR()
    WT("sync", "vector")
    E("tensor", lambda e: e.matmul(psR[:, 0:48], onesNf[:], strip1[0:1, 448:496], start=True, stop=True))
    tm = SIG("tensor")
    WT("vector", "tensor", tm)
    E("vector", lambda e: e.tensor_copy(out=texp_f[:], in_=psR[:, 0:48]))
    E("vector", lambda e: e.tensor_scalar(out=texp_f[:], in0=texp_f[:], scalar1=128.0, scalar2=0.0, op0=ALU.mult, op1=ALU.add))
    E("vector", lambda e: e.tensor_tensor(out=texp_f[:], in0=texp_f[:], in1=io_row[:, 0:48], op=ALU.add))
    E("vector", lambda e: e.tensor_copy(out=widx[:], in_=texp_f[:]))
    BAR()
    xn_sc = camax[:].bitcast(BF16)[:, 0:1024]
    for i in range(NT):
        tl = DMAL(xn_sc, xn_dram[i, :, :])
        WL("gpsimd", tl)
        tg = 0
        for j in range(4):
            tg = GDMA(slotbuf[:, :], xn_sc, out_off=IndirectOffsetOnAxis(sji[:, i, j:j+1], 0))
        WG("sync", tg)
    BAR()
    wcat_rows = wcat_bf.rearrange("e p q -> (e p) q")
    hs_bf = rnd_f[:].bitcast(BF16)[:, 0:512]
    yo_bf = camax[:].bitcast(BF16)[:, 0:1024]
    tsc_tp = {0: 0, 1: 0}
    for tj in range(48):
        WT("gpsimd", "tensor")
        tgw = GDMA(wbuf, wcat_rows, in_off=IndirectOffsetOnAxis(widx[:, tj:tj+1], 0))
        WT("sync", "tensor")
        tl = DMAL(xg_tm[:], slotbuf[512*tj:512*(tj+1), :].rearrange("(a p) d -> p a d", p=128))
        WL("tensor", tl)
        for a4 in range(4):
            for c in range(NCH):
                par = (a4 * NCH + c) % 2
                pp = psT if par == 0 else psU
                WT("tensor", "scalar", tsc_tp[par])
                E("tensor", lambda e, a4=a4, c=c, pp=pp: e.transpose(pp[:], xg_tm[:, a4, 128*c:128*(c+1)], ident[:]))
                tm = SIG("tensor")
                WT("scalar", "tensor", tm)
                E("scalar", lambda e, a4=a4, c=c, pp=pp: e.copy(out=xgF[:, c, 128*a4:128*(a4+1)], in_=pp[:]))
                tsc_tp[par] = C.sems["scalar"].n
        WT("tensor", "scalar", C.sems["scalar"].n)
        WG("tensor", tgw)
        for mi in range(4):
            for c in range(NCH):
                E("tensor", lambda e, mi=mi, c=c: e.matmul(psA[:], wbuf[:, c*512+128*mi : c*512+128*mi+128], xgF[:, c, :], start=(c == 0), stop=(c == NCH-1)))
            for c in range(NCH):
                E("tensor", lambda e, mi=mi, c=c: e.matmul(psB[:], wbuf[:, 4096+c*512+128*mi : 4096+c*512+128*mi+128], xgF[:, c, :], start=(c == 0), stop=(c == NCH-1)))
            tm = SIG("tensor")
            WT("scalar", "tensor", tm)
            E("scalar", lambda e, mi=mi: e.activation(out=hs_bf[:, :], in_=psA[:], func=AF.Silu))
            tsc = C.sems["scalar"].n
            WT("vector", "scalar", tsc)
            E("vector", lambda e, mi=mi: e.tensor_tensor(out=h_rt[:, mi, :], in0=hs_bf[:, :], in1=psB[:], op=ALU.mult))
            tv = SIG("vector")
            WT("tensor", "vector", tv)
        for s4 in range(4):
            for half in range(2):
                for ci in range(4):
                    E("tensor", lambda e, s4=s4, half=half, ci=ci: e.matmul(psW[:, 512*half:512*(half+1)], h_rt[:, ci, 128*s4:128*(s4+1)], wbuf[:, 8192+ci*1024+512*half : 8192+ci*1024+512*half+512], start=(ci == 0), stop=(ci == 3)))
            tm = SIG("tensor")
            WT("vector", "tensor", tm)
            E("vector", lambda e, s4=s4: e.tensor_copy(out=yo_bf[:, :], in_=psW[:]))
            tv = SIG("vector")
            WT("sync", "vector", tv)
            tst = DMAS(yobuf[512*tj+128*s4 : 512*tj+128*s4+128, :], yo_bf[:, :])
            WS("vector", tst)
            WT("tensor", "vector", tv)
    BAR()
    shd_res = kv_t[:].rearrange("p a c t -> p (a c t)").bitcast(BF16)[:, 0:8192].rearrange("p (c d) -> p c d", d=D)
    for q in range(4):
        tl = DMAL(cast_f32[:, :], shd_in.rearrange("p c d -> p (c d)")[:, 2048*q:2048*(q+1)])
        WL("vector", tl)
        E("vector", lambda e, q=q: e.tensor_copy(out=shd_res.rearrange("p c d -> p (c d)")[:, 2048*q:2048*(q+1)], in_=cast_f32[:, :]))
        tv = SIG("vector")
        WT("sync", "vector", tv)
    BAR()
    ybg = camax[:].bitcast(BF16)[:, 0:1024]
    gsil = sqf[:, 0, :]
    s0buf = sqf[:, 1, :]
    for grp in range(8):
        for it in range(4):
            i = grp * 4 + it
            for j in range(4):
                tg = GDMA(ybg, yobuf[:, :], in_off=IndirectOffsetOnAxis(sji[:, i, j:j+1], 0))
                WG("vector", tg)
                if j == 0:
                    E("vector", lambda e, it=it, i=i: e.tensor_scalar(out=g0acc[:, it, :], in0=ybg, scalar1=gj_tm[:, i, 0:1], scalar2=0.0, op0=ALU.mult, op1=ALU.add))
                else:
                    E("vector", lambda e, it=it, i=i, j=j: e.scalar_tensor_tensor(out=g0acc[:, it, :], in0=ybg, scalar=gj_tm[:, i, j:j+1], in1=g0acc[:, it, :], op0=ALU.mult, op1=ALU.add))
                tv = SIG("vector")
                WT("gpsimd", "vector", tv)
        BAR()
        tl = DMAL(xnf_seg[:], xnf_dram[:, :, grp*TC:(grp+1)*TC])
        WL("tensor", tl)
        hsh_bf = rnd_f[:].bitcast(BF16)[:, 0:512]
        for mi in range(8):
            for c in range(NCH):
                E("tensor", lambda e, mi=mi, c=c: e.matmul(psA[:], wq_sb[:, 0, c, 128*mi:128*(mi+1)], xnf_seg[:, c, :], start=(c == 0), stop=(c == NCH-1)))
            for c in range(NCH):
                E("tensor", lambda e, mi=mi, c=c: e.matmul(psB[:], wq_sb[:, 1, c, 128*mi:128*(mi+1)], xnf_seg[:, c, :], start=(c == 0), stop=(c == NCH-1)))
            tm = SIG("tensor")
            WT("scalar", "tensor", tm)
            E("scalar", lambda e, mi=mi: e.activation(out=hsh_bf[:, :], in_=psA[:], func=AF.Silu))
            tsc = C.sems["scalar"].n
            WT("vector", "scalar", tsc)
            E("vector", lambda e, mi=mi: e.tensor_tensor(out=h_sh[:, mi, :], in0=hsh_bf[:, :], in1=psB[:], op=ALU.mult))
            tv = SIG("vector")
            WT("tensor", "vector", tv)
        BAR()
        for s4 in range(4):
            i = grp * 4 + s4
            for half in range(2):
                for ci in range(NCH):
                    E("tensor", lambda e, s4=s4, half=half, ci=ci: e.matmul(psW[:, 512*half:512*(half+1)], h_sh[:, ci, 128*s4:128*(s4+1)], shd_res[:, ci, 512*half:512*(half+1)], start=(ci == 0), stop=(ci == NCH-1)))
            tm = SIG("tensor")
            WT("vector", "tensor", tm)
            E("vector", lambda e, s4=s4: e.tensor_tensor(out=g0acc[:, s4, :], in0=g0acc[:, s4, :], in1=psW[:], op=ALU.add))
            tv = SIG("vector")
            WT("tensor", "vector", tv)
            WT("sync", "vector", tv)
            tl1 = DMAL(x_tile[:, 0, :], x_in[i // 4, 1+(i % 4)*128 : 1+(i % 4)*128+128, :])
            tl2 = DMAL(s0buf, s0_dram[i, :, :])
            WT("scalar", "vector", tv)
            E("scalar", lambda e, s4=s4: e.activation(out=gsil, in_=g0acc[:, s4, :], func=AF.Silu))
            tsc = C.sems["scalar"].n
            WL("vector", tl2)
            WL("vector", tl1)
            WT("vector", "scalar", tsc)
            E("vector", lambda e: e.tensor_tensor(out=gsil, in0=s0buf, in1=gsil, op=ALU.mult))
            E("vector", lambda e: e.tensor_tensor(out=s0buf, in0=x_tile[:, 0, :], in1=gsil, op=ALU.add))
            tv = SIG("vector")
            WT("sync", "vector", tv)
            t1 = DMAS(x1_dram[i, :, :], s0buf)
            WS("vector", t1)
            WS("scalar", t1)
        BAR()
    BAR()
    WS("sync", s_st.n)
    BAR()


    for mat_src, dsel in ((wqT_in, 0), (wkT_in, 1), (wvT_in, 2)):
        for qq in range(4):
            tl = DMAL(cast_f32[:, :], mat_src.rearrange("p c d -> p (c d)")[:, 2048*qq:2048*(qq+1)])
            WL("vector", tl)
            if dsel < 2:
                dstap = wq_sb[:, dsel, :, :].rearrange("p c d -> p (c d)")[:, 2048*qq:2048*(qq+1)]
            else:
                dstap = wstage[:].rearrange("p c d -> p (c d)")[:, 2048*qq:2048*(qq+1)]
            E("vector", lambda e, dstap=dstap: e.tensor_copy(out=dstap, in_=cast_f32[:, :]))
            tv = SIG("vector")
            WT("sync", "vector", tv)
    tl = DMAL(cast_f32[:, 0:1408].rearrange("p (j h) -> p j h", h=16), pe_in.rearrange("j p h -> p j h"))
    WL("vector", tl)
    E("vector", lambda e: e.tensor_copy(out=peT[:].rearrange("p j h -> p (j h)"), in_=cast_f32[:, 0:1408]))
    SIG("vector")
    tl = DMAL(selT[:], selc_in.rearrange("c p h -> p c h"))
    tl = DMAL(rselT[:].rearrange("p c d -> p (c d)"), rsel_in.rearrange("p c d -> p (c d)"))
    BAR()
    for i in range(NTILE):
        if i >= 2:
            WT("sync", "scalar"); WT("sync", "vector")
        tl = DMAL(x_tile[:, i % 2, :], x1_dram[i, :, :])
        WL("scalar", tl)
        E("scalar", lambda e, i=i: e.activation(out=sqf[:, i % 2, :], in_=x_tile[:, i % 2, :], func=AF.Square, accum_out=s2_all[:, i:i+1]))
        SIG("scalar")
        WL("vector", tl)
        E("vector", lambda e, i=i: e.reduce_sum(out=s1_all[:, i:i+1], in_=x_tile[:, i % 2, :], axis=AX.X))
        SIG("vector")
    BAR()
    E("vector", lambda e: e.tensor_scalar(out=s1_all[:], in0=s1_all[:], scalar1=1.0/D, scalar2=None, op0=ALU.mult))
    E("vector", lambda e: e.tensor_scalar(out=s2_all[:], in0=s2_all[:], scalar1=1.0/D, scalar2=None, op0=ALU.mult))
    SIG("vector")
    WT("scalar", "vector")
    E("scalar", lambda e: e.activation(out=rstd_all[:], in_=s1_all[:], func=AF.Square))
    SIG("scalar")
    WT("vector", "scalar")
    E("vector", lambda e: e.tensor_sub(out=rstd_all[:], in0=s2_all[:], in1=rstd_all[:]))
    E("vector", lambda e: e.tensor_scalar(out=rstd_all[:], in0=rstd_all[:], scalar1=LN_EPS, scalar2=None, op0=ALU.add))
    SIG("vector")
    WT("scalar", "vector")
    E("scalar", lambda e: e.activation(out=rstd_all[:], in_=rstd_all[:], func=AF.Sqrt))
    SIG("scalar")
    WT("vector", "scalar")
    E("vector", lambda e: e.reciprocal(out=rstd_all[:], in_=rstd_all[:]))
    E("vector", lambda e: e.tensor_scalar(out=nmu_all[:], in0=s1_all[:], scalar1=-1.0, scalar2=None, op0=ALU.mult))
    E("vector", lambda e: e.tensor_tensor(out=nmu_all[:], in0=nmu_all[:], in1=rstd_all[:], op=ALU.mult))
    SIG("vector")
    BAR()
    xn1_store_t = {}
    xnf1_store_t = {}
    for i in range(NTILE):
        if i >= 2:
            WT("sync", "scalar")
        tl = DMAL(x_tile[:, i % 2, :], x1_dram[i, :, :])
        WL("scalar", tl)
        if i >= 2:
            WS("scalar", xn1_store_t[i - 2])
        E("scalar", lambda e, i=i: e.activation(out=xn_st[:, i % 2, :], in_=x_tile[:, i % 2, :], func=AF.Identity, bias=nmu_all[:, i:i+1], scale=rstd_all[:, i:i+1]))
        tap = SIG("scalar")
        WT("sync", "scalar", tap)
        xn1_store_t[i] = DMAS(xn1_dram[i, :, :], xn_st[:, i % 2, :])
        WT("tensor", "scalar", tap)
        for c in range(NCH):
            pp = psT if c % 2 == 0 else psU
            if i > 0 or c >= 2:
                WT("tensor", "scalar", C.sems["scalar"].n - 1)
            E("tensor", lambda e, i=i, c=c, pp=pp: e.transpose(pp[:], xn_st[:, i % 2, 128*c:128*(c+1)], ident[:]))
            tm = SIG("tensor")
            WT("scalar", "tensor", tm)
            if c == 0 and i >= 2:
                WS("scalar", xnf1_store_t[i - 2])
            E("scalar", lambda e, i=i, c=c, pp=pp: e.copy(out=xnf_blk[:, i % 2, c, :], in_=pp[:]))
            SIG("scalar")
        WT("sync", "scalar")
        b = i // 4; toff = (i % 4) * 128
        xnf1_store_t[i] = DMAS(xnf1_dram[:, :, b*TC+toff : b*TC+toff+128], xnf_blk[:, i % 2, :, :])
    BAR()
    WS("sync", s_st.n)
    BAR()
    BAR()
    E("gpsimd", lambda e: e.collective_compute("AllGather", ALU.bypass, replica_groups=[list(range(NCORE))],
                                               ins=[xn1_dram.rearrange("i p d -> (i p) d")], outs=[xn1_ag[:, :, :]]))
    SIG("gpsimd")
    BAR()
    qC = r_seg
    kC = mix_t
    vC = xq_t
    e_full = kv_t[0:16, :, :, :].rearrange("p a c t -> p (a c t)")
    e_buf = e_full[:, 0:5632].rearrange("p (j t) -> p j t", t=TC)
    mrow = e_full[:, 5632:6144]
    m32 = e_full[:, 6144:6656]
    denb = e_full[:, 6656:7168]
    rden = e_full[:, 7168:7680]
    abf_full = kv_t[0:16, :, :, :].rearrange("p a c t -> p (a c t)").bitcast(BF16)
    abf = [abf_full[:, 15360:15872], abf_full[:, 15872:16384]]
    dmat_bf = Dmat[:].rearrange("p c t -> p (c t)").bitcast(BF16)
    pC = [dmat_bf[:, 0:4096].rearrange("p (c t) -> p c t", t=TC),
          dmat_bf[:, 4096:8192].rearrange("p (c t) -> p c t", t=TC)]
    s1bf = dmat_bf[:, 4096:8192].rearrange("p (c t) -> p c t", t=TC)
    s1acc = powm
    tmpv = camax
    s1tm_st = xn_st
    xnf1_4d = xnf1_dram.rearrange("p c (l t) -> p c l t", t=TC)
    tsc_s1 = {0: 0, 1: 0}
    for ct in range(8):
        tl = DMAL(xnf_seg[:].rearrange("p c (l t) -> p c l t", t=64), xnf1_4d[:, :, :, 64*ct:64*ct+64])
        WL("tensor", tl)
        tsc_pp = {0: 0, 1: 0}
        for si, dsts in enumerate((qC, kC, vC)):
            for mi in range(NCH):
                gidx = si * NCH + mi
                pp = psA if gidx % 2 == 0 else psB
                WT("tensor", "scalar", tsc_pp[gidx % 2])
                for c in range(NCH):
                    if si < 2:
                        lh = wq_sb[:, si, c, 128*mi:128*(mi+1)]
                    else:
                        lh = wstage[:, c, 128*mi:128*(mi+1)]
                    E("tensor", lambda e, lh=lh, c=c, pp=pp: e.matmul(pp[:], lh, xnf_seg[:, c, :], start=(c == 0), stop=(c == NCH-1)))
                tm = SIG("tensor")
                WT("scalar", "tensor", tm)
                E("scalar", lambda e, dsts=dsts, mi=mi, pp=pp: e.copy(out=dsts[:, mi, :], in_=pp[:]))
                tsc_pp[gidx % 2] = SIG("scalar")
        BAR()
        tm_sel = {}
        tv_evac = {}
        for j in range(11):
            dlt = j - 5
            lo = max(0, -64*dlt); hi = 512 - max(0, 64*dlt)
            if j >= 2:
                WT("vector", "tensor", tm_sel[j - 2])
            for c in range(NCH):
                if lo > 0:
                    E("vector", lambda e, c=c, lo=lo, j=j: e.memset(pC[j % 2][:, c, 0:lo], 0.0))
                if hi < TC:
                    E("vector", lambda e, c=c, hi=hi, j=j: e.memset(pC[j % 2][:, c, hi:TC], 0.0))
                E("vector", lambda e, c=c, lo=lo, hi=hi, dlt=dlt, j=j: e.tensor_tensor(
                    out=pC[j % 2][:, c, lo:hi], in0=qC[:, c, lo:hi], in1=kC[:, c, lo+64*dlt:hi+64*dlt], op=ALU.mult))
            tv_prod = SIG("vector")
            if j >= 1:
                WT("tensor", "vector", tv_evac[j - 1])
            for c in range(NCH):
                E("tensor", lambda e, j=j, c=c: e.matmul(psR[0:16, :], peT[:, j*8+c, :], qC[:, c, :], start=(c == 0), stop=False))
            WT("tensor", "vector", tv_prod)
            for c in range(NCH):
                E("tensor", lambda e, c=c, j=j: e.matmul(psR[0:16, :], selT[:, c, :], pC[j % 2][:, c, :], start=False, stop=(c == NCH-1)))
            tm_sel[j] = SIG("tensor")
            WT("vector", "tensor", tm_sel[j])
            E("vector", lambda e, j=j: e.tensor_copy(out=e_buf[:, j, :], in_=psR[0:16, :]))
            tv_evac[j] = SIG("vector")
        BAR()
        E("vector", lambda e: e.tensor_copy(out=mrow[:], in_=e_buf[:, 0, :]))
        for j in range(1, 11):
            E("vector", lambda e, j=j: e.tensor_tensor(out=mrow[:], in0=mrow[:], in1=e_buf[:, j, :], op=ALU.max))
        E("vector", lambda e: e.tensor_scalar(out=m32[:], in0=mrow[:], scalar1=0.03125, scalar2=None, op0=ALU.mult))
        SIG("vector")
        for j in range(11):
            E("vector", lambda e, j=j: e.tensor_scalar(out=e_buf[:, j, :], in0=e_buf[:, j, :], scalar1=0.03125, scalar2=None, op0=ALU.mult))
            E("vector", lambda e, j=j: e.tensor_sub(out=e_buf[:, j, :], in0=e_buf[:, j, :], in1=m32[:]))
            tv = SIG("vector")
            WT("scalar", "vector", tv)
            E("scalar", lambda e, j=j: e.activation(out=e_buf[:, j, :], in_=e_buf[:, j, :], func=AF.Exp))
            tsc = SIG("scalar")
            WT("vector", "scalar", tsc)
            if j == 0:
                E("vector", lambda e: e.tensor_copy(out=denb[:], in_=e_buf[:, 0, :]))
            else:
                E("vector", lambda e, j=j: e.tensor_add(out=denb[:], in0=denb[:], in1=e_buf[:, j, :]))
        E("vector", lambda e: e.reciprocal(out=rden[:], in_=denb[:]))
        E("vector", lambda e: e.memset(s1acc[:].rearrange("p c t -> p (c t)"), 0.0))
        SIG("vector")
        BAR()
        tv_pp = {0: 0, 1: 0}
        for j in range(11):
            dlt = j - 5
            lo = max(0, -64*dlt); hi = 512 - max(0, 64*dlt)
            E("vector", lambda e, j=j: e.tensor_tensor(out=abf[j % 2][:, :], in0=e_buf[:, j, :], in1=rden[:], op=ALU.mult))
            tv_abf = SIG("vector")
            WT("tensor", "vector", tv_abf)
            for c in range(NCH):
                pp = psA if c % 2 == 0 else psB
                WT("tensor", "vector", tv_pp[c % 2])
                E("tensor", lambda e, c=c, pp=pp, lo=lo, hi=hi, j=j: e.matmul(pp[:, lo:hi], rselT[:, c, :], abf[j % 2][:, lo:hi], start=True, stop=True))
                tm = SIG("tensor")
                WT("vector", "tensor", tm)
                E("vector", lambda e, c=c, pp=pp, lo=lo, hi=hi, dlt=dlt: e.tensor_tensor(out=tmpv[:, lo:hi], in0=pp[:, lo:hi], in1=vC[:, c, lo+64*dlt:hi+64*dlt], op=ALU.mult))
                E("vector", lambda e, c=c, lo=lo, hi=hi: e.tensor_add(out=s1acc[:, c, lo:hi], in0=s1acc[:, c, lo:hi], in1=tmpv[:, lo:hi]))
                tv_pp[c % 2] = SIG("vector")
        BAR()
        for c in range(NCH):
            E("vector", lambda e, c=c: e.tensor_copy(out=s1bf[:, c, :], in_=s1acc[:, c, :]))
        SIG("vector")
        BAR()
        s1st_t = {}
        for q4 in range(4):
            for c in range(NCH):
                par = c % 2
                pp = psT if par == 0 else psU
                WT("tensor", "scalar", tsc_s1[par])
                E("tensor", lambda e, q4=q4, c=c, pp=pp: e.transpose(pp[:], s1bf[:, c, 128*q4:128*(q4+1)], ident[:]))
                tm = SIG("tensor")
                WT("scalar", "tensor", tm)
                if c == 0 and q4 >= 2:
                    WS("scalar", s1st_t[q4 - 2])
                E("scalar", lambda e, q4=q4, c=c, pp=pp: e.copy(out=s1tm_st[:, q4 % 2, 128*c:128*(c+1)], in_=pp[:]))
                tsc_s1[par] = SIG("scalar")
            WT("sync", "scalar")
            s1st_t[q4] = DMAS(s1_dram[ct, 128*q4:128*(q4+1), :], s1tm_st[:, q4 % 2, :])
        BAR()
    BAR()
    WS("sync", s_st.n)
    BAR()


    kv_bf = kv_t[:].rearrange("p a c t -> p (a c t)").bitcast(BF16)
    wq_bf = wq_sb[:].rearrange("p s c d -> p (s c d)")
    rw1a = kv_bf[:, 0:16384].rearrange("p (c h) -> p c h", h=4096)
    rw1b = wq_bf[:, 0:16384].rearrange("p (c h) -> p c h", h=4096)
    hbuf0 = dmat_bf[:, 0:8192].rearrange("p (c t) -> p c t", t=TC)
    hbuf1 = powm_bf[:, 0:8192].rearrange("p (c t) -> p c t", t=TC)
    sd = scan_ar[:].rearrange("p a t -> p (a t)")
    sdi = sd.bitcast(I32)
    lg_tm = sd[:, 0:256].rearrange("p (i e) -> p i e", e=8)
    sel0 = sd[:, 256:512].rearrange("p (i e) -> p i e", e=8)
    sel1 = sd[:, 512:768].rearrange("p (i e) -> p i e", e=8)
    cmpD = sd[:, 768:1024].rearrange("p (i e) -> p i e", e=8)
    rnk8 = sd[:, 1024:1280].rearrange("p (i e) -> p i e", e=8)
    e0f = sd[:, 1280:1312]; e1f = sd[:, 1312:1344]
    p0c = sd[:, 1344:1376]; p1c = sd[:, 1376:1408]
    pos0 = sd[:, 1408:1440]; pos1 = sd[:, 1440:1472]
    keep0 = sd[:, 1472:1504]; keep1 = sd[:, 1504:1536]
    sid0 = sd[:, 1536:1568]; sid1 = sd[:, 1568:1600]
    tmp32 = sd[:, 1600:1632]; tmp32b = sd[:, 1632:1664]
    sid0_i = sdi[:, 1664:1696]; sid1_i = sdi[:, 1696:1728]
    isc0_i = sdi[:, 1728:1760]; isc1_i = sdi[:, 1760:1792]
    tmp8a = sd[:, 1792:1800]; tmp8b = sd[:, 1800:1808]
    vaC = sd[:, 1808:1840]; vbC = sd[:, 1840:1872]; dvab = sd[:, 1872:1904]
    rw2_sb = a_scan[:].bitcast(BF16)[:, 0:256].rearrange("p (c e) -> p c e", e=8)
    base_rep = b_scan[:, 0:64]
    eb1_sb = b_scan[:, 64:96]
    rb1_sb = b_scan[:, 96:128]
    rb2_sb = b_scan[:, 128:129]
    corebase_sb = b_scan[:, 129:130]
    crep = b_scan[:, 130:131]
    payst = b_scan[:, 131:133]
    trib8_sb = b_scan[:, 192:256]
    ilst = b_scan[0:8, 256:512].rearrange("p (t j) -> p t j", j=2)
    tpst = b_scan[0:16, 256:384]
    lgf = ev_t[0:16, :]
    DmatF = Dmat[:].rearrange("p c t -> p (c t)")
    A_all = DmatF[:, 0:1024]
    onesD = DmatF[:, 1024:2048]
    Acum = DmatF[:, 2048:3072]
    Acum3 = DmatF[:, 2048:3072].rearrange("p (t j) -> p t j", j=2)
    cnt_s = DmatF[0:1, 3072:3584]
    cntl_s = DmatF[0:1, 3584:3648]
    base1_s = DmatF[0:1, 3648:3712]
    base2_s = DmatF[0:1, 3712:3776]
    baseT_s = DmatF[0:1, 3776:3840]
    mask_sb = csum[0:1, 0:512]
    for k in range(16):
        tl = DMAL(cast_f32[:, :], rw1T_in.rearrange("p c h -> p (c h)")[:, 2048*k:2048*(k+1)])
        WL("vector", tl)
        dstap = (kv_bf if k < 8 else wq_bf)[:, 2048*(k % 8):2048*(k % 8)+2048]
        E("vector", lambda e, dstap=dstap: e.tensor_copy(out=dstap, in_=cast_f32[:, :]))
        tv = SIG("vector")
        WT("sync", "vector", tv)
    tl = DMAL(cast_f32[:, 0:256].rearrange("p (c e) -> p c e", e=8), rw2T_in.rearrange("c p e -> p c e"))
    WL("vector", tl)
    E("vector", lambda e: e.tensor_copy(out=rw2_sb[:].rearrange("p c e -> p (c e)"), in_=cast_f32[:, 0:256]))
    SIG("vector")
    DMAL(rb1_sb, rb1_in[:, :])
    DMAL(rb2_sb[0:8, :], rb2_in[:, :])
    DMAL(trib8_sb, trib8_in[:, :])
    DMAL(mskc512_in_sb := mask_sb, mskc512_in[:, :])
    DMAL(corebase_sb[0:1, :], corebase_in[:, :])
    DMAL(eb1_sb, eb1_in[:, :])
    E("vector", lambda e: e.memset(lgf[:, :], 0.0))
    SIG("vector")
    BAR()
    for grp in range(8):
        tl = DMAL(xnf_seg[:], xnf1_dram[:, :, grp*TC:(grp+1)*TC])
        WL("tensor", tl)
        tsc_pp = {0: 0, 1: 0}
        for hc in range(32):
            pp = psA if hc % 2 == 0 else psB
            WT("tensor", "scalar", tsc_pp[hc % 2])
            for c in range(NCH):
                lh = (rw1a if c < 4 else rw1b)[:, c % 4, 128*hc:128*hc+128]
                E("tensor", lambda e, lh=lh, c=c, pp=pp: e.matmul(pp[:], lh, xnf_seg[:, c, :], start=(c == 0), stop=(c == NCH-1)))
            tm = SIG("tensor")
            WT("scalar", "tensor", tm)
            hb = (hbuf0 if hc < 16 else hbuf1)
            E("scalar", lambda e, hb=hb, hc=hc, pp=pp: e.activation(out=hb[:, hc % 16, :], in_=pp[:], func=AF.Relu, bias=rb1_sb[:, hc:hc+1]))
            tsc_pp[hc % 2] = SIG("scalar")
        BAR()
        for hc in range(32):
            hb = (hbuf0 if hc < 16 else hbuf1)
            E("tensor", lambda e, hb=hb, hc=hc: e.matmul(psR[0:8, :], rw2_sb[:, hc, :], hb[:, hc % 16, :], start=(hc == 0), stop=(hc == 31)))
        tm = SIG("tensor")
        WT("scalar", "tensor", tm)
        E("scalar", lambda e: e.activation(out=lgf[0:8, :], in_=psR[0:8, :], func=AF.Identity, bias=rb2_sb[0:8, 0:1]))
        tsc = SIG("scalar")
        WT("tensor", "scalar", tsc)
        for q4 in range(4):
            E("tensor", lambda e, q4=q4: e.transpose(psTf[:], lgf[:, 128*q4:128*(q4+1)], identf[0:16, :]))
            tm = SIG("tensor")
            WT("vector", "tensor", tm)
            E("vector", lambda e, grp=grp, q4=q4: e.tensor_copy(out=lg_tm[:, grp*4+q4, :], in_=psTf[:, 0:8]))
            tv = SIG("vector")
            WT("tensor", "vector", tv)
        BAR()
    WT("sync", "vector")
    E("vector", lambda e: e.memset(rnk8[:].rearrange("p i e -> p (i e)"), 0.0))
    for e2 in range(8):
        E("vector", lambda e, e2=e2: e.tensor_tensor(out=cmpD[:], in0=lg_tm[:, :, e2:e2+1].broadcast_to([128, NTILE, 8]), in1=lg_tm[:], op=ALU.is_gt))
        E("vector", lambda e: e.tensor_add(out=rnk8[:], in0=rnk8[:], in1=cmpD[:]))
        E("vector", lambda e, e2=e2: e.tensor_tensor(out=cmpD[:], in0=lg_tm[:, :, e2:e2+1].broadcast_to([128, NTILE, 8]), in1=lg_tm[:], op=ALU.is_equal))
        E("vector", lambda e, e2=e2: e.tensor_tensor(out=cmpD[:], in0=cmpD[:], in1=trib8_sb[:, e2*8:(e2+1)*8].unsqueeze(1).broadcast_to([128, NTILE, 8]), op=ALU.mult))
        E("vector", lambda e: e.tensor_add(out=rnk8[:], in0=rnk8[:], in1=cmpD[:]))
    E("vector", lambda e: e.tensor_scalar(out=sel0[:].rearrange("p i e -> p (i e)"), in0=rnk8[:].rearrange("p i e -> p (i e)"), scalar1=0.0, scalar2=None, op0=ALU.is_equal))
    E("vector", lambda e: e.tensor_scalar(out=sel1[:].rearrange("p i e -> p (i e)"), in0=rnk8[:].rearrange("p i e -> p (i e)"), scalar1=1.0, scalar2=None, op0=ALU.is_equal))
    for sel, ef, vf in ((sel0, e0f, vaC), (sel1, e1f, vbC)):
        E("vector", lambda e, sel=sel: e.tensor_tensor(out=cmpD[:], in0=sel[:], in1=io_col[:, 0:8].unsqueeze(1).broadcast_to([128, NTILE, 8]), op=ALU.mult))
        E("vector", lambda e, ef=ef: e.tensor_reduce(out=ef.rearrange("p (i q) -> p i q", q=1), in_=cmpD[:], axis=AX.X, op=ALU.add))
        E("vector", lambda e, sel=sel: e.tensor_tensor(out=cmpD[:], in0=sel[:], in1=lg_tm[:], op=ALU.mult))
        E("vector", lambda e, vf=vf: e.tensor_reduce(out=vf.rearrange("p (i q) -> p i q", q=1), in_=cmpD[:], axis=AX.X, op=ALU.add))
    E("vector", lambda e: e.tensor_sub(out=dvab, in0=vaC, in1=vbC))
    tv = SIG("vector")
    WT("scalar", "vector", tv)
    E("scalar", lambda e: e.activation(out=p0c, in_=dvab, func=AF.Sigmoid))
    tsc = SIG("scalar")
    WT("vector", "scalar", tsc)
    E("vector", lambda e: e.tensor_scalar(out=p1c, in0=p0c, scalar1=-1.0, scalar2=1.0, op0=ALU.mult, op1=ALU.add))
    E("vector", lambda e: e.memset(A_all, 0.0))
    E("vector", lambda e: e.memset(onesD, 1.0))
    SIG("vector")
    BAR()
    ilst_t = 0
    tsc_a = 0
    for i in range(NTILE):
        l = i // 4
        for j, sel in ((0, sel0), (1, sel1)):
            WT("tensor", "scalar", tsc_a)
            E("tensor", lambda e, sel=sel, i=i: e.transpose(psTf[0:8, :], sel[:, i, :], identf[:]))
            tm = SIG("tensor")
            WT("scalar", "tensor", tm)
            if j == 0:
                WS("scalar", ilst_t)
            E("scalar", lambda e, j=j: e.copy(out=ilst[:, :, j], in_=psTf[0:8, :]))
            tsc_a = SIG("scalar")
        WT("sync", "scalar")
        ilst_t = DMAS(DmatF[l*8:l*8+8, 256*(i % 4):256*(i % 4)+256], ilst[:].rearrange("p t j -> p (t j)"))
    BAR()
    WS("vector", s_st.n)
    E("vector", lambda e: e.tensor_tensor_scan(out=Acum, data0=onesD, data1=A_all, initial=zcol[:, 0:1], op0=ALU.mult, op1=ALU.add))
    tv = SIG("vector")
    WT("sync", "vector", tv)
    tst = DMAS(cnt2_in[:, :], DmatF[0:64, 3071:3072])
    WS("gpsimd", tst)
    BAR()
    E("gpsimd", lambda e: e.collective_compute("AllGather", ALU.bypass, replica_groups=[list(range(NCORE))],
                                               ins=[cnt2_in[:, :]], outs=[cnt2_out[:, :, :]]))
    SIG("gpsimd")
    BAR()
    WT("sync", "gpsimd")
    tl = DMAL(cnt_s, cnt2_out.rearrange("c q o -> o (c q)"))
    WL("vector", tl)
    E("vector", lambda e: e.tensor_copy(out=cntl_s, in_=cnt_s[0:1, 0:64]))
    for cc in range(1, 8):
        E("vector", lambda e, cc=cc: e.tensor_add(out=cntl_s, in0=cntl_s, in1=cnt_s[0:1, 64*cc:64*cc+64]))
    E("vector", lambda e: e.memset(base1_s[0:1, 0:8], 0.0))
    for l in range(1, 8):
        E("vector", lambda e, l=l: e.tensor_add(out=base1_s[0:1, 8*l:8*l+8], in0=base1_s[0:1, 8*(l-1):8*l], in1=cntl_s[0:1, 8*(l-1):8*l]))
    E("vector", lambda e: e.tensor_tensor(out=cnt_s, in0=cnt_s, in1=mask_sb, op=ALU.mult))
    E("vector", lambda e: e.tensor_copy(out=base2_s, in_=cnt_s[0:1, 0:64]))
    for cc in range(1, 8):
        E("vector", lambda e, cc=cc: e.tensor_add(out=base2_s, in0=base2_s, in1=cnt_s[0:1, 64*cc:64*cc+64]))
    E("vector", lambda e: e.tensor_add(out=baseT_s, in0=base1_s, in1=base2_s))
    SIG("vector")
    BAR()
    E("tensor", lambda e: e.matmul(psR[:, 0:64], onesNf[:], baseT_s, start=True, stop=True))
    tm = SIG("tensor")
    WT("vector", "tensor", tm)
    E("vector", lambda e: e.tensor_copy(out=base_rep, in_=psR[:, 0:64]))
    E("vector", lambda e: e.memset(tpst[:, :], 0.0))
    SIG("vector")
    BAR()
    tm_tp = 0
    for i in range(NTILE):
        l = i // 4; toff = (i % 4) * 128
        for j, sel, posX in ((0, sel0, pos0), (1, sel1, pos1)):
            WT("sync", "tensor", tm_tp)
            tst = DMAL(tpst[0:8, :], Acum3[l*8:l*8+8, toff:toff+128, j], allow_slow_non_contiguous=True)
            WL("tensor", tst)
            E("tensor", lambda e: e.transpose(psTf[:], tpst[:, :], identf[0:16, :]))
            tm_tp = SIG("tensor")
            WT("vector", "tensor", tm_tp)
            E("vector", lambda e: e.tensor_copy(out=tmp8a, in_=psTf[:, 0:8]))
            E("vector", lambda e, l=l: e.tensor_tensor(out=tmp8b, in0=tmp8a, in1=base_rep[:, l*8:l*8+8], op=ALU.add))
            E("vector", lambda e, sel=sel, i=i: e.tensor_tensor(out=tmp8b, in0=tmp8b, in1=sel[:, i, :], op=ALU.mult))
            E("vector", lambda e, posX=posX, i=i: e.tensor_reduce(out=posX[:, i:i+1].rearrange("p (i q) -> p i q", q=1), in_=tmp8b.rearrange("p (i e) -> p i e", e=8), axis=AX.X, op=ALU.add))
            E("vector", lambda e, posX=posX, i=i: e.tensor_scalar(out=posX[:, i:i+1], in0=posX[:, i:i+1], scalar1=-1.0, scalar2=None, op0=ALU.add))
            tv = SIG("vector")
    BAR()
    for posX, keepX, eXf, sidX, sidX_i, iscX_i in ((pos0, keep0, e0f, sid0, sid0_i, isc0_i), (pos1, keep1, e1f, sid1, sid1_i, isc1_i)):
        E("vector", lambda e, posX=posX, keepX=keepX: e.tensor_scalar(out=keepX, in0=posX, scalar1=4096.0, scalar2=None, op0=ALU.is_lt))
        E("vector", lambda e, posX=posX: e.tensor_scalar(out=tmp32, in0=posX, scalar1=4095.0, scalar2=None, op0=ALU.min))
        E("vector", lambda e, eXf=eXf: e.tensor_scalar(out=tmp32b, in0=eXf, scalar1=4096.0, scalar2=None, op0=ALU.mult))
        E("vector", lambda e, sidX=sidX: e.tensor_add(out=sidX, in0=tmp32b, in1=tmp32))
        E("vector", lambda e, sidX=sidX, sidX_i=sidX_i: e.tensor_copy(out=sidX_i, in_=sidX))
        E("vector", lambda e, posX=posX: e.tensor_scalar(out=tmp32, in0=posX, scalar1=5119.0, scalar2=None, op0=ALU.min))
        E("vector", lambda e, eXf=eXf: e.tensor_scalar(out=tmp32b, in0=eXf, scalar1=5120.0, scalar2=None, op0=ALU.mult))
        E("vector", lambda e: e.tensor_add(out=tmp32, in0=tmp32, in1=tmp32b))
        E("vector", lambda e, iscX_i=iscX_i: e.tensor_copy(out=iscX_i, in_=tmp32))
    SIG("vector")
    WT("sync", "vector")
    BAR()
    WS("sync", s_st.n)
    BAR()


    xqf = xq_t[:].rearrange("p c t -> p (c t)").bitcast(F32)
    xqi = xq_t[:].rearrange("p c t -> p (c t)").bitcast(I32)
    eb2r = xqf[:, 0:1024]
    disp_sb = xqf[:, 1024:1088].rearrange("p (a w) -> p a w", w=2)
    idc_i = xqi[:, 1088:1120]
    E("vector", lambda e: e.memset(cast_f32[:, 0:640], 0.0))
    tv = SIG("vector")
    WT("sync", "vector", tv)
    tz = DMAS(dispT.rearrange("(p n) w -> p (n w)", p=128), cast_f32[:, 0:640])
    E("tensor", lambda e: e.matmul(psR[:, 0:1], onesNf[:], corebase_sb[0:1, :], start=True, stop=True))
    tm = SIG("tensor")
    WT("vector", "tensor", tm)
    E("vector", lambda e: e.tensor_copy(out=crep, in_=psR[:, 0:1]))
    E("vector", lambda e: e.tensor_add(out=crep, in0=crep, in1=io_row[:, 0:1]))
    SIG("vector")
    WS("gpsimd", tz)
    tgd = 0
    for i in range(NTILE):
        l = i // 4; toff = (i % 4) * 128
        for j, pc, iscX in ((0, p0c, isc0_i), (1, p1c, isc1_i)):
            WG("vector", tgd)
            E("vector", lambda e, l=l, toff=toff: e.tensor_scalar(out=payst[:, 0:1], in0=crep, scalar1=float(l*512 + toff), scalar2=None, op0=ALU.add))
            E("vector", lambda e, pc=pc, i=i: e.tensor_copy(out=payst[:, 1:2], in_=pc[:, i:i+1]))
            tv = SIG("vector")
            WT("gpsimd", "vector", tv)
            tgd = GDMA(dispT[:, :], payst, out_off=IndirectOffsetOnAxis(iscX[:, i:i+1], 0))
    BAR()
    E("gpsimd", lambda e: e.collective_compute("ReduceScatter", ALU.add, replica_groups=[list(range(NCORE))],
                                               ins=[dispT[:, :]], outs=[disp_rs[:, :]]))
    SIG("gpsimd")
    BAR()
    tl = DMAL(disp_sb[:], disp_rs[0:4096, :].rearrange("(a p) w -> p a w", p=128))
    tl2 = DMAL(eb2r, eb2_in[:, :])
    WL("vector", tl)
    E("vector", lambda e: e.tensor_copy(out=idc_i, in_=disp_sb[:, :, 0]))
    SIG("vector")
    BAR()
    for mat_src, mat_dst in ((ew1T_in, ew1_bf), (ew2T_in, ew2_bf)):
        msrc = mat_src.rearrange("p c h -> p (c h)")
        mdst = mat_dst.rearrange("p c h -> p (c h)")
        for k in range(16):
            tl = DMAL(cast_f32[:, :], msrc[:, 2048*k:2048*(k+1)])
            WL("vector", tl)
            E("vector", lambda e: e.tensor_copy(out=cast_bf[:, :], in_=cast_f32[:, :]))
            tv = SIG("vector")
            WT("sync", "vector", tv)
            tstw = DMAS(mdst[:, 2048*k:2048*(k+1)], cast_bf[:, :])
            WS("vector", tstw)
            WS("sync", tstw)
    BAR()
    xeF = [(kv_bf if t < 4 else wq_bf)[:, 4096*(t % 4):4096*(t % 4)+4096].rearrange("p (c s) -> p c s", s=TC) for t in range(8)]
    xg_st = [camax[:].bitcast(BF16)[:, 0:1024], csum[:].bitcast(BF16)[:, 0:1024]]
    xn1_flat = xn1_ag.rearrange("c n d -> (c n) d")
    tm_xg = {0: 0, 1: 0}
    tsc_xe = {0: 0, 1: 0}
    for t in range(8):
        for q in range(4):
            a = t * 4 + q
            pb = a % 2
            WT("gpsimd", "tensor", tm_xg[pb])
            tg = GDMA(xg_st[pb], xn1_flat, in_off=IndirectOffsetOnAxis(idc_i[:, a:a+1], 0))
            WG("tensor", tg)
            for c in range(NCH):
                par = c % 2
                pp = psT if par == 0 else psU
                WT("tensor", "scalar", tsc_xe[par])
                E("tensor", lambda e, pb=pb, c=c, pp=pp: e.transpose(pp[:], xg_st[pb][:, 128*c:128*(c+1)], ident[:]))
                tmx = SIG("tensor")
                WT("scalar", "tensor", tmx)
                E("scalar", lambda e, t=t, c=c, q=q, pp=pp: e.copy(out=xeF[t][:, c, 128*q:128*(q+1)], in_=pp[:]))
                tsc_xe[par] = SIG("scalar")
            tm_xg[pb] = C.sems["tensor"].n
    BAR()
    mixf = mix_t[:].rearrange("p c t -> p (c t)")
    ew1st = [mixf[:, 0:1024].rearrange("p (c d) -> p c d", d=128), mixf[:, 1024:2048].rearrange("p (c d) -> p c d", d=128)]
    hst = [mixf[:, 2048:2560], mixf[:, 2560:3072]]
    tls = {0: DMAL(ew1st[0][:, :, :], ew1_bf[:, :, 0:128])}
    tm_done = {}
    hst_st = {0: 0, 1: 0}
    tsc_pp = {0: 0, 1: 0}
    for hc in range(32):
        if hc + 1 < 32:
            if hc - 1 >= 0:
                WT("sync", "tensor", tm_done[hc - 1])
            tls[hc + 1] = DMAL(ew1st[(hc + 1) % 2][:, :, :], ew1_bf[:, :, 128*(hc+1):128*(hc+2)])
        WL("tensor", tls[hc])
        for t in range(8):
            pp = psA if t % 2 == 0 else psB
            WT("tensor", "scalar", tsc_pp[t % 2])
            for c in range(NCH):
                E("tensor", lambda e, hc=hc, c=c, t=t, pp=pp: e.matmul(pp[:], ew1st[hc % 2][:, c, :], xeF[t][:, c, :], start=(c == 0), stop=(c == NCH-1)))
            tmx = SIG("tensor")
            WT("scalar", "tensor", tmx)
            WS("scalar", hst_st[t % 2])
            E("scalar", lambda e, t=t, hc=hc: e.activation(out=hst[t % 2], in_=(psA if t % 2 == 0 else psB)[:], func=AF.Relu, bias=eb1_sb[:, hc:hc+1]))
            tsc_pp[t % 2] = SIG("scalar")
            WT("sync", "scalar", tsc_pp[t % 2])
            hst_st[t % 2] = DMAS(h_dram[t, hc, :, :], hst[t % 2])
        tm_done[hc] = C.sems["tensor"].n
    BAR()
    WS("sync", s_st.n)
    for hc in range(32):
        dst = (kv_bf if hc < 16 else wq_bf)[:, 1024*(hc % 16):1024*(hc % 16)+1024]
        DMAL(dst, ew2_bf[:, hc, :])
    BAR()
    yo_bf2 = camax[:].bitcast(BF16)[:, 0:1024]
    tmpW = cast_f32[:, 0:1024]
    for t in range(8):
        tlh1 = DMAL(hbuf0[:, :, :], h_dram[t, 0:16].rearrange("c p t -> p c t"))
        tlh2 = DMAL(hbuf1[:, :, :], h_dram[t, 16:32].rearrange("c p t -> p c t"))
        WL("tensor", tlh2)
        tv_yo = 0
        for s4 in range(4):
            a = t * 4 + s4
            WT("tensor", "vector", tv_yo)
            for half in range(2):
                for hc in range(32):
                    hb = hbuf0 if hc < 16 else hbuf1
                    ew2r = (kv_bf if hc < 16 else wq_bf)[:, 1024*(hc % 16):1024*(hc % 16)+1024]
                    E("tensor", lambda e, hb=hb, hc=hc, s4=s4, half=half, ew2r=ew2r: e.matmul(
                        psW[:, 512*half:512*(half+1)], hb[:, hc % 16, 128*s4:128*(s4+1)], ew2r[:, 512*half:512*(half+1)], start=(hc == 0), stop=(hc == 31)))
            tmx = SIG("tensor")
            WT("vector", "tensor", tmx)
            E("vector", lambda e: e.tensor_add(out=tmpW, in0=psW[:], in1=eb2r))
            E("vector", lambda e, a=a: e.tensor_scalar(out=yo_bf2, in0=tmpW, scalar1=disp_sb[:, a, 1:2], scalar2=None, op0=ALU.mult))
            tv_yo = SIG("vector")
            WT("sync", "vector", tv_yo)
            tst = DMAS(yo_dram[512*t+128*s4 : 512*t+128*s4+128, :], yo_bf2)
            WS("vector", tst)
        BAR()
    BAR()
    WS("sync", s_st.n)
    BAR()
    E("gpsimd", lambda e: e.collective_compute("AllGather", ALU.bypass, replica_groups=[list(range(NCORE))],
                                               ins=[yo_dram[:, :]], outs=[yo_ag[:, :, :]]))
    SIG("gpsimd")
    E("gpsimd", lambda e: e.collective_compute("AllGather", ALU.bypass, replica_groups=[list(range(NCORE))],
                                               ins=[cnt2_in[:, :]], outs=[cnt2_out[:, :, :]]))
    SIG("gpsimd")
    BAR()
    yoflat = yo_ag.rearrange("c n d -> (c n) d")
    g1acc = cast_f32[:, 0:1024]
    gsil2 = cast_f32[:, 1024:2048]
    ybg1 = camax[:].bitcast(BF16)[:, 0:1024]
    ybg2 = csum[:].bitcast(BF16)[:, 0:1024]
    s1sb = camax[:].bitcast(BF16)[:, 0:1024]
    tv_fin = 0
    to_st = 0
    for i in list(range(1, NTILE)) + [0]:
        l = i // 4
        WT("gpsimd", "vector", tv_fin)
        tg1 = GDMA(ybg1, yoflat, in_off=IndirectOffsetOnAxis(sid0_i[:, i:i+1], 0))
        tg2 = GDMA(ybg2, yoflat, in_off=IndirectOffsetOnAxis(sid1_i[:, i:i+1], 0))
        WG("vector", tg1)
        E("vector", lambda e, i=i: e.tensor_scalar(out=g1acc, in0=ybg1, scalar1=keep0[:, i:i+1], scalar2=0.0, op0=ALU.mult, op1=ALU.add))
        WG("vector", tg2)
        E("vector", lambda e, i=i: e.scalar_tensor_tensor(out=g1acc, in0=ybg2, scalar=keep1[:, i:i+1], in1=g1acc, op0=ALU.mult, op1=ALU.add))
        tv = SIG("vector")
        WT("sync", "vector", tv)
        tl1 = DMAL(x_tile[:, 0, :], x1_dram[i, :, :])
        tl2 = DMAL(s1sb.rearrange("p (q d) -> p q d", d=D)[0:64, 0, :], s1_dram[2*(i % 4), l*64:l*64+64, :])
        tl3 = DMAL(s1sb.rearrange("p (q d) -> p q d", d=D)[64:128, 0, :], s1_dram[2*(i % 4)+1, l*64:l*64+64, :])
        WT("scalar", "vector", tv)
        E("scalar", lambda e: e.activation(out=gsil2, in_=g1acc, func=AF.Silu))
        tsc = SIG("scalar")
        WL("vector", tl2)
        WL("vector", tl3)
        WL("vector", tl1)
        WT("vector", "scalar", tsc)
        WS("vector", to_st)
        E("vector", lambda e: e.tensor_tensor(out=gsil2, in0=s1sb, in1=gsil2, op=ALU.mult))
        E("vector", lambda e: e.tensor_tensor(out=x_tile[:, 1, :], in0=x_tile[:, 0, :], in1=gsil2, op=ALU.add))
        tv_fin = SIG("vector")
        WT("sync", "vector", tv_fin)
        to_st = DMAS(out_dram[i, :, :], x_tile[:, 1, :])
    BAR()
    WS("sync", s_st.n)
    BAR()

    @block.sync
    def _(eng):
        for f in C.q["sync"]:
            f(eng)

    @block.vector
    def _(eng):
        for f in C.q["vector"]:
            f(eng)

    @block.scalar
    def _(eng):
        for f in C.q["scalar"]:
            f(eng)

    @block.tensor
    def _(eng):
        for f in C.q["tensor"]:
            f(eng)

    @block.gpsimd
    def _(eng):
        for f in C.q["gpsimd"]:
            f(eng)

    st.close()
    return nc




_CACHE = {}


def _host_prep(ins):
    x = np.ascontiguousarray(ins["L0_attn_hidden"], np.float32)
    gw = ins["L0_ffn_gate_w"]; uw = ins["L0_ffn_up_w"]; dw = ins["L0_ffn_down_w"]
    gpart = np.transpose(np.ascontiguousarray(gw, np.float32).reshape(16, 512, 8, 128), (0, 3, 2, 1)).reshape(16, 128, 4096)
    upart = np.transpose(np.ascontiguousarray(uw, np.float32).reshape(16, 512, 8, 128), (0, 3, 2, 1)).reshape(16, 128, 4096)
    dpart = np.transpose(np.ascontiguousarray(dw, np.float32).reshape(16, 1024, 4, 128), (0, 3, 2, 1)).reshape(16, 128, 4096)
    wcat = np.ascontiguousarray(np.concatenate([gpart, upart, dpart], axis=2), np.float32)
    shg = np.ascontiguousarray(np.transpose(np.asarray(ins["L0_ffn_sh_gate_w"], np.float32).reshape(1024, 8, 128), (2, 1, 0)))
    shu = np.ascontiguousarray(np.transpose(np.asarray(ins["L0_ffn_sh_up_w"], np.float32).reshape(1024, 8, 128), (2, 1, 0)))
    shd = np.ascontiguousarray(np.transpose(np.asarray(ins["L0_ffn_sh_down_w"], np.float32).reshape(1024, 8, 128), (2, 1, 0)))
    wqT = np.ascontiguousarray(np.transpose(np.asarray(ins["L1_attn_Wq"], np.float32).reshape(1024, 8, 128), (2, 1, 0)))
    wkT = np.ascontiguousarray(np.transpose(np.asarray(ins["L1_attn_Wk"], np.float32).reshape(1024, 8, 128), (2, 1, 0)))
    wvT = np.ascontiguousarray(np.transpose(np.asarray(ins["L1_attn_Wv"], np.float32).reshape(1024, 8, 128), (2, 1, 0)))
    pe = np.asarray(ins["L1_attn_pe"], np.float32)
    pe_arr = np.zeros((88, 128, 16), np.float32)
    selc = np.zeros((8, 128, 16), np.float32)
    rsel = np.zeros((16, 8, 128), np.float32)
    for c8 in range(8):
        for hh in range(2):
            h = 2*c8 + hh
            for j in range(11):
                pe_arr[j*8+c8, 64*hh:64*hh+64, h] = pe[h, :, j]
            selc[c8, 64*hh:64*hh+64, h] = 1.0
            rsel[h, c8, 64*hh:64*hh+64] = 1.0
    selc_bf = selc.astype(ml_dtypes.bfloat16)
    rsel_bf = rsel.astype(ml_dtypes.bfloat16)
    rw1T = np.ascontiguousarray(np.transpose(np.asarray(ins["L1_ffn_rw1"], np.float32).reshape(8, 128, 4096), (1, 0, 2)))
    rw2T = np.ascontiguousarray(np.asarray(ins["L1_ffn_rw2"], np.float32).reshape(32, 128, 8))
    rb1h = np.ascontiguousarray(np.asarray(ins["L1_ffn_rb1"], np.float32).reshape(32, 128).T)
    rb2h = np.asarray(ins["L1_ffn_rb2"], np.float32).reshape(8, 1)
    tri8 = (np.arange(8)[:, None] < np.arange(8)[None, :]).astype(np.float32).reshape(64)
    trib8h = np.tile(tri8, (128, 1)).astype(np.float32)
    ew1h = np.ascontiguousarray(np.transpose(np.asarray(ins["L1_ffn_ew1"], np.float32).reshape(8, 8, 128, 4096), (0, 2, 1, 3)))
    ew2h = np.ascontiguousarray(np.transpose(np.asarray(ins["L1_ffn_ew2"], np.float32).reshape(8, 32, 128, 1024), (0, 2, 1, 3)))
    eb1h = np.ascontiguousarray(np.transpose(np.asarray(ins["L1_ffn_eb1"], np.float32).reshape(8, 32, 128), (0, 2, 1)))
    eb2h = np.asarray(ins["L1_ffn_eb2"], np.float32)
    per_core = []
    for r in range(NCORE):
        xs = np.zeros((NB, SEG, D), np.float32)
        t0 = r * TC
        xs[:, 1:, :] = x[:, t0:t0+TC, :]
        if r > 0:
            xs[:, 0, :] = x[:, t0-1, :]
        m = {
            "x_in": xs,
            "w_key": np.asarray(ins["L0_attn_w_key"], np.float32),
            "w_value": np.asarray(ins["L0_attn_w_value"], np.float32),
            "w_recep": np.asarray(ins["L0_attn_w_receptance"], np.float32),
            "w_output": np.asarray(ins["L0_attn_w_output"], np.float32),
            "rms_key": np.asarray(ins["L0_attn_rms_key"], np.float32),
            "rms_value": np.asarray(ins["L0_attn_rms_value"], np.float32),
            "rms_recep": np.asarray(ins["L0_attn_rms_receptance"], np.float32),
            "rms_output": np.asarray(ins["L0_attn_rms_output"], np.float32),
            "tmix_k": np.asarray(ins["L0_attn_time_mix_key"], np.float32).ravel(),
            "tmix_v": np.asarray(ins["L0_attn_time_mix_value"], np.float32).ravel(),
            "tmix_r": np.asarray(ins["L0_attn_time_mix_receptance"], np.float32).ravel(),
            "tdecay": np.asarray(ins["L0_attn_time_decay"], np.float32),
            "tfirst": np.asarray(ins["L0_attn_time_first"], np.float32),
            "rankm": (np.arange(NCORE) < r).astype(np.float32),
            "rankw": np.maximum(r - 1 - np.arange(NCORE), 0).astype(np.float32),
            "wcat_in": wcat,
            "shg_in": shg, "shu_in": shu, "shd_in": shd,
            "rw_in": np.asarray(ins["L0_ffn_router_weight"], np.float32),
            "ebias_in": np.asarray(ins["L0_ffn_e_bias"], np.float32),
            "wqT_in": wqT, "wkT_in": wkT, "wvT_in": wvT,
            "pe_in": pe_arr, "selc_in": selc_bf, "rsel_in": rsel_bf,
            "rw1T_in": rw1T, "rw2T_in": rw2T, "rb1_in": rb1h, "rb2_in": rb2h,
            "trib8_in": trib8h,
            "mskc512_in": np.tile((np.arange(8) < r).astype(np.float32)[:, None], (1, 64)).reshape(1, 512),
            "corebase_in": np.array([[r * 4096.0]], np.float32),
            "ew1T_in": ew1h[r], "ew2T_in": ew2h[r], "eb1_in": eb1h[r],
            "eb2_in": np.tile(eb2h[r], (128, 1)),
        }
        per_core.append(m)
    return per_core


def kernel(**inputs):
    if "nc" not in _CACHE:
        _CACHE["nc"] = build_program()
    nc = _CACHE["nc"]
    per_core = _host_prep(inputs)
    res = run_bass_kernel_spmd(nc, per_core, core_ids=list(range(NCORE)))
    full = np.empty((NB, NTOK, D), np.float32)
    for r in range(NCORE):
        o = np.asarray(res.results[r]["out_dram"], np.float32).reshape(NB, TC, D)
        full[:, r*TC:(r+1)*TC, :] = o
    return full


# version 04 (final)
# speedup vs baseline: 1.0712x; 1.0153x over previous
import contextlib
import numpy as np
import ml_dtypes
import concourse.bass as bass
import concourse.mybir as mybir
from concourse.bass import IndirectOffsetOnAxis
from concourse.bass_utils import run_bass_kernel_spmd

F32 = mybir.dt.float32
BF16 = mybir.dt.bfloat16
I32 = mybir.dt.int32
AF = mybir.ActivationFunctionType
ALU = mybir.AluOpType
AX = mybir.AxisListType

NB, TC, D, NCH = 8, 512, 1024, 8
SEG = TC + 1
NTOK = NB * TC
NTILE = 32
MAGIC = 1.5 * 2.0 ** 23
RMS_EPS = 1e-8
LN_EPS = 1e-5
ENGINES = ("sync", "vector", "scalar", "tensor", "gpsimd")
NCORE = 8


class Sem:
    def __init__(self, h):
        self.h, self.n = h, 0


AUTO_SYNC = ("vector", "scalar", "gpsimd")


class Ctx:
    def __init__(self):
        self.q = {e: [] for e in ENGINES}
        self.sems = {}

    def emit(self, eng, fn):
        if eng in AUTO_SYNC:
            sem = self.sems[eng]
            prev = sem.n
            sem.n += 1
            def wrapped(e, f=fn, s=sem.h, prev=prev):
                if prev > 0:
                    e.wait_ge(s, prev)
                f(e).then_inc(s, 1)
            self.q[eng].append(wrapped)
        else:
            self.q[eng].append(fn)

    def sig(self, eng, k=1):
        if eng in AUTO_SYNC:
            return self.sems[eng].n
        sem = self.sems[eng]
        f = self.q[eng].pop()
        sem.n += k
        self.q[eng].append(lambda e, f=f, s=sem.h, k=k: f(e).then_inc(s, k))
        return sem.n

    def wait(self, eng, other, val=None):
        sem = self.sems[other]
        v = sem.n if val is None else val
        if v > 0:
            self.q[eng].append(lambda e, s=sem.h, v=v: e.wait_ge(s, v))

    def barrier(self):
        vals = {e: self.sems[e].n for e in ENGINES}
        for e1 in ENGINES:
            for e2 in ENGINES:
                if e1 != e2 and vals[e2] > 0:
                    self.q[e1].append(lambda e, s=self.sems[e2].h, v=vals[e2]: e.wait_ge(s, v))
            for sx in self.dma_sems:
                if sx.n > 0:
                    self.q[e1].append(lambda e, s=sx.h, v=sx.n: e.wait_ge(s, v))


def build_program():
    nc = bass.Bass("TRN2", target_bir_lowering=False, debug=False)
    C = Ctx()
    st = contextlib.ExitStack()
    din = lambda n, s, d=F32: nc.dram_tensor(n, s, d, kind="ExternalInput").ap()
    dou = lambda n, s, d=F32: nc.dram_tensor(n, s, d, kind="ExternalOutput").ap()
    dit = lambda n, s, d=F32: nc.dram_tensor(n, s, d, kind="Internal").ap()
    dsh = lambda n, s, d=F32: nc.dram_tensor(n, s, d, kind="Internal", addr_space="Shared").ap()
    sb = lambda n, s, d=F32: st.enter_context(nc.sbuf_tensor(n, s, d))
    psm = lambda n, s, d=F32: st.enter_context(nc.psum_tensor(n, s, d))
    E, SIG, WT, BAR = C.emit, C.sig, C.wait, C.barrier

    x_in = din("x_in", [NB, SEG, D])
    w_mats = [din(f"w_{m}", [D, D]) for m in ("key", "value", "recep", "output")]
    rms_in = [din(f"rms_{m}", [D]) for m in ("key", "value", "recep", "output")]
    tmix_in = [din(f"tmix_{m}", [D]) for m in ("k", "v", "r")]
    tdecay = din("tdecay", [D]); tfirst = din("tfirst", [D])
    rankm = din("rankm", [NCORE])
    rankw = din("rankw", [NCORE])
    wcat_in = din("wcat_in", [16, 128, 12288])
    shg_in = din("shg_in", [128, NCH, D])
    shu_in = din("shu_in", [128, NCH, D])
    shd_in = din("shd_in", [128, NCH, D])
    rw_in = din("rw_in", [16, D])
    ebias_in = din("ebias_in", [16])

    kv_spill = dit("kv_spill", [2, NCH, 128, NTOK], F32)
    cc_in = dit("cc_in", [128, 128], F32)
    cc_out = dit("cc_out", [NCORE, 128, 128], F32)
    xn_dram = dit("xn_dram", [NTILE + 1, 128, D], BF16)
    xnf_dram = dit("xnf_dram", [128, NCH, NTOK], BF16)
    wq_dram = dit("wq_dram", [4, 128, NCH * D], BF16)
    r_dram = dit("r_dram", [128, NCH, NTOK], BF16)
    s0_dram = dit("s0_dram", [NTILE, 128, D], F32)
    wcat_bf = dit("wcat_bf", [16, 128, 12288], BF16)
    NSLOT = 48 * 512
    slotbuf = dit("slotbuf", [NSLOT + 1, D], BF16)
    yobuf = dit("yobuf", [NSLOT + 1, D], BF16)
    x1_dram = dit("x1_dram", [NTILE, 128, D], F32)

    wqT_in = din("wqT_in", [128, NCH, D])
    wkT_in = din("wkT_in", [128, NCH, D])
    wvT_in = din("wvT_in", [128, NCH, D])
    pe_in = din("pe_in", [88, 128, 16])
    selc_in = din("selc_in", [8, 128, 16], BF16)
    rsel_in = din("rsel_in", [16, NCH, 128], BF16)
    xn1_dram = dit("xn1_dram", [NTILE, 128, D], BF16)
    xnf1_dram = dit("xnf1_dram", [128, NCH, NTOK], BF16)
    s1_dram = dit("s1_dram", [8, TC, D], BF16)
    rw1T_in = din("rw1T_in", [128, NCH, 4096])
    rw2T_in = din("rw2T_in", [32, 128, 8])
    rb1_in = din("rb1_in", [128, 32])
    rb2_in = din("rb2_in", [8, 1])
    trib8_in = din("trib8_in", [128, 64])
    mskc512_in = din("mskc512_in", [1, 512])
    corebase_in = din("corebase_in", [1, 1])
    ew1T_in = din("ew1T_in", [128, NCH, 4096])
    ew2T_in = din("ew2T_in", [128, 32, D])
    eb1_in = din("eb1_in", [128, 32])
    eb2_in = din("eb2_in", [128, D])
    xn1_ag = dit("xn1_ag", [NCORE, NTILE * 128, D], BF16)
    cnt2_in = dit("cnt2_in", [64, 1], F32)
    cnt2_out = dit("cnt2_out", [NCORE, 64, 1], F32)
    dispT = dit("dispT", [40960, 2], F32)
    disp_rs = dit("disp_rs", [5120, 2], F32)
    ew1_bf = dit("ew1_bf", [128, NCH, 4096], BF16)
    ew2_bf = dit("ew2_bf", [128, 32, D], BF16)
    h_dram = dit("h_dram", [8, 32, 128, TC], BF16)
    yo_dram = dit("yo_dram", [NTOK, D], BF16)
    yo_ag = dit("yo_ag", [NCORE, NTOK, D], BF16)
    out_dram = dou("out_dram", [NTILE, 128, D], F32)

    io_col = sb("io_col", [128, 128]); io_row = sb("io_row", [128, 128])
    ident = sb("ident", [128, 128], BF16)
    identf = sb("identf", [128, 128])
    onesNf = sb("onesNf", [1, 128])
    ones4 = sb("ones4", [4, 128])
    onesKb = sb("onesKb", [128, 1], BF16)
    zcol = sb("zcol", [128, 1])
    rmsv = sb("rmsv", [128, 4 * NCH])
    tmixv = sb("tmixv", [128, 3 * NCH])
    tmixm1 = sb("tmixm1", [128, 3 * NCH])
    xnb_sb = sb("xnb_sb", [128, NCH, NB], BF16)
    tdec = sb("tdec", [128, NCH]); tfir = sb("tfir", [128, NCH])
    expu = sb("expu", [128, NCH])
    wlog = sb("wlog", [128, NCH])
    Dmat = sb("Dmat", [128, NCH, TC])
    powm = sb("powm", [128, NCH, TC])
    kv_t = sb("kv_t", [128, 2, NCH, TC])
    wq_sb = sb("wq_sb", [128, 2, NCH, D], BF16)
    inv_sw = sb("inv_sw", [128, 4])
    sw_rep = sb("sw_rep", [128, 4])
    wstage = sb("wstage", [128, NCH, D], BF16)
    d_sh = wstage[:, :, 0:TC]
    m_t = wstage[:, :, TC:2*TC]
    wabs = sb("wabs", [128, 2])
    wabs_bf = sb("wabs_bf", [128, 1], BF16)
    wm_sb = sb("wm_sb", [1, 1])

    xn_st = sb("xn_st", [128, 2, D], BF16)
    xnf_blk = sb("xnf_blk", [128, 2, NCH, 128], BF16)
    xnf_seg = sb("xnf_seg", [128, NCH, TC], BF16)
    x_tile = sb("x_tile", [128, 2, D])
    sqf = sb("sqf", [128, 2, D])
    s1_all = sb("s1_all", [128, NTILE + 1]); s2_all = sb("s2_all", [128, NTILE + 1])
    rstd_all = sb("rstd_all", [128, NTILE + 1]); nmu_all = sb("nmu_all", [128, NTILE + 1])

    mix_t = sb("mix_t", [128, NCH, TC], BF16)
    camax = sb("camax", [128, TC]); csum = sb("csum", [128, TC])
    stat_tm = sb("stat_tm", [128, 8])
    rs_tm = sb("rs_tm", [128, 4]); isx_tm = sb("isx_tm", [128, 4])
    stag4 = sb("stag4", [128, 128])
    rhs512 = sb("rhs512", [128, 512])
    onesF = sb("onesF", [128, 128])
    rs_fm = sb("rs_fm", [128, TC]); isx_fm = sb("isx_fm", [128, TC])
    rnd_f = sb("rnd_f", [128, TC])
    csum2 = rnd_f
    xq_t = sb("xq_t", [128, NCH, TC], BF16)

    r_seg = sb("r_seg", [128, NCH, TC], BF16)
    e_t = sb("e_t", [128, TC]); ev_t = sb("ev_t", [128, TC])
    a_scan = sb("a_scan", [128, TC]); b_scan = sb("b_scan", [128, TC])
    scan_ar = sb("scan_ar", [128, 4, TC])
    eu_t = scan_ar[:, 0, :]; evu_t = scan_ar[:, 1, :]
    n1_t = scan_ar[:, 2, :]; dd_t = scan_ar[:, 3, :]
    gin_t = mix_t
    aend = sb("aend", [128, 2 * NCH * NB])
    carry = sb("carry", [128, 2 * NCH * NB])
    ccg = sb("ccg", [128, NCORE, 128])
    rk_sb = sb("rk_sb", [1, 2 * NCORE])
    rkrep = sb("rkrep", [128, 2 * NCORE])
    powr = sb("powr", [128, NCORE, NCH])
    s0_seg = kv_t[:, 0, :, :].rearrange("p c t -> p (c t)").rearrange("p (s d) -> p s d", s=4)

    scores_tm = scan_ar[:, 0, :].rearrange("p (i e) -> p i e", e=16)
    S_tm = scan_ar[:, 1, :].rearrange("p (i e) -> p i e", e=16)
    cw_tm = scan_ar[:, 2, :].rearrange("p (i e) -> p i e", e=16)
    rnk_sc = rs_fm[:].rearrange("p (i e) -> p i e", e=16)
    rank16 = isx_fm[:].rearrange("p (i e) -> p i e", e=16)
    xq_f32 = xq_t[:].rearrange("p c t -> p (c t)").bitcast(F32)
    ones128s = xq_f32[:, 0:1024]
    cmp_sc = xq_f32[:, 1024:1536].rearrange("p (i e) -> p i e", e=16)
    tri16b = xq_f32[:, 1536:1792].rearrange("p (i j) -> p i j", j=16)
    iota16 = xq_f32[:, 1792:1808]
    tri4b = xq_f32[:, 1808:1824].rearrange("p (i j) -> p i j", j=4)
    gsc4 = csum[:, 0:128].rearrange("p (i g) -> p i g", g=4)
    grk4 = csum[:, 128:256].rearrange("p (i g) -> p i g", g=4)
    powm_f32 = powm[:].rearrange("p c t -> p (c t)")
    powm_i32 = powm[:].rearrange("p c t -> p (c t)").bitcast(I32)
    powm_bf = powm[:].rearrange("p c t -> p (c t)").bitcast(BF16)
    tri16a = powm_f32[:, 3072:3328].rearrange("p (i j) -> p i j", j=16)
    tri4a = powm_f32[:, 3328:3344].rearrange("p (i j) -> p i j", j=4)
    ebias_rep = powm_f32[:, 3344:3360]
    rwT = powm_bf[:, 6784:6912].rearrange("p (c e) -> p c e", e=16)
    rw_sb = wstage[0:16, :, :].rearrange("p c d -> p (c d)").bitcast(F32)[:, 1024:2048]
    rw_bf = mix_t[0:16, :, :].rearrange("p c t -> p (c t)")[:, 2048:3072]
    rseg_f32 = r_seg[:].rearrange("p c t -> p (c t)").bitcast(F32)
    Me128 = rseg_f32[:, 0:1024]; pos128 = rseg_f32[:, 1024:2048]
    segsum = powm_f32[:, 3400:3401]; segoff = powm_f32[:, 3401:3402]
    strip1 = Dmat[0:1, :, :].rearrange("p c t -> p (c t)")[:, 0:656]
    sjf = powm_f32[:, 3424:3552].rearrange("p (i j) -> p i j", j=4)
    sji = powm_i32[:, 3728:3856].rearrange("p (i j) -> p i j", j=4)
    gj_tm = powm_f32[:, 3552:3680].rearrange("p (i j) -> p i j", j=4)
    texp_f = powm_f32[:, 3680:3728]
    widx = powm_i32[:, 3856:3904]
    wbuf = kv_t[:].rearrange("p a c t -> p (a c t)").bitcast(BF16)[:, 0:12288]
    xgF = powm[:].rearrange("p c t -> p (c t)").bitcast(BF16)[:, 0:4096].rearrange("p (c t) -> p c t", t=TC)
    h_rt = powm[:].rearrange("p c t -> p (c t)").bitcast(BF16)[:, 4096:6144].rearrange("p (c t) -> p c t", t=TC)
    h_sh = kv_t[:].rearrange("p a c t -> p (a c t)").bitcast(BF16)[:, 12288:16384].rearrange("p (c t) -> p c t", t=TC)
    xg_tm = wstage[:, 0:4, :]
    g0acc = Dmat[:].rearrange("p c t -> p (c t)")[:, 0:4096].rearrange("p (a d) -> p a d", d=D)
    scan_bf = scan_ar[:].rearrange("p a t -> p (a t)").bitcast(BF16)
    peT = scan_bf[:, 0:1408].rearrange("p (j h) -> p j h", h=16)
    selT = scan_bf[:, 1408:1536].rearrange("p (c h) -> p c h", h=16)
    rselT = e_t[0:16, :].bitcast(BF16)[:, 0:1024].rearrange("p (c d) -> p c d", d=128)
    psA = psm("psA", [128, 512]); psB = psm("psB", [128, 512])
    psT = psm("psT", [128, 128], BF16); psU = psm("psU", [128, 128], BF16)
    psTf = psm("psTf", [128, 128])
    psR = psm("psR", [128, 512])
    psS = psR[0:1, :]
    psW = psm("psW", [128, 1024])

    for e in ENGINES:
        C.sems[e] = Sem(st.enter_context(nc.semaphore("sem_" + e)))
    s_ld = Sem(st.enter_context(nc.semaphore("s_ld")))
    s_st = Sem(st.enter_context(nc.semaphore("s_st")))
    C.dma_sems = (s_ld, s_st)
    block = st.enter_context(nc.Block())

    def DMAL(out, in_, **kw):
        s_ld.n += 16
        C.q["sync"].append(lambda e, out=out, in_=in_, kw=kw: e.dma_start(out=out, in_=in_, **kw).then_inc(s_ld.h, 16))
        return s_ld.n

    def DMAS(out, in_, **kw):
        s_st.n += 16
        C.q["sync"].append(lambda e, out=out, in_=in_, kw=kw: e.dma_start(out=out, in_=in_, **kw).then_inc(s_st.h, 16))
        return s_st.n

    def WL(eng, val):
        C.q[eng].append(lambda e, v=val: e.wait_ge(s_ld.h, v))

    def WS(eng, val):
        C.q[eng].append(lambda e, v=val: e.wait_ge(s_st.h, v))

    twk = kv_t[:, 0, :, :]
    E("gpsimd", lambda e: e.iota(out=io_col[:], pattern=[[1, 128]], base=0, channel_multiplier=0, allow_small_or_imprecise_dtypes=True))
    E("gpsimd", lambda e: e.iota(out=io_row[:], pattern=[[0, 128]], base=0, channel_multiplier=1, allow_small_or_imprecise_dtypes=True))
    E("gpsimd", lambda e: e.iota(out=twk.rearrange("p c t -> p (c t)"), pattern=[[0, NCH], [1, TC]], base=0, channel_multiplier=0, allow_small_or_imprecise_dtypes=True))
    SIG("gpsimd")
    for vec, dst in (
        (rms_in[0], rmsv[:, 0:NCH]), (rms_in[1], rmsv[:, NCH:2*NCH]),
        (rms_in[2], rmsv[:, 2*NCH:3*NCH]), (rms_in[3], rmsv[:, 3*NCH:4*NCH]),
        (tmix_in[0], tmixv[:, 0:NCH]), (tmix_in[1], tmixv[:, NCH:2*NCH]), (tmix_in[2], tmixv[:, 2*NCH:3*NCH]),
        (tdecay, tdec[:]), (tfirst, tfir[:]),
    ):
        DMAL(dst, vec.rearrange("(c p) -> p c", p=128), allow_slow_non_contiguous=True)
    DMAL(rk_sb[0:1, 0:NCORE], rankm[None, :], allow_slow_non_contiguous=True)
    DMAL(rk_sb[0:1, NCORE:2*NCORE], rankw[None, :], allow_slow_non_contiguous=True)
    WL("vector", s_ld.n)
    WT("vector", "gpsimd")
    E("vector", lambda e: e.tensor_tensor(out=ident[:], in0=io_col[:], in1=io_row[:], op=ALU.is_equal))
    E("vector", lambda e: e.tensor_tensor(out=identf[:], in0=io_col[:], in1=io_row[:], op=ALU.is_equal))
    E("vector", lambda e: e.memset(onesNf[:], 1.0))
    E("vector", lambda e: e.memset(ones4[:], 1.0))
    E("vector", lambda e: e.memset(onesF[:], 1.0))
    E("vector", lambda e: e.memset(stag4[:], 0.0))
    E("vector", lambda e: e.memset(rhs512[:], 0.0))
    E("vector", lambda e: e.memset(onesKb[:], 1.0))
    E("vector", lambda e: e.memset(zcol[:], 0.0))
    E("vector", lambda e: e.tensor_scalar(out=tmixm1[:], in0=tmixv[:], scalar1=-1.0, scalar2=None, op0=ALU.add))
    SIG("vector")
    WL("scalar", s_ld.n)
    E("scalar", lambda e: e.activation(out=wlog[:], in_=tdec[:], func=AF.Exp))
    E("scalar", lambda e: e.activation(out=expu[:], in_=tfir[:], func=AF.Exp))
    SIG("scalar")
    WT("vector", "scalar")
    E("vector", lambda e: e.tensor_scalar(out=wlog[:], in0=wlog[:], scalar1=-1.0, scalar2=None, op0=ALU.mult))
    SIG("vector")
    for c in range(NCH):
        E("vector", lambda e, c=c: e.tensor_scalar(out=twk[:, c, :], in0=twk[:, c, :], scalar1=wlog[:, c:c+1], scalar2=None, op0=ALU.mult))
    SIG("vector")
    WT("scalar", "vector")
    E("scalar", lambda e: e.activation(out=powm[:].rearrange("p c t -> p (c t)"), in_=twk.rearrange("p c t -> p (c t)"), func=AF.Exp))
    for c in range(NCH):
        E("scalar", lambda e, c=c: e.activation(out=Dmat[:, c, :], in_=twk[:, c, 1:2].broadcast_to([128, TC]), func=AF.Exp))
    SIG("scalar")
    BAR()

    for wi in range(4):
        wsrc = w_mats[wi]
        for i in range(NCH):
            if i >= 2:
                WT("sync", "vector")
            tl = DMAL(x_tile[:, i % 2, :], wsrc[128*i:128*(i+1), :])
            WL("vector", tl)
            if i == 0:
                E("vector", lambda e: e.tensor_reduce(out=wabs[:, 0:1], in_=x_tile[:, 0, :], axis=AX.X, op=ALU.add, apply_absolute_value=True))
            else:
                E("vector", lambda e, i=i: e.tensor_reduce(out=wabs[:, 1:2], in_=x_tile[:, i % 2, :], axis=AX.X, op=ALU.add, apply_absolute_value=True))
                E("vector", lambda e: e.tensor_add(out=wabs[:, 0:1], in0=wabs[:, 0:1], in1=wabs[:, 1:2]))
            SIG("vector")
        E("vector", lambda e: e.tensor_copy(out=wabs_bf[:], in_=wabs[:, 0:1]))
        SIG("vector")
        BAR()
        E("tensor", lambda e: e.matmul(psS[0:1, 0:1], wabs_bf[:], onesKb[:], start=True, stop=True))
        SIG("tensor")
        WT("vector", "tensor")
        E("vector", lambda e: e.tensor_scalar(out=wm_sb[0:1, 0:1], in0=psS[0:1, 0:1], scalar1=1.0/(D*D), scalar2=1e-5, op0=ALU.mult, op1=ALU.max))
        SIG("vector")
        WT("tensor", "vector")
        E("tensor", lambda e: e.matmul(psR[:, 0:1], onesNf[:], wm_sb[0:1, 0:1], start=True, stop=True))
        SIG("tensor")
        WT("vector", "tensor")
        E("vector", lambda e, wi=wi: e.tensor_copy(out=inv_sw[:, wi:wi+1], in_=psR[:, 0:1]))
        E("vector", lambda e, wi=wi: e.reciprocal(out=sw_rep[:, wi:wi+1], in_=psR[:, 0:1]))
        SIG("vector")
        for i in range(NCH):
            WT("sync", "vector")
            tl = DMAL(x_tile[:, i % 2, :], wsrc[128*i:128*(i+1), :])
            WL("vector", tl)
            E("vector", lambda e, i=i, wi=wi: e.tensor_scalar(out=sqf[:, i % 2, :], in0=x_tile[:, i % 2, :], scalar1=sw_rep[:, wi:wi+1], scalar2=MAGIC, op0=ALU.mult, op1=ALU.add))
            E("vector", lambda e, i=i: e.tensor_scalar(out=sqf[:, i % 2, :], in0=sqf[:, i % 2, :], scalar1=MAGIC, scalar2=None, op0=ALU.subtract))
            E("vector", lambda e, i=i: e.tensor_scalar(out=wstage[:, i, :], in0=sqf[:, i % 2, :], scalar1=1.0, scalar2=-1.0, op0=ALU.min, op1=ALU.max))
            SIG("vector")
        BAR()
        slot = wi % 2
        for c in range(NCH):
            for dd2 in range(NCH):
                pp = psT if (c * NCH + dd2) % 2 == 0 else psU
                if c * NCH + dd2 >= 2:
                    WT("tensor", "scalar", C.sems["scalar"].n - 1)
                E("tensor", lambda e, c=c, dd2=dd2, pp=pp: e.transpose(pp[:], wstage[:, dd2, 128*c:128*(c+1)], ident[:]))
                tm = SIG("tensor")
                WT("scalar", "tensor", tm)
                E("scalar", lambda e, c=c, dd2=dd2, pp=pp, slot=slot: e.copy(out=wq_sb[:, slot, c, 128*dd2:128*(dd2+1)], in_=pp[:]))
                SIG("scalar")
        WT("sync", "scalar")
        DMAS(wq_dram[wi, :, :], wq_sb[:, slot, :, :].rearrange("p c d -> p (c d)"))
        BAR()

    def x_src_dst(i):
        if i < NTILE:
            b = i // 4; toff = (i % 4) * 128
            return x_in[b, 1+toff:1+toff+128, :], x_tile[:, i % 2, :]
        return x_in[:, 0, :], x_tile[0:NB, i % 2, :]

    for i in range(NTILE + 1):
        src, dst = x_src_dst(i)
        if i >= 2:
            WT("sync", "scalar"); WT("sync", "vector")
        tl = DMAL(dst, src)
        WL("scalar", tl)
        E("scalar", lambda e, i=i: e.activation(out=sqf[:, i % 2, :], in_=x_tile[:, i % 2, :], func=AF.Square, accum_out=s2_all[:, i:i+1]))
        SIG("scalar")
        WL("vector", tl)
        E("vector", lambda e, i=i: e.reduce_sum(out=s1_all[:, i:i+1], in_=x_tile[:, i % 2, :], axis=AX.X))
        SIG("vector")
    BAR()
    E("vector", lambda e: e.tensor_scalar(out=s1_all[:], in0=s1_all[:], scalar1=1.0/D, scalar2=None, op0=ALU.mult))
    E("vector", lambda e: e.tensor_scalar(out=s2_all[:], in0=s2_all[:], scalar1=1.0/D, scalar2=None, op0=ALU.mult))
    SIG("vector")
    WT("scalar", "vector")
    E("scalar", lambda e: e.activation(out=rstd_all[:], in_=s1_all[:], func=AF.Square))
    SIG("scalar")
    WT("vector", "scalar")
    E("vector", lambda e: e.tensor_sub(out=rstd_all[:], in0=s2_all[:], in1=rstd_all[:]))
    E("vector", lambda e: e.tensor_scalar(out=rstd_all[:], in0=rstd_all[:], scalar1=LN_EPS, scalar2=None, op0=ALU.add))
    SIG("vector")
    WT("scalar", "vector")
    E("scalar", lambda e: e.activation(out=rstd_all[:], in_=rstd_all[:], func=AF.Sqrt))
    SIG("scalar")
    WT("vector", "scalar")
    E("vector", lambda e: e.reciprocal(out=rstd_all[:], in_=rstd_all[:]))
    E("vector", lambda e: e.tensor_scalar(out=nmu_all[:], in0=s1_all[:], scalar1=-1.0, scalar2=None, op0=ALU.mult))
    E("vector", lambda e: e.tensor_tensor(out=nmu_all[:], in0=nmu_all[:], in1=rstd_all[:], op=ALU.mult))
    SIG("vector")
    WT("sync", "vector")
    BAR()
    xn_store_t = {}
    xnf_store_t = {}
    for i in range(NTILE + 1):
        src, dst = x_src_dst(i)
        if i >= 2:
            WT("sync", "scalar")
        tl = DMAL(dst, src)
        WL("scalar", tl)
        if i >= 2:
            WS("scalar", xn_store_t[i - 2])
        E("scalar", lambda e, i=i: e.activation(out=xn_st[:, i % 2, :], in_=x_tile[:, i % 2, :], func=AF.Identity, bias=nmu_all[:, i:i+1], scale=rstd_all[:, i:i+1]))
        tap = SIG("scalar")
        WT("sync", "scalar", tap)
        xn_store_t[i] = DMAS(xn_dram[i, :, :], xn_st[:, i % 2, :])
        WT("tensor", "scalar", tap)
        for c in range(NCH):
            pp = psT if c % 2 == 0 else psU
            if i > 0 or c >= 2:
                WT("tensor", "scalar", C.sems["scalar"].n - 1)
            E("tensor", lambda e, i=i, c=c, pp=pp: e.transpose(pp[:], xn_st[:, i % 2, 128*c:128*(c+1)], ident[:]))
            tm = SIG("tensor")
            WT("scalar", "tensor", tm)
            if i < NTILE:
                if c == 0 and i >= 2:
                    WS("scalar", xnf_store_t[i - 2])
                E("scalar", lambda e, i=i, c=c, pp=pp: e.copy(out=xnf_blk[:, i % 2, c, :], in_=pp[:]))
            else:
                E("scalar", lambda e, c=c, pp=pp: e.copy(out=xnb_sb[:, c, :], in_=pp[:, 0:NB]))
            SIG("scalar")
        WT("sync", "scalar")
        if i < NTILE:
            b = i // 4; toff = (i % 4) * 128
            xnf_store_t[i] = DMAS(xnf_dram[:, :, b*TC+toff : b*TC+toff+128], xnf_blk[:, i % 2, :, :])
    BAR()

    def quant_chain(rms_col, src_of_chunk):
        for c in range(NCH):
            E("vector", lambda e, c=c, src_of_chunk=src_of_chunk, rms_col=rms_col: e.tensor_scalar(
                out=m_t[:, c, :], in0=src_of_chunk(c), scalar1=rmsv[:, rms_col*NCH+c : rms_col*NCH+c+1], scalar2=None, op0=ALU.mult))
        SIG("vector")
        for c in range(NCH):
            E("scalar", lambda e, c=c, src_of_chunk=src_of_chunk: e.activation(out=sqf[:, c % 2, 0:TC], in_=src_of_chunk(c), func=AF.Square))
            tsq = SIG("scalar")
            WT("vector", "scalar", tsq)
            if c == 0:
                E("vector", lambda e: e.tensor_copy(out=csum[:], in_=sqf[:, 0, 0:TC]))
            else:
                E("vector", lambda e, c=c: e.tensor_add(out=csum[:], in0=csum[:], in1=sqf[:, c % 2, 0:TC]))
            tvv = SIG("vector")
            if c >= 1:
                WT("scalar", "vector", tvv - 1)
        E("vector", lambda e: e.tensor_copy(out=camax[:], in_=m_t[:, 0, :]))
        E("vector", lambda e: e.tensor_copy(out=csum2[:], in_=m_t[:, 0, :]))
        for c in range(1, NCH):
            E("vector", lambda e, c=c: e.tensor_tensor(out=camax[:], in0=camax[:], in1=m_t[:, c, :], op=ALU.max))
            E("vector", lambda e, c=c: e.tensor_tensor(out=csum2[:], in0=csum2[:], in1=m_t[:, c, :], op=ALU.min))
        E("vector", lambda e: e.tensor_scalar(out=csum2[:], in0=csum2[:], scalar1=-1.0, scalar2=None, op0=ALU.mult))
        E("vector", lambda e: e.tensor_tensor(out=camax[:], in0=camax[:], in1=csum2[:], op=ALU.max))
        SIG("vector")
        BAR()
        for blk in range(4):
            for wch, srcbuf in ((0, csum), (1, camax)):
                E("tensor", lambda e, blk=blk, srcbuf=srcbuf: e.transpose(psTf[:], srcbuf[:, 128*blk:128*(blk+1)], identf[:]))
                tm = SIG("tensor")
                WT("vector", "tensor", tm)
                if wch == 0:
                    E("vector", lambda e, blk=blk: e.reduce_sum(out=stat_tm[:, blk:blk+1], in_=psTf[:], axis=AX.X))
                else:
                    E("vector", lambda e, blk=blk: e.reduce_max(out=stat_tm[:, 4+blk:4+blk+1], in_=psTf[:], axis=AX.X))
                tv = SIG("vector")
                WT("tensor", "vector", tv)
        E("vector", lambda e: e.tensor_scalar(out=rs_tm[:], in0=stat_tm[:, 0:4], scalar1=1.0/D, scalar2=RMS_EPS, op0=ALU.mult, op1=ALU.add))
        SIG("vector")
        WT("scalar", "vector")
        E("scalar", lambda e: e.activation(out=rs_tm[:], in_=rs_tm[:], func=AF.Sqrt))
        SIG("scalar")
        WT("vector", "scalar")
        E("vector", lambda e: e.reciprocal(out=rs_tm[:], in_=rs_tm[:]))
        E("vector", lambda e: e.tensor_tensor(out=isx_tm[:], in0=stat_tm[:, 4:8], in1=rs_tm[:], op=ALU.mult))
        E("vector", lambda e: e.tensor_scalar(out=isx_tm[:], in0=isx_tm[:], scalar1=1e-5, scalar2=None, op0=ALU.max))
        E("vector", lambda e: e.reciprocal(out=stat_tm[:, 0:4], in_=isx_tm[:]))
        E("vector", lambda e: e.tensor_tensor(out=rs_tm[:], in0=rs_tm[:], in1=stat_tm[:, 0:4], op=ALU.mult))
        E("vector", lambda e: e.tensor_scalar(out=rs_tm[:], in0=rs_tm[:], scalar1=127.0, scalar2=None, op0=ALU.mult))
        E("vector", lambda e: e.tensor_scalar(out=isx_tm[:], in0=isx_tm[:], scalar1=1.0/127.0, scalar2=None, op0=ALU.mult))
        SIG("vector")
        for stm, fmbuf in ((rs_tm, rs_fm), (isx_tm, isx_fm)):
            BAR()
            for blk in range(4):
                E("vector", lambda e, stm=stm, blk=blk: e.tensor_copy(out=stag4[:, 32*blk:32*blk+1], in_=stm[:, blk:blk+1]))
            tv0 = SIG("vector")
            WT("tensor", "vector", tv0)
            E("tensor", lambda e: e.transpose(psTf[:], stag4[:], identf[:]))
            tm = SIG("tensor")
            WT("vector", "tensor", tm)
            for blk in range(4):
                E("vector", lambda e, blk=blk: e.tensor_copy(out=rhs512[32*blk:32*blk+32, 128*blk:128*(blk+1)], in_=psTf[32*blk:32*blk+32, 0:128]))
            tv = SIG("vector")
            WT("tensor", "vector", tv)
            E("tensor", lambda e: e.matmul(psR[:], onesF[:], rhs512[:], start=True, stop=True))
            tm2 = SIG("tensor")
            WT("vector", "tensor", tm2)
            E("vector", lambda e, fmbuf=fmbuf: e.tensor_copy(out=fmbuf[:], in_=psR[:]))
            SIG("vector")
        BAR()
        for c in range(NCH):
            E("vector", lambda e, c=c: e.tensor_tensor(out=rnd_f[:], in0=m_t[:, c, :], in1=rs_fm[:], op=ALU.mult))
            E("vector", lambda e, c=c: e.tensor_scalar(out=rnd_f[:], in0=rnd_f[:], scalar1=MAGIC, scalar2=MAGIC, op0=ALU.add, op1=ALU.subtract))
            E("vector", lambda e, c=c: e.tensor_copy(out=xq_t[:, c, :], in_=rnd_f[:]))
        SIG("vector")
        BAR()

    def project_fm(wslot, winv, dst_of_dc, sigmoid=False):
        for dc in range(NCH):
            pp = psA if dc % 2 == 0 else psB
            if dc >= 2:
                WT("tensor", "vector", C.sems["vector"].n)
            for c in range(NCH):
                E("tensor", lambda e, dc=dc, c=c, pp=pp, wslot=wslot: e.matmul(pp[:], wq_sb[:, wslot, c, 128*dc:128*(dc+1)], xq_t[:, c, :], start=(c == 0), stop=(c == NCH-1)))
            tm = SIG("tensor")
            WT("vector", "tensor", tm)
            if not sigmoid:
                E("vector", lambda e, dc=dc, pp=pp, winv=winv, dst_of_dc=dst_of_dc: e.scalar_tensor_tensor(out=dst_of_dc(dc), in0=pp[:], scalar=inv_sw[:, winv:winv+1], in1=isx_fm[:], op0=ALU.mult, op1=ALU.mult))
                SIG("vector")
            else:
                E("vector", lambda e, dc=dc, pp=pp, winv=winv: e.scalar_tensor_tensor(out=sqf[:, dc % 2, 0:TC], in0=pp[:], scalar=inv_sw[:, winv:winv+1], in1=isx_fm[:], op0=ALU.mult, op1=ALU.mult))
                tv = SIG("vector")
                WT("scalar", "vector", tv)
                E("scalar", lambda e, dc=dc, dst_of_dc=dst_of_dc: e.activation(out=dst_of_dc(dc), in_=sqf[:, dc % 2, 0:TC], func=AF.Sigmoid))
                SIG("scalar")
        BAR()

    def load_seg_xnf(b):
        return DMAL(xnf_seg[:], xnf_dram[:, :, b*TC:(b+1)*TC])

    def emit_mix(mi):
        for c in range(NCH):
            E("vector", lambda e, c=c, mi=mi: e.scalar_tensor_tensor(
                out=mix_t[:, c, :], in0=d_sh[:, c, :], scalar=tmixm1[:, mi*NCH+c:mi*NCH+c+1],
                in1=xnf_seg[:, c, 0:TC], op0=ALU.mult, op1=ALU.add))
        SIG("vector")

    def emit_dsh(b):
        for c in range(NCH):
            E("vector", lambda e, c=c: e.tensor_sub(out=d_sh[:, c, 1:TC], in0=xnf_seg[:, c, 1:TC], in1=xnf_seg[:, c, 0:TC-1]))
            E("vector", lambda e, c=c, b=b: e.tensor_sub(out=d_sh[:, c, 0:1], in0=xnf_seg[:, c, 0:1], in1=xnb_sb[:, c, b:b+1]))
        SIG("vector")

    DMAL(wq_sb[:, 0, :, :].rearrange("p c d -> p (c d)"), wq_dram[0, :, :])
    DMAL(wq_sb[:, 1, :, :].rearrange("p c d -> p (c d)"), wq_dram[1, :, :])
    WL("tensor", s_ld.n)
    BAR()

    kv_store_t = 0
    for b in range(NB):
        tl = load_seg_xnf(b)
        WL("vector", tl)
        if b > 0:
            WS("vector", kv_store_t)
        emit_dsh(b)
        BAR()
        for mi in range(2):
            emit_mix(mi)
            BAR()
            quant_chain(mi, lambda c: mix_t[:, c, :])
            project_fm(mi, mi, lambda dc, mi=mi: kv_t[:, mi, dc, :])
            BAR()
        WT("sync", "vector")
        for kvi in (0, 1):
            for c in range(NCH):
                kv_store_t = DMAS(kv_spill[kvi, c, :, b*TC:(b+1)*TC], kv_t[:, kvi, c, :])
        for c in range(NCH):
            E("scalar", lambda e, c=c: e.activation(out=e_t[:], in_=kv_t[:, 0, c, :], func=AF.Exp))
            tse = SIG("scalar")
            WT("vector", "scalar", tse)
            E("vector", lambda e, c=c: e.tensor_tensor(out=ev_t[:], in0=e_t[:], in1=kv_t[:, 1, c, :], op=ALU.mult))
            E("vector", lambda e, c=c: e.tensor_tensor_scan(out=a_scan[:], data0=Dmat[:, c, :], data1=ev_t[:], initial=zcol[:, 0:1], op0=ALU.mult, op1=ALU.add))
            E("vector", lambda e, c=c, b=b: e.tensor_copy(out=aend[:, b*NCH+c : b*NCH+c+1], in_=a_scan[:, TC-1:TC]))
            E("vector", lambda e, c=c: e.tensor_tensor_scan(out=b_scan[:], data0=Dmat[:, c, :], data1=e_t[:], initial=zcol[:, 0:1], op0=ALU.mult, op1=ALU.add))
            E("vector", lambda e, c=c, b=b: e.tensor_copy(out=aend[:, NCH*NB + b*NCH+c : NCH*NB + b*NCH+c+1], in_=b_scan[:, TC-1:TC]))
            tvs = SIG("vector")
            WT("scalar", "vector", tvs)
        BAR()

    DMAL(wq_sb[:, 0, :, :].rearrange("p c d -> p (c d)"), wq_dram[2, :, :])
    WL("tensor", s_ld.n)
    BAR()
    r_store_t = 0
    for b in range(NB):
        tl = load_seg_xnf(b)
        WL("vector", tl)
        emit_dsh(b)
        if b > 0:
            WS("scalar", r_store_t)
        emit_mix(2)
        quant_chain(2, lambda c: mix_t[:, c, :])
        project_fm(0, 2, lambda dc: r_seg[:, dc, :], sigmoid=True)
        WT("sync", "scalar")
        for c in range(NCH):
            r_store_t = DMAS(r_dram[:, c, b*TC:(b+1)*TC], r_seg[:, c, :])
    BAR()

    WT("sync", "vector")
    tst = DMAS(cc_in[:, :], aend[:])
    WS("gpsimd", tst)
    E("gpsimd", lambda e: e.collective_compute("AllGather", ALU.bypass, replica_groups=[list(range(NCORE))],
                                               ins=[cc_in[:, :]], outs=[cc_out[:, :, :]]))
    SIG("gpsimd")
    WT("sync", "gpsimd")
    DMAL(ccg[:], cc_out.rearrange("r q i -> q r i"))
    BAR()
    E("tensor", lambda e: e.matmul(psR[:, 0:2*NCORE], onesNf[:], rk_sb[0:1, :], start=True, stop=True))
    SIG("tensor")
    WT("vector", "tensor")
    E("vector", lambda e: e.tensor_copy(out=rkrep[:], in_=psR[:, 0:2*NCORE]))
    SIG("vector")
    for r in range(NCORE):
        E("vector", lambda e, r=r: e.tensor_scalar(out=powr[:, r, :], in0=wlog[:], scalar1=rkrep[:, NCORE+r:NCORE+r+1], scalar2=None, op0=ALU.mult))
    E("vector", lambda e: e.tensor_scalar(out=powr[:].rearrange("p r c -> p (r c)"), in0=powr[:].rearrange("p r c -> p (r c)"), scalar1=float(TC), scalar2=None, op0=ALU.mult))
    SIG("vector")
    WT("scalar", "vector")
    E("scalar", lambda e: e.activation(out=powr[:].rearrange("p r c -> p (r c)"), in_=powr[:].rearrange("p r c -> p (r c)"), func=AF.Exp))
    SIG("scalar")
    WT("vector", "scalar")
    for r in range(NCORE):
        E("vector", lambda e, r=r: e.tensor_scalar(out=powr[:, r, :], in0=powr[:, r, :], scalar1=rkrep[:, r:r+1], scalar2=None, op0=ALU.mult))
    E("vector", lambda e: e.memset(carry[:], 0.0))
    SIG("vector")
    for r in range(NCORE):
        for kv in range(2):
            for b in range(NB):
                idx0 = kv * NB * NCH + b * NCH
                E("vector", lambda e, r=r, idx0=idx0: e.tensor_tensor(out=sqf[:, 0, 0:NCH], in0=ccg[:, r, idx0:idx0+NCH], in1=powr[:, r, :], op=ALU.mult))
                E("vector", lambda e, idx0=idx0: e.tensor_add(out=carry[:, idx0:idx0+NCH], in0=carry[:, idx0:idx0+NCH], in1=sqf[:, 0, 0:NCH]))
    SIG("vector")
    BAR()

    DMAL(wq_sb[:, 1, :, :].rearrange("p c d -> p (c d)"), wq_dram[3, :, :])
    WL("tensor", s_ld.n)
    BAR()
    for b in range(NB):
        for kvi in range(2):
            for c in range(NCH):
                DMAL(kv_t[:, kvi, c, :], kv_spill[kvi, c, :, b*TC:(b+1)*TC])
        for c in range(NCH):
            DMAL(r_seg[:, c, :], r_dram[:, c, b*TC:(b+1)*TC])
        WL("scalar", s_ld.n); WL("vector", s_ld.n)
        for c in range(NCH):
            E("scalar", lambda e, c=c: e.activation(out=e_t[:], in_=kv_t[:, 0, c, :], func=AF.Exp))
            tse = SIG("scalar")
            WT("vector", "scalar", tse)
            E("vector", lambda e, c=c: e.tensor_tensor(out=ev_t[:], in0=e_t[:], in1=kv_t[:, 1, c, :], op=ALU.mult))
            E("vector", lambda e, c=c: e.tensor_tensor_scan(out=a_scan[:], data0=Dmat[:, c, :], data1=ev_t[:], initial=zcol[:, 0:1], op0=ALU.mult, op1=ALU.add))
            E("vector", lambda e, c=c: e.tensor_tensor_scan(out=b_scan[:], data0=Dmat[:, c, :], data1=e_t[:], initial=zcol[:, 0:1], op0=ALU.mult, op1=ALU.add))
            E("vector", lambda e, c=c: e.tensor_scalar(out=eu_t[:], in0=e_t[:], scalar1=expu[:, c:c+1], scalar2=None, op0=ALU.mult))
            E("vector", lambda e, c=c: e.tensor_scalar(out=evu_t[:], in0=ev_t[:], scalar1=expu[:, c:c+1], scalar2=None, op0=ALU.mult))
            E("vector", lambda e, c=c, b=b: e.scalar_tensor_tensor(out=n1_t[:, 1:TC], in0=powm[:, c, 1:TC], scalar=carry[:, b*NCH+c : b*NCH+c+1], in1=a_scan[:, 0:TC-1], op0=ALU.mult, op1=ALU.add))
            E("vector", lambda e, c=c, b=b: e.tensor_scalar(out=n1_t[:, 0:1], in0=powm[:, c, 0:1], scalar1=carry[:, b*NCH+c : b*NCH+c+1], scalar2=None, op0=ALU.mult))
            E("vector", lambda e, c=c, b=b: e.scalar_tensor_tensor(out=dd_t[:, 1:TC], in0=powm[:, c, 1:TC], scalar=carry[:, NCH*NB + b*NCH+c : NCH*NB + b*NCH+c+1], in1=b_scan[:, 0:TC-1], op0=ALU.mult, op1=ALU.add))
            E("vector", lambda e, c=c, b=b: e.tensor_scalar(out=dd_t[:, 0:1], in0=powm[:, c, 0:1], scalar1=carry[:, NCH*NB + b*NCH+c : NCH*NB + b*NCH+c+1], scalar2=None, op0=ALU.mult))
            E("vector", lambda e, c=c: e.tensor_add(out=n1_t[:], in0=n1_t[:], in1=evu_t[:]))
            E("vector", lambda e, c=c: e.tensor_add(out=dd_t[:], in0=dd_t[:], in1=eu_t[:]))
            E("vector", lambda e, c=c: e.reciprocal(out=dd_t[:], in_=dd_t[:]))
            E("vector", lambda e, c=c: e.tensor_tensor(out=n1_t[:], in0=n1_t[:], in1=dd_t[:], op=ALU.mult))
            E("vector", lambda e, c=c: e.tensor_tensor(out=gin_t[:, c, :], in0=n1_t[:], in1=r_seg[:, c, :], op=ALU.mult))
            tvs = SIG("vector")
            WT("scalar", "vector", tvs)
        quant_chain(3, lambda c: gin_t[:, c, :])
        for s in range(4):
            for half in range(2):
                WT("tensor", "vector", C.sems["vector"].n)
                for c in range(NCH):
                    E("tensor", lambda e, s=s, half=half, c=c: e.matmul(
                        psW[:, half*512:(half+1)*512], xq_t[:, c, 128*s:128*(s+1)],
                        wq_sb[:, 1, c, half*512:(half+1)*512], start=(c == 0), stop=(c == NCH-1)))
                tm = SIG("tensor")
                WT("vector", "tensor", tm)
                E("vector", lambda e, s=s, half=half: e.tensor_scalar(
                    out=s0_seg[:, s, half*512:(half+1)*512], in0=psW[:, half*512:(half+1)*512],
                    scalar1=isx_tm[:, s:s+1], scalar2=inv_sw[:, 3:4], op0=ALU.mult, op1=ALU.mult))
                SIG("vector")
        WT("sync", "vector")
        for s in range(4):
            DMAS(s0_dram[b*4+s, :, :], s0_seg[:, s, :])
        BAR()
    BAR()
    s_gd = Sem(st.enter_context(nc.semaphore("s_gd")))
    C.dma_sems = (s_ld, s_st, s_gd)

    def GDMA(out, in_, out_off=None, in_off=None, **kw):
        s_gd.n += 16
        C.q["gpsimd"].append(lambda e, out=out, in_=in_, oo=out_off, io=in_off, kw=kw:
            e.indirect_dma_start(out=out, out_offset=oo, in_=in_, in_offset=io, **kw).then_inc(s_gd.h, 16))
        return s_gd.n

    def WG(eng, val):
        C.q[eng].append(lambda e, v=val: e.wait_ge(s_gd.h, v))

    E("gpsimd", lambda e: e.iota(out=iota16[:], pattern=[[1, 16]], base=0, channel_multiplier=0, allow_small_or_imprecise_dtypes=True))
    E("gpsimd", lambda e: e.iota(out=tri16a[:].rearrange("p i j -> p (i j)"), pattern=[[0, 16], [1, 16]], base=0, channel_multiplier=0, allow_small_or_imprecise_dtypes=True))
    E("gpsimd", lambda e: e.iota(out=tri16b[:].rearrange("p i j -> p (i j)"), pattern=[[1, 16], [0, 16]], base=0, channel_multiplier=0, allow_small_or_imprecise_dtypes=True))
    E("gpsimd", lambda e: e.iota(out=tri4a[:].rearrange("p i j -> p (i j)"), pattern=[[0, 4], [1, 4]], base=0, channel_multiplier=0, allow_small_or_imprecise_dtypes=True))
    E("gpsimd", lambda e: e.iota(out=tri4b[:].rearrange("p i j -> p (i j)"), pattern=[[1, 4], [0, 4]], base=0, channel_multiplier=0, allow_small_or_imprecise_dtypes=True))
    BAR()
    E("vector", lambda e: e.tensor_tensor(out=tri16a[:].rearrange("p i j -> p (i j)"), in0=tri16a[:].rearrange("p i j -> p (i j)"), in1=tri16b[:].rearrange("p i j -> p (i j)"), op=ALU.is_lt))
    E("vector", lambda e: e.tensor_tensor(out=tri4a[:].rearrange("p i j -> p (i j)"), in0=tri4a[:].rearrange("p i j -> p (i j)"), in1=tri4b[:].rearrange("p i j -> p (i j)"), op=ALU.is_lt))
    E("vector", lambda e: e.memset(ones128s[:], 1.0))
    tl = DMAL(rw_sb[:], rw_in[:, :])
    WL("vector", tl)
    E("vector", lambda e: e.tensor_copy(out=rw_bf[:], in_=rw_sb[:]))
    BAR()
    for c in range(NCH):
        pp = psT if c % 2 == 0 else psU
        E("tensor", lambda e, c=c, pp=pp: e.transpose(pp[:], rw_bf[0:16, 128*c:128*(c+1)], ident[0:16, :]))
        tm = SIG("tensor")
        WT("scalar", "tensor", tm)
        E("scalar", lambda e, c=c, pp=pp: e.copy(out=rwT[:, c, :], in_=pp[:, 0:16]))
        tsc = C.sems["scalar"].n
        WT("tensor", "scalar", tsc)
    BAR()
    tl = DMAL(strip1[0:1, 0:16], ebias_in[None, :], allow_slow_non_contiguous=True)
    WL("tensor", tl)
    E("tensor", lambda e: e.matmul(psR[:, 0:16], onesNf[:], strip1[0:1, 0:16], start=True, stop=True))
    tm = SIG("tensor")
    WT("vector", "tensor", tm)
    E("vector", lambda e: e.tensor_copy(out=ebias_rep[:], in_=psR[:, 0:16]))
    BAR()
    cast_f32 = sqf[:].rearrange("p a d -> p (a d)")
    cast_bf = mix_t[:].rearrange("p c t -> p (c t)")[:, 0:2048]
    for e_i in range(16):
        for q in range(6):
            tl = DMAL(cast_f32[:, :], wcat_in[e_i, :, 2048*q:2048*(q+1)])
            WL("vector", tl)
            E("vector", lambda e: e.tensor_copy(out=cast_bf[:, :], in_=cast_f32[:, :]))
            tv = SIG("vector")
            WT("sync", "vector", tv)
            tstw = DMAS(wcat_bf[e_i, :, 2048*q:2048*(q+1)], cast_bf[:, :])
            WS("vector", tstw)
            WS("sync", tstw)
    for mat_src, dstslot in ((shg_in, 0), (shu_in, 1)):
        for q in range(4):
            tl = DMAL(cast_f32[:, :], mat_src.rearrange("p c d -> p (c d)")[:, 2048*q:2048*(q+1)])
            WL("vector", tl)
            E("vector", lambda e, dstslot=dstslot, q=q: e.tensor_copy(out=wq_sb[:, dstslot, :, :].rearrange("p c d -> p (c d)")[:, 2048*q:2048*(q+1)], in_=cast_f32[:, :]))
            tv = SIG("vector")
            WT("sync", "vector", tv)
    BAR()
    for g in range(8):
        tl = DMAL(xnf_seg[:], xnf_dram[:, :, g*TC:(g+1)*TC])
        WL("tensor", tl)
        BAR()
        for c in range(NCH):
            E("tensor", lambda e, c=c: e.matmul(psA[0:16, :], rwT[:, c, :], xnf_seg[:, c, :], start=(c == 0), stop=(c == NCH-1)))
        tm = SIG("tensor")
        WT("vector", "tensor", tm)
        E("vector", lambda e: e.tensor_copy(out=rw_sb[0:16, 0:512], in_=psA[0:16, :]))
        tv = SIG("vector")
        WT("tensor", "vector", tv)
        for qi in range(4):
            E("tensor", lambda e, qi=qi: e.transpose(psTf[:], rw_sb[0:16, 128*qi:128*(qi+1)], identf[0:16, :]))
            tm = SIG("tensor")
            WT("scalar", "tensor", tm)
            E("scalar", lambda e, g=g, qi=qi: e.activation(out=scores_tm[:, 4*g+qi, :], in_=psTf[:, 0:16], func=AF.Sigmoid))
            tsc = C.sems["scalar"].n
            WT("tensor", "scalar", tsc)
        BAR()
    NT = NTILE
    cmp1 = cmp_sc[:].rearrange("p i e -> p (i e)")
    cmp2 = wstage[:].rearrange("p c d -> p (c d)").bitcast(F32)[:, 0:512]
    cmp2b = wstage[:].rearrange("p c d -> p (c d)").bitcast(F32)[:, 512:1024]
    rank4 = rank16[:].rearrange("p i e -> p (i e)")
    E("vector", lambda e: e.tensor_tensor(out=rnk_sc[:, 0:NT, :], in0=scores_tm[:, 0:NT, :], in1=ebias_rep[:].unsqueeze(1).broadcast_to([128, NT, 16]), op=ALU.add))
    BAR()
    E("vector", lambda e: e.memset(rank4[:, :], 0.0))
    for j4 in range(4):
        xj = rnk_sc[:, 0:NT, :].rearrange("p i (g m) -> p i g m", m=4)[:, :, :, j4:j4+1].broadcast_to([128, NT, 4, 4])
        xi = rnk_sc[:, 0:NT, :].rearrange("p i (g m) -> p i g m", m=4)
        E("vector", lambda e, xj=xj, xi=xi: e.tensor_tensor(out=cmp1[:, 0:512].rearrange("p (i g m) -> p i g m", g=4, m=4), in0=xj, in1=xi, op=ALU.is_gt))
        E("vector", lambda e: e.tensor_add(out=rank4[:, :], in0=rank4[:, :], in1=cmp1[:, 0:512]))
        E("vector", lambda e, xj=xj, xi=xi: e.tensor_tensor(out=cmp1[:, 0:512].rearrange("p (i g m) -> p i g m", g=4, m=4), in0=xj, in1=xi, op=ALU.is_equal))
        ltj = tri4a[:, :, j4]
        E("vector", lambda e, ltj=ltj: e.tensor_tensor(out=cmp2[:, 0:512].rearrange("p (i g m) -> p i g m", g=4, m=4), in0=cmp1[:, 0:512].rearrange("p (i g m) -> p i g m", g=4, m=4), in1=ltj.unsqueeze(1).unsqueeze(2).broadcast_to([128, NT, 4, 4]), op=ALU.mult))
        E("vector", lambda e: e.tensor_add(out=rank4[:, :], in0=rank4[:, :], in1=cmp2[:, 0:512]))
    BAR()
    E("vector", lambda e: e.tensor_scalar(out=cmp1[:, 0:512], in0=rank4[:, :], scalar1=2.0, scalar2=1.0, op0=ALU.is_lt, op1=ALU.mult))
    E("vector", lambda e: e.tensor_tensor(out=cmp1[:, 0:512], in0=cmp1[:, 0:512], in1=rnk_sc[:, 0:NT, :].rearrange("p i e -> p (i e)"), op=ALU.mult))
    E("vector", lambda e: e.tensor_reduce(out=gsc4[:, 0:NT, :], in_=cmp1[:, 0:512].rearrange("p (i g m) -> p i g m", g=4, m=4), axis=AX.X, op=ALU.add))
    BAR()
    E("vector", lambda e: e.memset(grk4[:].rearrange("p i g -> p (i g)"), 0.0))
    for j4 in range(4):
        xj = gsc4[:, 0:NT, j4:j4+1].broadcast_to([128, NT, 4])
        E("vector", lambda e, xj=xj: e.tensor_tensor(out=cmp1[:, 0:128].rearrange("p (i g) -> p i g", g=4), in0=xj, in1=gsc4[:, 0:NT, :], op=ALU.is_gt))
        E("vector", lambda e: e.tensor_add(out=grk4[:].rearrange("p i g -> p (i g)"), in0=grk4[:].rearrange("p i g -> p (i g)"), in1=cmp1[:, 0:128]))
        E("vector", lambda e, xj=xj: e.tensor_tensor(out=cmp1[:, 0:128].rearrange("p (i g) -> p i g", g=4), in0=xj, in1=gsc4[:, 0:NT, :], op=ALU.is_equal))
        ltj = tri4a[:, :, j4]
        E("vector", lambda e, ltj=ltj: e.tensor_tensor(out=cmp2[:, 0:128].rearrange("p (i g) -> p i g", g=4), in0=cmp1[:, 0:128].rearrange("p (i g) -> p i g", g=4), in1=ltj.unsqueeze(1).broadcast_to([128, NT, 4]), op=ALU.mult))
        E("vector", lambda e: e.tensor_add(out=grk4[:].rearrange("p i g -> p (i g)"), in0=grk4[:].rearrange("p i g -> p (i g)"), in1=cmp2[:, 0:128]))
    E("vector", lambda e: e.tensor_scalar(out=grk4[:].rearrange("p i g -> p (i g)"), in0=grk4[:].rearrange("p i g -> p (i g)"), scalar1=2.0, scalar2=1.0, op0=ALU.is_lt, op1=ALU.mult))
    BAR()
    E("vector", lambda e: e.tensor_tensor(out=rnk_sc[:, 0:NT, :].rearrange("p i (g m) -> p i g m", m=4), in0=rnk_sc[:, 0:NT, :].rearrange("p i (g m) -> p i g m", m=4), in1=grk4[:, 0:NT, :].unsqueeze(3).broadcast_to([128, NT, 4, 4]), op=ALU.mult))
    BAR()
    E("vector", lambda e: e.memset(rank16[:].rearrange("p i e -> p (i e)"), 0.0))
    for j16 in range(16):
        xj = rnk_sc[:, 0:NT, j16:j16+1].broadcast_to([128, NT, 16])
        E("vector", lambda e, xj=xj: e.tensor_tensor(out=cmp1[:, 0:512].rearrange("p (i e) -> p i e", e=16), in0=xj, in1=rnk_sc[:, 0:NT, :], op=ALU.is_gt))
        E("vector", lambda e: e.tensor_add(out=rank16[:].rearrange("p i e -> p (i e)"), in0=rank16[:].rearrange("p i e -> p (i e)"), in1=cmp1[:, 0:512]))
        E("vector", lambda e, xj=xj: e.tensor_tensor(out=cmp1[:, 0:512].rearrange("p (i e) -> p i e", e=16), in0=xj, in1=rnk_sc[:, 0:NT, :], op=ALU.is_equal))
        ltj = tri16a[:, :, j16]
        E("vector", lambda e, ltj=ltj: e.tensor_tensor(out=cmp2[:, 0:512].rearrange("p (i e) -> p i e", e=16), in0=cmp1[:, 0:512].rearrange("p (i e) -> p i e", e=16), in1=ltj.unsqueeze(1).broadcast_to([128, NT, 16]), op=ALU.mult))
        E("vector", lambda e: e.tensor_add(out=rank16[:].rearrange("p i e -> p (i e)"), in0=rank16[:].rearrange("p i e -> p (i e)"), in1=cmp2[:, 0:512]))
    BAR()
    E("vector", lambda e: e.tensor_scalar(out=cmp1[:, 0:512], in0=rank16[:].rearrange("p i e -> p (i e)"), scalar1=4.0, scalar2=1.0, op0=ALU.is_lt, op1=ALU.mult))
    E("vector", lambda e: e.tensor_tensor(out=cmp2[:, 0:512], in0=cmp1[:, 0:512], in1=scores_tm[:, 0:NT, :].rearrange("p i e -> p (i e)"), op=ALU.mult))
    E("vector", lambda e: e.tensor_reduce(out=cmp2b[:, 0:32].rearrange("p (i q) -> p i q", q=1), in_=cmp2[:, 0:512].rearrange("p (i e) -> p i e", e=16), axis=AX.X, op=ALU.add))
    E("vector", lambda e: e.tensor_scalar(out=cmp2b[:, 0:32], in0=cmp2b[:, 0:32], scalar1=1e-20, scalar2=0.4, op0=ALU.add, op1=ALU.mult))
    E("vector", lambda e: e.reciprocal(out=cmp2b[:, 32:64], in_=cmp2b[:, 0:32]))
    E("vector", lambda e: e.tensor_tensor(out=cw_tm[:, 0:NT, :], in0=cmp2[:, 0:512].rearrange("p (i e) -> p i e", e=16), in1=cmp2b[:, 32:64].unsqueeze(2).broadcast_to([128, NT, 16]), op=ALU.mult))
    BAR()
    WT("sync", "vector")
    selm = cmp1[:, 0:512].rearrange("p (i e) -> p i e", e=16)
    for i in range(NT):
        si = i // 8
        E("tensor", lambda e, i=i: e.transpose(psTf[0:16, :], selm[:, i, :], identf[:]))
        tm = SIG("tensor")
        WT("vector", "tensor", tm)
        E("vector", lambda e, i=i, si=si: e.tensor_copy(out=Me128[32*si:32*si+16, 128*(i % 8):128*(i % 8)+128], in_=psTf[0:16, 0:128]))
        tv = SIG("vector")
        WT("tensor", "vector", tv)
    BAR()
    E("vector", lambda e: e.tensor_tensor_scan(out=pos128[:], data0=ones128s[:], data1=Me128[:], initial=zcol[:, 0:1], op0=ALU.mult, op1=ALU.add))
    E("vector", lambda e: e.tensor_copy(out=segsum[:], in_=pos128[:, 1023:1024]))
    E("vector", lambda e: e.tensor_sub(out=pos128[:], in0=pos128[:], in1=Me128[:]))
    BAR()
    E("tensor", lambda e: e.matmul(psR[0:1, 0:128], segsum[:], identf[:], start=True, stop=True))
    tm = SIG("tensor")
    WT("vector", "tensor", tm)
    E("vector", lambda e: e.tensor_copy(out=strip1[0:1, 0:128], in_=psR[0:1, 0:128]))
    E("vector", lambda e: e.tensor_tensor(out=strip1[0:1, 128:160], in0=strip1[0:1, 0:32], in1=strip1[0:1, 32:64], op=ALU.add))
    E("vector", lambda e: e.tensor_tensor(out=strip1[0:1, 160:192], in0=strip1[0:1, 64:96], in1=strip1[0:1, 96:128], op=ALU.add))
    E("vector", lambda e: e.tensor_tensor(out=strip1[0:1, 128:160], in0=strip1[0:1, 128:160], in1=strip1[0:1, 160:192], op=ALU.add))
    E("vector", lambda e: e.memset(strip1[0:1, 192:224], 0.0))
    E("vector", lambda e: e.tensor_copy(out=strip1[0:1, 224:256], in_=strip1[0:1, 0:32]))
    E("vector", lambda e: e.tensor_tensor(out=strip1[0:1, 256:288], in0=strip1[0:1, 224:256], in1=strip1[0:1, 32:64], op=ALU.add))
    E("vector", lambda e: e.tensor_tensor(out=strip1[0:1, 288:320], in0=strip1[0:1, 256:288], in1=strip1[0:1, 64:96], op=ALU.add))
    E("vector", lambda e: e.tensor_scalar(out=strip1[0:1, 320:336], in0=strip1[0:1, 128:144], scalar1=1.0/512.0, scalar2=(511.0/512.0 - 0.5 + 2.0**-12), op0=ALU.mult, op1=ALU.add))
    E("vector", lambda e: e.tensor_scalar(out=strip1[0:1, 320:336], in0=strip1[0:1, 320:336], scalar1=MAGIC, scalar2=MAGIC, op0=ALU.add, op1=ALU.subtract))
    E("vector", lambda e: e.tensor_copy(out=strip1[0:1, 336:352], in_=strip1[0:1, 320:336]))
    for sh in (1, 2, 4, 8):
        E("vector", lambda e, sh=sh: e.tensor_tensor(out=strip1[0:1, 352+sh:368], in0=strip1[0:1, 336+sh:352], in1=strip1[0:1, 336:352-sh], op=ALU.add))
        E("vector", lambda e, sh=sh: e.tensor_copy(out=strip1[0:1, 352:352+sh], in_=strip1[0:1, 336:336+sh]))
        E("vector", lambda e: e.tensor_copy(out=strip1[0:1, 336:352], in_=strip1[0:1, 352:368]))
    E("vector", lambda e: e.tensor_sub(out=strip1[0:1, 368:384], in0=strip1[0:1, 336:352], in1=strip1[0:1, 320:336]))
    E("vector", lambda e: e.tensor_scalar(out=strip1[0:1, 368:384], in0=strip1[0:1, 368:384], scalar1=512.0, scalar2=None, op0=ALU.mult))
    BAR()
    E("gpsimd", lambda e: e.iota(out=strip1[0:1, 400:448], pattern=[[512, 48]], base=0, channel_multiplier=0, allow_small_or_imprecise_dtypes=True))
    BAR()
    E("vector", lambda e: e.tensor_tensor(out=rw_sb[0:1, 0:768].rearrange("p (j e) -> p j e", e=16), in0=strip1[0:1, 368:384].unsqueeze(1).broadcast_to([1, 48, 16]), in1=strip1[0:1, 400:448].unsqueeze(2).broadcast_to([1, 48, 16]), op=ALU.is_le))
    E("vector", lambda e: e.tensor_reduce(out=strip1[0:1, 448:496].rearrange("p (j q) -> p j q", q=1), in_=rw_sb[0:1, 0:768].rearrange("p (j e) -> p j e", e=16), axis=AX.X, op=ALU.add))
    E("vector", lambda e: e.tensor_scalar(out=strip1[0:1, 448:496], in0=strip1[0:1, 448:496], scalar1=-1.0, scalar2=None, op0=ALU.add))
    E("vector", lambda e: e.memset(strip1[0:1, 624:656], 0.0))
    E("vector", lambda e: e.tensor_copy(out=strip1[0:1, 624:640], in_=strip1[0:1, 368:384]))
    E("vector", lambda e: e.tensor_tensor(out=strip1[0:1, 496:624].rearrange("p (s q) -> p s q", q=32), in0=strip1[0:1, 192:320].rearrange("p (s q) -> p s q", q=32), in1=strip1[0:1, 624:656].unsqueeze(1).broadcast_to([1, 4, 32]), op=ALU.add))
    BAR()
    E("tensor", lambda e: e.matmul(psR[:, 0:1], strip1[0:1, 496:624], onesNf[0:1, 0:1], start=True, stop=True))
    tm = SIG("tensor")
    WT("vector", "tensor", tm)
    E("vector", lambda e: e.tensor_copy(out=segoff[:], in_=psR[:, 0:1]))
    E("vector", lambda e: e.tensor_scalar(out=pos128[:], in0=pos128[:], scalar1=segoff[:, 0:1], scalar2=0.0, op0=ALU.add, op1=ALU.add))
    BAR()
    for i in range(NT):
        si = i // 8
        E("tensor", lambda e, i=i: e.transpose(psTf[:], pos128[:, 128*(i % 8):128*(i % 8)+128].rearrange("p (q t) -> p q t", q=1)[:, 0, :] if False else pos128[:, 128*(i % 8):128*(i % 8)+128], identf[:]))
        tm = SIG("tensor")
        WT("vector", "tensor", tm)
        E("vector", lambda e, i=i, si=si: e.tensor_copy(out=S_tm[:, i, :], in_=psTf[:, 32*si:32*si+16]))
        tv = SIG("vector")
        WT("tensor", "vector", tv)
    BAR()
    for j in range(4):
        E("vector", lambda e, j=j: e.tensor_scalar(out=cmp1[:, 0:512], in0=rank16[:].rearrange("p i e -> p (i e)"), scalar1=float(j), scalar2=1.0, op0=ALU.is_equal, op1=ALU.mult))
        E("vector", lambda e, j=j: e.tensor_tensor(out=cmp2[:, 0:512], in0=cmp1[:, 0:512], in1=S_tm[:, 0:NT, :].rearrange("p i e -> p (i e)"), op=ALU.mult))
        E("vector", lambda e, j=j: e.tensor_reduce(out=sjf[:, :, j:j+1], in_=cmp2[:, 0:512].rearrange("p (i e) -> p i e", e=16), axis=AX.X, op=ALU.add))
        E("vector", lambda e, j=j: e.tensor_tensor(out=cmp2[:, 0:512], in0=cmp1[:, 0:512], in1=cw_tm[:, 0:NT, :].rearrange("p i e -> p (i e)"), op=ALU.mult))
        E("vector", lambda e, j=j: e.tensor_reduce(out=gj_tm[:, :, j:j+1], in_=cmp2[:, 0:512].rearrange("p (i e) -> p i e", e=16), axis=AX.X, op=ALU.add))
    E("vector", lambda e: e.tensor_copy(out=sji[:].rearrange("p i j -> p (i j)"), in_=sjf[:].rearrange("p i j -> p (i j)")))
    BAR()
    WT("sync", "vector")
    E("tensor", lambda e: e.matmul(psR[:, 0:48], onesNf[:], strip1[0:1, 448:496], start=True, stop=True))
    tm = SIG("tensor")
    WT("vector", "tensor", tm)
    E("vector", lambda e: e.tensor_copy(out=texp_f[:], in_=psR[:, 0:48]))
    E("vector", lambda e: e.tensor_scalar(out=texp_f[:], in0=texp_f[:], scalar1=128.0, scalar2=0.0, op0=ALU.mult, op1=ALU.add))
    E("vector", lambda e: e.tensor_tensor(out=texp_f[:], in0=texp_f[:], in1=io_row[:, 0:48], op=ALU.add))
    E("vector", lambda e: e.tensor_copy(out=widx[:], in_=texp_f[:]))
    BAR()
    xn_sc = camax[:].bitcast(BF16)[:, 0:1024]
    for i in range(NT):
        tl = DMAL(xn_sc, xn_dram[i, :, :])
        WL("gpsimd", tl)
        tg = 0
        for j in range(4):
            tg = GDMA(slotbuf[:, :], xn_sc, out_off=IndirectOffsetOnAxis(sji[:, i, j:j+1], 0))
        WG("sync", tg)
    BAR()
    wcat_rows = wcat_bf.rearrange("e p q -> (e p) q")
    hs_bf = rnd_f[:].bitcast(BF16)[:, 0:512]
    yo_bf = camax[:].bitcast(BF16)[:, 0:1024]
    tsc_tp = {0: 0, 1: 0}
    for tj in range(48):
        WT("gpsimd", "tensor")
        tgw = GDMA(wbuf, wcat_rows, in_off=IndirectOffsetOnAxis(widx[:, tj:tj+1], 0))
        WT("sync", "tensor")
        tl = DMAL(xg_tm[:], slotbuf[512*tj:512*(tj+1), :].rearrange("(a p) d -> p a d", p=128))
        WL("tensor", tl)
        for a4 in range(4):
            for c in range(NCH):
                par = (a4 * NCH + c) % 2
                pp = psT if par == 0 else psU
                WT("tensor", "scalar", tsc_tp[par])
                E("tensor", lambda e, a4=a4, c=c, pp=pp: e.transpose(pp[:], xg_tm[:, a4, 128*c:128*(c+1)], ident[:]))
                tm = SIG("tensor")
                WT("scalar", "tensor", tm)
                E("scalar", lambda e, a4=a4, c=c, pp=pp: e.copy(out=xgF[:, c, 128*a4:128*(a4+1)], in_=pp[:]))
                tsc_tp[par] = C.sems["scalar"].n
        WT("tensor", "scalar", C.sems["scalar"].n)
        WG("tensor", tgw)
        for mi in range(4):
            for c in range(NCH):
                E("tensor", lambda e, mi=mi, c=c: e.matmul(psA[:], wbuf[:, c*512+128*mi : c*512+128*mi+128], xgF[:, c, :], start=(c == 0), stop=(c == NCH-1)))
            for c in range(NCH):
                E("tensor", lambda e, mi=mi, c=c: e.matmul(psB[:], wbuf[:, 4096+c*512+128*mi : 4096+c*512+128*mi+128], xgF[:, c, :], start=(c == 0), stop=(c == NCH-1)))
            tm = SIG("tensor")
            WT("scalar", "tensor", tm)
            E("scalar", lambda e, mi=mi: e.activation(out=hs_bf[:, :], in_=psA[:], func=AF.Silu))
            tsc = C.sems["scalar"].n
            WT("vector", "scalar", tsc)
            E("vector", lambda e, mi=mi: e.tensor_tensor(out=h_rt[:, mi, :], in0=hs_bf[:, :], in1=psB[:], op=ALU.mult))
            tv = SIG("vector")
            WT("tensor", "vector", tv)
        for s4 in range(4):
            for half in range(2):
                for ci in range(4):
                    E("tensor", lambda e, s4=s4, half=half, ci=ci: e.matmul(psW[:, 512*half:512*(half+1)], h_rt[:, ci, 128*s4:128*(s4+1)], wbuf[:, 8192+ci*1024+512*half : 8192+ci*1024+512*half+512], start=(ci == 0), stop=(ci == 3)))
            tm = SIG("tensor")
            WT("vector", "tensor", tm)
            E("vector", lambda e, s4=s4: e.tensor_copy(out=yo_bf[:, :], in_=psW[:]))
            tv = SIG("vector")
            WT("sync", "vector", tv)
            tst = DMAS(yobuf[512*tj+128*s4 : 512*tj+128*s4+128, :], yo_bf[:, :])
            WS("vector", tst)
            WT("tensor", "vector", tv)
    BAR()
    shd_res = kv_t[:].rearrange("p a c t -> p (a c t)").bitcast(BF16)[:, 0:8192].rearrange("p (c d) -> p c d", d=D)
    for q in range(4):
        tl = DMAL(cast_f32[:, :], shd_in.rearrange("p c d -> p (c d)")[:, 2048*q:2048*(q+1)])
        WL("vector", tl)
        E("vector", lambda e, q=q: e.tensor_copy(out=shd_res.rearrange("p c d -> p (c d)")[:, 2048*q:2048*(q+1)], in_=cast_f32[:, :]))
        tv = SIG("vector")
        WT("sync", "vector", tv)
    BAR()
    ybg = camax[:].bitcast(BF16)[:, 0:1024]
    gsil = sqf[:, 0, :]
    s0buf = sqf[:, 1, :]
    for grp in range(8):
        for it in range(4):
            i = grp * 4 + it
            for j in range(4):
                tg = GDMA(ybg, yobuf[:, :], in_off=IndirectOffsetOnAxis(sji[:, i, j:j+1], 0))
                WG("vector", tg)
                if j == 0:
                    E("vector", lambda e, it=it, i=i: e.tensor_scalar(out=g0acc[:, it, :], in0=ybg, scalar1=gj_tm[:, i, 0:1], scalar2=0.0, op0=ALU.mult, op1=ALU.add))
                else:
                    E("vector", lambda e, it=it, i=i, j=j: e.scalar_tensor_tensor(out=g0acc[:, it, :], in0=ybg, scalar=gj_tm[:, i, j:j+1], in1=g0acc[:, it, :], op0=ALU.mult, op1=ALU.add))
                tv = SIG("vector")
                WT("gpsimd", "vector", tv)
        BAR()
        tl = DMAL(xnf_seg[:], xnf_dram[:, :, grp*TC:(grp+1)*TC])
        WL("tensor", tl)
        hsh_bf = rnd_f[:].bitcast(BF16)[:, 0:512]
        for mi in range(8):
            for c in range(NCH):
                E("tensor", lambda e, mi=mi, c=c: e.matmul(psA[:], wq_sb[:, 0, c, 128*mi:128*(mi+1)], xnf_seg[:, c, :], start=(c == 0), stop=(c == NCH-1)))
            for c in range(NCH):
                E("tensor", lambda e, mi=mi, c=c: e.matmul(psB[:], wq_sb[:, 1, c, 128*mi:128*(mi+1)], xnf_seg[:, c, :], start=(c == 0), stop=(c == NCH-1)))
            tm = SIG("tensor")
            WT("scalar", "tensor", tm)
            E("scalar", lambda e, mi=mi: e.activation(out=hsh_bf[:, :], in_=psA[:], func=AF.Silu))
            tsc = C.sems["scalar"].n
            WT("vector", "scalar", tsc)
            E("vector", lambda e, mi=mi: e.tensor_tensor(out=h_sh[:, mi, :], in0=hsh_bf[:, :], in1=psB[:], op=ALU.mult))
            tv = SIG("vector")
            WT("tensor", "vector", tv)
        BAR()
        for s4 in range(4):
            i = grp * 4 + s4
            for half in range(2):
                for ci in range(NCH):
                    E("tensor", lambda e, s4=s4, half=half, ci=ci: e.matmul(psW[:, 512*half:512*(half+1)], h_sh[:, ci, 128*s4:128*(s4+1)], shd_res[:, ci, 512*half:512*(half+1)], start=(ci == 0), stop=(ci == NCH-1)))
            tm = SIG("tensor")
            WT("vector", "tensor", tm)
            E("vector", lambda e, s4=s4: e.tensor_tensor(out=g0acc[:, s4, :], in0=g0acc[:, s4, :], in1=psW[:], op=ALU.add))
            tv = SIG("vector")
            WT("tensor", "vector", tv)
            WT("sync", "vector", tv)
            tl1 = DMAL(x_tile[:, 0, :], x_in[i // 4, 1+(i % 4)*128 : 1+(i % 4)*128+128, :])
            tl2 = DMAL(s0buf, s0_dram[i, :, :])
            WT("scalar", "vector", tv)
            E("scalar", lambda e, s4=s4: e.activation(out=gsil, in_=g0acc[:, s4, :], func=AF.Silu))
            tsc = C.sems["scalar"].n
            WL("vector", tl2)
            WL("vector", tl1)
            WT("vector", "scalar", tsc)
            E("vector", lambda e: e.tensor_tensor(out=gsil, in0=s0buf, in1=gsil, op=ALU.mult))
            E("vector", lambda e: e.tensor_tensor(out=s0buf, in0=x_tile[:, 0, :], in1=gsil, op=ALU.add))
            tv = SIG("vector")
            WT("sync", "vector", tv)
            t1 = DMAS(x1_dram[i, :, :], s0buf)
            WS("vector", t1)
            WS("scalar", t1)
        BAR()
    BAR()
    WS("sync", s_st.n)
    BAR()


    for mat_src, dsel in ((wqT_in, 0), (wkT_in, 1), (wvT_in, 2)):
        for qq in range(4):
            tl = DMAL(cast_f32[:, :], mat_src.rearrange("p c d -> p (c d)")[:, 2048*qq:2048*(qq+1)])
            WL("vector", tl)
            if dsel < 2:
                dstap = wq_sb[:, dsel, :, :].rearrange("p c d -> p (c d)")[:, 2048*qq:2048*(qq+1)]
            else:
                dstap = wstage[:].rearrange("p c d -> p (c d)")[:, 2048*qq:2048*(qq+1)]
            E("vector", lambda e, dstap=dstap: e.tensor_copy(out=dstap, in_=cast_f32[:, :]))
            tv = SIG("vector")
            WT("sync", "vector", tv)
    tl = DMAL(cast_f32[:, 0:1408].rearrange("p (j h) -> p j h", h=16), pe_in.rearrange("j p h -> p j h"))
    WL("vector", tl)
    E("vector", lambda e: e.tensor_copy(out=peT[:].rearrange("p j h -> p (j h)"), in_=cast_f32[:, 0:1408]))
    SIG("vector")
    tl = DMAL(selT[:], selc_in.rearrange("c p h -> p c h"))
    tl = DMAL(rselT[:].rearrange("p c d -> p (c d)"), rsel_in.rearrange("p c d -> p (c d)"))
    BAR()
    for i in range(NTILE):
        if i >= 2:
            WT("sync", "scalar"); WT("sync", "vector")
        tl = DMAL(x_tile[:, i % 2, :], x1_dram[i, :, :])
        WL("scalar", tl)
        E("scalar", lambda e, i=i: e.activation(out=sqf[:, i % 2, :], in_=x_tile[:, i % 2, :], func=AF.Square, accum_out=s2_all[:, i:i+1]))
        SIG("scalar")
        WL("vector", tl)
        E("vector", lambda e, i=i: e.reduce_sum(out=s1_all[:, i:i+1], in_=x_tile[:, i % 2, :], axis=AX.X))
        SIG("vector")
    BAR()
    E("vector", lambda e: e.tensor_scalar(out=s1_all[:], in0=s1_all[:], scalar1=1.0/D, scalar2=None, op0=ALU.mult))
    E("vector", lambda e: e.tensor_scalar(out=s2_all[:], in0=s2_all[:], scalar1=1.0/D, scalar2=None, op0=ALU.mult))
    SIG("vector")
    WT("scalar", "vector")
    E("scalar", lambda e: e.activation(out=rstd_all[:], in_=s1_all[:], func=AF.Square))
    SIG("scalar")
    WT("vector", "scalar")
    E("vector", lambda e: e.tensor_sub(out=rstd_all[:], in0=s2_all[:], in1=rstd_all[:]))
    E("vector", lambda e: e.tensor_scalar(out=rstd_all[:], in0=rstd_all[:], scalar1=LN_EPS, scalar2=None, op0=ALU.add))
    SIG("vector")
    WT("scalar", "vector")
    E("scalar", lambda e: e.activation(out=rstd_all[:], in_=rstd_all[:], func=AF.Sqrt))
    SIG("scalar")
    WT("vector", "scalar")
    E("vector", lambda e: e.reciprocal(out=rstd_all[:], in_=rstd_all[:]))
    E("vector", lambda e: e.tensor_scalar(out=nmu_all[:], in0=s1_all[:], scalar1=-1.0, scalar2=None, op0=ALU.mult))
    E("vector", lambda e: e.tensor_tensor(out=nmu_all[:], in0=nmu_all[:], in1=rstd_all[:], op=ALU.mult))
    SIG("vector")
    BAR()
    xn1_store_t = {}
    xnf1_store_t = {}
    for i in range(NTILE):
        if i >= 2:
            WT("sync", "scalar")
        tl = DMAL(x_tile[:, i % 2, :], x1_dram[i, :, :])
        WL("scalar", tl)
        if i >= 2:
            WS("scalar", xn1_store_t[i - 2])
        E("scalar", lambda e, i=i: e.activation(out=xn_st[:, i % 2, :], in_=x_tile[:, i % 2, :], func=AF.Identity, bias=nmu_all[:, i:i+1], scale=rstd_all[:, i:i+1]))
        tap = SIG("scalar")
        WT("sync", "scalar", tap)
        xn1_store_t[i] = DMAS(xn1_dram[i, :, :], xn_st[:, i % 2, :])
        WT("tensor", "scalar", tap)
        for c in range(NCH):
            pp = psT if c % 2 == 0 else psU
            if i > 0 or c >= 2:
                WT("tensor", "scalar", C.sems["scalar"].n - 1)
            E("tensor", lambda e, i=i, c=c, pp=pp: e.transpose(pp[:], xn_st[:, i % 2, 128*c:128*(c+1)], ident[:]))
            tm = SIG("tensor")
            WT("scalar", "tensor", tm)
            if c == 0 and i >= 2:
                WS("scalar", xnf1_store_t[i - 2])
            E("scalar", lambda e, i=i, c=c, pp=pp: e.copy(out=xnf_blk[:, i % 2, c, :], in_=pp[:]))
            SIG("scalar")
        WT("sync", "scalar")
        b = i // 4; toff = (i % 4) * 128
        xnf1_store_t[i] = DMAS(xnf1_dram[:, :, b*TC+toff : b*TC+toff+128], xnf_blk[:, i % 2, :, :])
    BAR()
    WS("sync", s_st.n)
    BAR()
    BAR()
    E("gpsimd", lambda e: e.collective_compute("AllGather", ALU.bypass, replica_groups=[list(range(NCORE))],
                                               ins=[xn1_dram.rearrange("i p d -> (i p) d")], outs=[xn1_ag[:, :, :]]))
    SIG("gpsimd")
    BAR()
    qC = r_seg
    kC = mix_t
    vC = xq_t
    e_full = kv_t[0:16, :, :, :].rearrange("p a c t -> p (a c t)")
    e_buf = e_full[:, 0:5632].rearrange("p (j t) -> p j t", t=TC)
    mrow = e_full[:, 5632:6144]
    m32 = e_full[:, 6144:6656]
    denb = e_full[:, 6656:7168]
    rden = e_full[:, 7168:7680]
    abf_full = kv_t[0:16, :, :, :].rearrange("p a c t -> p (a c t)").bitcast(BF16)
    abf = [abf_full[:, 15360:15872], abf_full[:, 15872:16384]]
    dmat_bf = Dmat[:].rearrange("p c t -> p (c t)").bitcast(BF16)
    pC = [dmat_bf[:, 0:4096].rearrange("p (c t) -> p c t", t=TC),
          dmat_bf[:, 4096:8192].rearrange("p (c t) -> p c t", t=TC)]
    s1bf = dmat_bf[:, 4096:8192].rearrange("p (c t) -> p c t", t=TC)
    s1acc = powm
    tmpv = camax
    s1tm_st = xn_st
    xnf1_4d = xnf1_dram.rearrange("p c (l t) -> p c l t", t=TC)
    tsc_s1 = {0: 0, 1: 0}
    for ct in range(8):
        tl = DMAL(xnf_seg[:].rearrange("p c (l t) -> p c l t", t=64), xnf1_4d[:, :, :, 64*ct:64*ct+64])
        WL("tensor", tl)
        tsc_pp = {0: 0, 1: 0}
        for si, dsts in enumerate((qC, kC, vC)):
            for mi in range(NCH):
                gidx = si * NCH + mi
                pp = psA if gidx % 2 == 0 else psB
                WT("tensor", "scalar", tsc_pp[gidx % 2])
                for c in range(NCH):
                    if si < 2:
                        lh = wq_sb[:, si, c, 128*mi:128*(mi+1)]
                    else:
                        lh = wstage[:, c, 128*mi:128*(mi+1)]
                    E("tensor", lambda e, lh=lh, c=c, pp=pp: e.matmul(pp[:], lh, xnf_seg[:, c, :], start=(c == 0), stop=(c == NCH-1)))
                tm = SIG("tensor")
                WT("scalar", "tensor", tm)
                E("scalar", lambda e, dsts=dsts, mi=mi, pp=pp: e.copy(out=dsts[:, mi, :], in_=pp[:]))
                tsc_pp[gidx % 2] = SIG("scalar")
        BAR()
        tm_sel = {}
        tv_evac = {}
        for j in range(11):
            dlt = j - 5
            lo = max(0, -64*dlt); hi = 512 - max(0, 64*dlt)
            if j >= 2:
                WT("vector", "tensor", tm_sel[j - 2])
            if lo > 0:
                E("vector", lambda e, lo=lo, j=j: e.memset(pC[j % 2][:, :, 0:lo], 0.0))
            if hi < TC:
                E("vector", lambda e, hi=hi, j=j: e.memset(pC[j % 2][:, :, hi:TC], 0.0))
            E("vector", lambda e, lo=lo, hi=hi, dlt=dlt, j=j: e.tensor_tensor(
                out=pC[j % 2][:, :, lo:hi], in0=qC[:, :, lo:hi], in1=kC[:, :, lo+64*dlt:hi+64*dlt], op=ALU.mult))
            tv_prod = SIG("vector")
            if j >= 1:
                WT("tensor", "vector", tv_evac[j - 1])
            for c in range(NCH):
                E("tensor", lambda e, j=j, c=c: e.matmul(psR[0:16, :], peT[:, j*8+c, :], qC[:, c, :], start=(c == 0), stop=False))
            WT("tensor", "vector", tv_prod)
            for c in range(NCH):
                E("tensor", lambda e, c=c, j=j: e.matmul(psR[0:16, :], selT[:, c, :], pC[j % 2][:, c, :], start=False, stop=(c == NCH-1)))
            tm_sel[j] = SIG("tensor")
            WT("vector", "tensor", tm_sel[j])
            E("vector", lambda e, j=j: e.tensor_copy(out=e_buf[:, j, :], in_=psR[0:16, :]))
            tv_evac[j] = SIG("vector")
        BAR()
        E("vector", lambda e: e.tensor_copy(out=mrow[:], in_=e_buf[:, 0, :]))
        for j in range(1, 11):
            E("vector", lambda e, j=j: e.tensor_tensor(out=mrow[:], in0=mrow[:], in1=e_buf[:, j, :], op=ALU.max))
        E("vector", lambda e: e.tensor_scalar(out=m32[:], in0=mrow[:], scalar1=0.03125, scalar2=None, op0=ALU.mult))
        SIG("vector")
        for j in range(11):
            E("vector", lambda e, j=j: e.tensor_scalar(out=e_buf[:, j, :], in0=e_buf[:, j, :], scalar1=0.03125, scalar2=None, op0=ALU.mult))
            E("vector", lambda e, j=j: e.tensor_sub(out=e_buf[:, j, :], in0=e_buf[:, j, :], in1=m32[:]))
            tv = SIG("vector")
            WT("scalar", "vector", tv)
            E("scalar", lambda e, j=j: e.activation(out=e_buf[:, j, :], in_=e_buf[:, j, :], func=AF.Exp))
            tsc = SIG("scalar")
            WT("vector", "scalar", tsc)
            if j == 0:
                E("vector", lambda e: e.tensor_copy(out=denb[:], in_=e_buf[:, 0, :]))
            else:
                E("vector", lambda e, j=j: e.tensor_add(out=denb[:], in0=denb[:], in1=e_buf[:, j, :]))
        E("vector", lambda e: e.reciprocal(out=rden[:], in_=denb[:]))
        E("vector", lambda e: e.memset(s1acc[:].rearrange("p c t -> p (c t)"), 0.0))
        SIG("vector")
        BAR()
        tv_pp = {0: 0, 1: 0}
        for j in range(11):
            dlt = j - 5
            lo = max(0, -64*dlt); hi = 512 - max(0, 64*dlt)
            E("vector", lambda e, j=j: e.tensor_tensor(out=abf[j % 2][:, :], in0=e_buf[:, j, :], in1=rden[:], op=ALU.mult))
            tv_abf = SIG("vector")
            WT("tensor", "vector", tv_abf)
            for c in range(NCH):
                pp = psA if c % 2 == 0 else psB
                WT("tensor", "vector", tv_pp[c % 2])
                E("tensor", lambda e, c=c, pp=pp, lo=lo, hi=hi, j=j: e.matmul(pp[:, lo:hi], rselT[:, c, :], abf[j % 2][:, lo:hi], start=True, stop=True))
                tm = SIG("tensor")
                WT("vector", "tensor", tm)
                E("vector", lambda e, c=c, pp=pp, lo=lo, hi=hi, dlt=dlt: e.tensor_tensor(out=tmpv[:, lo:hi], in0=pp[:, lo:hi], in1=vC[:, c, lo+64*dlt:hi+64*dlt], op=ALU.mult))
                E("vector", lambda e, c=c, lo=lo, hi=hi: e.tensor_add(out=s1acc[:, c, lo:hi], in0=s1acc[:, c, lo:hi], in1=tmpv[:, lo:hi]))
                tv_pp[c % 2] = SIG("vector")
        BAR()
        for c in range(NCH):
            E("vector", lambda e, c=c: e.tensor_copy(out=s1bf[:, c, :], in_=s1acc[:, c, :]))
        SIG("vector")
        BAR()
        s1st_t = {}
        for q4 in range(4):
            for c in range(NCH):
                par = c % 2
                pp = psT if par == 0 else psU
                WT("tensor", "scalar", tsc_s1[par])
                E("tensor", lambda e, q4=q4, c=c, pp=pp: e.transpose(pp[:], s1bf[:, c, 128*q4:128*(q4+1)], ident[:]))
                tm = SIG("tensor")
                WT("scalar", "tensor", tm)
                if c == 0 and q4 >= 2:
                    WS("scalar", s1st_t[q4 - 2])
                E("scalar", lambda e, q4=q4, c=c, pp=pp: e.copy(out=s1tm_st[:, q4 % 2, 128*c:128*(c+1)], in_=pp[:]))
                tsc_s1[par] = SIG("scalar")
            WT("sync", "scalar")
            s1st_t[q4] = DMAS(s1_dram[ct, 128*q4:128*(q4+1), :], s1tm_st[:, q4 % 2, :])
        BAR()
    BAR()
    WS("sync", s_st.n)
    BAR()


    kv_bf = kv_t[:].rearrange("p a c t -> p (a c t)").bitcast(BF16)
    wq_bf = wq_sb[:].rearrange("p s c d -> p (s c d)")
    rw1a = kv_bf[:, 0:16384].rearrange("p (c h) -> p c h", h=4096)
    rw1b = wq_bf[:, 0:16384].rearrange("p (c h) -> p c h", h=4096)
    hbuf0 = dmat_bf[:, 0:8192].rearrange("p (c t) -> p c t", t=TC)
    hbuf1 = powm_bf[:, 0:8192].rearrange("p (c t) -> p c t", t=TC)
    sd = scan_ar[:].rearrange("p a t -> p (a t)")
    sdi = sd.bitcast(I32)
    lg_tm = sd[:, 0:256].rearrange("p (i e) -> p i e", e=8)
    sel0 = sd[:, 256:512].rearrange("p (i e) -> p i e", e=8)
    sel1 = sd[:, 512:768].rearrange("p (i e) -> p i e", e=8)
    cmpD = sd[:, 768:1024].rearrange("p (i e) -> p i e", e=8)
    rnk8 = sd[:, 1024:1280].rearrange("p (i e) -> p i e", e=8)
    e0f = sd[:, 1280:1312]; e1f = sd[:, 1312:1344]
    p0c = sd[:, 1344:1376]; p1c = sd[:, 1376:1408]
    pos0 = sd[:, 1408:1440]; pos1 = sd[:, 1440:1472]
    keep0 = sd[:, 1472:1504]; keep1 = sd[:, 1504:1536]
    sid0 = sd[:, 1536:1568]; sid1 = sd[:, 1568:1600]
    tmp32 = sd[:, 1600:1632]; tmp32b = sd[:, 1632:1664]
    sid0_i = sdi[:, 1664:1696]; sid1_i = sdi[:, 1696:1728]
    isc0_i = sdi[:, 1728:1760]; isc1_i = sdi[:, 1760:1792]
    tmp8a = sd[:, 1792:1800]; tmp8b = sd[:, 1800:1808]
    vaC = sd[:, 1808:1840]; vbC = sd[:, 1840:1872]; dvab = sd[:, 1872:1904]
    rw2_sb = a_scan[:].bitcast(BF16)[:, 0:256].rearrange("p (c e) -> p c e", e=8)
    base_rep = b_scan[:, 0:64]
    eb1_sb = b_scan[:, 64:96]
    rb1_sb = b_scan[:, 96:128]
    rb2_sb = b_scan[:, 128:129]
    corebase_sb = b_scan[:, 129:130]
    crep = b_scan[:, 130:131]
    payst = b_scan[:, 131:133]
    trib8_sb = b_scan[:, 192:256]
    ilst = b_scan[0:8, 256:512].rearrange("p (t j) -> p t j", j=2)
    tpst = b_scan[0:16, 256:384]
    lgf = ev_t[0:16, :]
    DmatF = Dmat[:].rearrange("p c t -> p (c t)")
    A_all = DmatF[:, 0:1024]
    onesD = DmatF[:, 1024:2048]
    Acum = DmatF[:, 2048:3072]
    Acum3 = DmatF[:, 2048:3072].rearrange("p (t j) -> p t j", j=2)
    cnt_s = DmatF[0:1, 3072:3584]
    cntl_s = DmatF[0:1, 3584:3648]
    base1_s = DmatF[0:1, 3648:3712]
    base2_s = DmatF[0:1, 3712:3776]
    baseT_s = DmatF[0:1, 3776:3840]
    mask_sb = csum[0:1, 0:512]
    for k in range(16):
        tl = DMAL(cast_f32[:, :], rw1T_in.rearrange("p c h -> p (c h)")[:, 2048*k:2048*(k+1)])
        WL("vector", tl)
        dstap = (kv_bf if k < 8 else wq_bf)[:, 2048*(k % 8):2048*(k % 8)+2048]
        E("vector", lambda e, dstap=dstap: e.tensor_copy(out=dstap, in_=cast_f32[:, :]))
        tv = SIG("vector")
        WT("sync", "vector", tv)
    tl = DMAL(cast_f32[:, 0:256].rearrange("p (c e) -> p c e", e=8), rw2T_in.rearrange("c p e -> p c e"))
    WL("vector", tl)
    E("vector", lambda e: e.tensor_copy(out=rw2_sb[:].rearrange("p c e -> p (c e)"), in_=cast_f32[:, 0:256]))
    SIG("vector")
    DMAL(rb1_sb, rb1_in[:, :])
    DMAL(rb2_sb[0:8, :], rb2_in[:, :])
    DMAL(trib8_sb, trib8_in[:, :])
    DMAL(mskc512_in_sb := mask_sb, mskc512_in[:, :])
    DMAL(corebase_sb[0:1, :], corebase_in[:, :])
    DMAL(eb1_sb, eb1_in[:, :])
    E("vector", lambda e: e.memset(lgf[:, :], 0.0))
    SIG("vector")
    BAR()
    for grp in range(8):
        tl = DMAL(xnf_seg[:], xnf1_dram[:, :, grp*TC:(grp+1)*TC])
        WL("tensor", tl)
        tsc_pp = {0: 0, 1: 0}
        for hc in range(32):
            pp = psA if hc % 2 == 0 else psB
            WT("tensor", "scalar", tsc_pp[hc % 2])
            for c in range(NCH):
                lh = (rw1a if c < 4 else rw1b)[:, c % 4, 128*hc:128*hc+128]
                E("tensor", lambda e, lh=lh, c=c, pp=pp: e.matmul(pp[:], lh, xnf_seg[:, c, :], start=(c == 0), stop=(c == NCH-1)))
            tm = SIG("tensor")
            WT("scalar", "tensor", tm)
            hb = (hbuf0 if hc < 16 else hbuf1)
            E("scalar", lambda e, hb=hb, hc=hc, pp=pp: e.activation(out=hb[:, hc % 16, :], in_=pp[:], func=AF.Relu, bias=rb1_sb[:, hc:hc+1]))
            tsc_pp[hc % 2] = SIG("scalar")
        BAR()
        for hc in range(32):
            hb = (hbuf0 if hc < 16 else hbuf1)
            E("tensor", lambda e, hb=hb, hc=hc: e.matmul(psR[0:8, :], rw2_sb[:, hc, :], hb[:, hc % 16, :], start=(hc == 0), stop=(hc == 31)))
        tm = SIG("tensor")
        WT("scalar", "tensor", tm)
        E("scalar", lambda e: e.activation(out=lgf[0:8, :], in_=psR[0:8, :], func=AF.Identity, bias=rb2_sb[0:8, 0:1]))
        tsc = SIG("scalar")
        WT("tensor", "scalar", tsc)
        for q4 in range(4):
            E("tensor", lambda e, q4=q4: e.transpose(psTf[:], lgf[:, 128*q4:128*(q4+1)], identf[0:16, :]))
            tm = SIG("tensor")
            WT("vector", "tensor", tm)
            E("vector", lambda e, grp=grp, q4=q4: e.tensor_copy(out=lg_tm[:, grp*4+q4, :], in_=psTf[:, 0:8]))
            tv = SIG("vector")
            WT("tensor", "vector", tv)
        BAR()
    WT("sync", "vector")
    E("vector", lambda e: e.memset(rnk8[:].rearrange("p i e -> p (i e)"), 0.0))
    for e2 in range(8):
        E("vector", lambda e, e2=e2: e.tensor_tensor(out=cmpD[:], in0=lg_tm[:, :, e2:e2+1].broadcast_to([128, NTILE, 8]), in1=lg_tm[:], op=ALU.is_gt))
        E("vector", lambda e: e.tensor_add(out=rnk8[:], in0=rnk8[:], in1=cmpD[:]))
        E("vector", lambda e, e2=e2: e.tensor_tensor(out=cmpD[:], in0=lg_tm[:, :, e2:e2+1].broadcast_to([128, NTILE, 8]), in1=lg_tm[:], op=ALU.is_equal))
        E("vector", lambda e, e2=e2: e.tensor_tensor(out=cmpD[:], in0=cmpD[:], in1=trib8_sb[:, e2*8:(e2+1)*8].unsqueeze(1).broadcast_to([128, NTILE, 8]), op=ALU.mult))
        E("vector", lambda e: e.tensor_add(out=rnk8[:], in0=rnk8[:], in1=cmpD[:]))
    E("vector", lambda e: e.tensor_scalar(out=sel0[:].rearrange("p i e -> p (i e)"), in0=rnk8[:].rearrange("p i e -> p (i e)"), scalar1=0.0, scalar2=None, op0=ALU.is_equal))
    E("vector", lambda e: e.tensor_scalar(out=sel1[:].rearrange("p i e -> p (i e)"), in0=rnk8[:].rearrange("p i e -> p (i e)"), scalar1=1.0, scalar2=None, op0=ALU.is_equal))
    for sel, ef, vf in ((sel0, e0f, vaC), (sel1, e1f, vbC)):
        E("vector", lambda e, sel=sel: e.tensor_tensor(out=cmpD[:], in0=sel[:], in1=io_col[:, 0:8].unsqueeze(1).broadcast_to([128, NTILE, 8]), op=ALU.mult))
        E("vector", lambda e, ef=ef: e.tensor_reduce(out=ef.rearrange("p (i q) -> p i q", q=1), in_=cmpD[:], axis=AX.X, op=ALU.add))
        E("vector", lambda e, sel=sel: e.tensor_tensor(out=cmpD[:], in0=sel[:], in1=lg_tm[:], op=ALU.mult))
        E("vector", lambda e, vf=vf: e.tensor_reduce(out=vf.rearrange("p (i q) -> p i q", q=1), in_=cmpD[:], axis=AX.X, op=ALU.add))
    E("vector", lambda e: e.tensor_sub(out=dvab, in0=vaC, in1=vbC))
    tv = SIG("vector")
    WT("scalar", "vector", tv)
    E("scalar", lambda e: e.activation(out=p0c, in_=dvab, func=AF.Sigmoid))
    tsc = SIG("scalar")
    WT("vector", "scalar", tsc)
    E("vector", lambda e: e.tensor_scalar(out=p1c, in0=p0c, scalar1=-1.0, scalar2=1.0, op0=ALU.mult, op1=ALU.add))
    E("vector", lambda e: e.memset(A_all, 0.0))
    E("vector", lambda e: e.memset(onesD, 1.0))
    SIG("vector")
    BAR()
    ilst_t = 0
    tsc_a = 0
    for i in range(NTILE):
        l = i // 4
        for j, sel in ((0, sel0), (1, sel1)):
            WT("tensor", "scalar", tsc_a)
            E("tensor", lambda e, sel=sel, i=i: e.transpose(psTf[0:8, :], sel[:, i, :], identf[:]))
            tm = SIG("tensor")
            WT("scalar", "tensor", tm)
            if j == 0:
                WS("scalar", ilst_t)
            E("scalar", lambda e, j=j: e.copy(out=ilst[:, :, j], in_=psTf[0:8, :]))
            tsc_a = SIG("scalar")
        WT("sync", "scalar")
        ilst_t = DMAS(DmatF[l*8:l*8+8, 256*(i % 4):256*(i % 4)+256], ilst[:].rearrange("p t j -> p (t j)"))
    BAR()
    WS("vector", s_st.n)
    E("vector", lambda e: e.tensor_tensor_scan(out=Acum, data0=onesD, data1=A_all, initial=zcol[:, 0:1], op0=ALU.mult, op1=ALU.add))
    tv = SIG("vector")
    WT("sync", "vector", tv)
    tst = DMAS(cnt2_in[:, :], DmatF[0:64, 3071:3072])
    WS("gpsimd", tst)
    BAR()
    E("gpsimd", lambda e: e.collective_compute("AllGather", ALU.bypass, replica_groups=[list(range(NCORE))],
                                               ins=[cnt2_in[:, :]], outs=[cnt2_out[:, :, :]]))
    SIG("gpsimd")
    BAR()
    WT("sync", "gpsimd")
    tl = DMAL(cnt_s, cnt2_out.rearrange("c q o -> o (c q)"))
    WL("vector", tl)
    E("vector", lambda e: e.tensor_copy(out=cntl_s, in_=cnt_s[0:1, 0:64]))
    for cc in range(1, 8):
        E("vector", lambda e, cc=cc: e.tensor_add(out=cntl_s, in0=cntl_s, in1=cnt_s[0:1, 64*cc:64*cc+64]))
    E("vector", lambda e: e.memset(base1_s[0:1, 0:8], 0.0))
    for l in range(1, 8):
        E("vector", lambda e, l=l: e.tensor_add(out=base1_s[0:1, 8*l:8*l+8], in0=base1_s[0:1, 8*(l-1):8*l], in1=cntl_s[0:1, 8*(l-1):8*l]))
    E("vector", lambda e: e.tensor_tensor(out=cnt_s, in0=cnt_s, in1=mask_sb, op=ALU.mult))
    E("vector", lambda e: e.tensor_copy(out=base2_s, in_=cnt_s[0:1, 0:64]))
    for cc in range(1, 8):
        E("vector", lambda e, cc=cc: e.tensor_add(out=base2_s, in0=base2_s, in1=cnt_s[0:1, 64*cc:64*cc+64]))
    E("vector", lambda e: e.tensor_add(out=baseT_s, in0=base1_s, in1=base2_s))
    SIG("vector")
    BAR()
    E("tensor", lambda e: e.matmul(psR[:, 0:64], onesNf[:], baseT_s, start=True, stop=True))
    tm = SIG("tensor")
    WT("vector", "tensor", tm)
    E("vector", lambda e: e.tensor_copy(out=base_rep, in_=psR[:, 0:64]))
    E("vector", lambda e: e.memset(tpst[:, :], 0.0))
    SIG("vector")
    BAR()
    tm_tp = 0
    for i in range(NTILE):
        l = i // 4; toff = (i % 4) * 128
        for j, sel, posX in ((0, sel0, pos0), (1, sel1, pos1)):
            WT("sync", "tensor", tm_tp)
            tst = DMAL(tpst[0:8, :], Acum3[l*8:l*8+8, toff:toff+128, j], allow_slow_non_contiguous=True)
            WL("tensor", tst)
            E("tensor", lambda e: e.transpose(psTf[:], tpst[:, :], identf[0:16, :]))
            tm_tp = SIG("tensor")
            WT("vector", "tensor", tm_tp)
            E("vector", lambda e: e.tensor_copy(out=tmp8a, in_=psTf[:, 0:8]))
            E("vector", lambda e, l=l: e.tensor_tensor(out=tmp8b, in0=tmp8a, in1=base_rep[:, l*8:l*8+8], op=ALU.add))
            E("vector", lambda e, sel=sel, i=i: e.tensor_tensor(out=tmp8b, in0=tmp8b, in1=sel[:, i, :], op=ALU.mult))
            E("vector", lambda e, posX=posX, i=i: e.tensor_reduce(out=posX[:, i:i+1].rearrange("p (i q) -> p i q", q=1), in_=tmp8b.rearrange("p (i e) -> p i e", e=8), axis=AX.X, op=ALU.add))
            E("vector", lambda e, posX=posX, i=i: e.tensor_scalar(out=posX[:, i:i+1], in0=posX[:, i:i+1], scalar1=-1.0, scalar2=None, op0=ALU.add))
            tv = SIG("vector")
    BAR()
    for posX, keepX, eXf, sidX, sidX_i, iscX_i in ((pos0, keep0, e0f, sid0, sid0_i, isc0_i), (pos1, keep1, e1f, sid1, sid1_i, isc1_i)):
        E("vector", lambda e, posX=posX, keepX=keepX: e.tensor_scalar(out=keepX, in0=posX, scalar1=4096.0, scalar2=None, op0=ALU.is_lt))
        E("vector", lambda e, posX=posX: e.tensor_scalar(out=tmp32, in0=posX, scalar1=4095.0, scalar2=None, op0=ALU.min))
        E("vector", lambda e, eXf=eXf: e.tensor_scalar(out=tmp32b, in0=eXf, scalar1=4096.0, scalar2=None, op0=ALU.mult))
        E("vector", lambda e, sidX=sidX: e.tensor_add(out=sidX, in0=tmp32b, in1=tmp32))
        E("vector", lambda e, sidX=sidX, sidX_i=sidX_i: e.tensor_copy(out=sidX_i, in_=sidX))
        E("vector", lambda e, posX=posX: e.tensor_scalar(out=tmp32, in0=posX, scalar1=5119.0, scalar2=None, op0=ALU.min))
        E("vector", lambda e, eXf=eXf: e.tensor_scalar(out=tmp32b, in0=eXf, scalar1=5120.0, scalar2=None, op0=ALU.mult))
        E("vector", lambda e: e.tensor_add(out=tmp32, in0=tmp32, in1=tmp32b))
        E("vector", lambda e, iscX_i=iscX_i: e.tensor_copy(out=iscX_i, in_=tmp32))
    SIG("vector")
    WT("sync", "vector")
    BAR()
    WS("sync", s_st.n)
    BAR()


    xqf = xq_t[:].rearrange("p c t -> p (c t)").bitcast(F32)
    xqi = xq_t[:].rearrange("p c t -> p (c t)").bitcast(I32)
    eb2r = xqf[:, 0:1024]
    disp_sb = xqf[:, 1024:1088].rearrange("p (a w) -> p a w", w=2)
    idc_i = xqi[:, 1088:1120]
    E("vector", lambda e: e.memset(cast_f32[:, 0:640], 0.0))
    tv = SIG("vector")
    WT("sync", "vector", tv)
    tz = DMAS(dispT.rearrange("(p n) w -> p (n w)", p=128), cast_f32[:, 0:640])
    E("tensor", lambda e: e.matmul(psR[:, 0:1], onesNf[:], corebase_sb[0:1, :], start=True, stop=True))
    tm = SIG("tensor")
    WT("vector", "tensor", tm)
    E("vector", lambda e: e.tensor_copy(out=crep, in_=psR[:, 0:1]))
    E("vector", lambda e: e.tensor_add(out=crep, in0=crep, in1=io_row[:, 0:1]))
    SIG("vector")
    WS("gpsimd", tz)
    tgd = 0
    for i in range(NTILE):
        l = i // 4; toff = (i % 4) * 128
        for j, pc, iscX in ((0, p0c, isc0_i), (1, p1c, isc1_i)):
            WG("vector", tgd)
            E("vector", lambda e, l=l, toff=toff: e.tensor_scalar(out=payst[:, 0:1], in0=crep, scalar1=float(l*512 + toff), scalar2=None, op0=ALU.add))
            E("vector", lambda e, pc=pc, i=i: e.tensor_copy(out=payst[:, 1:2], in_=pc[:, i:i+1]))
            tv = SIG("vector")
            WT("gpsimd", "vector", tv)
            tgd = GDMA(dispT[:, :], payst, out_off=IndirectOffsetOnAxis(iscX[:, i:i+1], 0))
    BAR()
    E("gpsimd", lambda e: e.collective_compute("ReduceScatter", ALU.add, replica_groups=[list(range(NCORE))],
                                               ins=[dispT[:, :]], outs=[disp_rs[:, :]]))
    SIG("gpsimd")
    BAR()
    tl = DMAL(disp_sb[:], disp_rs[0:4096, :].rearrange("(a p) w -> p a w", p=128))
    tl2 = DMAL(eb2r, eb2_in[:, :])
    WL("vector", tl)
    E("vector", lambda e: e.tensor_copy(out=idc_i, in_=disp_sb[:, :, 0]))
    SIG("vector")
    BAR()
    for mat_src, mat_dst in ((ew1T_in, ew1_bf), (ew2T_in, ew2_bf)):
        msrc = mat_src.rearrange("p c h -> p (c h)")
        mdst = mat_dst.rearrange("p c h -> p (c h)")
        for k in range(16):
            tl = DMAL(cast_f32[:, :], msrc[:, 2048*k:2048*(k+1)])
            WL("vector", tl)
            E("vector", lambda e: e.tensor_copy(out=cast_bf[:, :], in_=cast_f32[:, :]))
            tv = SIG("vector")
            WT("sync", "vector", tv)
            tstw = DMAS(mdst[:, 2048*k:2048*(k+1)], cast_bf[:, :])
            WS("vector", tstw)
            WS("sync", tstw)
    BAR()
    xeF = [(kv_bf if t < 4 else wq_bf)[:, 4096*(t % 4):4096*(t % 4)+4096].rearrange("p (c s) -> p c s", s=TC) for t in range(8)]
    xg_st = [camax[:].bitcast(BF16)[:, 0:1024], csum[:].bitcast(BF16)[:, 0:1024]]
    xn1_flat = xn1_ag.rearrange("c n d -> (c n) d")
    tm_xg = {0: 0, 1: 0}
    tsc_xe = {0: 0, 1: 0}
    for t in range(8):
        for q in range(4):
            a = t * 4 + q
            pb = a % 2
            WT("gpsimd", "tensor", tm_xg[pb])
            tg = GDMA(xg_st[pb], xn1_flat, in_off=IndirectOffsetOnAxis(idc_i[:, a:a+1], 0))
            WG("tensor", tg)
            for c in range(NCH):
                par = c % 2
                pp = psT if par == 0 else psU
                WT("tensor", "scalar", tsc_xe[par])
                E("tensor", lambda e, pb=pb, c=c, pp=pp: e.transpose(pp[:], xg_st[pb][:, 128*c:128*(c+1)], ident[:]))
                tmx = SIG("tensor")
                WT("scalar", "tensor", tmx)
                E("scalar", lambda e, t=t, c=c, q=q, pp=pp: e.copy(out=xeF[t][:, c, 128*q:128*(q+1)], in_=pp[:]))
                tsc_xe[par] = SIG("scalar")
            tm_xg[pb] = C.sems["tensor"].n
    BAR()
    mixf = mix_t[:].rearrange("p c t -> p (c t)")
    ew1st = [mixf[:, 0:1024].rearrange("p (c d) -> p c d", d=128), mixf[:, 1024:2048].rearrange("p (c d) -> p c d", d=128)]
    hst = [mixf[:, 2048:2560], mixf[:, 2560:3072]]
    tls = {0: DMAL(ew1st[0][:, :, :], ew1_bf[:, :, 0:128])}
    tm_done = {}
    hst_st = {0: 0, 1: 0}
    tsc_pp = {0: 0, 1: 0}
    for hc in range(32):
        if hc + 1 < 32:
            if hc - 1 >= 0:
                WT("sync", "tensor", tm_done[hc - 1])
            tls[hc + 1] = DMAL(ew1st[(hc + 1) % 2][:, :, :], ew1_bf[:, :, 128*(hc+1):128*(hc+2)])
        WL("tensor", tls[hc])
        for t in range(8):
            pp = psA if t % 2 == 0 else psB
            WT("tensor", "scalar", tsc_pp[t % 2])
            for c in range(NCH):
                E("tensor", lambda e, hc=hc, c=c, t=t, pp=pp: e.matmul(pp[:], ew1st[hc % 2][:, c, :], xeF[t][:, c, :], start=(c == 0), stop=(c == NCH-1)))
            tmx = SIG("tensor")
            WT("scalar", "tensor", tmx)
            WS("scalar", hst_st[t % 2])
            E("scalar", lambda e, t=t, hc=hc: e.activation(out=hst[t % 2], in_=(psA if t % 2 == 0 else psB)[:], func=AF.Relu, bias=eb1_sb[:, hc:hc+1]))
            tsc_pp[t % 2] = SIG("scalar")
            WT("sync", "scalar", tsc_pp[t % 2])
            hst_st[t % 2] = DMAS(h_dram[t, hc, :, :], hst[t % 2])
        tm_done[hc] = C.sems["tensor"].n
    BAR()
    WS("sync", s_st.n)
    for hc in range(32):
        dst = (kv_bf if hc < 16 else wq_bf)[:, 1024*(hc % 16):1024*(hc % 16)+1024]
        DMAL(dst, ew2_bf[:, hc, :])
    BAR()
    yo_bf2 = camax[:].bitcast(BF16)[:, 0:1024]
    tmpW = cast_f32[:, 0:1024]
    for t in range(8):
        tlh1 = DMAL(hbuf0[:, :, :], h_dram[t, 0:16].rearrange("c p t -> p c t"))
        tlh2 = DMAL(hbuf1[:, :, :], h_dram[t, 16:32].rearrange("c p t -> p c t"))
        WL("tensor", tlh2)
        tv_yo = 0
        for s4 in range(4):
            a = t * 4 + s4
            WT("tensor", "vector", tv_yo)
            for half in range(2):
                for hc in range(32):
                    hb = hbuf0 if hc < 16 else hbuf1
                    ew2r = (kv_bf if hc < 16 else wq_bf)[:, 1024*(hc % 16):1024*(hc % 16)+1024]
                    E("tensor", lambda e, hb=hb, hc=hc, s4=s4, half=half, ew2r=ew2r: e.matmul(
                        psW[:, 512*half:512*(half+1)], hb[:, hc % 16, 128*s4:128*(s4+1)], ew2r[:, 512*half:512*(half+1)], start=(hc == 0), stop=(hc == 31)))
            tmx = SIG("tensor")
            WT("vector", "tensor", tmx)
            E("vector", lambda e: e.tensor_add(out=tmpW, in0=psW[:], in1=eb2r))
            E("vector", lambda e, a=a: e.tensor_scalar(out=yo_bf2, in0=tmpW, scalar1=disp_sb[:, a, 1:2], scalar2=None, op0=ALU.mult))
            tv_yo = SIG("vector")
            WT("sync", "vector", tv_yo)
            tst = DMAS(yo_dram[512*t+128*s4 : 512*t+128*s4+128, :], yo_bf2)
            WS("vector", tst)
        BAR()
    BAR()
    WS("sync", s_st.n)
    BAR()
    E("gpsimd", lambda e: e.collective_compute("AllGather", ALU.bypass, replica_groups=[list(range(NCORE))],
                                               ins=[yo_dram[:, :]], outs=[yo_ag[:, :, :]]))
    SIG("gpsimd")
    E("gpsimd", lambda e: e.collective_compute("AllGather", ALU.bypass, replica_groups=[list(range(NCORE))],
                                               ins=[cnt2_in[:, :]], outs=[cnt2_out[:, :, :]]))
    SIG("gpsimd")
    BAR()
    yoflat = yo_ag.rearrange("c n d -> (c n) d")
    g1acc = cast_f32[:, 0:1024]
    gsil2 = cast_f32[:, 1024:2048]
    ybg1 = camax[:].bitcast(BF16)[:, 0:1024]
    ybg2 = csum[:].bitcast(BF16)[:, 0:1024]
    s1sb = camax[:].bitcast(BF16)[:, 0:1024]
    tv_fin = 0
    to_st = 0
    for i in list(range(1, NTILE)) + [0]:
        l = i // 4
        WT("gpsimd", "vector", tv_fin)
        tg1 = GDMA(ybg1, yoflat, in_off=IndirectOffsetOnAxis(sid0_i[:, i:i+1], 0))
        tg2 = GDMA(ybg2, yoflat, in_off=IndirectOffsetOnAxis(sid1_i[:, i:i+1], 0))
        WG("vector", tg1)
        E("vector", lambda e, i=i: e.tensor_scalar(out=g1acc, in0=ybg1, scalar1=keep0[:, i:i+1], scalar2=0.0, op0=ALU.mult, op1=ALU.add))
        WG("vector", tg2)
        E("vector", lambda e, i=i: e.scalar_tensor_tensor(out=g1acc, in0=ybg2, scalar=keep1[:, i:i+1], in1=g1acc, op0=ALU.mult, op1=ALU.add))
        tv = SIG("vector")
        WT("sync", "vector", tv)
        tl1 = DMAL(x_tile[:, 0, :], x1_dram[i, :, :])
        tl2 = DMAL(s1sb.rearrange("p (q d) -> p q d", d=D)[0:64, 0, :], s1_dram[2*(i % 4), l*64:l*64+64, :])
        tl3 = DMAL(s1sb.rearrange("p (q d) -> p q d", d=D)[64:128, 0, :], s1_dram[2*(i % 4)+1, l*64:l*64+64, :])
        WT("scalar", "vector", tv)
        E("scalar", lambda e: e.activation(out=gsil2, in_=g1acc, func=AF.Silu))
        tsc = SIG("scalar")
        WL("vector", tl2)
        WL("vector", tl3)
        WL("vector", tl1)
        WT("vector", "scalar", tsc)
        WS("vector", to_st)
        E("vector", lambda e: e.tensor_tensor(out=gsil2, in0=s1sb, in1=gsil2, op=ALU.mult))
        E("vector", lambda e: e.tensor_tensor(out=x_tile[:, 1, :], in0=x_tile[:, 0, :], in1=gsil2, op=ALU.add))
        tv_fin = SIG("vector")
        WT("sync", "vector", tv_fin)
        to_st = DMAS(out_dram[i, :, :], x_tile[:, 1, :])
    BAR()
    WS("sync", s_st.n)
    BAR()

    @block.sync
    def _(eng):
        for f in C.q["sync"]:
            f(eng)

    @block.vector
    def _(eng):
        for f in C.q["vector"]:
            f(eng)

    @block.scalar
    def _(eng):
        for f in C.q["scalar"]:
            f(eng)

    @block.tensor
    def _(eng):
        for f in C.q["tensor"]:
            f(eng)

    @block.gpsimd
    def _(eng):
        for f in C.q["gpsimd"]:
            f(eng)

    st.close()
    return nc




_CACHE = {}


def _host_prep(ins):
    x = np.ascontiguousarray(ins["L0_attn_hidden"], np.float32)
    gw = ins["L0_ffn_gate_w"]; uw = ins["L0_ffn_up_w"]; dw = ins["L0_ffn_down_w"]
    gpart = np.transpose(np.ascontiguousarray(gw, np.float32).reshape(16, 512, 8, 128), (0, 3, 2, 1)).reshape(16, 128, 4096)
    upart = np.transpose(np.ascontiguousarray(uw, np.float32).reshape(16, 512, 8, 128), (0, 3, 2, 1)).reshape(16, 128, 4096)
    dpart = np.transpose(np.ascontiguousarray(dw, np.float32).reshape(16, 1024, 4, 128), (0, 3, 2, 1)).reshape(16, 128, 4096)
    wcat = np.ascontiguousarray(np.concatenate([gpart, upart, dpart], axis=2), np.float32)
    shg = np.ascontiguousarray(np.transpose(np.asarray(ins["L0_ffn_sh_gate_w"], np.float32).reshape(1024, 8, 128), (2, 1, 0)))
    shu = np.ascontiguousarray(np.transpose(np.asarray(ins["L0_ffn_sh_up_w"], np.float32).reshape(1024, 8, 128), (2, 1, 0)))
    shd = np.ascontiguousarray(np.transpose(np.asarray(ins["L0_ffn_sh_down_w"], np.float32).reshape(1024, 8, 128), (2, 1, 0)))
    wqT = np.ascontiguousarray(np.transpose(np.asarray(ins["L1_attn_Wq"], np.float32).reshape(1024, 8, 128), (2, 1, 0)))
    wkT = np.ascontiguousarray(np.transpose(np.asarray(ins["L1_attn_Wk"], np.float32).reshape(1024, 8, 128), (2, 1, 0)))
    wvT = np.ascontiguousarray(np.transpose(np.asarray(ins["L1_attn_Wv"], np.float32).reshape(1024, 8, 128), (2, 1, 0)))
    pe = np.asarray(ins["L1_attn_pe"], np.float32)
    pe_arr = np.zeros((88, 128, 16), np.float32)
    selc = np.zeros((8, 128, 16), np.float32)
    rsel = np.zeros((16, 8, 128), np.float32)
    for c8 in range(8):
        for hh in range(2):
            h = 2*c8 + hh
            for j in range(11):
                pe_arr[j*8+c8, 64*hh:64*hh+64, h] = pe[h, :, j]
            selc[c8, 64*hh:64*hh+64, h] = 1.0
            rsel[h, c8, 64*hh:64*hh+64] = 1.0
    selc_bf = selc.astype(ml_dtypes.bfloat16)
    rsel_bf = rsel.astype(ml_dtypes.bfloat16)
    rw1T = np.ascontiguousarray(np.transpose(np.asarray(ins["L1_ffn_rw1"], np.float32).reshape(8, 128, 4096), (1, 0, 2)))
    rw2T = np.ascontiguousarray(np.asarray(ins["L1_ffn_rw2"], np.float32).reshape(32, 128, 8))
    rb1h = np.ascontiguousarray(np.asarray(ins["L1_ffn_rb1"], np.float32).reshape(32, 128).T)
    rb2h = np.asarray(ins["L1_ffn_rb2"], np.float32).reshape(8, 1)
    tri8 = (np.arange(8)[:, None] < np.arange(8)[None, :]).astype(np.float32).reshape(64)
    trib8h = np.tile(tri8, (128, 1)).astype(np.float32)
    ew1h = np.ascontiguousarray(np.transpose(np.asarray(ins["L1_ffn_ew1"], np.float32).reshape(8, 8, 128, 4096), (0, 2, 1, 3)))
    ew2h = np.ascontiguousarray(np.transpose(np.asarray(ins["L1_ffn_ew2"], np.float32).reshape(8, 32, 128, 1024), (0, 2, 1, 3)))
    eb1h = np.ascontiguousarray(np.transpose(np.asarray(ins["L1_ffn_eb1"], np.float32).reshape(8, 32, 128), (0, 2, 1)))
    eb2h = np.asarray(ins["L1_ffn_eb2"], np.float32)
    per_core = []
    for r in range(NCORE):
        xs = np.zeros((NB, SEG, D), np.float32)
        t0 = r * TC
        xs[:, 1:, :] = x[:, t0:t0+TC, :]
        if r > 0:
            xs[:, 0, :] = x[:, t0-1, :]
        m = {
            "x_in": xs,
            "w_key": np.asarray(ins["L0_attn_w_key"], np.float32),
            "w_value": np.asarray(ins["L0_attn_w_value"], np.float32),
            "w_recep": np.asarray(ins["L0_attn_w_receptance"], np.float32),
            "w_output": np.asarray(ins["L0_attn_w_output"], np.float32),
            "rms_key": np.asarray(ins["L0_attn_rms_key"], np.float32),
            "rms_value": np.asarray(ins["L0_attn_rms_value"], np.float32),
            "rms_recep": np.asarray(ins["L0_attn_rms_receptance"], np.float32),
            "rms_output": np.asarray(ins["L0_attn_rms_output"], np.float32),
            "tmix_k": np.asarray(ins["L0_attn_time_mix_key"], np.float32).ravel(),
            "tmix_v": np.asarray(ins["L0_attn_time_mix_value"], np.float32).ravel(),
            "tmix_r": np.asarray(ins["L0_attn_time_mix_receptance"], np.float32).ravel(),
            "tdecay": np.asarray(ins["L0_attn_time_decay"], np.float32),
            "tfirst": np.asarray(ins["L0_attn_time_first"], np.float32),
            "rankm": (np.arange(NCORE) < r).astype(np.float32),
            "rankw": np.maximum(r - 1 - np.arange(NCORE), 0).astype(np.float32),
            "wcat_in": wcat,
            "shg_in": shg, "shu_in": shu, "shd_in": shd,
            "rw_in": np.asarray(ins["L0_ffn_router_weight"], np.float32),
            "ebias_in": np.asarray(ins["L0_ffn_e_bias"], np.float32),
            "wqT_in": wqT, "wkT_in": wkT, "wvT_in": wvT,
            "pe_in": pe_arr, "selc_in": selc_bf, "rsel_in": rsel_bf,
            "rw1T_in": rw1T, "rw2T_in": rw2T, "rb1_in": rb1h, "rb2_in": rb2h,
            "trib8_in": trib8h,
            "mskc512_in": np.tile((np.arange(8) < r).astype(np.float32)[:, None], (1, 64)).reshape(1, 512),
            "corebase_in": np.array([[r * 4096.0]], np.float32),
            "ew1T_in": ew1h[r], "ew2T_in": ew2h[r], "eb1_in": eb1h[r],
            "eb2_in": np.tile(eb2h[r], (128, 1)),
        }
        per_core.append(m)
    return per_core


def kernel(**inputs):
    if "nc" not in _CACHE:
        _CACHE["nc"] = build_program()
    nc = _CACHE["nc"]
    per_core = _host_prep(inputs)
    res = run_bass_kernel_spmd(nc, per_core, core_ids=list(range(NCORE)))
    full = np.empty((NB, NTOK, D), np.float32)
    for r in range(NCORE):
        o = np.asarray(res.results[r]["out_dram"], np.float32).reshape(NB, TC, D)
        full[:, r*TC:(r+1)*TC, :] = o
    return full
